# version 03
# baseline (speedup 1.0000x reference)
import numpy as np
import os
import ml_dtypes
import concourse.bass as bass
import concourse.mybir as mybir
from concourse.bass_utils import run_bass_kernel_spmd
from contextlib import ExitStack

F32 = mybir.dt.float32
BF16 = mybir.dt.bfloat16
AF = mybir.ActivationFunctionType
ALU = mybir.AluOpType
AX = mybir.AxisListType

T = 4096
D = 1024
NCORES = 8

ENGINES = ("sync", "tensor", "vector", "scalar", "gpsimd")


class Sem:
    __slots__ = ("h", "count")

    def __init__(self, h):
        self.h = h
        self.count = 0


class P:

    def __init__(self, nc, ctx):
        self.nc = nc
        self.ctx = ctx
        self.sems = {}
        self.cur = None
        self.sbufs = {}
        self.psums = {}
        self.block_tag = ""
        self.sem_pfx = ""
        self.arena = None
        self.arena_off = 0
        self.psum_map = None
        self._hwm = {}

    def set_block(self, pfx, psmap=None):
        self._hwm[self.block_tag] = max(
            self._hwm.get(self.block_tag, 0), self.arena_off)
        self.block_tag = pfx
        self.sem_pfx = pfx
        self.psum_map = psmap
        self.arena_off = self._hwm.get(pfx, 0)

    def barrier(self, prev_pfx):
        for name, s in self.sems.items():
            if name.startswith(prev_pfx) and s.count > 0:
                if self.cur is not None:
                    getattr(self.nc, self.cur).wait_ge(s.h, s.count)

    def sem(self, name):
        name = self.sem_pfx + name
        if name not in self.sems:
            self.sems[name] = Sem(self.ctx.enter_context(self.nc.semaphore(name)))
        return self.sems[name]

    def sbuf(self, name, shape, dtype):
        key = (self.block_tag, name)
        if key not in self.sbufs:
            if self.arena is None:
                self.sbufs[key] = self.ctx.enter_context(
                    self.nc.sbuf_tensor(name, list(shape), dtype))
            else:
                pp, cc = shape
                nbytes = cc * (4 if dtype == F32 else 2)
                off = self.arena_off
                self.arena_off = off + ((nbytes + 31) // 32) * 32
                sl = self.arena[0:pp, off // 2: off // 2 + ((nbytes + 1) // 2)]
                self.sbufs[key] = sl if dtype == BF16 else sl.bitcast(dtype)
        return self.sbufs[key]

    def psum(self, name, shape, dtype=F32):
        key = (self.block_tag, name)
        if key not in self.psums:
            if self.psum_map is None:
                self.psums[key] = self.ctx.enter_context(
                    self.nc.psum_tensor(name, list(shape), dtype))
            else:
                self.psums[key] = self.psum_map[name]
        return self.psums[key]

    def begin(self, eng):
        self.cur = eng
        for s in self.sems.values():
            s.count = 0

    def wait(self, eng, sem, val=None):
        s = self.sems[self.sem_pfx + sem]
        v = s.count if val is None else val
        if self.cur == eng and v > 0:
            getattr(self.nc, eng).wait_ge(s.h, v)
        return v

    def op(self, eng, method, *args, inc=None, **kw):
        r = None
        if self.cur == eng:
            e = getattr(self.nc, eng)
            if eng in ("vector", "scalar", "gpsimd") and not \
                    method.startswith("dma"):
                e.drain()
            r = getattr(e, method)(*args, **kw)
        if inc is not None:
            s = self.sems[self.sem_pfx + inc]
            step = 16 if method.startswith("dma") else 1
            s.count += step
            if r is not None:
                r.then_inc(s.h, step)
        return r

    def dma(self, eng, out, in_, inc=None, **kw):
        return self.op(eng, "dma_start", inc=inc, out=out, in_=in_, **kw)

    def val(self, sem):
        return self.sems[self.sem_pfx + sem].count


class NCProxy:

    def __init__(self, nc, pfx, x_ap, xo_ap):
        self._nc = nc
        self._pfx = pfx
        self._x = x_ap
        self._xo = xo_ap

    def dram_tensor(self, name, shape, dtype, kind="Internal"):
        if name == "x" and kind == "ExternalInput":
            return _FakeT(self._x)
        if name == "xo" and kind == "ExternalOutput":
            return _FakeT(self._xo)
        k = "ExternalInput" if kind == "ExternalInput" else "Internal"
        return self._nc.dram_tensor(self._pfx + name, shape, dtype, kind=k)

    def __getattr__(self, a):
        return getattr(self._nc, a)


class _FakeT:
    def __init__(self, ap):
        self._ap = ap

    def ap(self):
        return self._ap


def emit_all(p, body):
    with p.nc.Block() as block:
        for eng in ENGINES:
            def make(e):
                def f(_):
                    p.begin(e)
                    body()
                return f
            getattr(block, eng)(make(eng))



def build_l1_ffn(fused=None):
    nc = fused["nc"] if fused else bass.Bass("TRN2", target_bir_lowering=False)
    x = nc.dram_tensor("x", [D, T], F32, kind="ExternalInput").ap()
    wr = nc.dram_tensor("wr_t", [D, D], F32, kind="ExternalInput").ap()
    wk = nc.dram_tensor("wk_t", [D, D], F32, kind="ExternalInput").ap()
    wv = nc.dram_tensor("wv_t", [D, D], F32, kind="ExternalInput").ap()
    mu_r = nc.dram_tensor("mu_r", [128, 8], F32, kind="ExternalInput").ap()
    mu_k = nc.dram_tensor("mu_k", [128, 8], F32, kind="ExternalInput").ap()
    xo = nc.dram_tensor("xo", [D, T], F32, kind="ExternalOutput").ap()

    NT = 8
    W = 512

    p = fused["p"] if fused else P(nc, ExitStack())
    wtiles = {}
    for nm in ("wr", "wk", "wv"):
        wtiles[nm] = [p.sbuf(f"{nm}_{kc}", [128, D], BF16) for kc in range(8)]
    mur = p.sbuf("mur", [128, 8], F32)
    muk = p.sbuf("muk", [128, 8], F32)
    omur = p.sbuf("omur", [128, 8], F32)
    omuk = p.sbuf("omuk", [128, 8], F32)
    ones_col = p.sbuf("ones_col", [128, 1], F32)
    ones_row = p.sbuf("ones_row", [1, 128], F32)
    eps_col = p.sbuf("eps_col", [128, 1], F32)

    xch = [[p.sbuf(f"x_{c}_{b}", [128, W + 1], F32) for b in range(2)]
           for c in range(8)]
    xln = [[p.sbuf(f"xln_{c}_{b}", [128, W + 1], F32) for b in range(2)]
           for c in range(8)]
    sq = [p.sbuf(f"sq_{b}", [128, W], F32) for b in range(2)]
    rb = p.sbuf("rb", [128, W], F32)
    mrb = p.sbuf("mrb", [128, W], F32)
    tsc = p.sbuf("tsc", [128, W], F32)
    xr16 = [p.sbuf(f"xr16_{c}", [128, W], BF16) for c in range(8)]
    xk16 = [p.sbuf(f"xk16_{c}", [128, W], BF16) for c in range(8)]
    rsig = [p.sbuf(f"rsig_{c}", [128, W], F32) for c in range(8)]
    k16a = [p.sbuf(f"k16a_{c}", [128, W], BF16) for c in range(8)]
    kr = [p.sbuf(f"kr_{b}", [128, W], F32) for b in range(2)]
    vout = [p.sbuf(f"vout_{b}", [128, W], F32) for b in range(2)]
    m_r = p.sbuf("m_r", [1, W], F32)
    q_r = p.sbuf("q_r", [1, W], F32)
    m2_r = p.sbuf("m2_r", [1, W], F32)
    var_r = p.sbuf("var_r", [1, W], F32)
    lnv_r = p.sbuf("lnv_r", [1, W], F32)
    rs_r = p.sbuf("rs_r", [1, W], F32)
    mr_r = p.sbuf("mr_r", [1, W], F32)

    ps_sum = p.psum("ps_sum", [1, W])
    ps_sq = p.psum("ps_sq", [1, W])
    ps_b1 = p.psum("ps_b1", [128, W])
    ps_b2 = p.psum("ps_b2", [128, W])
    ps_mm = [p.psum(f"ps_mm{i}", [128, W]) for i in range(2)]
    ps_v = [p.psum(f"ps_v{i}", [128, W]) for i in range(2)]

    for s in ("ldc", "ldw", "lda", "ldb", "pe", "dv", "ac", "st0", "st1"):
        p.sem(s)

    def body():
        for nm, src in (("wr", wr), ("wk", wk), ("wv", wv)):
            for kc in range(8):
                p.dma("gpsimd", wtiles[nm][kc][:],
                      src[128 * kc:128 * (kc + 1), :], inc="ldw")
        p.dma("sync", mur[:], mu_r, inc="ldc")
        p.dma("sync", muk[:], mu_k, inc="ldc")
        p.op("vector", "memset", ones_col[:], 1.0)
        p.op("vector", "memset", ones_row[:], 1.0)
        p.op("vector", "memset", eps_col[:], 1e-5, inc="dv")
        p.wait("scalar", "dv")
        p.wait("vector", "ldc", 32)
        p.op("vector", "tensor_scalar", omur[:], mur[:], -1.0, 1.0,
             ALU.mult, ALU.add)
        p.op("vector", "tensor_scalar", omuk[:], muk[:], -1.0, 1.0,
             ALU.mult, ALU.add)
        for c in range(8):
            p.op("vector", "memset", xln[c][0][:, 0:1], 0.0, inc="dv")

        rec = {"rows_read_psum": 0, "copies_done": 0,
               "sig_evac": [0, 0], "v_evac": [0, 0]}

        for i in range(NT):
            buf = i % 2
            t0 = i * W
            ldsem = "lda" if buf == 0 else "ldb"
            if i >= 2:
                p.wait("sync", "st0", rec[f"st_tile0_{i-2}"])
                p.wait("sync", "st1", rec[f"st_tile1_{i-2}"])
            for c in range(8):
                r0 = 128 * c
                if i == 0:
                    p.dma("sync", xch[c][buf][:, 1:W + 1],
                          x[r0:r0 + 128, 0:W], inc=ldsem)
                else:
                    p.dma("sync", xch[c][buf][:, 0:W + 1],
                          x[r0:r0 + 128, t0 - 1:t0 + W], inc=ldsem)
            ld_tile = p.val(ldsem)

            p.wait("scalar", ldsem, ld_tile)
            p.wait("tensor", "dv", rec["rows_read_psum"])
            pe_after_chunk = []
            for c in range(8):
                if c >= 2:
                    p.wait("scalar", "pe", pe_after_chunk[c - 2])
                p.op("scalar", "activation", sq[c % 2][:],
                     xch[c][buf][:, 1:W + 1], AF.Square, bias=0.0,
                     scale=1.0, inc="ac")
                p.wait("tensor", "ac")
                p.op("tensor", "matmul", ps_sum[:], ones_col[:],
                     xch[c][buf][:, 1:W + 1], start=(c == 0), stop=(c == 7))
                p.op("tensor", "matmul", ps_sq[:], ones_col[:], sq[c % 2][:],
                     start=(c == 0), stop=(c == 7), inc="pe")
                pe_after_chunk.append(p.val("pe"))

            p.wait("vector", "pe")
            p.op("vector", "tensor_scalar", m_r[:], ps_sum[:], 1.0 / D, 0.0,
                 ALU.mult, ALU.add)
            p.op("vector", "tensor_scalar", q_r[:], ps_sq[:], 1.0 / D, 0.0,
                 ALU.mult, ALU.add, inc="dv")
            rec["rows_read_psum"] = p.val("dv")
            p.op("vector", "tensor_tensor", m2_r[:], m_r[:], m_r[:], ALU.mult)
            p.op("vector", "tensor_tensor", var_r[:], q_r[:], m2_r[:],
                 ALU.subtract, inc="dv")
            p.wait("scalar", "dv")
            p.op("scalar", "activation", lnv_r[:], var_r[:], AF.Ln,
                 bias=eps_col[0:1, 0:1], scale=1.0)
            p.op("scalar", "activation", rs_r[:], lnv_r[:], AF.Exp,
                 bias=0.0, scale=-0.5, inc="ac")
            p.wait("vector", "ac")
            p.op("vector", "tensor_tensor", mr_r[:], m_r[:], rs_r[:],
                 ALU.mult, inc="dv")

            p.wait("tensor", "dv")
            p.wait("tensor", "ac", rec["copies_done"])
            p.op("tensor", "matmul", ps_b1[:], ones_row[:], rs_r[:],
                 start=True, stop=True)
            p.op("tensor", "matmul", ps_b2[:], ones_row[:], mr_r[:],
                 start=True, stop=True, inc="pe")
            p.wait("scalar", "pe")
            p.op("scalar", "copy", rb[:], ps_b1[:], inc="ac")
            p.op("scalar", "copy", mrb[:], ps_b2[:], inc="ac")
            rec["copies_done"] = p.val("ac")

            p.wait("vector", "ac")
            for c in range(8):
                if i > 0:
                    p.op("vector", "tensor_copy", xln[c][buf][:, 0:1],
                         xln[c][1 - buf][:, W:W + 1])
                p.op("vector", "tensor_tensor", xln[c][buf][:, 1:W + 1],
                     xch[c][buf][:, 1:W + 1], rb[:], ALU.mult)
                p.op("vector", "tensor_tensor", xln[c][buf][:, 1:W + 1],
                     xln[c][buf][:, 1:W + 1], mrb[:], ALU.subtract)
            for c in range(8):
                p.op("vector", "tensor_scalar", tsc[:],
                     xln[c][buf][:, 1:W + 1], mur[:, c:c + 1], 0.0,
                     ALU.mult, ALU.add)
                p.op("vector", "scalar_tensor_tensor", xr16[c][:],
                     xln[c][buf][:, 0:W], omur[:, c:c + 1], tsc[:],
                     ALU.mult, ALU.add)
                p.op("vector", "tensor_scalar", tsc[:],
                     xln[c][buf][:, 1:W + 1], muk[:, c:c + 1], 0.0,
                     ALU.mult, ALU.add)
                p.op("vector", "scalar_tensor_tensor", xk16[c][:],
                     xln[c][buf][:, 0:W], omuk[:, c:c + 1], tsc[:],
                     ALU.mult, ALU.add, inc="dv" if c == 7 else None)
            mixes_done = p.val("dv")

            p.wait("tensor", "dv", mixes_done)
            if i == 0:
                p.wait("tensor", "ldw", 16 * 24)
            for oc in range(8):
                bank = oc % 2
                p.wait("tensor", "ac", rec["sig_evac"][bank])
                for kc in range(8):
                    p.op("tensor", "matmul", ps_mm[bank][:],
                         wtiles["wr"][kc][:, 128 * oc:128 * (oc + 1)],
                         xr16[kc][:], start=(kc == 0), stop=(kc == 7),
                         inc="pe" if kc == 7 else None)
                p.wait("scalar", "pe")
                p.op("scalar", "activation", rsig[oc][:], ps_mm[bank][:],
                     AF.Sigmoid, bias=0.0, scale=1.0, inc="ac")
                rec["sig_evac"][bank] = p.val("ac")
            for oc in range(8):
                bank = oc % 2
                p.wait("tensor", "dv", rec["v_evac"][bank])
                for kc in range(8):
                    p.op("tensor", "matmul", ps_v[bank][:],
                         wtiles["wk"][kc][:, 128 * oc:128 * (oc + 1)],
                         xk16[kc][:], start=(kc == 0), stop=(kc == 7),
                         inc="pe" if kc == 7 else None)
                p.wait("vector", "pe")
                if oc >= 2:
                    p.wait("vector", "ac", rec[f"ksq_{bank}"])
                p.op("vector", "tensor_scalar", kr[bank][:], ps_v[bank][:],
                     0.0, 0.0, ALU.max, ALU.add, inc="dv")
                rec["v_evac"][bank] = p.val("dv")
                p.wait("scalar", "dv")
                p.op("scalar", "activation", k16a[oc][:], kr[bank][:],
                     AF.Square, bias=0.0, scale=1.0, inc="ac")
                rec[f"ksq_{bank}"] = p.val("ac")
            ksq_all = p.val("ac")
            for oc in range(8):
                bank = oc % 2
                p.wait("tensor", "ac", ksq_all)
                p.wait("tensor", "dv", rec.get(f"fin_{bank}", 0))
                for kc in range(8):
                    p.op("tensor", "matmul", ps_mm[bank][:],
                         wtiles["wv"][kc][:, 128 * oc:128 * (oc + 1)],
                         k16a[kc][:], start=(kc == 0), stop=(kc == 7),
                         inc="pe" if kc == 7 else None)
                p.wait("vector", "pe")
                if f"store_{bank}" in rec:
                    p.wait("vector", f"st{bank}", rec[f"store_{bank}"])
                p.op("vector", "tensor_tensor", vout[bank][:],
                     ps_mm[bank][:], rsig[oc][:], ALU.mult)
                p.op("vector", "tensor_tensor", vout[bank][:], vout[bank][:],
                     xch[oc][buf][:, 1:W + 1], ALU.add, inc="dv")
                rec[f"fin_{bank}"] = p.val("dv")
                p.wait("sync", "dv")
                p.dma("sync", xo[128 * oc:128 * (oc + 1), t0:t0 + W],
                      vout[bank][:], inc=f"st{bank}")
                rec[f"store_{bank}"] = p.val(f"st{bank}")
            rec[f"st_tile0_{i}"] = p.val("st0")
            rec[f"st_tile1_{i}"] = p.val("st1")
        p.wait("gpsimd", "st0", p.val("st0"))
        p.wait("gpsimd", "st1", p.val("st1"))

    if fused:
        return body
    emit_all(p, body)
    return nc



def l1_ffn_inmap(x_fm_b, inp):
    return {
        "x": np.ascontiguousarray(x_fm_b),
        "wr_t": np.ascontiguousarray(np.asarray(inp["L1_ffn_W_r"]).T),
        "wk_t": np.ascontiguousarray(np.asarray(inp["L1_ffn_W_k"]).T),
        "wv_t": np.ascontiguousarray(np.asarray(inp["L1_ffn_W_v"]).T),
        "mu_r": np.ascontiguousarray(
            np.asarray(inp["L1_ffn_mu_r"]).reshape(8, 128).T),
        "mu_k": np.ascontiguousarray(
            np.asarray(inp["L1_ffn_mu_k"]).reshape(8, 128).T),
    }


def run_l1_ffn(x_fm, inp):
    nc = build_l1_ffn()
    in_maps = [l1_ffn_inmap(x_fm[b], inp) for b in range(NCORES)]
    res = run_bass_kernel_spmd(nc, in_maps, core_ids=list(range(NCORES)))
    return np.stack([r["xo"] for r in res.results])


ARENA_W = 106416


def build_fused():
    nc = bass.Bass("TRN2", target_bir_lowering=False)
    x = nc.dram_tensor("x", [D, T], F32, kind="ExternalInput").ap()
    xo = nc.dram_tensor("xo", [D, T], F32, kind="ExternalOutput").ap()
    x1 = nc.dram_tensor("x1_d", [D, T], F32, kind="Internal").ap()
    x2 = nc.dram_tensor("x2_d", [D, T], F32, kind="Internal").ap()
    x3 = nc.dram_tensor("x3_d", [D, T], F32, kind="Internal").ap()
    ctx = ExitStack()
    with ctx:
        p = P(nc, ctx)
        p.arena = ctx.enter_context(
            nc.sbuf_tensor("arena", [128, ARENA_W], BF16))
        S1 = ctx.enter_context(nc.psum_tensor("S1", [128, 1024], F32))
        S2 = ctx.enter_context(nc.psum_tensor("S2", [128, 1024], F32))
        S3 = ctx.enter_context(nc.psum_tensor("S3", [128, 1024], F32))
        S4 = ctx.enter_context(nc.psum_tensor("S4", [128, 512], F32))
        S5 = ctx.enter_context(nc.psum_tensor("S5", [128, 512], F32))
        mapA = {"ps1": S1[:, 0:512], "ps2": S1[:, 512:1024],
                "ps3": S2[:, 0:512], "ps4": S2[:, 512:1024],
                "ps5": S3[:, 0:512], "ps6": S3[:, 512:1024],
                "ps7": S4[:], "pstr": S5[:].bitcast(BF16)}
        mapB = {"ps_a": S1[:, 0:512], "ps_b": S1[:, 512:1024],
                "ps_c": S2[:, 0:512], "ps_d": S2[:, 512:1024],
                "ps_n": S4[:], "ps_s2": S3[:]}
        mapC = {"ps_qk0": S4[:], "ps_qk1": S5[:], "ps_s0": S1[:],
                "ps_s1": S2[:], "ps_y": S3[:]}
        mapD = {"ps_sum": S4[0:1, :], "ps_sq": S5[0:1, :],
                "ps_b1": S1[:, 0:512], "ps_b2": S1[:, 512:1024],
                "ps_mm0": S2[:, 0:512], "ps_mm1": S2[:, 512:1024],
                "ps_v0": S3[:, 0:512], "ps_v1": S3[:, 512:1024]}
        blocks = [("A_", build_l0_attn, mapA, x, x1),
                  ("B_", build_l0_ffn, mapB, x1, x2),
                  ("C_", build_l1_attn, mapC, x2, x3),
                  ("D_", build_l1_ffn, mapD, x3, xo)]
        bodies = []
        for pfx, bld, psmap, xi, xout in blocks:
            p.set_block(pfx, psmap)
            proxy = NCProxy(nc, pfx, xi, xout)
            bodies.append(bld(fused={"nc": proxy, "p": p}))
            assert p.arena_off <= 2 * ARENA_W, \
                f"{pfx} arena overflow: {p.arena_off} B"
            if os.environ.get("ARENA_DEBUG"):
                print(f"arena {pfx}: {p.arena_off} / {2 * ARENA_W} B")

        def whole():
            for i, (pfx, bld, psmap, xi, xout) in enumerate(blocks):
                p.set_block(pfx, psmap)
                if i:
                    p.barrier(blocks[i - 1][0])
                bodies[i]()

        emit_all(p, whole)
    return nc


def fused_inmap(x_fm_b, inp):
    m = {"x": np.ascontiguousarray(x_fm_b)}
    for pfx, f in (("A_", l0_attn_inmap), ("B_", l0_ffn_inmap),
                   ("C_", l1_attn_inmap), ("D_", l1_ffn_inmap)):
        for k, v in f(x_fm_b, inp).items():
            if k != "x":
                m[pfx + k] = v
    return m


def kernel(**inp):
    x = np.asarray(inp["L0_attn_x"], dtype=np.float32)
    x_fm = np.ascontiguousarray(np.transpose(x, (0, 2, 1)))
    nc = build_fused()
    in_maps = [fused_inmap(x_fm[b], inp) for b in range(NCORES)]
    res = run_bass_kernel_spmd(nc, in_maps, core_ids=list(range(NCORES)))
    out = np.stack([r["xo"] for r in res.results])
    return np.ascontiguousarray(np.transpose(out, (0, 2, 1)))


def kernel_unfused(**inp):
    x = np.asarray(inp["L0_attn_x"], dtype=np.float32)
    x_fm = np.ascontiguousarray(np.transpose(x, (0, 2, 1)))
    x_fm = run_l0_attn(x_fm, inp)
    x_fm = run_l0_ffn(x_fm, inp)
    x_fm = run_l1_attn(x_fm, inp)
    x_fm = run_l1_ffn(x_fm, inp)
    return np.ascontiguousarray(np.transpose(x_fm, (0, 2, 1)))



def build_l1_attn(fused=None):
    nc = fused["nc"] if fused else bass.Bass("TRN2", target_bir_lowering=False)
    x = nc.dram_tensor("x", [D, T], F32, kind="ExternalInput").ap()
    wq = nc.dram_tensor("wq_t", [64, 64], F32, kind="ExternalInput").ap()
    wk = nc.dram_tensor("wk_t", [64, 64], F32, kind="ExternalInput").ap()
    wv = nc.dram_tensor("wv_t", [64, 64], F32, kind="ExternalInput").ap()
    wu = nc.dram_tensor("wu_t", [D, D], F32, kind="ExternalInput").ap()
    bu = nc.dram_tensor("bu", [128, 8], F32, kind="ExternalInput").ap()
    masks = nc.dram_tensor("masks", [128, 1920], F32,
                           kind="ExternalInput").ap()
    sel = nc.dram_tensor("sel", [64, 31], F32, kind="ExternalInput").ap()
    iden = nc.dram_tensor("iden", [16, 16], F32, kind="ExternalInput").ap()
    xo = nc.dram_tensor("xo", [D, T], F32, kind="ExternalOutput").ap()
    yt_d = nc.dram_tensor("yt_scr", [D, T], BF16, kind="Internal").ap()

    NT = 8
    W = 512
    SCALE = float(D) ** -0.25

    p = fused["p"] if fused else P(nc, ExitStack())
    wq_s = p.sbuf("wq_s", [128, 64], BF16)
    wk_s = p.sbuf("wk_s", [128, 64], BF16)
    wv_s = p.sbuf("wv_s", [128, 64], BF16)
    wut = [p.sbuf(f"wut_{kc}", [128, D], BF16) for kc in range(8)]
    bu_s = p.sbuf("bu_s", [128, 8], F32)
    maskm = p.sbuf("maskm", [128, 1920], BF16)
    mask_s = [maskm[:, 896 - 128 * m:1920 - 128 * m] for m in range(8)]
    sel_s = p.sbuf("sel_s", [64, 31], BF16)
    iden_s = p.sbuf("iden_s", [16, 16], F32)
    ones_col = p.sbuf("ones_col", [128, 1], F32)
    ones_row = p.sbuf("ones_row", [1, 128], F32)
    eps_col = p.sbuf("eps_col", [128, 1], F32)

    xln16 = [p.sbuf(f"xln16_{c}", [128, T], BF16) for c in range(8)]
    qT = p.sbuf("qT", [65, T], BF16)
    kT = p.sbuf("kT", [65, T], BF16)
    vh = p.sbuf("vh", [128, 32 * 65], BF16)
    ptile = [p.sbuf(f"pt_{b}", [128, 1024], BF16) for b in range(2)]
    sq64 = [p.sbuf(f"sq64_{b}", [64, W], BF16) for b in range(2)]
    nq = p.sbuf("nq", [16, T], BF16)
    nk = p.sbuf("nk", [16, T], BF16)
    nkm = p.sbuf("nkm", [16, 1], F32)
    nkr = p.sbuf("nkr", [1, 16], F32)
    kmax1 = p.sbuf("kmax1", [1, 1], F32)
    kmaxb = p.sbuf("kmaxb", [16, 1], F32)
    ball = p.sbuf("ball", [16, T], BF16)
    ytile = p.sbuf("ytile", [64, 1024], BF16)
    rl = p.sbuf("rl", [1, 1024], F32)
    rb64 = p.sbuf("rb64", [64, 1024], F32)

    xch = [[p.sbuf(f"x_{c}_{b}", [128, W], F32) for b in range(2)]
           for c in range(8)]
    sq = [p.sbuf(f"sq_{b}", [128, W], F32) for b in range(2)]
    rb = p.sbuf("rb", [128, W], F32)
    mrb = p.sbuf("mrb", [128, W], F32)
    m_r = p.sbuf("m_r", [1, W], F32)
    q_r = p.sbuf("q_r", [1, W], F32)
    m2_r = p.sbuf("m2_r", [1, W], F32)
    var_r = p.sbuf("var_r", [1, W], F32)
    lnv_r = p.sbuf("lnv_r", [1, W], F32)
    rs_r = p.sbuf("rs_r", [1, W], F32)
    mr_r = p.sbuf("mr_r", [1, W], F32)
    yld = [[xln16[c][:, W * b:W * (b + 1)] for b in range(2)]
           for c in range(8)]
    vout = [rb, mrb]

    ps_qk = [p.psum(f"ps_qk{i}", [128, W]) for i in range(2)]
    ps_s = [p.psum(f"ps_s{i}", [128, 1024]) for i in range(2)]
    ps_y = p.psum("ps_y", [128, 1024])
    ps_sum = ps_s[0]
    ps_sq = ps_s[1]
    ps_b1 = ps_y
    ps_b2 = ps_y

    for s in ("ldc", "ldw", "lda", "ldb", "ldya", "ldyb", "pe", "dv",
              "ac", "st0", "st1", "sty", "qrow"):
        p.sem(s)

    def body():
        for kc in range(8):
            p.dma("gpsimd", wut[kc][:], wu[128 * kc:128 * (kc + 1), :],
                  inc="ldw")
        for wdst, wsrc in ((wq_s, wq), (wk_s, wk), (wv_s, wv)):
            p.dma("gpsimd", wdst[0:64, :], wsrc, inc="ldw")
            p.dma("gpsimd", wdst[64:128, :], wsrc, inc="ldw")
        p.dma("gpsimd", sel_s[:], sel, inc="ldw")
        p.dma("gpsimd", maskm[:], masks, inc="ldw")
        p.dma("sync", bu_s[:], bu, inc="ldc")
        p.dma("sync", iden_s[:], iden, inc="ldc")
        p.op("vector", "memset", ones_col[:], 1.0)
        p.op("vector", "memset", ones_row[:], 1.0)
        p.op("vector", "memset", eps_col[:], 1e-5)
        p.op("vector", "memset", vh[:], 1.0)
        p.op("vector", "memset", kT[64:65, :], 1.0)
        p.op("vector", "memset", nq[:], 0.0)
        p.op("vector", "memset", nk[:], 0.0, inc="dv")
        memset_done = p.val("dv")

        rec = {"rows_read_psum": 0, "copies_done": 0}

        for i in range(NT):
            buf = i % 2
            t0 = i * W
            ldsem = "lda" if buf == 0 else "ldb"
            if i >= 2:
                p.wait("sync", "dv", rec[f"xln_done_{i-2}"])
            for c in range(8):
                r0 = 128 * c
                p.dma("sync", xch[c][buf][:], x[r0:r0 + 128, t0:t0 + W],
                      inc=ldsem)
            ld_tile = p.val(ldsem)
            p.wait("scalar", ldsem, ld_tile)
            p.wait("scalar", "dv", rec.get(f"xln_done_{i-1}", 0))
            p.wait("tensor", "dv", max(rec["rows_read_psum"],
                                       memset_done))
            p.wait("tensor", ldsem, ld_tile)
            pe_after_chunk = []
            for c in range(8):
                if c >= 2:
                    p.wait("scalar", "pe", pe_after_chunk[c - 2])
                p.op("scalar", "activation", sq[c % 2][:], xch[c][buf][:],
                     AF.Square, bias=0.0, scale=1.0, inc="ac")
                p.wait("tensor", "ac")
                p.op("tensor", "matmul", ps_sum[0:1, 0:W], ones_col[:],
                     xch[c][buf][:], start=(c == 0), stop=(c == 7))
                p.op("tensor", "matmul", ps_sq[0:1, 0:W], ones_col[:],
                     sq[c % 2][:],
                     start=(c == 0), stop=(c == 7), inc="pe")
                pe_after_chunk.append(p.val("pe"))
            p.wait("vector", "pe")
            p.op("vector", "tensor_scalar", m_r[:], ps_sum[0:1, 0:W],
                 1.0 / D, 0.0, ALU.mult, ALU.add)
            p.op("vector", "tensor_scalar", q_r[:], ps_sq[0:1, 0:W],
                 1.0 / D, 0.0, ALU.mult, ALU.add, inc="dv")
            rec["rows_read_psum"] = p.val("dv")
            p.op("vector", "tensor_tensor", m2_r[:], m_r[:], m_r[:],
                 ALU.mult)
            p.op("vector", "tensor_tensor", var_r[:], q_r[:], m2_r[:],
                 ALU.subtract, inc="dv")
            p.wait("scalar", "dv")
            p.op("scalar", "activation", lnv_r[:], var_r[:], AF.Ln,
                 bias=eps_col[0:1, 0:1], scale=1.0)
            p.op("scalar", "activation", rs_r[:], lnv_r[:], AF.Exp,
                 bias=0.0, scale=-0.5, inc="ac")
            p.wait("vector", "ac")
            p.op("vector", "tensor_tensor", mr_r[:], m_r[:], rs_r[:],
                 ALU.mult, inc="dv")
            p.wait("tensor", "dv")
            p.wait("tensor", "ac", rec["copies_done"])
            p.op("tensor", "matmul", ps_b1[0:128, 0:W], ones_row[:],
                 rs_r[:], start=True, stop=True)
            p.op("tensor", "matmul", ps_b2[0:128, W:2 * W], ones_row[:],
                 mr_r[:], start=True, stop=True, inc="pe")
            p.wait("scalar", "pe")
            p.op("scalar", "copy", rb[:], ps_b1[0:128, 0:W], inc="ac")
            p.op("scalar", "copy", mrb[:], ps_b2[0:128, W:2 * W],
                 inc="ac")
            rec["copies_done"] = p.val("ac")
            p.wait("vector", "ac")
            for c in range(8):
                p.op("vector", "tensor_tensor", sq[c % 2][:],
                     xch[c][buf][:], rb[:], ALU.mult)
                p.op("vector", "tensor_tensor",
                     xln16[c][:, t0:t0 + W], sq[c % 2][:], mrb[:],
                     ALU.subtract, inc="dv" if c == 7 else None)
            rec[f"xln_done_{i}"] = p.val("dv")
        xln_all = p.val("dv")

        p.wait("tensor", "ldw", p.val("ldw"))
        p.wait("tensor", "ldc", 32)
        p.wait("tensor", "dv", xln_all)
        for j in range(NT):
            t0 = j * W
            p.wait("tensor", "dv", rec.get("norm_read", 0))
            for h in range(16):
                ch, half = h // 2, (h % 2) * 64
                for qk, (wmat, ps_norm) in enumerate(
                        ((wq_s, ps_b1), (wk_s, ps_b2))):
                    bank = (2 * h + qk) % 2
                    p.wait("tensor", "ac", p.val("ac"))
                    p.op("tensor", "matmul", ps_qk[bank][0:64, :],
                         wmat[half:half + 64, :],
                         xln16[ch][half:half + 64, t0:t0 + W],
                         start=True, stop=True, inc="pe")
                    p.wait("scalar", "pe")
                    p.op("scalar", "activation", sq64[bank][:],
                         ps_qk[bank][0:64, :], AF.Square, bias=0.0,
                         scale=SCALE, inc="ac")
                    p.wait("tensor", "ac")
                    nsl = (slice(0, 16), slice(0, W)) if qk == 0 else \
                        (slice(0, 16), slice(W, 2 * W))
                    p.op("tensor", "matmul", ps_norm[nsl[0], nsl[1]],
                         sel_s[:, 15 - h:31 - h], sq64[bank][:],
                         start=(h == 0), stop=(h == 15),
                         inc="pe" if h == 15 else None)
            p.wait("vector", "pe")
            p.op("vector", "tensor_copy", nq[:, t0:t0 + W],
                 ps_b1[0:16, 0:W])
            p.op("vector", "tensor_copy", nk[:, t0:t0 + W],
                 ps_b2[0:16, W:2 * W], inc="dv")
            rec["norm_read"] = p.val("dv")
        p.op("vector", "tensor_reduce", nkm[:], nk[:], AX.X, ALU.max,
             inc="dv")
        p.wait("tensor", "dv")
        p.wait("tensor", "ac", p.val("ac"))
        p.op("tensor", "matmul", ps_qk[0][0:1, 0:16], nkm[:], iden_s[:],
             start=True, stop=True, inc="pe")
        p.wait("vector", "pe")
        p.op("vector", "tensor_copy", nkr[:], ps_qk[0][0:1, 0:16])
        p.op("vector", "tensor_reduce", kmax1[:], nkr[:], AX.X, ALU.max,
             inc="dv")
        p.wait("tensor", "dv")
        p.op("tensor", "matmul", ps_qk[1][0:16, 0:1], ones_row[0:1, 0:16],
             kmax1[:], start=True, stop=True, inc="pe")
        p.wait("vector", "pe")
        p.op("vector", "tensor_copy", kmaxb[:], ps_qk[1][0:16, 0:1])
        p.op("vector", "tensor_scalar", nq[:], nq[:], kmaxb[:, 0:1], 0.0,
             ALU.mult, ALU.add, inc="dv")
        p.wait("scalar", "dv")
        p.op("scalar", "activation", nk[:], nq[:], AF.Sqrt, bias=0.0,
             scale=1.0, inc="ac")
        p.wait("vector", "ac")
        p.op("vector", "tensor_scalar", ball[:], nk[:], -1.0, 0.0,
             ALU.mult, ALU.add, inc="dv")
        ball_done = p.val("dv")

        rec["pv_done"] = 0
        rec["exp_done"] = [0, 0]
        rec["mask_done"] = [0, 0]
        rec["pv_par"] = [0, 0]
        rec["y_evac"] = 0
        for h in range(16):
            ch, half = h // 2, (h % 2) * 64
            p.wait("sync", "dv", ball_done)
            p.wait("sync", "pe", rec["pv_done"])
            p.dma("sync", qT[64:65, :], ball[h:h + 1, :], inc="qrow")
            p.wait("scalar", "pe", rec["pv_done"])
            p.wait("vector", "pe", rec["pv_done"])
            for j in range(8):
                t0 = j * W
                bank = j % 2
                for (wmat, dstT) in ((wq_s, qT), (wk_s, kT)):
                    p.wait("tensor", "ac", p.val("ac"))
                    p.wait("tensor", "dv", p.val("dv"))
                    p.op("tensor", "matmul", ps_qk[bank][0:64, :],
                         wmat[half:half + 64, :],
                         xln16[ch][half:half + 64, t0:t0 + W],
                         start=True, stop=True, inc="pe")
                    p.wait("scalar", "pe")
                    p.op("scalar", "activation", dstT[0:64, t0:t0 + W],
                         ps_qk[bank][0:64, :], AF.Copy, bias=0.0,
                         scale=SCALE, inc="ac")
                for s4 in range(4):
                    st = t0 + 128 * s4
                    sidx = j * 4 + s4
                    vbank = s4 % 2
                    p.wait("tensor", "ac", p.val("ac"))
                    p.wait("tensor", "dv", p.val("dv"))
                    p.op("tensor", "matmul", ps_qk[vbank][0:128, 0:64],
                         xln16[ch][half:half + 64, st:st + 128],
                         wv_s[half:half + 64, :], start=True, stop=True,
                         inc="pe")
                    p.wait("vector", "pe")
                    p.op("vector", "tensor_copy",
                         vh[:, 65 * sidx:65 * sidx + 64],
                         ps_qk[vbank][0:128, 0:64], inc="dv")
            qkv_done = p.val("dv")
            ac_qk = p.val("ac")
            p.wait("tensor", "qrow")
            p.wait("tensor", "dv", qkv_done)
            p.wait("tensor", "ac", ac_qk)

            for j4 in range(4):
                q0 = 1024 * j4
                nkc = 8 * j4 + 8
                for kc in range(nkc):
                    par = kc % 2
                    k0 = 128 * kc
                    p.wait("tensor", "dv", rec["mask_done"][par])
                    p.wait("tensor", "ac", rec["exp_done"][par])
                    if kc < 2:
                        p.wait("tensor", "dv", rec["y_evac"])
                    for hf in range(2):
                        p.op("tensor", "matmul",
                             ps_s[par][:, 512 * hf:512 * (hf + 1)],
                             kT[:, k0:k0 + 128],
                             qT[:, q0 + 512 * hf:q0 + 512 * (hf + 1)],
                             start=True, stop=True,
                             inc="pe" if hf == 1 else None)
                    s_done = p.val("pe")
                    m = kc - 8 * j4
                    if 0 <= m < 8:
                        p.wait("vector", "pe", s_done)
                        p.op("vector", "tensor_tensor", ps_s[par][:],
                             ps_s[par][:], mask_s[m], ALU.add,
                             inc="dv")
                        rec["mask_done"][par] = p.val("dv")
                        p.wait("scalar", "dv", rec["mask_done"][par])
                    else:
                        p.wait("scalar", "pe", s_done)
                    p.wait("scalar", "pe", rec["pv_par"][par])
                    p.op("scalar", "activation", ptile[par][:],
                         ps_s[par][:], AF.Exp, bias=0.0, scale=1.0,
                         inc="ac")
                    rec["exp_done"][par] = p.val("ac")
                    p.wait("tensor", "ac", rec["exp_done"][par])
                    for hf in range(2):
                        p.op("tensor", "matmul",
                             ps_y[0:65, 512 * hf:512 * (hf + 1)],
                             vh[:, 65 * kc:65 * (kc + 1)],
                             ptile[par][:, 512 * hf:512 * (hf + 1)],
                             start=(kc == 0), stop=(kc == nkc - 1),
                             inc="pe" if hf == 1 else None)
                    rec["pv_par"][par] = p.val("pe")
                rec["pv_done"] = p.val("pe")
                rec["vh_rd"] = rec["pv_done"]
                p.wait("vector", "pe", rec["pv_done"])
                p.op("vector", "reciprocal", rl[0:1, 0:512],
                     ps_y[64:65, 0:512])
                p.op("vector", "reciprocal", rl[0:1, 512:1024],
                     ps_y[64:65, 512:1024], inc="dv")
                p.wait("tensor", "dv")
                p.wait("tensor", "ac", p.val("ac"))
                for hf in range(2):
                    p.op("tensor", "matmul", ps_qk[hf][0:64, :],
                         ones_row[0:1, 0:64],
                         rl[0:1, 512 * hf:512 * (hf + 1)],
                         start=True, stop=True, inc="pe")
                p.wait("scalar", "pe")
                p.op("scalar", "copy", rb64[:, 0:512], ps_qk[0][0:64, :],
                     inc="ac")
                p.op("scalar", "copy", rb64[:, 512:1024],
                     ps_qk[1][0:64, :], inc="ac")
                p.wait("vector", "ac")
                p.op("vector", "tensor_tensor", ytile[:],
                     ps_y[0:64, :], rb64[:], ALU.mult, inc="dv")
                rec["y_evac"] = p.val("dv")
                p.wait("sync", "dv", rec["y_evac"])
                p.dma("sync", yt_d[64 * h:64 * h + 64, q0:q0 + 1024],
                      ytile[:], inc="sty")
                p.wait("vector", "sty")

        rec2 = {}
        for i in range(NT):
            buf = i % 2
            t0 = i * W
            ldsem = "lda" if buf == 0 else "ldb"
            ldy = "ldya" if buf == 0 else "ldyb"
            if i == 0:
                p.wait("sync", "sty", p.val("sty"))
            if i >= 2:
                p.wait("sync", "st0", rec2[f"st0_{i-2}"])
                p.wait("sync", "st1", rec2[f"st1_{i-2}"])
            for c in range(8):
                r0 = 128 * c
                p.dma("sync", xch[c][buf][:], x[r0:r0 + 128, t0:t0 + W],
                      inc=ldsem)
                p.dma("sync", yld[c][buf],
                      yt_d[r0:r0 + 128, t0:t0 + W], inc=ldy)
            ld_tile = p.val(ldsem)
            ldy_tile = p.val(ldy)
            p.wait("tensor", ldsem, ld_tile)
            p.wait("tensor", ldy, ldy_tile)
            if i == 0:
                p.wait("vector", "ldc", 32)
            for oc in range(8):
                bank = oc % 2
                p.wait("tensor", "dv", rec2.get(f"fin_{bank}", 0))
                for kc in range(8):
                    p.op("tensor", "matmul", ps_qk[bank][:],
                         wut[kc][:, 128 * oc:128 * (oc + 1)],
                         yld[kc][buf], start=(kc == 0), stop=(kc == 7),
                         inc="pe" if kc == 7 else None)
                p.wait("vector", "pe")
                if f"store_{bank}" in rec2:
                    p.wait("vector", f"st{bank}", rec2[f"store_{bank}"])
                p.op("vector", "scalar_tensor_tensor", vout[bank][:],
                     ps_qk[bank][:], bu_s[:, oc:oc + 1], xch[oc][buf][:],
                     ALU.add, ALU.add, inc="dv")
                rec2[f"fin_{bank}"] = p.val("dv")
                p.wait("sync", "dv")
                p.dma("sync", xo[128 * oc:128 * (oc + 1), t0:t0 + W],
                      vout[bank][:], inc=f"st{bank}")
                rec2[f"store_{bank}"] = p.val(f"st{bank}")
            rec2[f"st0_{i}"] = p.val("st0")
            rec2[f"st1_{i}"] = p.val("st1")
        p.wait("gpsimd", "st0", p.val("st0"))
        p.wait("gpsimd", "st1", p.val("st1"))

    if fused:
        return body
    emit_all(p, body)
    return nc


def l1_attn_consts():
    r = np.arange(128)[:, None]
    u = np.arange(1920)[None, :]
    masks = np.where(u < r + 896, -1e38, 0.0).astype(np.float32)
    sel = np.zeros((64, 31), dtype=np.float32)
    sel[:, 15] = 1.0
    iden = np.eye(16, dtype=np.float32)
    return masks, sel, iden


def l1_attn_inmap(x_fm_b, inp):
    masks, sel, iden = l1_attn_consts()
    return {
        "x": np.ascontiguousarray(x_fm_b),
        "wq_t": np.ascontiguousarray(np.asarray(inp["L1_attn_Wq"]).T),
        "wk_t": np.ascontiguousarray(np.asarray(inp["L1_attn_Wk"]).T),
        "wv_t": np.ascontiguousarray(np.asarray(inp["L1_attn_Wv"]).T),
        "wu_t": np.ascontiguousarray(np.asarray(inp["L1_attn_Wu"]).T),
        "bu": np.ascontiguousarray(
            np.asarray(inp["L1_attn_bu"]).reshape(8, 128).T),
        "masks": masks, "sel": sel, "iden": iden,
    }


def run_l1_attn(x_fm, inp):
    nc = build_l1_attn()
    in_maps = [l1_attn_inmap(x_fm[b], inp) for b in range(NCORES)]
    res = run_bass_kernel_spmd(nc, in_maps, core_ids=list(range(NCORES)))
    return np.stack([r["xo"] for r in res.results])



def build_l0_ffn(debug=False, dbg_lay=0, fused=None):
    nc = fused["nc"] if fused else bass.Bass("TRN2", target_bir_lowering=False)
    x = nc.dram_tensor("x", [D, T], F32, kind="ExternalInput").ap()
    wqkv = nc.dram_tensor("wqkv", [2, 8, 128, 1536], F32,
                          kind="ExternalInput").ap()
    wo = nc.dram_tensor("wo", [2, 4, 128, 1024], F32,
                        kind="ExternalInput").ap()
    w1r = nc.dram_tensor("w1r", [2, 21, 128, 2048], F32,
                         kind="ExternalInput").ap()
    w1t = nc.dram_tensor("w1t", [2, 128, 672], F32, kind="ExternalInput").ap()
    w2r = nc.dram_tensor("w2r", [2, 21, 128, 1024], F32,
                         kind="ExternalInput").ap()
    w2t = nc.dram_tensor("w2t", [2, 42, 1024], F32, kind="ExternalInput").ap()
    b1r = nc.dram_tensor("b1r", [2, 128, 44], F32, kind="ExternalInput").ap()
    b2c = nc.dram_tensor("b2c", [2, 128, 8], F32, kind="ExternalInput").ap()
    qsc = nc.dram_tensor("qsc", [2, 128, 1], F32, kind="ExternalInput").ap()
    ksc = nc.dram_tensor("ksc", [2, 128, 1], F32, kind="ExternalInput").ap()
    lgc = nc.dram_tensor("lgc", [2, 128, 8], F32, kind="ExternalInput").ap()
    lbc = nc.dram_tensor("lbc", [2, 128, 8], F32, kind="ExternalInput").ap()
    c1q = nc.dram_tensor("c1q", [128, T], F32, kind="ExternalInput").ap()
    c2q = nc.dram_tensor("c2q", [128, T], F32, kind="ExternalInput").ap()
    c1k = nc.dram_tensor("c1k", [128, T], F32, kind="ExternalInput").ap()
    c2k = nc.dram_tensor("c2k", [128, T], F32, kind="ExternalInput").ap()
    prot = nc.dram_tensor("prot", [128, 128], F32, kind="ExternalInput").ap()
    masklo = nc.dram_tensor("masklo", [128, 128], F32,
                            kind="ExternalInput").ap()
    maskhi = nc.dram_tensor("maskhi", [128, 128], F32,
                            kind="ExternalInput").ap()
    idenf = nc.dram_tensor("idenf", [128, 128], F32,
                           kind="ExternalInput").ap()
    nsel2 = nc.dram_tensor("nsel2", [128, 2], F32, kind="ExternalInput").ap()
    bsel2 = nc.dram_tensor("bsel2", [2, 128], F32, kind="ExternalInput").ap()
    xo = nc.dram_tensor("xo", [D, T], F32, kind="ExternalOutput").ap()
    xb_d = nc.dram_tensor("xb_scr", [D, T], F32,
                          kind="ExternalOutput" if debug else "Internal").ap()
    if debug:
        dbg_xbl = nc.dram_tensor("dbg_xbl", [D, W0 := 512], F32,
                                 kind="ExternalOutput").ap()
        dbg_qrot = nc.dram_tensor("dbg_qrot", [512, 512], F32,
                                  kind="ExternalOutput").ap()
        dbg_krot = nc.dram_tensor("dbg_krot", [512, 512], F32,
                                  kind="ExternalOutput").ap()
        dbg_o = nc.dram_tensor("dbg_o", [512, 512], F32,
                               kind="ExternalOutput").ap()
        dbg_xmid = nc.dram_tensor("dbg_xmid", [D, 512], F32,
                                  kind="ExternalOutput").ap()

    NT = 8
    W = 512

    p = fused["p"] if fused else P(nc, ExitStack())
    wqkv_s = [p.sbuf(f"wqkv_{kc}", [128, 1536], BF16) for kc in range(8)]
    wo_s = [p.sbuf(f"wo_{kc}", [128, 1024], BF16) for kc in range(4)]
    w1sl = [p.sbuf(f"w1sl_{b}", [128, 1024], BF16) for b in range(4)]
    w2sl = [p.sbuf(f"w2sl_{b}", [128, 512], BF16) for b in range(4)]
    lsems = ("ldwfa", "ldwfb", "ldwfc", "ldwfd")
    b1_s = p.sbuf("b1_s", [128, 44], F32)
    b2_s = p.sbuf("b2_s", [128, 8], F32)
    qs_s = p.sbuf("qs_s", [128, 1], F32)
    ks_s = p.sbuf("ks_s", [128, 1], F32)
    lg_s = p.sbuf("lg_s", [128, 8], F32)
    lb_s = p.sbuf("lb_s", [128, 8], F32)
    prot_s = p.sbuf("prot_s", [128, 128], F32)
    mlo_s = p.sbuf("mlo_s", [128, 128], BF16)
    mhi_s = p.sbuf("mhi_s", [128, 128], BF16)
    idf_s = p.sbuf("idf_s", [128, 128], BF16)
    nsel_s = p.sbuf("nsel_s", [128, 2], F32)
    bsel_s = p.sbuf("bsel_s", [2, 128], F32)
    ct = {nm: p.sbuf(f"ct_{nm}", [128, W], F32)
          for nm in ("c1q", "c2q", "c1k", "c2k")}
    ones_col = p.sbuf("ones_col", [128, 1], F32)
    ones_row = p.sbuf("ones_row", [1, 128], F32)
    eps_col = p.sbuf("eps_col", [128, 1], F32)

    xch = [[p.sbuf(f"x_{c}_{b}", [128, W + 1], F32) for b in range(2)]
           for c in range(8)]
    xbl = [p.sbuf(f"xbl_{c}", [128, W + 1], F32) for c in range(8)]
    xh16 = [p.sbuf(f"xh16_{c}", [128, W + 1], BF16) for c in range(8)]
    xmid = [p.sbuf(f"xmid_{c}", [128, W + 1], F32) for c in range(8)]
    hn16 = [xh16[c][:, 0:W] for c in range(8)]
    sq = [p.sbuf(f"sq_{b}", [128, W], F32) for b in range(2)]
    rb = p.sbuf("rb", [128, W], F32)
    mrb = p.sbuf("mrb", [128, W], F32)
    m_r = p.sbuf("m_r", [1, W], F32)
    q_r = p.sbuf("q_r", [1, W], F32)
    m2_r = p.sbuf("m2_r", [1, W], F32)
    var_r = p.sbuf("var_r", [1, W], F32)
    lnv_r = p.sbuf("lnv_r", [1, W], F32)
    rs_r = p.sbuf("rs_r", [1, W], F32)
    mr_r = p.sbuf("mr_r", [1, W], F32)
    qf = p.sbuf("qf", [128, W], F32)
    rn2 = p.sbuf("rn2", [2, W], F32)
    lnn = p.sbuf("lnn", [2, W], F32)
    qhat = p.sbuf("qhat", [128, W], F32)
    glu = p.sbuf("glu", [128, W], F32)
    t1f = glu
    t2f = None
    qrot = [p.sbuf(f"qrot_{c}", [128, W], BF16) for c in range(4)]
    krot = [[p.sbuf(f"krot_{c}_{b}", [128, W], BF16) for b in range(2)]
            for c in range(4)]
    vtok = [[p.sbuf(f"vtok_{s}_{b}", [128, W], BF16) for b in range(2)]
            for s in range(4)]
    ptile = p.sbuf("ptile", [128, 640], BF16)
    mcol = p.sbuf("mcol", [128, 1], F32)
    nmcol = p.sbuf("nmcol", [128, 1], F32)
    lcol = p.sbuf("lcol", [128, 1], F32)
    rlc = p.sbuf("rlc", [128, 1], F32)
    diag = p.sbuf("diag", [128, 128], BF16)
    ptr = ptile
    o16 = qrot
    gat = [p.sbuf(f"gat_{pc}", [128, W], BF16) for pc in range(21)]
    gat_t = p.sbuf("gat_t", [42, W], BF16)
    vout = [p.sbuf(f"vout_{b}", [128, W], F32) for b in range(2)]

    ps_a = p.psum("ps_a", [128, W])
    ps_b = p.psum("ps_b", [128, W])
    ps_c = p.psum("ps_c", [128, W])
    ps_d = p.psum("ps_d", [128, W])
    ps_n = p.psum("ps_n", [128, W])
    ps_s2 = p.psum("ps_s2", [128, 1024])
    ps_tr = ps_s2

    for s in ("ldc", "ldw", "ldwv", "lda", "ldb", "ldwfa", "ldwfb",
              "ldwfc", "ldwfd", "pe", "dv", "ac", "st0", "st1", "sta0",
              "sta1", "xcp", "dbg", "dbgs"):
        p.sem(s)

    def body():
        p.op("vector", "memset", ones_col[:], 1.0)
        p.op("vector", "memset", ones_row[:], 1.0)
        p.op("vector", "memset", eps_col[:], 1e-5, inc="dv")
        memset_done = p.val("dv")
        for dst, srcd in ((prot_s, prot), (idf_s, idenf),
                          (mlo_s, masklo), (mhi_s, maskhi),
                          (nsel_s, nsel2), (bsel_s, bsel2)):
            p.dma("gpsimd", dst[:], srcd, inc="ldc")
        ldc_all = p.val("ldc")

        rec = {"rows_read_psum": 0, "copies_done": 0, "lay_pe": 0,
               "xbl_done": 0}

        for lay in range(2):
            if lay == 1:
                p.wait("gpsimd", "pe", rec["lay_pe"])
                p.wait("sync", "pe", rec["lay_pe"])
                for c in range(8):
                    p.dma("sync", xo[128 * c:128 * c + 128, :],
                          x[128 * c:128 * c + 128, :], inc="xcp")
            for kc in range(8):
                p.dma("gpsimd", wqkv_s[kc][:], wqkv[lay, kc], inc="ldw")
            for kc in range(4):
                p.dma("gpsimd", wo_s[kc][:], wo[lay, kc], inc="ldw")
            for dst, srcd in ((qs_s, qsc[lay]), (ks_s, ksc[lay]),
                              (lg_s, lgc[lay]), (lb_s, lbc[lay]),
                              (b1_s, b1r[lay]), (b2_s, b2c[lay])):
                p.dma("sync", dst[:], srcd, inc="ldwv")
            ldw_lay = p.val("ldw")
            ldwv_lay = p.val("ldwv")

            p.wait("vector", "pe", rec["lay_pe"])
            for c in range(4):
                p.op("vector", "memset", krot[c][1][:], 0.0)
                p.op("vector", "memset", vtok[c][1][:], 0.0)
            for c in range(8):
                p.op("vector", "memset", xh16[c][:, 0:1], 0.0)
                p.op("vector", "memset", xbl[c][:, 0:1], 0.0)
                p.op("vector", "memset", xmid[c][:, 0:1], 0.0,
                     inc="dv" if c == 7 else None)
            lay_memset = p.val("dv")

            for i in range(NT):
                buf = i % 2
                pbuf = 1 - buf
                t0 = i * W
                ldsem = "lda" if buf == 0 else "ldb"
                src_d = x if lay == 0 else xb_d
                sem0 = "st0" if lay == 0 else "sta0"
                sem1 = "st1" if lay == 0 else "sta1"
                if i >= 2:
                    p.wait("sync", sem0, rec[f"{sem0}_{i-2}"])
                    p.wait("sync", sem1, rec[f"{sem1}_{i-2}"])
                elif lay == 1 and i < 2:
                    p.wait("sync", "st0", rec["lay0_st0"])
                    p.wait("sync", "st1", rec["lay0_st1"])
                for c in range(8):
                    r0 = 128 * c
                    if i == 0:
                        p.dma("sync", xch[c][buf][:, 1:W + 1],
                              src_d[r0:r0 + 128, 0:W], inc=ldsem)
                    else:
                        p.dma("sync", xch[c][buf][:, 0:W + 1],
                              src_d[r0:r0 + 128, t0 - 1:t0 + W],
                              inc=ldsem)
                if i >= 1:
                    p.wait("sync", "dv", rec[f"qkrot_{i-1}"])
                for nm, csrc in (("c1q", c1q), ("c2q", c2q),
                                 ("c1k", c1k), ("c2k", c2k)):
                    p.dma("sync", ct[nm][:], csrc[:, t0:t0 + W],
                          inc=ldsem)
                ld_tile = p.val(ldsem)

                if lay == 0:
                    p.wait("scalar", ldsem, ld_tile)
                    p.wait("scalar", "dv", rec["xbl_done"])
                    p.wait("tensor", "dv",
                           max(rec["rows_read_psum"], memset_done,
                               lay_memset))
                    p.wait("tensor", ldsem, ld_tile)
                    pe_after = []
                    for c in range(8):
                        if c >= 2:
                            p.wait("scalar", "pe", pe_after[c - 2])
                        p.op("scalar", "activation", sq[c % 2][:],
                             xch[c][buf][:, 1:W + 1], AF.Square,
                             bias=0.0, scale=1.0, inc="ac")
                        p.wait("tensor", "ac")
                        p.op("tensor", "matmul", ps_a[0:1, :],
                             ones_col[:], xch[c][buf][:, 1:W + 1],
                             start=(c == 0), stop=(c == 7))
                        p.op("tensor", "matmul", ps_b[0:1, :],
                             ones_col[:], sq[c % 2][:],
                             start=(c == 0), stop=(c == 7), inc="pe")
                        pe_after.append(p.val("pe"))
                    p.wait("vector", "pe")
                    p.op("vector", "tensor_scalar", m_r[:], ps_a[0:1, :],
                         1.0 / D, 0.0, ALU.mult, ALU.add)
                    p.op("vector", "tensor_scalar", q_r[:], ps_b[0:1, :],
                         1.0 / D, 0.0, ALU.mult, ALU.add, inc="dv")
                    rec["rows_read_psum"] = p.val("dv")
                    p.op("vector", "tensor_tensor", m2_r[:], m_r[:],
                         m_r[:], ALU.mult)
                    p.op("vector", "tensor_tensor", var_r[:], q_r[:],
                         m2_r[:], ALU.subtract, inc="dv")
                    p.wait("scalar", "dv")
                    p.op("scalar", "activation", lnv_r[:], var_r[:],
                         AF.Ln, bias=eps_col[0:1, 0:1], scale=1.0)
                    p.op("scalar", "activation", rs_r[:], lnv_r[:],
                         AF.Exp, bias=0.0, scale=-0.5, inc="ac")
                    p.wait("vector", "ac")
                    p.op("vector", "tensor_tensor", mr_r[:], m_r[:],
                         rs_r[:], ALU.mult, inc="dv")
                    p.wait("tensor", "dv")
                    p.wait("tensor", "ac", rec["copies_done"])
                    p.op("tensor", "matmul", ps_c[:], ones_row[:],
                         rs_r[:], start=True, stop=True)
                    p.op("tensor", "matmul", ps_d[:], ones_row[:],
                         mr_r[:], start=True, stop=True, inc="pe")
                    p.wait("scalar", "pe")
                    p.op("scalar", "copy", rb[:], ps_c[:], inc="ac")
                    p.op("scalar", "copy", mrb[:], ps_d[:], inc="ac")
                    rec["copies_done"] = p.val("ac")
                    p.wait("vector", "ac")
                    if debug and i == 1 and lay == dbg_lay:
                        p.wait("vector", "dbgs", 256)
                    for c in range(8):
                        if i > 0:
                            p.op("vector", "tensor_copy",
                                 xbl[c][:, 0:1], xbl[c][:, W:W + 1])
                            p.op("vector", "tensor_copy",
                                 xh16[c][:, 0:1], xbl[c][:, 0:1])
                        p.op("vector", "tensor_tensor",
                             sq[c % 2][:], xch[c][buf][:, 1:W + 1],
                             rb[:], ALU.mult)
                        p.op("vector", "tensor_tensor",
                             xbl[c][:, 1:W + 1], sq[c % 2][:], mrb[:],
                             ALU.subtract)
                        p.op("vector", "tensor_copy",
                             xh16[c][:, 1:W + 1], xbl[c][:, 1:W + 1],
                             inc="dv" if c == 7 else None)
                    rec["xbl_done"] = p.val("dv")
                else:
                    p.wait("vector", ldsem, ld_tile)
                    p.wait("vector", "dv", rec["xbl_done"])
                    p.wait("vector", "pe", rec.get("qkv_pe", 0))
                    if debug and i == 1 and lay == dbg_lay:
                        p.wait("vector", "dbgs", 256)
                    for c in range(8):
                        if i > 0:
                            p.op("vector", "tensor_copy",
                                 xbl[c][:, 0:1], xbl[c][:, W:W + 1])
                            p.op("vector", "tensor_copy",
                                 xh16[c][:, 0:1], xbl[c][:, 0:1])
                        p.op("vector", "tensor_copy",
                             xbl[c][:, 1:W + 1],
                             xch[c][buf][:, 1:W + 1])
                        p.op("vector", "tensor_copy",
                             xh16[c][:, 1:W + 1],
                             xch[c][buf][:, 1:W + 1],
                             inc="dv" if c == 7 else None)
                    rec["xbl_done"] = p.val("dv")
                xbl_done = rec["xbl_done"]

                def hsl(c, wdt=W):
                    return xh16[c][:, 1:wdt + 1] if c < 4 else \
                        xh16[c][:, 0:wdt]

                if debug and lay == dbg_lay and i == 1:
                    p.wait("vector", "dbg", 192)
                p.wait("tensor", "dv", xbl_done)
                if i == 0:
                    p.wait("tensor", "ldw", ldw_lay)
                    p.wait("tensor", "ldwv", ldwv_lay)
                    p.wait("tensor", "ldc", ldc_all)
                    p.wait("vector", "ldwv", ldwv_lay)
                    p.wait("scalar", "ldwv", ldwv_lay)
                for of in range(8):
                    isq = of < 4
                    cc = of % 4
                    bank = ps_a if of % 2 == 0 else ps_b
                    p.wait("tensor", "ac", rec.get(f"qk_ev{of % 2}", 0))
                    p.wait("tensor", "dv", rec.get("rot_dn_last", 0))
                    for kc in range(8):
                        p.op("tensor", "matmul", bank[:],
                             wqkv_s[kc][:, 128 * of:128 * of + 128],
                             hsl(kc), start=(kc == 0), stop=(kc == 7),
                             inc="pe" if kc == 7 else None)
                    p.wait("scalar", "pe")
                    p.op("scalar", "copy", qf[:], bank[:], inc="ac")
                    p.wait("scalar", "ac")
                    p.op("scalar", "activation", sq[0][:], qf[:],
                         AF.Square, bias=0.0, scale=1.0, inc="ac")
                    rec[f"qk_ev{of % 2}"] = p.val("ac")
                    p.wait("tensor", "ac", rec[f"qk_ev{of % 2}"])
                    p.op("tensor", "matmul", ps_n[0:2, :], nsel_s[:],
                         sq[0][:], start=True, stop=True, inc="pe")
                    p.wait("vector", "pe")
                    p.op("vector", "tensor_scalar", rn2[:], ps_n[0:2, :],
                         1.0, 1e-24, ALU.mult, ALU.max, inc="dv")
                    p.wait("scalar", "dv")
                    p.op("scalar", "activation", lnn[:], rn2[:], AF.Ln,
                         bias=0.0, scale=1.0)
                    p.op("scalar", "activation", rn2[:], lnn[:], AF.Exp,
                         bias=0.0, scale=-0.5, inc="ac")
                    p.wait("tensor", "ac")
                    p.op("tensor", "matmul", ps_n[:], bsel_s[:], rn2[:],
                         start=True, stop=True, inc="pe")
                    p.wait("vector", "pe")
                    p.op("vector", "tensor_tensor", qhat[:], qf[:],
                         ps_n[:], ALU.mult)
                    p.op("vector", "tensor_scalar", qhat[:], qhat[:],
                         qs_s[:, 0:1] if isq else ks_s[:, 0:1], 0.0,
                         ALU.mult, ALU.add, inc="dv")
                    qhat_dn = p.val("dv")
                    p.wait("tensor", "dv", qhat_dn)
                    p.op("tensor", "matmul", ps_n[:], prot_s[:],
                         qhat[:], start=True, stop=True, inc="pe")
                    c1 = ct["c1q" if isq else "c1k"]
                    c2 = ct["c2q" if isq else "c2k"]
                    p.wait("vector", "pe")
                    p.op("vector", "tensor_tensor", t1f[:], qhat[:],
                         c1[:], ALU.mult)
                    p.op("vector", "tensor_tensor", sq[1][:], ps_n[:],
                         c2[:], ALU.mult)
                    dst = qrot[cc] if isq else krot[cc][buf]
                    p.op("vector", "tensor_tensor", dst[:], t1f[:],
                         sq[1][:], ALU.add, inc="dv")
                    rec[f"rot_dn{of % 2}"] = p.val("dv")
                    rec["rot_dn_last"] = p.val("dv")
                qk_done = p.val("dv")
                rec[f"qkrot_{i}"] = qk_done
                if debug and lay == dbg_lay and i == 0:
                    p.wait("sync", "dv", qk_done)
                    p.wait("gpsimd", "dv", qk_done)
                    for c in range(8):
                        p.dma("sync", dbg_xbl[128 * c:128 * c + 128, :],
                              xbl[c][:, 1:W + 1], inc="dbgs")
                    for c in range(4):
                        p.dma("gpsimd", dbg_qrot[128 * c:128 * c + 128, :],
                              qrot[c][:], inc="dbg")
                        p.dma("gpsimd", dbg_krot[128 * c:128 * c + 128, :],
                              krot[c][buf][:], inc="dbg")
                    rec["dbg_q"] = p.val("dbg")

                for s4 in range(4):
                    bank = ps_a if s4 % 2 == 0 else ps_b
                    p.wait("tensor", "dv",
                           rec.get(f"v_ev{s4 % 2}", 0))
                    for kc in range(8):
                        st = (1 if kc < 4 else 0) + 128 * s4
                        p.op("tensor", "matmul", bank[:],
                             xh16[kc][:, st:st + 128],
                             wqkv_s[kc][:, 1024:1536],
                             start=(kc == 0), stop=(kc == 7),
                             inc="pe" if kc == 7 else None)
                    p.wait("vector", "pe")
                    p.op("vector", "tensor_copy", vtok[s4][buf][:],
                         bank[:], inc="dv")
                    rec[f"v_ev{s4 % 2}"] = p.val("dv")
                v_done = p.val("dv")

                if debug and lay == dbg_lay and i == 0:
                    p.wait("vector", "dbg", rec["dbg_q"])
                p.wait("tensor", "dv", v_done)
                for h in range(8):
                    hc, hb = h // 2, 64 * (h % 2)
                    for s4 in range(4):
                        wprev = W - 128 * s4
                        p.wait("tensor", "ac",
                               max(rec.get("exp_dn", 0),
                                   rec.get("ptr_ev", 0)))
                        p.wait("tensor", "dv", rec.get("msk_dn", 0))
                        qsl = qrot[hc][hb:hb + 64,
                                       128 * s4:128 * s4 + 128]
                        if i > 0:
                            p.op("tensor", "matmul",
                                 ps_s2[:, 0:wprev], qsl,
                                 krot[hc][pbuf][hb:hb + 64,
                                                128 * s4:W],
                                 start=True, stop=True)
                        spans = ([(wprev, W), (W, 640)]
                                 if wprev < W else [(W, 640)])
                        for (lo, hi) in spans:
                            p.op("tensor", "matmul",
                                 ps_s2[:, lo:hi], qsl,
                                 krot[hc][buf][hb:hb + 64,
                                               lo - wprev:hi - wprev],
                                 start=True, stop=True,
                                 inc="pe" if hi == 640 else None)
                        sdone = p.val("pe")
                        p.wait("vector", "pe", sdone)
                        if i == 0:
                            p.op("vector", "memset",
                                 ps_s2[:, 0:wprev], -1e30)
                        else:
                            p.op("vector", "tensor_tensor",
                                 ps_s2[:, 0:128], ps_s2[:, 0:128],
                                 mlo_s[:], ALU.add)
                        p.op("vector", "tensor_tensor",
                             ps_s2[:, 512:640], ps_s2[:, 512:640],
                             mhi_s[:], ALU.add, inc="dv")
                        rec["msk_dn"] = p.val("dv")
                        p.op("vector", "tensor_reduce", mcol[:],
                             ps_s2[:, 0:640], AX.X, ALU.max)
                        p.op("vector", "tensor_scalar", nmcol[:],
                             mcol[:], -1.0, 0.0, ALU.mult, ALU.add,
                             inc="dv")
                        p.wait("scalar", "dv")
                        p.op("scalar", "activation", ptile[:],
                             ps_s2[:, 0:640], AF.Exp, bias=nmcol[:],
                             scale=1.0, accum_out=lcol[:], inc="ac")
                        rec["exp_dn"] = p.val("ac")
                        p.wait("vector", "ac", rec["exp_dn"])
                        p.op("vector", "reciprocal", rlc[:], lcol[:])
                        p.op("vector", "tensor_scalar", diag[:],
                             idf_s[:], rlc[:, 0:1], 0.0, ALU.mult,
                             ALU.add, inc="dv")
                        diag_dn = p.val("dv")
                        p.wait("tensor", "dv", diag_dn)
                        for uc in range(5):
                            p.op("tensor", "matmul",
                                 ps_tr[:, 128 * uc:128 * uc + 128],
                                 ptile[:, 128 * uc:128 * uc + 128],
                                 diag[:], start=True, stop=True,
                                 inc="pe" if uc == 4 else None)
                        p.wait("scalar", "pe")
                        p.op("scalar", "copy", ptr[:],
                             ps_tr[:, 0:640], inc="ac")
                        rec["ptr_ev"] = p.val("ac")
                        p.wait("tensor", "ac", rec["ptr_ev"])
                        p.wait("tensor", "dv", rec.get("y_ev", 0))
                        for uc in range(5):
                            ku = 128 * s4 + 128 * uc
                            vpar = pbuf if ku < W else buf
                            vsub = (ku % W) // 128
                            p.op("tensor", "matmul", ps_d[0:64, 0:128],
                                 vtok[vsub][vpar][:, 64 * h:64 * h + 64],
                                 ptr[:, 128 * uc:128 * uc + 128],
                                 start=(uc == 0), stop=(uc == 4),
                                 inc="pe" if uc == 4 else None)
                        p.wait("vector", "pe")
                        p.op("vector", "tensor_copy",
                             o16[hc][hb:hb + 64,
                                     128 * s4:128 * s4 + 128],
                             ps_d[0:64, 0:128], inc="dv")
                        rec["y_ev"] = p.val("dv")
                attn_done = p.val("dv")
                if debug and lay == dbg_lay and i == 0:
                    p.wait("gpsimd", "dv", attn_done)
                    for c in range(4):
                        p.dma("gpsimd", dbg_o[128 * c:128 * c + 128, :],
                              o16[c][:], inc="dbg")

                p.wait("tensor", "dv", attn_done)
                if debug and i == 1 and lay == dbg_lay:
                    p.wait("vector", "dbgs", p.val("dbgs"))
                for oc in range(8):
                    bank = ps_a if oc % 2 == 0 else ps_b
                    p.wait("tensor", "dv", rec.get(f"xm_dn{oc % 2}", 0))
                    for kc in range(4):
                        p.op("tensor", "matmul", bank[:],
                             wo_s[kc][:, 128 * oc:128 * oc + 128],
                             o16[kc][:], start=(kc == 0), stop=(kc == 3),
                             inc="pe" if kc == 3 else None)
                    p.wait("vector", "pe")
                    if oc < 8:
                        pass
                    p.op("vector", "tensor_copy", xmid[oc][:, 0:1],
                         xmid[oc][:, W:W + 1]) if i > 0 else None
                    p.op("vector", "tensor_tensor",
                         xmid[oc][:, 1:W + 1], bank[:],
                         xbl[oc][:, 1:W + 1], ALU.add,
                         inc="dv")
                    rec[f"xm_dn{oc % 2}"] = p.val("dv")
                xmid_done = p.val("dv")
                if debug and lay == dbg_lay and i == 0:
                    p.wait("sync", "dv", xmid_done)
                    for c in range(8):
                        p.dma("sync", dbg_xmid[128 * c:128 * c + 128, :],
                              xmid[c][:, 1:W + 1], inc="dbgs")

                def h2sl(c, wdt=W):
                    return xmid[c][:, 1:wdt + 1] if c < 4 else \
                        xmid[c][:, 0:wdt]
                p.wait("scalar", "dv", xmid_done)
                p.wait("tensor", "dv", xmid_done)
                pe_after = []
                for c in range(8):
                    if c >= 2:
                        p.wait("scalar", "pe", pe_after[c - 2])
                    p.op("scalar", "activation", sq[c % 2][:], h2sl(c),
                         AF.Square, bias=0.0, scale=1.0, inc="ac")
                    p.wait("tensor", "ac")
                    p.op("tensor", "matmul", ps_a[0:1, :], ones_col[:],
                         h2sl(c), start=(c == 0), stop=(c == 7))
                    p.op("tensor", "matmul", ps_b[0:1, :], ones_col[:],
                         sq[c % 2][:], start=(c == 0), stop=(c == 7),
                         inc="pe")
                    pe_after.append(p.val("pe"))
                p.wait("vector", "pe")
                p.op("vector", "tensor_scalar", m_r[:], ps_a[0:1, :],
                     1.0 / D, 0.0, ALU.mult, ALU.add)
                p.op("vector", "tensor_scalar", q_r[:], ps_b[0:1, :],
                     1.0 / D, 0.0, ALU.mult, ALU.add, inc="dv")
                p.op("vector", "tensor_tensor", m2_r[:], m_r[:], m_r[:],
                     ALU.mult)
                p.op("vector", "tensor_tensor", var_r[:], q_r[:],
                     m2_r[:], ALU.subtract, inc="dv")
                p.wait("scalar", "dv")
                p.op("scalar", "activation", lnv_r[:], var_r[:], AF.Ln,
                     bias=eps_col[0:1, 0:1], scale=1.0)
                p.op("scalar", "activation", rs_r[:], lnv_r[:], AF.Exp,
                     bias=0.0, scale=-0.5, inc="ac")
                p.wait("vector", "ac")
                p.op("vector", "tensor_tensor", mr_r[:], m_r[:], rs_r[:],
                     ALU.mult, inc="dv")
                p.wait("tensor", "dv")
                p.wait("tensor", "ac", rec["copies_done"])
                p.op("tensor", "matmul", ps_c[:], ones_row[:], rs_r[:],
                     start=True, stop=True)
                p.op("tensor", "matmul", ps_d[:], ones_row[:], mr_r[:],
                     start=True, stop=True, inc="pe")
                p.wait("scalar", "pe")
                p.op("scalar", "copy", rb[:], ps_c[:], inc="ac")
                p.op("scalar", "copy", mrb[:], ps_d[:], inc="ac")
                rec["copies_done"] = p.val("ac")
                p.wait("vector", "ac")
                for c in range(8):
                    p.op("vector", "tensor_tensor", sq[c % 2][:],
                         h2sl(c), rb[:], ALU.mult)
                    p.op("vector", "tensor_tensor", sq[c % 2][:],
                         sq[c % 2][:], mrb[:], ALU.subtract)
                    p.op("vector", "tensor_scalar", hn16[c],
                         sq[c % 2][:], lg_s[:, c:c + 1],
                         lb_s[:, c:c + 1], ALU.mult, ALU.add,
                         inc="dv" if c == 7 else None)
                hn_done = p.val("dv")

                p.wait("tensor", "dv", hn_done)
                units = [(pc, ph) for pc in range(21) for ph in range(2)]
                units.append((21, 0))
                for ui, (pc, ph) in enumerate(units):
                    wb = ui % 4
                    lsem = lsems[wb]
                    p.wait("gpsimd", "pe", rec.get(f"w1_used_{wb}", 0))
                    p.wait("gpsimd", lsem, p.val(lsem))
                    if pc < 21:
                        p.dma("gpsimd", w1sl[wb][:],
                              w1r[lay, pc][:, 1024 * ph:1024 * ph + 1024],
                              inc=lsem)
                    else:
                        p.dma("gpsimd", w1sl[wb][:, 0:672],
                              w1t[lay], inc=lsem)
                    ldwf_now = p.val(lsem)
                    par = pc % 2
                    bank_a, bank_g = (ps_a, ps_b) if par == 0 else \
                        (ps_c, ps_d)
                    if ph == 0:
                        p.wait("tensor", "dv", rec.get(f"gate_dn{par}", 0))
                        p.wait("tensor", "ac", rec.get(f"glu_dn{par}", 0))
                    p.wait("tensor", lsem, ldwf_now)
                    if pc < 21:
                        for kk2 in range(4):
                            kc = 4 * ph + kk2
                            p.op("tensor", "matmul", bank_a[:],
                                 w1sl[wb][:, 256 * kk2:256 * kk2 + 128],
                                 hn16[kc], start=(kc == 0),
                                 stop=(kc == 7))
                            p.op("tensor", "matmul", bank_g[:],
                                 w1sl[wb][:, 256 * kk2 + 128:256 * kk2 + 256],
                                 hn16[kc], start=(kc == 0),
                                 stop=(kc == 7),
                                 inc="pe" if kk2 == 3 else None)
                        rec[f"w1_used_{wb}"] = p.val("pe")
                        if ph == 0:
                            continue
                        asl, gsl = bank_a[:], bank_g[:]
                        gdst = gat[pc][:]
                        bia = b1_s[:, 2 * pc:2 * pc + 1]
                        big = b1_s[:, 2 * pc + 1:2 * pc + 2]
                    else:
                        for kc in range(8):
                            p.op("tensor", "matmul", bank_a[0:42, :],
                                 w1sl[wb][:, 84 * kc:84 * kc + 42],
                                 hn16[kc], start=(kc == 0),
                                 stop=(kc == 7))
                            p.op("tensor", "matmul", bank_g[0:42, :],
                                 w1sl[wb][:, 84 * kc + 42:84 * kc + 84],
                                 hn16[kc], start=(kc == 0),
                                 stop=(kc == 7),
                                 inc="pe" if kc == 7 else None)
                        rec[f"w1_used_{wb}"] = p.val("pe")
                        asl, gsl = bank_a[0:42, :], bank_g[0:42, :]
                        gdst = gat_t[:]
                        bia = b1_s[0:42, 42:43]
                        big = b1_s[0:42, 43:44]
                    p.wait("scalar", "pe")
                    p.op("scalar", "activation", gdst, gsl, AF.Gelu,
                         bias=big, scale=1.0, inc="ac")
                    rec[f"glu_dn{par}"] = p.val("ac")
                    p.wait("vector", "ac", rec[f"glu_dn{par}"])
                    p.op("vector", "scalar_tensor_tensor", gdst, asl,
                         bia, gdst, ALU.add, ALU.mult, inc="dv")
                    rec[f"gate_dn{par}"] = p.val("dv")
                gate_all = p.val("dv")

                p.wait("tensor", "dv", gate_all)
                for half in range(2):
                    for pc in range(22):
                        wb = pc % 4
                        lsem = lsems[wb]
                        p.wait("gpsimd", "pe", rec.get(
                            f"w2_used_{wb}", 0))
                        p.wait("gpsimd", lsem, p.val(lsem))
                        if pc < 21:
                            p.dma("gpsimd", w2sl[wb][:],
                                  w2r[lay, pc][:, 512 * half:
                                               512 * half + 512],
                                  inc=lsem)
                            wsl = w2sl[wb]
                            gsl2 = gat[pc][:]
                        else:
                            p.dma("gpsimd", w2sl[wb][0:42, :],
                                  w2t[lay][:, 512 * half:512 * half + 512],
                                  inc=lsem)
                            wsl = w2sl[wb]
                            gsl2 = gat_t[:]
                        ldwf_now = p.val(lsem)
                        p.wait("tensor", lsem, ldwf_now)
                        for q4 in range(4):
                            bankq = [ps_a, ps_b, ps_c, ps_d][q4]
                            if pc == 0:
                                p.wait("tensor", "dv",
                                       rec.get(f"ffn_out{q4}", 0))
                            if pc < 21:
                                p.op("tensor", "matmul", bankq[:],
                                     wsl[:, 128 * q4:128 * q4 + 128],
                                     gsl2, start=(pc == 0), stop=False,
                                     inc="pe" if q4 == 3 else None)
                            else:
                                p.op("tensor", "matmul", bankq[:],
                                     wsl[0:42, 128 * q4:128 * q4 + 128],
                                     gsl2, start=False, stop=True,
                                     inc="pe" if q4 == 3 else None)
                        rec[f"w2_used_{wb}"] = p.val("pe")
                    w2_pe = p.val("pe")
                    p.wait("vector", "pe", w2_pe)
                    if lay == 1 and i == 0:
                        pass
                    for q4 in range(4):
                        oc = 4 * half + q4
                        bankq = [ps_a, ps_b, ps_c, ps_d][q4]
                        sb = oc % 2
                        stsem = sem0 if sb == 0 else sem1
                        if f"store_{stsem}" in rec:
                            p.wait("vector", stsem,
                                   rec[f"store_{stsem}"])
                        p.op("vector", "scalar_tensor_tensor",
                             vout[sb][:], bankq[:],
                             b2_s[:, oc:oc + 1], xmid[oc][:, 1:W + 1],
                             ALU.add, ALU.add, inc="dv")
                        rec[f"ffn_out{q4}"] = p.val("dv")
                        if lay == 0:
                            p.wait("sync", "dv")
                            p.dma("sync",
                                  xb_d[128 * oc:128 * oc + 128,
                                       t0:t0 + W],
                                  vout[sb][:], inc=stsem)
                        else:
                            p.wait("gpsimd", "dv")
                            p.wait("gpsimd", "xcp", 128)
                            p.dma("gpsimd",
                                  xo[128 * oc:128 * oc + 128, t0:t0 + W],
                                  vout[sb][:], accum_op=ALU.add,
                                  inc=stsem)
                        rec[f"store_{stsem}"] = p.val(stsem)
                rec[f"{sem0}_{i}"] = p.val(sem0)
                rec[f"{sem1}_{i}"] = p.val(sem1)
                rec["qkv_pe"] = p.val("pe")
            rec["lay_pe"] = p.val("pe")
            rec["lay0_st0"] = p.val("st0")
            rec["lay0_st1"] = p.val("st1")
            for k in ("st0_0", "st1_0"):
                pass
        p.wait("gpsimd", "st0", p.val("st0"))
        p.wait("gpsimd", "st1", p.val("st1"))
        p.wait("gpsimd", "sta0", p.val("sta0"))
        p.wait("gpsimd", "sta1", p.val("sta1"))
        if debug:
            p.wait("gpsimd", "dbg", p.val("dbg"))
            p.wait("gpsimd", "dbgs", p.val("dbgs"))

    if fused:
        return body
    emit_all(p, body)
    return nc


def l0_ffn_consts():
    half = np.arange(0, 64, 2, dtype=np.float64)
    inv_freq = 1.0 / 10000.0 ** (half / 64.0)
    tpos = np.arange(T, dtype=np.float64)
    freqs = tpos[:, None] * inv_freq[None, :]
    freqs = np.concatenate([freqs, freqs], axis=-1)
    base = (half + 0.4 * 64) / (1.4 * 64)
    power = (tpos - T // 2) / 256.0
    scale = base[None, :] ** power[:, None]
    scale = np.concatenate([scale, scale], axis=-1)
    cos, sin = np.cos(freqs), np.sin(freqs)

    def dup(a):
        a = np.ascontiguousarray(a.T.astype(np.float32))
        return np.concatenate([a, a], axis=0)
    c1q = dup(cos * scale * 8.0)
    c2q = dup(sin * scale * 8.0)
    c1k = dup(cos / scale)
    c2k = dup(sin / scale)
    pr = np.zeros((128, 128), dtype=np.float32)
    for blk in range(2):
        b0 = 64 * blk
        for c in range(32):
            pr[b0 + c + 32, b0 + c] = -1.0
            pr[b0 + c, b0 + c + 32] = 1.0
    r = np.arange(128)[:, None]
    u = np.arange(128)[None, :]
    masklo = np.where(u < r, -1e38, 0.0).astype(np.float32)
    maskhi = np.where(u > r, -1e38, 0.0).astype(np.float32)
    idenf = np.eye(128, dtype=np.float32)
    nsel2 = np.zeros((128, 2), dtype=np.float32)
    nsel2[0:64, 0] = 1.0
    nsel2[64:128, 1] = 1.0
    bsel2 = np.zeros((2, 128), dtype=np.float32)
    bsel2[0, 0:64] = 1.0
    bsel2[1, 64:128] = 1.0
    return dict(c1q=c1q, c2q=c2q, c1k=c1k, c2k=c2k, prot=pr,
                masklo=masklo, maskhi=maskhi, idenf=idenf,
                nsel2=nsel2, bsel2=bsel2)


def l0_ffn_inmap(x_fm_b, inp):
    HID = 2730
    wqkv = np.asarray(inp["L0_ffn_wqkv"], np.float32)
    wo_ = np.asarray(inp["L0_ffn_wo"], np.float32)
    w1 = np.asarray(inp["L0_ffn_w1"], np.float32)
    b1 = np.asarray(inp["L0_ffn_b1"], np.float32)
    w2 = np.asarray(inp["L0_ffn_w2"], np.float32)
    b2 = np.asarray(inp["L0_ffn_b2"], np.float32)
    qs = np.asarray(inp["L0_ffn_q_scale"], np.float32)
    ks = np.asarray(inp["L0_ffn_k_scale"], np.float32)
    lg = np.asarray(inp["L0_ffn_ln_g"], np.float32)
    lb = np.asarray(inp["L0_ffn_ln_b"], np.float32)
    w1r = np.zeros((2, 21, 128, 2048), np.float32)
    w1t = np.zeros((2, 128, 672), np.float32)
    b1r = np.zeros((2, 128, 44), np.float32)
    for lay in range(2):
        for pc in range(21):
            a = w1[lay][:, 128 * pc:128 * pc + 128]
            g = w1[lay][:, HID + 128 * pc:HID + 128 * pc + 128]
            sl = np.concatenate([a, g], axis=1)
            w1r[lay, pc] = sl.reshape(8, 128, 256).transpose(
                1, 0, 2).reshape(128, 2048)
            b1r[lay, 0:128, 2 * pc] = b1[lay][128 * pc:128 * pc + 128]
            b1r[lay, 0:128, 2 * pc + 1] = b1[lay][HID + 128 * pc:
                                                  HID + 128 * pc + 128]
        at = w1[lay][:, 2688:2730]
        gt = w1[lay][:, HID + 2688:HID + 2730]
        sl = np.concatenate([at, gt], axis=1)
        w1t[lay] = sl.reshape(8, 128, 84).transpose(1, 0, 2).reshape(128, 672)
        b1r[lay, 0:42, 42] = b1[lay][2688:2730]
        b1r[lay, 0:42, 43] = b1[lay][HID + 2688:HID + 2730]
    w2rr = np.zeros((2, 21, 128, 1024), np.float32)
    w2tt = np.zeros((2, 42, 1024), np.float32)
    for lay in range(2):
        for pc in range(21):
            w2rr[lay, pc] = w2[lay][128 * pc:128 * pc + 128, :]
        w2tt[lay] = w2[lay][2688:2730, :]
    m = dict(
        x=np.ascontiguousarray(x_fm_b),
        wqkv=np.ascontiguousarray(wqkv.reshape(2, 8, 128, 1536)),
        wo=np.ascontiguousarray(wo_.reshape(2, 4, 128, 1024)),
        w1r=w1r, w1t=w1t, w2r=w2rr, w2t=w2tt, b1r=b1r,
        b2c=np.ascontiguousarray(
            b2.reshape(2, 8, 128).transpose(0, 2, 1)),
        qsc=np.ascontiguousarray(
            np.tile(qs, (1, 2)).reshape(2, 128, 1)),
        ksc=np.ascontiguousarray(
            np.tile(ks, (1, 2)).reshape(2, 128, 1)),
        lgc=np.ascontiguousarray(lg.reshape(2, 8, 128).transpose(0, 2, 1)),
        lbc=np.ascontiguousarray(lb.reshape(2, 8, 128).transpose(0, 2, 1)),
    )
    m.update(l0_ffn_consts())
    return m


def run_l0_ffn(x_fm, inp):
    nc = build_l0_ffn()
    in_maps = [l0_ffn_inmap(x_fm[b], inp) for b in range(NCORES)]
    res = run_bass_kernel_spmd(nc, in_maps, core_ids=list(range(NCORES)))
    return np.stack([r["xo"] for r in res.results])



def build_l0_attn(debug=False, fused=None):
    nc = fused["nc"] if fused else bass.Bass("TRN2", target_bir_lowering=False)
    x = nc.dram_tensor("x", [D, T], F32, kind="ExternalInput").ap()
    wq = nc.dram_tensor("wq", [8, 128, D], F32, kind="ExternalInput").ap()
    wk = nc.dram_tensor("wk", [8, 128, D], F32, kind="ExternalInput").ap()
    wv = nc.dram_tensor("wv", [8, 128, D], F32, kind="ExternalInput").ap()
    wo = nc.dram_tensor("wo", [8, 128, D], F32, kind="ExternalInput").ap()
    wf1 = nc.dram_tensor("wf1", [8, 128, 4096], F32,
                         kind="ExternalInput").ap()
    wf2 = nc.dram_tensor("wf2", [8, 128, 4096], F32,
                         kind="ExternalInput").ap()
    thw1 = nc.dram_tensor("thw1", [8, 128, 48], F32,
                          kind="ExternalInput").ap()
    thw2 = nc.dram_tensor("thw2", [48, 512], F32, kind="ExternalInput").ap()
    bqr = nc.dram_tensor("bqr", [1, D], F32, kind="ExternalInput").ap()
    bkr = nc.dram_tensor("bkr", [1, D], F32, kind="ExternalInput").ap()
    bvr = nc.dram_tensor("bvr", [1, D], F32, kind="ExternalInput").ap()
    bor = nc.dram_tensor("bor", [128, 8], F32, kind="ExternalInput").ap()
    bf1r = nc.dram_tensor("bf1r", [128, 32], F32, kind="ExternalInput").ap()
    bf2r = nc.dram_tensor("bf2r", [128, 8], F32, kind="ExternalInput").ap()
    n1gr = nc.dram_tensor("n1gr", [128, 8], F32, kind="ExternalInput").ap()
    n1br = nc.dram_tensor("n1br", [128, 8], F32, kind="ExternalInput").ap()
    n2gr = nc.dram_tensor("n2gr", [128, 8], F32, kind="ExternalInput").ap()
    n2br = nc.dram_tensor("n2br", [128, 8], F32, kind="ExternalInput").ap()
    thbr = nc.dram_tensor("thbr", [1, 512], F32, kind="ExternalInput").ap()
    lbr = nc.dram_tensor("lbr", [1, 512], F32, kind="ExternalInput").ap()
    etar = nc.dram_tensor("etar", [1, 512], F32, kind="ExternalInput").ap()
    ltp = nc.dram_tensor("ltp", [128, 128], F32, kind="ExternalInput").ap()
    ltn = nc.dram_tensor("ltn", [128, 128], F32, kind="ExternalInput").ap()
    idenf = nc.dram_tensor("idenf", [128, 128], F32,
                           kind="ExternalInput").ap()
    gmask = nc.dram_tensor("gmask", [128, 128], F32,
                           kind="ExternalInput").ap()
    xo = nc.dram_tensor("xo", [D, T], F32, kind="ExternalOutput").ap()
    kk = "ExternalOutput" if debug else "Internal"
    if debug:
        dbg_th = nc.dram_tensor("dbg_th", [128, 512], F32,
                                kind="ExternalOutput").ap()
        dbg_lm = nc.dram_tensor("dbg_lm", [128, 512], F32,
                                kind="ExternalOutput").ap()
        dbg_phi = nc.dram_tensor("dbg_phi", [128, 512], F32,
                                 kind="ExternalOutput").ap()
        dbg_rt = nc.dram_tensor("dbg_rt", [128, 512], F32,
                                kind="ExternalOutput").ap()
        dbg_qre = nc.dram_tensor("dbg_qre", [128, 512], F32,
                                 kind="ExternalOutput").ap()
        dbg_kre = nc.dram_tensor("dbg_kre", [128, 512], F32,
                                 kind="ExternalOutput").ap()
        dbg_ure = nc.dram_tensor("dbg_ure", [128, 512], F32,
                                 kind="ExternalOutput").ap()
        dbg_uim = nc.dram_tensor("dbg_uim", [128, 512], F32,
                                 kind="ExternalOutput").ap()
        dbg_qtk = nc.dram_tensor("dbg_qtk", [128, 1024], BF16,
                                 kind="ExternalOutput").ap()
    xln_d = nc.dram_tensor("xln_scr", [D, T], F32, kind=kk).ap()
    q_d = nc.dram_tensor("q_scr", [T, D], BF16, kind=kk).ap()
    k_d = nc.dram_tensor("k_scr", [T, D], BF16, kind=kk).ap()
    v_d = nc.dram_tensor("v_scr", [T, D], BF16, kind=kk).ap()
    at_d = nc.dram_tensor("at_scr", [D, T], BF16, kind=kk).ap()

    NT = 8
    W = 512
    NCH = 32
    PI = float(np.pi)
    MAGIC = 12582912.0

    p = fused["p"] if fused else P(nc, ExitStack())
    wpool = p.sbuf("wpool", [128, 3 * 8 * 1024], BF16)
    wq_s = [wpool[:, 1024 * kc:1024 * (kc + 1)] for kc in range(8)]
    wk_s = [wpool[:, 8192 + 1024 * kc:8192 + 1024 * (kc + 1)]
            for kc in range(8)]
    wv_s = [wpool[:, 16384 + 1024 * kc:16384 + 1024 * (kc + 1)]
            for kc in range(8)]
    wf1sl = [wpool[:, 2048 * b:2048 * (b + 1)] for b in range(4)]
    wf2sl = [wpool[:, 8192 + 2048 * b:8192 + 2048 * (b + 1)]
             for b in range(4)]
    lsems4 = ("ldwfa", "ldwfb", "ldwfc", "ldwfd")
    h1pool = wpool[:, 16384:24576]
    bpool = p.sbuf("bpool", [128, 8192], BF16)
    bqb = bpool[:, 5120:6144]
    bkb = bpool[:, 6144:7168]
    bvb = bpool[:, 7168:8192]
    qkvt = bpool[:, 0:1024]
    h1t = [h1pool[:, 512 * i:512 * (i + 1)] for i in range(16)] + \
          [bpool[:, 512 * i:512 * (i + 1)] for i in range(16)]
    thw1_s = p.sbuf("thw1_s", [128, 8 * 48], F32)
    thw2_s = p.sbuf("thw2_s", [48, 512], F32)
    bo_s = p.sbuf("bo_s", [128, 8], F32)
    bf1_s = p.sbuf("bf1_s", [128, 32], F32)
    bf2_s = p.sbuf("bf2_s", [128, 8], F32)
    n1g_s = p.sbuf("n1g_s", [128, 8], F32)
    n1b_s = p.sbuf("n1b_s", [128, 8], F32)
    n2g_s = p.sbuf("n2g_s", [128, 8], F32)
    n2b_s = p.sbuf("n2b_s", [128, 8], F32)
    ltp_s = p.sbuf("ltp_s", [128, 128], F32)
    ltn_s = p.sbuf("ltn_s", [128, 128], F32)
    idf_s = p.sbuf("idf_s", [128, 128], F32)
    idb_s = p.sbuf("idb_s", [128, 128], BF16)
    gmask_s = p.sbuf("gmask_s", [128, 128], BF16)
    _bqkv_note = "bqb/bkb/bvb alias bpool tail; thbb etc alias xlnf"

    brow = p.sbuf("brow", [1, 1024], F32)
    ones_col = p.sbuf("ones_col", [128, 1], F32)
    ones_row = p.sbuf("ones_row", [1, 128], F32)
    eps5_col = p.sbuf("eps5_col", [128, 1], F32)
    eps6_col = p.sbuf("eps6_col", [128, 1], F32)
    pih_col = p.sbuf("pih_col", [128, 1], F32)

    xch = [[p.sbuf(f"x_{c}", [128, W], F32)] * 2 for c in range(8)]
    xlnf = [p.sbuf(f"xlnf_{c}", [128, W], F32) for c in range(8)]
    thbb = xlnf[0][:]
    lbb = xlnf[1][:]
    etab = xlnf[2][:]
    xnf = xlnf
    xn16 = [p.sbuf(f"xn16_{c}", [128, W], BF16) for c in range(8)]
    sq = [p.sbuf(f"sq_{b}", [128, W], F32) for b in range(2)]
    rb = p.sbuf("rb", [128, W], F32)
    mrb = p.sbuf("mrb", [128, W], F32)
    m_r = p.sbuf("m_r", [1, W], F32)
    q_r = p.sbuf("q_r", [1, W], F32)
    var_r = p.sbuf("var_r", [1, W], F32)
    rs_r = p.sbuf("rs_r", [1, W], F32)
    lnv_r = q_r
    mr_r = m_r
    midf = p.sbuf("midf", [48, T], F32)

    qtk = [p.sbuf(f"qtk_{b}", [128, 1024], BF16) for b in range(2)]
    ktk = [p.sbuf(f"ktk_{b}", [128, 1024], BF16) for b in range(2)]
    vtk = [p.sbuf(f"vtk_{b}", [128, 1024], BF16) for b in range(2)]
    _wo_list = None
    kphi = p.sbuf("kphi", [128, 1024], F32)
    wo_s = [qtk[0][:], qtk[1][:], ktk[0][:], ktk[1][:], vtk[0][:],
            vtk[1][:], kphi[:].bitcast(BF16)[:, 0:1024],
            kphi[:].bitcast(BF16)[:, 1024:2048]]
    tht = p.sbuf("tht", [128, 512], F32)
    lmt = p.sbuf("lmt", [128, 512], F32)
    phit = p.sbuf("phit", [128, 512], F32)
    rtt = p.sbuf("rtt", [128, 512], F32)
    uret = p.sbuf("uret", [128, 512], F32)
    uimt = p.sbuf("uimt", [128, 512], F32)
    scr1 = p.sbuf("scr1", [128, 512], F32)
    scr2 = p.sbuf("scr2", [128, 512], F32)
    qrot_re = bpool[:, 0:512]
    qrot_im = bpool[:, 512:1024]
    krot_re = bpool[:, 1024:1536]
    krot_imn = bpool[:, 1536:2048]
    efat = p.sbuf("efat", [128, 512], F32)
    rendb = p.sbuf("rendb", [128, 512], F32)
    rendr = p.sbuf("rendr", [1, 512], F32)
    eendt = p.sbuf("eendt", [128, 512], F32)
    kre_t = bpool[:, 2048:3072]
    rf = [p.sbuf(f"rf_{g}", [128, 128], F32) for g in range(4)]
    phif = [p.sbuf(f"phif_{g}", [128, 128], F32) for g in range(4)]
    efaf = [p.sbuf(f"efaf_{g}", [128, 128], F32) for g in range(4)]
    qr_f = [p.sbuf(f"qrf_{g}", [128, 128], BF16) for g in range(4)]
    qi_f = [p.sbuf(f"qif_{g}", [128, 128], BF16) for g in range(4)]
    kr_f = [p.sbuf(f"krf_{g}", [128, 128], BF16) for g in range(4)]
    ki_f = [p.sbuf(f"kif_{g}", [128, 128], BF16) for g in range(4)]
    pq_re = [p.sbuf(f"pqre_{g}", [128, 128], BF16) for g in range(4)]
    pq_im = [p.sbuf(f"pqim_{g}", [128, 128], BF16) for g in range(4)]
    qrall_re = bpool[:, 3072:4096]
    qrall_im = bpool[:, 4096:5120]
    krall_re = [p.sbuf(f"kar_{g}", [128, 128], BF16) for g in range(4)]
    krall_imn = [p.sbuf(f"kai_{g}", [128, 128], BF16) for g in range(4)]
    earg = kphi
    eargk = p.sbuf("eargk", [128, 128], F32)
    dend = [p.sbuf(f"dend_{g}", [128, 1], F32) for g in range(4)]
    uer_c = [p.sbuf(f"uerc_{g}", [128, 1], F32) for g in range(4)]
    uei_c = [p.sbuf(f"ueic_{g}", [128, 1], F32) for g in range(4)]
    col_s1 = p.sbuf("col_s1", [128, 1], F32)
    col_s2 = p.sbuf("col_s2", [128, 1], F32)
    cre = [p.sbuf(f"cre_{g}", [128, 64], BF16) for g in range(4)]
    cimn = [p.sbuf(f"cimn_{g}", [128, 64], BF16) for g in range(4)]
    tr1 = p.sbuf("tr1", [128, 64], F32)
    tr2 = p.sbuf("tr2", [128, 64], F32)
    vp = [bpool[:, 5120 + 256 * i:5120 + 256 * (i + 1)]
          for i in range(8)]
    gts = [bpool[:, 7168 + 128 * i:7168 + 128 * (i + 1)]
           for i in range(8)]
    of16 = [p.sbuf(f"of16_{c}", [128, 128], BF16) for c in range(8)]

    at16 = xn16
    hn16 = xn16
    vout = [tht, lmt]

    ps1 = p.psum("ps1", [128, W])
    ps2 = p.psum("ps2", [128, W])
    ps3 = p.psum("ps3", [128, W])
    ps4 = p.psum("ps4", [128, W])
    ps5 = p.psum("ps5", [128, W])
    ps6 = p.psum("ps6", [128, W])
    ps7 = p.psum("ps7", [128, W])
    pstr = p.psum("pstr", [128, 1024], BF16)

    for s in ("ldc", "ldw", "ldwv", "lda", "ldb", "lqa", "lqb", "ldwfa",
              "ldwfb", "ldwfc", "ldwfd", "pe", "dv", "ac", "stq", "stx",
              "stb", "sta0", "sta1", "xcp", "vpd", "dbgs"):
        p.sem(s)

    def body():
        for kc in range(8):
            p.dma("gpsimd", wq_s[kc], wq[kc], inc="ldw")
            p.dma("gpsimd", wk_s[kc], wk[kc], inc="ldw")
            p.dma("gpsimd", wv_s[kc], wv[kc], inc="ldw")
            p.dma("gpsimd", thw1_s[:, 48 * kc:48 * kc + 48], thw1[kc],
                  inc="ldw")
        p.dma("gpsimd", idb_s[:], idenf, inc="ldw")
        p.dma("gpsimd", gmask_s[:], gmask, inc="ldw")
        ldw_all = p.val("ldw")
        for dst, srcd in ((thw2_s, thw2), (bo_s, bor), (bf1_s, bf1r),
                          (bf2_s, bf2r), (n1g_s, n1gr), (n1b_s, n1br),
                          (n2g_s, n2gr), (n2b_s, n2br), (ltp_s, ltp),
                          (ltn_s, ltn), (idf_s, idenf)):
            p.dma("sync", dst[:], srcd, inc="ldwv")
        ldwv_all = p.val("ldwv")
        p.op("vector", "memset", ones_col[:], 1.0)
        p.op("vector", "memset", ones_row[:], 1.0)
        p.op("vector", "memset", eps5_col[:], 1e-5)
        p.op("vector", "memset", eps6_col[:], 1e-6)
        p.op("vector", "memset", pih_col[:], PI / 2.0)
        for g in range(4):
            p.op("vector", "memset", cre[g][:], 0.0)
            p.op("vector", "memset", cimn[g][:], 0.0)
        p.op("vector", "memset", ps1[:], 0.0)
        p.op("vector", "memset", ps2[:], 0.0, inc="dv")
        memset_done = p.val("dv")

        def bias_bcast(items):
            for name, srcd, dstt, width in items:
                p.wait("sync", "pe", p.val("pe"))
                p.dma("sync", brow[0:1, 0:width], srcd, inc="ldc")
                ldc_now = p.val("ldc")
                p.wait("tensor", "ldc", ldc_now)
                p.wait("tensor", "dv", memset_done)
                for seg in range(width // 512):
                    p.wait("tensor", "ac", p.val("ac"))
                    p.op("tensor", "matmul", ps1[:], ones_row[:],
                         brow[0:1, 512 * seg:512 * seg + 512],
                         start=True, stop=True, inc="pe")
                    p.wait("scalar", "pe")
                    p.op("scalar", "copy",
                         dstt[:, 512 * seg:512 * seg + 512], ps1[:],
                         inc="ac")
        bias_bcast((("bq", bqr, bqb, 1024), ("bk", bkr, bkb, 1024),
                    ("bv", bvr, bvb, 1024)))
        bias_done = p.val("ac")

        rec = {"rows_read_psum": 0, "copies_done": 0}

        def ln_rows(eps_ap):
            p.wait("vector", "pe")
            p.op("vector", "tensor_scalar", m_r[:], ps1[0:1, :],
                 1.0 / D, 0.0, ALU.mult, ALU.add)
            p.op("vector", "tensor_scalar", q_r[:], ps2[0:1, :],
                 1.0 / D, 0.0, ALU.mult, ALU.add, inc="dv")
            rec["rows_read_psum"] = p.val("dv")
            p.op("vector", "tensor_tensor", var_r[:], m_r[:], m_r[:],
                 ALU.mult)
            p.op("vector", "tensor_tensor", var_r[:], q_r[:], var_r[:],
                 ALU.subtract, inc="dv")
            p.wait("scalar", "dv")
            p.op("scalar", "activation", lnv_r[:], var_r[:], AF.Ln,
                 bias=eps_ap, scale=1.0)
            p.op("scalar", "activation", rs_r[:], lnv_r[:], AF.Exp,
                 bias=0.0, scale=-0.5, inc="ac")
            p.wait("vector", "ac")
            p.op("vector", "tensor_tensor", mr_r[:], m_r[:], rs_r[:],
                 ALU.mult, inc="dv")
            p.wait("tensor", "dv")
            p.wait("tensor", "ac", rec["copies_done"])
            p.op("tensor", "matmul", ps3[:], ones_row[:], rs_r[:],
                 start=True, stop=True)
            p.op("tensor", "matmul", ps4[:], ones_row[:], mr_r[:],
                 start=True, stop=True, inc="pe")
            p.wait("scalar", "pe")
            p.op("scalar", "copy", rb[:], ps3[:], inc="ac")
            p.op("scalar", "copy", mrb[:], ps4[:], inc="ac")
            rec["copies_done"] = p.val("ac")
            p.wait("vector", "ac")

        def ln_stats(srcs):
            pe_after = []
            for c in range(8):
                if c >= 2:
                    p.wait("scalar", "pe", pe_after[c - 2])
                p.op("scalar", "activation", sq[c % 2][:], srcs(c),
                     AF.Square, bias=0.0, scale=1.0, inc="ac")
                p.wait("tensor", "ac")
                p.op("tensor", "matmul", ps1[0:1, :], ones_col[:],
                     srcs(c), start=(c == 0), stop=(c == 7))
                p.op("tensor", "matmul", ps2[0:1, :], ones_col[:],
                     sq[c % 2][:], start=(c == 0), stop=(c == 7),
                     inc="pe")
                pe_after.append(p.val("pe"))

        for i in range(NT):
            buf = i % 2
            t0 = i * W
            ldsem = "lda" if buf == 0 else "ldb"
            if i >= 1:
                p.wait("sync", "dv", rec[f"xln_dn_{i-1}"])
            for c in range(8):
                p.dma("sync", xch[c][buf][:],
                      x[128 * c:128 * c + 128, t0:t0 + W], inc=ldsem)
            ld_tile = p.val(ldsem)
            p.wait("scalar", ldsem, ld_tile)
            p.wait("scalar", "dv", rec.get("apply_dn", 0))
            p.wait("tensor", "dv", max(rec["rows_read_psum"],
                                       memset_done))
            p.wait("tensor", ldsem, ld_tile)
            p.wait("tensor", "ac", bias_done)
            if i == 0:
                p.wait("vector", "ldwv", ldwv_all)
            ln_stats(lambda c: xch[c][buf][:])
            ln_rows(eps5_col[0:1, 0:1])
            for c in range(8):
                p.op("vector", "tensor_tensor", sq[c % 2][:],
                     xch[c][buf][:], rb[:], ALU.mult)
                p.op("vector", "tensor_tensor", xlnf[c][:], sq[c % 2][:],
                     mrb[:], ALU.subtract, inc="dv" if c == 7 else None)
            xln_dn = p.val("dv")
            rec[f"xln_dn_{i}"] = xln_dn
            p.wait("sync", "dv", xln_dn)
            for c in range(8):
                p.dma("sync", xln_d[128 * c:128 * c + 128, t0:t0 + W],
                      xlnf[c][:], inc="stx")
            p.wait("scalar", "dv", xln_dn)
            p.wait("tensor", "dv", xln_dn)
            ln_stats(lambda c: xlnf[c][:])
            ln_rows(eps6_col[0:1, 0:1])
            p.wait("vector", "stx", p.val("stx"))
            for c in range(8):
                p.op("vector", "tensor_tensor", sq[c % 2][:], xlnf[c][:],
                     rb[:], ALU.mult)
                p.op("vector", "tensor_tensor", xnf[c][:], sq[c % 2][:],
                     mrb[:], ALU.subtract)
                p.op("vector", "tensor_scalar", xnf[c][:], xnf[c][:],
                     n1g_s[:, c:c + 1], n1b_s[:, c:c + 1], ALU.mult,
                     ALU.add)
                p.op("vector", "tensor_copy", xn16[c][:], xnf[c][:],
                     inc="dv" if c == 7 else None)
            rec["apply_dn"] = p.val("dv")
            xn_dn = p.val("dv")

            p.wait("tensor", "dv", xn_dn)
            if i == 0:
                p.wait("tensor", "ldw", ldw_all)
            for (wmat, bb, dst_d) in ((wq_s, bqb, q_d), (wk_s, bkb, k_d),
                                      (wv_s, bvb, v_d)):
                for ts4 in range(4):
                    tb = t0 + 128 * ts4
                    for half in range(2):
                        bank = ps5 if half == 0 else ps6
                        p.wait("tensor", "dv",
                               rec.get(f"qkv_ev{half}", 0))
                        for kc in range(8):
                            p.op("tensor", "matmul", bank[:],
                                 xn16[kc][:, 128 * ts4:128 * ts4 + 128],
                                 wmat[kc][:, 512 * half:512 * half + 512],
                                 start=(kc == 0), stop=(kc == 7),
                                 inc="pe" if kc == 7 else None)
                        p.wait("vector", "pe")
                        if half == 0:
                            p.wait("vector", "stq",
                                   rec.get("qkvt_st", 0))
                        p.op("vector", "tensor_tensor",
                             qkvt[:, 512 * half:512 * half + 512],
                             bank[:], bb[:, 512 * half:512 * half + 512],
                             ALU.add, inc="dv")
                        rec[f"qkv_ev{half}"] = p.val("dv")
                    p.wait("sync", "dv")
                    p.dma("sync", dst_d[tb:tb + 128, :], qkvt[:],
                          inc="stq")
                    rec["qkvt_st"] = p.val("stq")
            p.wait("tensor", "dv", p.val("dv"))
            p.wait("tensor", "ac", rec.get("mid_ev", 0))
            for kc in range(8):
                p.op("tensor", "matmul", ps7[0:48, :],
                     thw1_s[:, 48 * kc:48 * kc + 48], xnf[kc][:],
                     start=(kc == 0), stop=(kc == 7),
                     inc="pe" if kc == 7 else None)
            p.wait("scalar", "pe")
            p.op("scalar", "activation", midf[:, t0:t0 + W],
                 ps7[0:48, :], AF.Tanh, bias=0.0, scale=1.0, inc="ac")
            rec["mid_ev"] = p.val("ac")
        phaseA_stq = p.val("stq")
        phaseA_stx = p.val("stx")

        p.wait("sync", "stq", phaseA_stq)
        p.wait("sync", "stx", phaseA_stx)
        p.wait("gpsimd", "stq", phaseA_stq)
        p.wait("sync", "dv", p.val("dv"))
        p.wait("sync", "ac", p.val("ac"))
        bias_bcast((("thb", thbr, thbb, 512), ("lb", lbr, lbb, 512),
                    ("eta", etar, etab, 512)))
        thb_done = p.val("ac")
        for n in range(NCH):
            buf = n % 2
            t0 = n * 128
            lq = "lqa" if buf == 0 else "lqb"
            if n >= 2:
                p.wait("sync", "pe", rec[f"chunk_pe_{n-2}"])
            p.dma("sync", qtk[buf][:], q_d[t0:t0 + 128, :], inc=lq)
            p.dma("sync", ktk[buf][:], k_d[t0:t0 + 128, :], inc=lq)
            p.dma("sync", vtk[buf][:], v_d[t0:t0 + 128, :], inc=lq)
            lq_now = p.val(lq)

            p.wait("tensor", "dv", rec.get("lam_read", 0))
            p.wait("tensor", "ac", max(thb_done, rec.get("mid_ev", 0)))
            p.op("tensor", "matmul", ps7[:], midf[:, t0:t0 + 128],
                 thw2_s[:], start=True, stop=True, inc="pe")
            p.wait("vector", "pe")
            p.op("vector", "tensor_tensor", tht[:], ps7[:], thbb[:],
                 ALU.add)
            p.op("vector", "tensor_scalar", tht[:], tht[:], PI / 2.0,
                 -PI / 2.0, ALU.min, ALU.max, inc="dv")
            th_dn = p.val("dv")
            p.wait("scalar", "dv", th_dn)
            p.op("scalar", "activation", scr1[:], tht[:], AF.Square,
                 bias=0.0, scale=1.0, inc="ac")
            p.wait("vector", "ac")
            p.op("vector", "tensor_tensor", lmt[:], etab[:], scr1[:],
                 ALU.mult)
            p.op("vector", "tensor_tensor", lmt[:], lmt[:], lbb[:],
                 ALU.add, inc="dv")
            lm_dn = p.val("dv")
            if debug and n == 0:
                p.wait("sync", "dv", lm_dn)
                p.dma("sync", dbg_th, tht[:], inc="stq")
                p.dma("sync", dbg_lm, lmt[:], inc="stq")
                p.wait("tensor", "stq", p.val("stq"))
            p.wait("tensor", "dv", lm_dn)
            if n == 0:
                p.wait("tensor", "ldwv", ldwv_all)
            p.wait("tensor", "ac", rec.get("cums_ev", 0))
            p.op("tensor", "matmul", ps5[:], ltp_s[:], tht[:],
                 start=True, stop=True)
            p.op("tensor", "matmul", ps6[:], ltn_s[:], lmt[:],
                 start=True, stop=True, inc="pe")
            p.wait("scalar", "pe")
            p.op("scalar", "copy", phit[:], ps5[:], inc="ac")
            p.op("scalar", "copy", rtt[:], ps6[:], inc="ac")
            rec["cums_ev"] = p.val("ac")
            cums_dn = p.val("ac")
            p.wait("vector", "ac", cums_dn)
            p.op("vector", "tensor_scalar", scr1[:], phit[:],
                 1.0 / (2 * PI), MAGIC, ALU.mult, ALU.add)
            p.op("vector", "tensor_scalar", scr1[:], scr1[:], 1.0,
                 -MAGIC, ALU.mult, ALU.add)
            p.op("vector", "scalar_tensor_tensor", scr1[:], scr1[:],
                 -2 * PI, phit[:], ALU.mult, ALU.add, inc="dv")
            p.wait("scalar", "dv")
            p.op("scalar", "activation", uimt[:], scr1[:], AF.Sin,
                 bias=0.0, scale=1.0, inc="ac")
            p.op("vector", "tensor_scalar", scr2[:], phit[:],
                 1.0 / (2 * PI), 0.25, ALU.mult, ALU.add)
            p.op("vector", "tensor_scalar", scr2[:], scr2[:], 1.0,
                 MAGIC, ALU.mult, ALU.add)
            p.op("vector", "tensor_scalar", scr2[:], scr2[:], 1.0,
                 -MAGIC, ALU.mult, ALU.add)
            p.op("vector", "tensor_scalar", scr2[:], scr2[:], 1.0,
                 -0.25, ALU.mult, ALU.add)
            p.op("vector", "scalar_tensor_tensor", scr2[:], scr2[:],
                 -2 * PI, phit[:], ALU.mult, ALU.add, inc="dv")
            p.wait("scalar", "dv")
            p.op("scalar", "activation", uret[:], scr2[:], AF.Sin,
                 bias=0.0, scale=1.0, inc="ac")
            trig_dn = p.val("ac")

            p.wait("vector", lq, lq_now)
            p.wait("vector", "ac", trig_dn)
            p.wait("vector", "pe", rec.get("rotread", 0))
            p.op("vector", "tensor_scalar", kphi[:], ktk[buf][:], 0.0,
                 0.0, ALU.min, ALU.add, inc="dv")
            p.wait("scalar", "dv")
            p.op("scalar", "activation", kphi[:], kphi[:], AF.Exp,
                 bias=0.0, scale=1.0, inc="ac")
            p.wait("vector", "ac")
            p.op("vector", "scalar_tensor_tensor", kphi[:], ktk[buf][:],
                 0.0, kphi[:], ALU.max, ALU.add)
            qv = qtk[buf].rearrange("p (h c) -> p h c", c=64)
            kv = kphi.rearrange("p (h c) -> p h c", c=64)
            urv = uret.rearrange("p (h j) -> p h j", j=32)
            uiv = uimt.rearrange("p (h j) -> p h j", j=32)
            s1v = scr1.rearrange("p (h j) -> p h j", j=32)
            s2v = scr2.rearrange("p (h j) -> p h j", j=32)
            for (srcv, dre, dimm, neg) in (
                    (qv, qrot_re, qrot_im, False),
                    (kv, krot_re, krot_imn, True)):
                p.op("vector", "tensor_tensor", s1v, srcv[:, :, 0:32],
                     urv, ALU.mult)
                p.op("vector", "tensor_tensor", s2v, srcv[:, :, 32:64],
                     uiv, ALU.mult)
                p.op("vector", "tensor_tensor", dre[:], scr1[:],
                     scr2[:], ALU.add)
                p.op("vector", "tensor_tensor", s1v, srcv[:, :, 0:32],
                     uiv, ALU.mult)
                p.op("vector", "tensor_tensor", s2v, srcv[:, :, 32:64],
                     urv, ALU.mult)
                p.op("vector", "tensor_tensor", dimm[:], scr1[:],
                     scr2[:], ALU.subtract,
                     inc="dv" if neg else None)
            rot_dn = p.val("dv")
            if debug and n == 0:
                p.wait("sync", "dv", rot_dn)
                p.wait("gpsimd", "dv", rot_dn)
                p.dma("sync", dbg_phi, phit[:], inc="stq")
                p.dma("sync", dbg_rt, rtt[:], inc="stq")
                p.dma("gpsimd", dbg_qre, qrot_re, inc="dbgs")
                p.dma("gpsimd", dbg_kre, krot_re, inc="dbgs")
                p.dma("gpsimd", dbg_ure, uret[:], inc="dbgs")
                p.dma("gpsimd", dbg_uim, uimt[:], inc="dbgs")
                p.dma("gpsimd", dbg_qtk, qtk[buf][:], inc="dbgs")
                p.wait("tensor", "stq", p.val("stq"))
                p.wait("vector", "dbgs", p.val("dbgs"))
                p.wait("tensor", "dbgs", p.val("dbgs"))
                p.wait("gpsimd", "dbgs", p.val("dbgs"))

            p.wait("tensor", "dv", rot_dn)
            for g in range(4):
                p.wait("tensor", "ac", rec.get("tp_ev", 0))
                p.op("tensor", "matmul", ps7[:, 0:128],
                     rtt[:, 128 * g:128 * g + 128], idf_s[:],
                     start=True, stop=True)
                p.op("tensor", "matmul", ps7[:, 128:256],
                     phit[:, 128 * g:128 * g + 128], idf_s[:],
                     start=True, stop=True, inc="pe")
                for si, (srct, dstf) in enumerate(
                        ((qrot_re, qr_f[g]), (qrot_im, qi_f[g]),
                         (krot_re, kr_f[g]), (krot_imn, ki_f[g]))):
                    p.op("tensor", "transpose",
                         pstr[:, 128 * si:128 * si + 128],
                         srct[:, 128 * g:128 * g + 128], idb_s[:],
                         inc="pe" if si == 3 else None)
                p.wait("scalar", "pe")
                p.op("scalar", "copy", rf[g][:], ps7[:, 0:128])
                p.op("scalar", "copy", phif[g][:], ps7[:, 128:256])
                for si, dstf in enumerate((qr_f[g], qi_f[g], kr_f[g],
                                           ki_f[g])):
                    p.op("scalar", "copy", dstf[:],
                         pstr[:, 128 * si:128 * si + 128],
                         inc="ac" if si == 3 else None)
                rec["tp_ev"] = p.val("ac")
            tp_dn = p.val("ac")

            p.wait("tensor", "dv", p.val("dv"))
            p.wait("tensor", "ac", max(tp_dn, rec.get("eend_ev", 0)))
            for g in range(4):
                p.op("tensor", "matmul", ps7[0:1, 128 * g:128 * g + 128],
                     rf[g][:, 127:128], idf_s[:], start=True, stop=True,
                     inc="pe" if g == 3 else None)
            p.wait("scalar", "pe")
            p.op("scalar", "copy", rendr[:], ps7[0:1, 0:512], inc="ac")
            p.wait("tensor", "ac")
            p.op("tensor", "matmul", ps7[:], ones_row[:], rendr[:],
                 start=True, stop=True, inc="pe")
            p.wait("scalar", "pe")
            p.op("scalar", "copy", rendb[:], ps7[:], inc="ac")
            rec["eend_ev"] = p.val("ac")
            p.wait("vector", "ac", rec["eend_ev"])
            p.op("vector", "tensor_tensor", scr2[:], rendb[:], rtt[:],
                 ALU.subtract, inc="dv")
            rec["lam_read"] = p.val("dv")
            p.wait("scalar", "dv")
            p.op("scalar", "activation", eendt[:], scr2[:], AF.Exp,
                 bias=0.0, scale=1.0, inc="ac")
            p.wait("vector", "ac")
            kev = kre_t.rearrange("p (h c) -> p h c", c=64)
            p.op("vector", "tensor_tensor", kev[:, :, 0:32],
                 krot_re.rearrange("p (h j) -> p h j", j=32),
                 eendt[:].rearrange("p (h j) -> p h j", j=32), ALU.mult)
            p.op("vector", "tensor_tensor", kev[:, :, 32:64],
                 krot_imn.rearrange("p (h j) -> p h j", j=32),
                 eendt[:].rearrange("p (h j) -> p h j", j=32), ALU.mult,
                 inc="dv")
            kre_dn = p.val("dv")

            p.wait("gpsimd", "pe", rec.get("y_pe", 0))
            for pp in range(8):
                vsrc = v_d[t0 + 16 * pp:t0 + 16 * pp + 16, :].rearrange(
                    "r (hi g vc) -> g r hi vc", hi=4, g=4, vc=64)
                for g2 in range(4):
                    p.dma("gpsimd",
                          vp[pp][32 * g2:32 * g2 + 16, :].rearrange(
                              "r (hi vc) -> r hi vc", vc=64),
                          vsrc[g2], inc="vpd")
            vp_dn = p.val("vpd")
            p.wait("tensor", "vpd", vp_dn)

            gb = [ps1, ps2]
            for g in range(4):
                p.wait("scalar", "ac", tp_dn)
                p.wait("scalar", "dv", rec.get("pq_dn", 0))
                p.op("scalar", "activation", efaf[g][:], rf[g][:],
                     AF.Exp, bias=0.0, scale=1.0, inc="ac")
                p.wait("vector", "ac")
                p.wait("vector", "pe", rec.get("y_pe", 0))
                p.op("vector", "tensor_tensor", pq_re[g][:], qr_f[g][:],
                     efaf[g][:], ALU.mult)
                p.op("vector", "tensor_tensor", pq_im[g][:], qi_f[g][:],
                     efaf[g][:], ALU.mult, inc="dv")
                rec["pq_dn"] = p.val("dv")
                p.op("vector", "tensor_tensor",
                     eargk[:].rearrange("p (b d) -> p b d", d=16),
                     _anchor16(rf[g]),
                     rf[g][:].rearrange("p (b d) -> p b d", d=16),
                     ALU.subtract, inc="dv")
                p.wait("scalar", "dv")
                p.op("scalar", "activation", eargk[:], eargk[:], AF.Exp,
                     bias=0.0, scale=1.0, inc="ac")
                p.wait("vector", "ac")
                p.op("vector", "tensor_tensor", krall_re[g][:],
                     kr_f[g][:], eargk[:], ALU.mult)
                p.op("vector", "tensor_tensor", krall_imn[g][:],
                     ki_f[g][:], eargk[:], ALU.mult)
                p.op("vector", "tensor_tensor",
                     earg[:].rearrange("p (b t) -> p b t", t=128),
                     _tile8(rf[g]), _anchor8(rf[g]), ALU.subtract)
                p.op("vector", "tensor_scalar", earg[:], earg[:],
                     80.0, 0.0, ALU.min, ALU.add, inc="dv")
                p.wait("scalar", "dv")
                p.op("scalar", "activation", earg[:], earg[:], AF.Exp,
                     bias=0.0, scale=1.0, inc="ac")
                p.wait("vector", "ac")
                p.op("vector", "tensor_tensor",
                     qrall_re.rearrange("p (b t) -> p b t", t=128),
                     earg[:].rearrange("p (b t) -> p b t", t=128),
                     _tile8(qr_f[g]), ALU.mult)
                p.op("vector", "tensor_tensor",
                     qrall_im.rearrange("p (b t) -> p b t", t=128),
                     earg[:].rearrange("p (b t) -> p b t", t=128),
                     _tile8(qi_f[g]), ALU.mult)
                p.op("vector", "tensor_copy", col_s1[:],
                     phif[g][:, 127:128])
                p.op("vector", "tensor_scalar", col_s1[:], col_s1[:],
                     1.0 / (2 * PI), MAGIC, ALU.mult, ALU.add)
                p.op("vector", "tensor_scalar", col_s1[:], col_s1[:],
                     1.0, -MAGIC, ALU.mult, ALU.add)
                p.op("vector", "scalar_tensor_tensor", col_s1[:],
                     col_s1[:], -2 * PI, phif[g][:, 127:128], ALU.mult,
                     ALU.add)
                p.op("vector", "tensor_copy", col_s2[:],
                     phif[g][:, 127:128])
                p.op("vector", "tensor_scalar", col_s2[:], col_s2[:],
                     1.0 / (2 * PI), 0.25, ALU.mult, ALU.add)
                p.op("vector", "tensor_scalar", col_s2[:], col_s2[:],
                     1.0, MAGIC, ALU.mult, ALU.add)
                p.op("vector", "tensor_scalar", col_s2[:], col_s2[:],
                     1.0, -MAGIC, ALU.mult, ALU.add)
                p.op("vector", "tensor_scalar", col_s2[:], col_s2[:],
                     1.0, -0.25, ALU.mult, ALU.add)
                p.op("vector", "scalar_tensor_tensor", col_s2[:],
                     col_s2[:], -2 * PI, phif[g][:, 127:128], ALU.mult,
                     ALU.add, inc="dv")
                fb_dn = p.val("dv")
                rec["fb_dn"] = fb_dn
                p.wait("scalar", "dv", fb_dn)
                p.op("scalar", "activation", uei_c[g][:], col_s1[:],
                     AF.Sin, bias=0.0, scale=1.0)
                p.op("scalar", "activation", uer_c[g][:], col_s2[:],
                     AF.Sin, bias=0.0, scale=1.0)
                p.op("scalar", "activation", dend[g][:],
                     rf[g][:, 127:128], AF.Exp, bias=0.0, scale=1.0,
                     inc="ac")
                cols_dn = p.val("ac")

                p.wait("tensor", "dv", fb_dn)
                p.wait("tensor", "ac", cols_dn)
                for pp in range(8):
                    par = pp % 2
                    nwid = 128 - 16 * pp
                    p.wait("tensor", "dv", rec.get(f"g_ev{par}", 0))
                    for hh in range(4):
                        base = 32 * hh
                        tp = (96, 96) if hh == 3 else None
                        p.op("tensor", "matmul",
                             gb[par][base:base + 8, 0:8],
                             krall_re[g][base:base + 32,
                                         16 * pp:16 * pp + 8],
                             qrall_re[base:base + 32,
                                      128 * pp + 16 * pp:
                                      128 * pp + 16 * pp + 8],
                             start=True, stop=False,
                             tile_position=tp)
                        p.op("tensor", "matmul",
                             gb[par][base:base + 8, 0:8],
                             krall_imn[g][base:base + 32,
                                          16 * pp:16 * pp + 8],
                             qrall_im[base:base + 32,
                                      128 * pp + 16 * pp:
                                      128 * pp + 16 * pp + 8],
                             start=False, stop=True,
                             tile_position=tp)
                        p.op("tensor", "matmul",
                             gb[par][base:base + 16, 8:nwid],
                             krall_re[g][base:base + 32,
                                         16 * pp:16 * pp + 16],
                             qrall_re[base:base + 32,
                                      128 * pp + 16 * pp + 8:
                                      128 * pp + 128],
                             start=True, stop=False,
                             tile_position=tp)
                        p.op("tensor", "matmul",
                             gb[par][base:base + 16, 8:nwid],
                             krall_imn[g][base:base + 32,
                                          16 * pp:16 * pp + 16],
                             qrall_im[base:base + 32,
                                      128 * pp + 16 * pp + 8:
                                      128 * pp + 128],
                             start=False, stop=True,
                             tile_position=tp,
                             inc="pe" if hh == 3 else None)
                    p.wait("vector", "pe")
                    p.op("vector", "tensor_tensor", gts[pp][:, 0:nwid],
                         gb[par][:, 0:nwid], gmask_s[:, 0:nwid],
                         ALU.mult, inc="dv")
                    rec[f"g_ev{par}"] = p.val("dv")
                g_dn = p.val("dv")

                p.wait("tensor", "dv", g_dn)
                for hh in range(4):
                    h = 4 * g + hh
                    base = 32 * hh
                    tp = (96, 0) if hh == 3 else None
                    p.wait("tensor", "ac", rec.get("y_ev", 0))
                    p.op("tensor", "matmul", ps3[0:64, 0:128],
                         cre[g][base:base + 32, :],
                         pq_re[g][base:base + 32, :],
                         start=True, stop=False, tile_position=tp)
                    p.op("tensor", "matmul", ps3[0:64, 0:128],
                         cimn[g][base:base + 32, :],
                         pq_im[g][base:base + 32, :],
                         start=False, stop=False, tile_position=tp)
                    for pp in range(8):
                        nwid = 128 - 16 * pp
                        p.op("tensor", "matmul",
                             ps3[0:64, 16 * pp:128],
                             vp[pp][base:base + 16,
                                    64 * (h // 4):64 * (h // 4) + 64],
                             gts[pp][base:base + 16, 0:nwid],
                             start=False, stop=(pp == 7),
                             tile_position=tp,
                             inc="pe" if pp == 7 else None)
                    p.wait("scalar", "pe")
                    p.wait("scalar", "stb", rec.get("stb_prev", 0))
                    p.op("scalar", "copy",
                         of16[h // 2][64 * (h % 2):64 * (h % 2) + 64, :],
                         ps3[0:64, 0:128], inc="ac")
                    rec["y_ev"] = p.val("ac")
                rec["y_pe"] = p.val("pe")

                p.wait("tensor", "dv", max(kre_dn,
                                           rec.get(f"u_read_{g}", 0)))
                for hh in range(4):
                    h = 4 * g + hh
                    tp = (0, 96) if hh == 3 else None
                    p.op("tensor", "matmul",
                         ps4[32 * hh:32 * hh + 32, 0:64],
                         kre_t[:, 64 * h:64 * h + 32],
                         vtk[buf][:, 64 * h:64 * h + 64],
                         start=True, stop=True, tile_position=tp)
                    p.op("tensor", "matmul",
                         ps4[32 * hh:32 * hh + 32, 64:128],
                         kre_t[:, 64 * h + 32:64 * h + 64],
                         vtk[buf][:, 64 * h:64 * h + 64],
                         start=True, stop=True, tile_position=tp,
                         inc="pe" if hh == 3 else None)
                p.wait("vector", "pe")
                p.op("vector", "scalar_tensor_tensor", tr1[:],
                     cre[g][:], dend[g][:, 0:1], ps4[:, 0:64],
                     ALU.mult, ALU.add)
                p.op("vector", "scalar_tensor_tensor", tr2[:],
                     cimn[g][:], dend[g][:, 0:1], ps4[:, 64:128],
                     ALU.mult, ALU.add, inc="dv")
                p.op("vector", "tensor_scalar", cre[g][:], tr1[:],
                     uer_c[g][:, 0:1], 0.0, ALU.mult, ALU.add)
                p.op("vector", "scalar_tensor_tensor", cre[g][:],
                     tr2[:], uei_c[g][:, 0:1], cre[g][:], ALU.mult,
                     ALU.add)
                p.op("vector", "tensor_scalar", cimn[g][:], tr2[:],
                     uer_c[g][:, 0:1], 0.0, ALU.mult, ALU.add)
                p.op("vector", "tensor_scalar", tr1[:], tr1[:],
                     uei_c[g][:, 0:1], 0.0, ALU.mult, ALU.add)
                p.op("vector", "tensor_tensor", cimn[g][:], cimn[g][:],
                     tr1[:], ALU.subtract, inc="dv")
                rec[f"u_read_{g}"] = p.val("dv")
            rec["rotread"] = p.val("pe")

            y_all = p.val("ac")
            p.wait("sync", "ac", y_all)
            for c in range(8):
                p.dma("sync", at_d[128 * c:128 * c + 128, t0:t0 + 128],
                      of16[c][:], inc="stb")
            rec["stb_prev"] = p.val("stb")
            rec[f"chunk_pe_{n}"] = p.val("pe")
        phaseB_stq = p.val("stb")

        p.wait("sync", "stb", phaseB_stq)
        p.wait("gpsimd", "pe", p.val("pe"))
        p.wait("gpsimd", "dv", p.val("dv"))
        for kc in range(8):
            p.dma("gpsimd", wo_s[kc], wo[kc], inc="ldw")
        ldw_c = p.val("ldw")
        for c in range(8):
            p.dma("sync", xo[128 * c:128 * c + 128, :],
                  x[128 * c:128 * c + 128, :], inc="xcp")
        for i in range(NT):
            buf = i % 2
            t0 = i * W
            ldsem = "lda" if buf == 0 else "ldb"
            if i >= 1:
                p.wait("sync", "dv", rec["c_tile_dn"])
            if i >= 2:
                p.wait("sync", "sta0", rec[f"sta0_{i-2}"])
                p.wait("sync", "sta1", rec[f"sta1_{i-2}"])
            for c in range(8):
                p.dma("sync", xlnf[c][:],
                      xln_d[128 * c:128 * c + 128, t0:t0 + W], inc=ldsem)
                p.dma("sync", at16[c][:],
                      at_d[128 * c:128 * c + 128, t0:t0 + W], inc=ldsem)
            ld_tile = p.val(ldsem)
            p.wait("tensor", ldsem, ld_tile)
            if i == 0:
                p.wait("tensor", "ldw", ldw_c)
            p.wait("tensor", "dv", rec.get("x1_dn", 0))
            for oc in range(8):
                bank = ps5 if oc % 2 == 0 else ps6
                p.wait("tensor", "dv", rec.get(f"wo_ev{oc % 2}", 0))
                for kc in range(8):
                    p.op("tensor", "matmul", bank[:],
                         wo_s[kc][:, 128 * oc:128 * oc + 128],
                         at16[kc][:], start=(kc == 0), stop=(kc == 7),
                         inc="pe" if kc == 7 else None)
                p.wait("vector", "pe")
                p.op("vector", "scalar_tensor_tensor", xlnf[oc][:],
                     bank[:], bo_s[:, oc:oc + 1], xlnf[oc][:],
                     ALU.add, ALU.add, inc="dv")
                rec[f"wo_ev{oc % 2}"] = p.val("dv")
            x1_dn = p.val("dv")
            rec["x1_dn"] = x1_dn
            p.wait("scalar", "dv", x1_dn)
            p.wait("tensor", "dv", x1_dn)
            ln_stats(lambda c: xlnf[c][:])
            ln_rows(eps6_col[0:1, 0:1])
            for c in range(8):
                p.op("vector", "tensor_tensor", sq[c % 2][:], xlnf[c][:],
                     rb[:], ALU.mult)
                p.op("vector", "tensor_tensor", sq[c % 2][:],
                     sq[c % 2][:], mrb[:], ALU.subtract)
                p.op("vector", "tensor_scalar", hn16[c][:], sq[c % 2][:],
                     n2g_s[:, c:c + 1], n2b_s[:, c:c + 1], ALU.mult,
                     ALU.add, inc="dv" if c == 7 else None)
            hn_dn = p.val("dv")
            p.wait("tensor", "dv", hn_dn)
            units = [(pcu, phu) for pcu in range(8) for phu in range(2)]
            for ui, (pc, ph) in enumerate(units):
                wb = ui % 4
                lsem = lsems4[wb]
                p.wait("gpsimd", "pe", rec.get(f"wf1_used_{wb}", 0))
                p.wait("gpsimd", lsem, p.val(lsem))
                p.dma("gpsimd", wf1sl[wb],
                      wf1[pc][:, 2048 * ph:2048 * ph + 2048], inc=lsem)
                ldwf_now = p.val(lsem)
                p.wait("tensor", lsem, ldwf_now)
                for sub in range(4):
                    bank = [ps1, ps2, ps3, ps4][sub]
                    if ph == 0:
                        p.wait("tensor", "ac", rec.get(f"h1_ev{sub}", 0))
                    for kk2 in range(4):
                        kc = 4 * ph + kk2
                        p.op("tensor", "matmul", bank[:],
                             wf1sl[wb][:, 512 * kk2 + 128 * sub:
                                       512 * kk2 + 128 * sub + 128],
                             hn16[kc][:], start=(kc == 0), stop=(kc == 7),
                             inc="pe" if kk2 == 3 else None)
                    rec[f"wf1_used_{wb}"] = p.val("pe")
                    if ph == 1:
                        p.wait("scalar", "pe")
                        p.op("scalar", "activation", h1t[4 * pc + sub],
                             bank[:], AF.Gelu_apprx_tanh,
                             bias=bf1_s[:, 4 * pc + sub:4 * pc + sub + 1],
                             scale=1.0, inc="ac")
                        rec[f"h1_ev{sub}"] = p.val("ac")
            h1_dn = p.val("ac")
            p.wait("tensor", "ac", h1_dn)
            for half in range(2):
                for pc2 in range(8):
                    wb = pc2 % 4
                    lsem = lsems4[wb]
                    p.wait("gpsimd", "pe", rec.get(f"wf2_used_{wb}", 0))
                    p.wait("gpsimd", lsem, p.val(lsem))
                    p.dma("gpsimd",
                          wf2sl[wb].rearrange("p (kk c) -> p kk c", c=512),
                          wf2[pc2].rearrange("p (kk c) -> p kk c", c=1024)
                          [:, :, 512 * half:512 * half + 512], inc=lsem)
                    ldwf_now = p.val(lsem)
                    p.wait("tensor", lsem, ldwf_now)
                    for kk in range(4):
                        for q4 in range(4):
                            bankq = [ps1, ps2, ps3, ps4][q4]
                            if pc2 == 0 and kk == 0:
                                p.wait("tensor", "dv",
                                       rec.get(f"ffn_out{q4}", 0))
                            p.op("tensor", "matmul", bankq[:],
                                 wf2sl[wb][:, 512 * kk + 128 * q4:
                                           512 * kk + 128 * q4 + 128],
                                 h1t[4 * pc2 + kk],
                                 start=(pc2 == 0 and kk == 0),
                                 stop=(pc2 == 7 and kk == 3),
                                 inc="pe" if q4 == 3 else None)
                    rec[f"wf2_used_{wb}"] = p.val("pe")
                w2_pe = p.val("pe")
                p.wait("vector", "pe", w2_pe)
                if i == 0 and half == 0:
                    p.wait("gpsimd", "xcp", 128)
                for q4 in range(4):
                    oc = 4 * half + q4
                    bankq = [ps1, ps2, ps3, ps4][q4]
                    sb = oc % 2
                    stsem = "sta0" if sb == 0 else "sta1"
                    if f"store_{stsem}" in rec:
                        p.wait("vector", stsem, rec[f"store_{stsem}"])
                    p.op("vector", "scalar_tensor_tensor", vout[sb][:],
                         bankq[:], bf2_s[:, oc:oc + 1], xlnf[oc][:],
                         ALU.add, ALU.add, inc="dv")
                    rec[f"ffn_out{q4}"] = p.val("dv")
                    p.wait("gpsimd", "dv")
                    p.dma("gpsimd",
                          xo[128 * oc:128 * oc + 128, t0:t0 + W],
                          vout[sb][:], accum_op=ALU.add, inc=stsem)
                    rec[f"store_{stsem}"] = p.val(stsem)
            rec[f"sta0_{i}"] = p.val("sta0")
            rec[f"sta1_{i}"] = p.val("sta1")
            rec["c_tile_dn"] = p.val("dv")
        p.wait("gpsimd", "sta0", p.val("sta0"))
        p.wait("gpsimd", "sta1", p.val("sta1"))

    if fused:
        return body
    emit_all(p, body)
    return nc


def _anchor16(hT):
    return hT[:].rearrange("p (b d) -> p b d", d=16)[:, :, 8:9].broadcast_to(
        [128, 8, 16])


def _anchor8(hT):
    return hT[:].rearrange("p (b d) -> p b d", d=16)[:, :, 8:9].broadcast_to(
        [128, 8, 128])


def _tile8(hT):
    return hT[:].unsqueeze(1).broadcast_to([128, 8, 128])


def l0_attn_consts():
    s = np.arange(128)[:, None]
    t = np.arange(128)[None, :]
    ltp = (s <= t).astype(np.float32)
    ltn = -ltp
    idenf = np.eye(128, dtype=np.float32)
    gmask = np.zeros((128, 128), dtype=np.float32)
    for g in range(4):
        for rr in range(16):
            gmask[32 * g + rr, rr:] = 1.0
    return ltp, ltn, idenf, gmask


def l0_attn_inmap(x_fm_b, inp):
    perm = np.zeros(1024, dtype=np.int64)
    for h in range(16):
        for j in range(32):
            perm[64 * h + j] = 64 * h + 2 * j
            perm[64 * h + 32 + j] = 64 * h + 2 * j + 1
    wq = np.asarray(inp["L0_attn_Wq"], np.float32)[:, perm]
    wk = np.asarray(inp["L0_attn_Wk"], np.float32)[:, perm]
    wv = np.asarray(inp["L0_attn_Wv"], np.float32)
    wo_ = np.asarray(inp["L0_attn_Wo"], np.float32)
    wf1_ = np.asarray(inp["L0_attn_Wf1"], np.float32)
    wf2_ = np.asarray(inp["L0_attn_Wf2"], np.float32)
    thw1_ = np.asarray(inp["L0_attn_theta_w1"], np.float32)
    thw2_ = np.asarray(inp["L0_attn_theta_w2"], np.float32)
    wf1r = np.zeros((8, 128, 4096), np.float32)
    wf2r = np.zeros((8, 128, 4096), np.float32)
    for pc in range(8):
        sl = wf1_[:, 512 * pc:512 * pc + 512]
        wf1r[pc] = sl.reshape(8, 128, 512).transpose(1, 0, 2).reshape(
            128, 4096)
        s2 = wf2_[512 * pc:512 * pc + 512, :]
        wf2r[pc] = s2.reshape(4, 128, 1024).transpose(1, 0, 2).reshape(
            128, 4096)
    ltp, ltn, idenf, gmask = l0_attn_consts()

    def col8(v):
        return np.ascontiguousarray(np.asarray(v, np.float32).reshape(
            8, 128).T)
    return dict(
        x=np.ascontiguousarray(x_fm_b),
        wq=np.ascontiguousarray(wq.reshape(8, 128, 1024)),
        wk=np.ascontiguousarray(wk.reshape(8, 128, 1024)),
        wv=np.ascontiguousarray(wv.reshape(8, 128, 1024)),
        wo=np.ascontiguousarray(wo_.reshape(8, 128, 1024)),
        wf1=wf1r, wf2=wf2r,
        thw1=np.ascontiguousarray(thw1_.reshape(8, 128, 48)),
        thw2=np.ascontiguousarray(thw2_),
        bqr=np.ascontiguousarray(
            np.asarray(inp["L0_attn_bq"], np.float32)[perm][None, :]),
        bkr=np.ascontiguousarray(
            np.asarray(inp["L0_attn_bk"], np.float32)[perm][None, :]),
        bvr=np.ascontiguousarray(
            np.asarray(inp["L0_attn_bv"], np.float32)[None, :]),
        bor=col8(inp["L0_attn_bo"]),
        bf1r=np.ascontiguousarray(
            np.asarray(inp["L0_attn_bf1"], np.float32).reshape(32, 128).T),
        bf2r=col8(inp["L0_attn_bf2"]),
        n1gr=col8(inp["L0_attn_n1g"]), n1br=col8(inp["L0_attn_n1b"]),
        n2gr=col8(inp["L0_attn_n2g"]), n2br=col8(inp["L0_attn_n2b"]),
        thbr=np.ascontiguousarray(
            np.asarray(inp["L0_attn_theta_base"], np.float32).reshape(1, 512)),
        lbr=np.ascontiguousarray(
            np.asarray(inp["L0_attn_lambda_base"],
                       np.float32).reshape(1, 512)),
        etar=np.ascontiguousarray(
            np.asarray(inp["L0_attn_eta"], np.float32).reshape(1, 512)),
        ltp=ltp, ltn=ltn, idenf=idenf, gmask=gmask,
    )


def run_l0_attn(x_fm, inp):
    nc = build_l0_attn()
    in_maps = [l0_attn_inmap(x_fm[b], inp) for b in range(NCORES)]
    res = run_bass_kernel_spmd(nc, in_maps, core_ids=list(range(NCORES)))
    return np.stack([r["xo"] for r in res.results])


# version 04
# speedup vs baseline: 1.3233x; 1.3233x over previous
import numpy as np
import os
import ml_dtypes
import concourse.bass as bass
import concourse.mybir as mybir
from concourse.bass_utils import run_bass_kernel_spmd
from contextlib import ExitStack

F32 = mybir.dt.float32
BF16 = mybir.dt.bfloat16
AF = mybir.ActivationFunctionType
ALU = mybir.AluOpType
AX = mybir.AxisListType

T = 4096
D = 1024
NCORES = 8

ENGINES = ("sync", "tensor", "vector", "scalar", "gpsimd")


class Sem:
    __slots__ = ("h", "count")

    def __init__(self, h):
        self.h = h
        self.count = 0


class P:

    def __init__(self, nc, ctx):
        self.nc = nc
        self.ctx = ctx
        self.sems = {}
        self.cur = None
        self.sbufs = {}
        self.psums = {}
        self.block_tag = ""
        self.sem_pfx = ""
        self.arena = None
        self.arena_off = 0
        self.psum_map = None
        self._hwm = {}

    def set_block(self, pfx, psmap=None):
        self._hwm[self.block_tag] = max(
            self._hwm.get(self.block_tag, 0), self.arena_off)
        self.block_tag = pfx
        self.sem_pfx = pfx
        self.psum_map = psmap
        self.arena_off = self._hwm.get(pfx, 0)

    def barrier(self, prev_pfx):
        for name, s in self.sems.items():
            if name.startswith(prev_pfx) and s.count > 0:
                if self.cur is not None:
                    getattr(self.nc, self.cur).wait_ge(s.h, s.count)

    def sem(self, name):
        name = self.sem_pfx + name
        if name not in self.sems:
            self.sems[name] = Sem(self.ctx.enter_context(self.nc.semaphore(name)))
        return self.sems[name]

    def sbuf(self, name, shape, dtype):
        key = (self.block_tag, name)
        if key not in self.sbufs:
            if self.arena is None:
                self.sbufs[key] = self.ctx.enter_context(
                    self.nc.sbuf_tensor(name, list(shape), dtype))
            else:
                pp, cc = shape
                nbytes = cc * (4 if dtype == F32 else 2)
                off = self.arena_off
                self.arena_off = off + ((nbytes + 31) // 32) * 32
                sl = self.arena[0:pp, off // 2: off // 2 + ((nbytes + 1) // 2)]
                self.sbufs[key] = sl if dtype == BF16 else sl.bitcast(dtype)
        return self.sbufs[key]

    def psum(self, name, shape, dtype=F32):
        key = (self.block_tag, name)
        if key not in self.psums:
            if self.psum_map is None:
                self.psums[key] = self.ctx.enter_context(
                    self.nc.psum_tensor(name, list(shape), dtype))
            else:
                self.psums[key] = self.psum_map[name]
        return self.psums[key]

    def begin(self, eng):
        self.cur = eng
        for s in self.sems.values():
            s.count = 0

    def wait(self, eng, sem, val=None):
        s = self.sems[self.sem_pfx + sem]
        v = s.count if val is None else val
        if self.cur == eng and v > 0:
            getattr(self.nc, eng).wait_ge(s.h, v)
        return v

    def op(self, eng, method, *args, inc=None, **kw):
        r = None
        if self.cur == eng:
            e = getattr(self.nc, eng)
            if eng in ("vector", "scalar", "gpsimd") and not \
                    method.startswith("dma"):
                e.drain()
            r = getattr(e, method)(*args, **kw)
        if inc is not None:
            s = self.sems[self.sem_pfx + inc]
            step = 16 if method.startswith("dma") else 1
            s.count += step
            if r is not None:
                r.then_inc(s.h, step)
        return r

    def dma(self, eng, out, in_, inc=None, **kw):
        return self.op(eng, "dma_start", inc=inc, out=out, in_=in_, **kw)

    def val(self, sem):
        return self.sems[self.sem_pfx + sem].count


class NCProxy:

    def __init__(self, nc, pfx, x_ap, xo_ap):
        self._nc = nc
        self._pfx = pfx
        self._x = x_ap
        self._xo = xo_ap

    def dram_tensor(self, name, shape, dtype, kind="Internal"):
        if name == "x" and kind == "ExternalInput":
            return _FakeT(self._x)
        if name == "xo" and kind == "ExternalOutput":
            return _FakeT(self._xo)
        k = "ExternalInput" if kind == "ExternalInput" else "Internal"
        return self._nc.dram_tensor(self._pfx + name, shape, dtype, kind=k)

    def __getattr__(self, a):
        return getattr(self._nc, a)


class _FakeT:
    def __init__(self, ap):
        self._ap = ap

    def ap(self):
        return self._ap


def emit_all(p, body):
    with p.nc.Block() as block:
        for eng in ENGINES:
            def make(e):
                def f(_):
                    p.begin(e)
                    body()
                return f
            getattr(block, eng)(make(eng))



def build_l1_ffn(fused=None):
    nc = fused["nc"] if fused else bass.Bass("TRN2", target_bir_lowering=False)
    x = nc.dram_tensor("x", [D, T], F32, kind="ExternalInput").ap()
    wr = nc.dram_tensor("wr_t", [D, D], F32, kind="ExternalInput").ap()
    wk = nc.dram_tensor("wk_t", [D, D], F32, kind="ExternalInput").ap()
    wv = nc.dram_tensor("wv_t", [D, D], F32, kind="ExternalInput").ap()
    mu_r = nc.dram_tensor("mu_r", [128, 8], F32, kind="ExternalInput").ap()
    mu_k = nc.dram_tensor("mu_k", [128, 8], F32, kind="ExternalInput").ap()
    xo = nc.dram_tensor("xo", [D, T], F32, kind="ExternalOutput").ap()

    NT = 8
    W = 512

    p = fused["p"] if fused else P(nc, ExitStack())
    wtiles = {}
    for nm in ("wr", "wk", "wv"):
        wtiles[nm] = [p.sbuf(f"{nm}_{kc}", [128, D], BF16) for kc in range(8)]
    mur = p.sbuf("mur", [128, 8], F32)
    muk = p.sbuf("muk", [128, 8], F32)
    omur = p.sbuf("omur", [128, 8], F32)
    omuk = p.sbuf("omuk", [128, 8], F32)
    ones_col = p.sbuf("ones_col", [128, 1], F32)
    ones_row = p.sbuf("ones_row", [1, 128], F32)
    eps_col = p.sbuf("eps_col", [128, 1], F32)

    xch = [[p.sbuf(f"x_{c}_{b}", [128, W + 1], F32) for b in range(2)]
           for c in range(8)]
    xln = [[p.sbuf(f"xln_{c}_{b}", [128, W + 1], F32) for b in range(2)]
           for c in range(8)]
    sq = [p.sbuf(f"sq_{b}", [128, W], F32) for b in range(2)]
    rb = p.sbuf("rb", [128, W], F32)
    mrb = p.sbuf("mrb", [128, W], F32)
    tsc = p.sbuf("tsc", [128, W], F32)
    xr16 = [p.sbuf(f"xr16_{c}", [128, W], BF16) for c in range(8)]
    xk16 = [p.sbuf(f"xk16_{c}", [128, W], BF16) for c in range(8)]
    rsig = [p.sbuf(f"rsig_{c}", [128, W], F32) for c in range(8)]
    k16a = [p.sbuf(f"k16a_{c}", [128, W], BF16) for c in range(8)]
    kr = [p.sbuf(f"kr_{b}", [128, W], F32) for b in range(2)]
    vout = [p.sbuf(f"vout_{b}", [128, W], F32) for b in range(2)]
    m_r = p.sbuf("m_r", [1, W], F32)
    q_r = p.sbuf("q_r", [1, W], F32)
    m2_r = p.sbuf("m2_r", [1, W], F32)
    var_r = p.sbuf("var_r", [1, W], F32)
    lnv_r = p.sbuf("lnv_r", [1, W], F32)
    rs_r = p.sbuf("rs_r", [1, W], F32)
    mr_r = p.sbuf("mr_r", [1, W], F32)

    ps_sum = p.psum("ps_sum", [1, W])
    ps_sq = p.psum("ps_sq", [1, W])
    ps_b1 = p.psum("ps_b1", [128, W])
    ps_b2 = p.psum("ps_b2", [128, W])
    ps_mm = [p.psum(f"ps_mm{i}", [128, W]) for i in range(2)]
    ps_v = [p.psum(f"ps_v{i}", [128, W]) for i in range(2)]

    for s in ("ldc", "ldw", "lda", "ldb", "pe", "dv", "ac", "st0", "st1"):
        p.sem(s)

    def body():
        for nm, src in (("wr", wr), ("wk", wk), ("wv", wv)):
            for kc in range(8):
                p.dma("gpsimd", wtiles[nm][kc][:],
                      src[128 * kc:128 * (kc + 1), :], inc="ldw")
        p.dma("sync", mur[:], mu_r, inc="ldc")
        p.dma("sync", muk[:], mu_k, inc="ldc")
        p.op("vector", "memset", ones_col[:], 1.0)
        p.op("vector", "memset", ones_row[:], 1.0)
        p.op("vector", "memset", eps_col[:], 1e-5, inc="dv")
        p.wait("scalar", "dv")
        p.wait("vector", "ldc", 32)
        p.op("vector", "tensor_scalar", omur[:], mur[:], -1.0, 1.0,
             ALU.mult, ALU.add)
        p.op("vector", "tensor_scalar", omuk[:], muk[:], -1.0, 1.0,
             ALU.mult, ALU.add)
        for c in range(8):
            p.op("vector", "memset", xln[c][0][:, 0:1], 0.0, inc="dv")

        rec = {"rows_read_psum": 0, "copies_done": 0,
               "sig_evac": [0, 0], "v_evac": [0, 0]}

        for i in range(NT):
            buf = i % 2
            t0 = i * W
            ldsem = "lda" if buf == 0 else "ldb"
            if i >= 2:
                p.wait("sync", "st0", rec[f"st_tile0_{i-2}"])
                p.wait("sync", "st1", rec[f"st_tile1_{i-2}"])
            for c in range(8):
                r0 = 128 * c
                if i == 0:
                    p.dma("sync", xch[c][buf][:, 1:W + 1],
                          x[r0:r0 + 128, 0:W], inc=ldsem)
                else:
                    p.dma("sync", xch[c][buf][:, 0:W + 1],
                          x[r0:r0 + 128, t0 - 1:t0 + W], inc=ldsem)
            ld_tile = p.val(ldsem)

            p.wait("scalar", ldsem, ld_tile)
            p.wait("tensor", "dv", rec["rows_read_psum"])
            pe_after_chunk = []
            for c in range(8):
                if c >= 2:
                    p.wait("scalar", "pe", pe_after_chunk[c - 2])
                p.op("scalar", "activation", sq[c % 2][:],
                     xch[c][buf][:, 1:W + 1], AF.Square, bias=0.0,
                     scale=1.0, inc="ac")
                p.wait("tensor", "ac")
                p.op("tensor", "matmul", ps_sum[:], ones_col[:],
                     xch[c][buf][:, 1:W + 1], start=(c == 0), stop=(c == 7))
                p.op("tensor", "matmul", ps_sq[:], ones_col[:], sq[c % 2][:],
                     start=(c == 0), stop=(c == 7), inc="pe")
                pe_after_chunk.append(p.val("pe"))

            p.wait("vector", "pe")
            p.op("vector", "tensor_scalar", m_r[:], ps_sum[:], 1.0 / D, 0.0,
                 ALU.mult, ALU.add)
            p.op("vector", "tensor_scalar", q_r[:], ps_sq[:], 1.0 / D, 0.0,
                 ALU.mult, ALU.add, inc="dv")
            rec["rows_read_psum"] = p.val("dv")
            p.op("vector", "tensor_tensor", m2_r[:], m_r[:], m_r[:], ALU.mult)
            p.op("vector", "tensor_tensor", var_r[:], q_r[:], m2_r[:],
                 ALU.subtract, inc="dv")
            p.wait("scalar", "dv")
            p.op("scalar", "activation", lnv_r[:], var_r[:], AF.Ln,
                 bias=eps_col[0:1, 0:1], scale=1.0)
            p.op("scalar", "activation", rs_r[:], lnv_r[:], AF.Exp,
                 bias=0.0, scale=-0.5, inc="ac")
            p.wait("vector", "ac")
            p.op("vector", "tensor_tensor", mr_r[:], m_r[:], rs_r[:],
                 ALU.mult, inc="dv")

            p.wait("tensor", "dv")
            p.wait("tensor", "ac", rec["copies_done"])
            p.op("tensor", "matmul", ps_b1[:], ones_row[:], rs_r[:],
                 start=True, stop=True)
            p.op("tensor", "matmul", ps_b2[:], ones_row[:], mr_r[:],
                 start=True, stop=True, inc="pe")
            p.wait("scalar", "pe")
            p.op("scalar", "copy", rb[:], ps_b1[:], inc="ac")
            p.op("scalar", "copy", mrb[:], ps_b2[:], inc="ac")
            rec["copies_done"] = p.val("ac")

            p.wait("vector", "ac")
            for c in range(8):
                if i > 0:
                    p.op("vector", "tensor_copy", xln[c][buf][:, 0:1],
                         xln[c][1 - buf][:, W:W + 1])
                p.op("vector", "tensor_tensor", xln[c][buf][:, 1:W + 1],
                     xch[c][buf][:, 1:W + 1], rb[:], ALU.mult)
                p.op("vector", "tensor_tensor", xln[c][buf][:, 1:W + 1],
                     xln[c][buf][:, 1:W + 1], mrb[:], ALU.subtract)
            for c in range(8):
                p.op("vector", "tensor_scalar", tsc[:],
                     xln[c][buf][:, 1:W + 1], mur[:, c:c + 1], 0.0,
                     ALU.mult, ALU.add)
                p.op("vector", "scalar_tensor_tensor", xr16[c][:],
                     xln[c][buf][:, 0:W], omur[:, c:c + 1], tsc[:],
                     ALU.mult, ALU.add)
                p.op("vector", "tensor_scalar", tsc[:],
                     xln[c][buf][:, 1:W + 1], muk[:, c:c + 1], 0.0,
                     ALU.mult, ALU.add)
                p.op("vector", "scalar_tensor_tensor", xk16[c][:],
                     xln[c][buf][:, 0:W], omuk[:, c:c + 1], tsc[:],
                     ALU.mult, ALU.add, inc="dv" if c == 7 else None)
            mixes_done = p.val("dv")

            p.wait("tensor", "dv", mixes_done)
            if i == 0:
                p.wait("tensor", "ldw", 16 * 24)
            for oc in range(8):
                bank = oc % 2
                p.wait("tensor", "ac", rec["sig_evac"][bank])
                for kc in range(8):
                    p.op("tensor", "matmul", ps_mm[bank][:],
                         wtiles["wr"][kc][:, 128 * oc:128 * (oc + 1)],
                         xr16[kc][:], start=(kc == 0), stop=(kc == 7),
                         inc="pe" if kc == 7 else None)
                p.wait("scalar", "pe")
                p.op("scalar", "activation", rsig[oc][:], ps_mm[bank][:],
                     AF.Sigmoid, bias=0.0, scale=1.0, inc="ac")
                rec["sig_evac"][bank] = p.val("ac")
            for oc in range(8):
                bank = oc % 2
                p.wait("tensor", "dv", rec["v_evac"][bank])
                for kc in range(8):
                    p.op("tensor", "matmul", ps_v[bank][:],
                         wtiles["wk"][kc][:, 128 * oc:128 * (oc + 1)],
                         xk16[kc][:], start=(kc == 0), stop=(kc == 7),
                         inc="pe" if kc == 7 else None)
                p.wait("vector", "pe")
                if oc >= 2:
                    p.wait("vector", "ac", rec[f"ksq_{bank}"])
                p.op("vector", "tensor_scalar", kr[bank][:], ps_v[bank][:],
                     0.0, 0.0, ALU.max, ALU.add, inc="dv")
                rec["v_evac"][bank] = p.val("dv")
                p.wait("scalar", "dv")
                p.op("scalar", "activation", k16a[oc][:], kr[bank][:],
                     AF.Square, bias=0.0, scale=1.0, inc="ac")
                rec[f"ksq_{bank}"] = p.val("ac")
            ksq_all = p.val("ac")
            for oc in range(8):
                bank = oc % 2
                p.wait("tensor", "ac", ksq_all)
                p.wait("tensor", "dv", rec.get(f"fin_{bank}", 0))
                for kc in range(8):
                    p.op("tensor", "matmul", ps_mm[bank][:],
                         wtiles["wv"][kc][:, 128 * oc:128 * (oc + 1)],
                         k16a[kc][:], start=(kc == 0), stop=(kc == 7),
                         inc="pe" if kc == 7 else None)
                p.wait("vector", "pe")
                if f"store_{bank}" in rec:
                    p.wait("vector", f"st{bank}", rec[f"store_{bank}"])
                p.op("vector", "tensor_tensor", vout[bank][:],
                     ps_mm[bank][:], rsig[oc][:], ALU.mult)
                p.op("vector", "tensor_tensor", vout[bank][:], vout[bank][:],
                     xch[oc][buf][:, 1:W + 1], ALU.add, inc="dv")
                rec[f"fin_{bank}"] = p.val("dv")
                p.wait("sync", "dv")
                p.dma("sync", xo[128 * oc:128 * (oc + 1), t0:t0 + W],
                      vout[bank][:], inc=f"st{bank}")
                rec[f"store_{bank}"] = p.val(f"st{bank}")
            rec[f"st_tile0_{i}"] = p.val("st0")
            rec[f"st_tile1_{i}"] = p.val("st1")
        p.wait("gpsimd", "st0", p.val("st0"))
        p.wait("gpsimd", "st1", p.val("st1"))

    if fused:
        return body
    emit_all(p, body)
    return nc



def l1_ffn_inmap(x_fm_b, inp):
    return {
        "x": np.ascontiguousarray(x_fm_b),
        "wr_t": np.ascontiguousarray(np.asarray(inp["L1_ffn_W_r"]).T),
        "wk_t": np.ascontiguousarray(np.asarray(inp["L1_ffn_W_k"]).T),
        "wv_t": np.ascontiguousarray(np.asarray(inp["L1_ffn_W_v"]).T),
        "mu_r": np.ascontiguousarray(
            np.asarray(inp["L1_ffn_mu_r"]).reshape(8, 128).T),
        "mu_k": np.ascontiguousarray(
            np.asarray(inp["L1_ffn_mu_k"]).reshape(8, 128).T),
    }


def run_l1_ffn(x_fm, inp):
    nc = build_l1_ffn()
    in_maps = [l1_ffn_inmap(x_fm[b], inp) for b in range(NCORES)]
    res = run_bass_kernel_spmd(nc, in_maps, core_ids=list(range(NCORES)))
    return np.stack([r["xo"] for r in res.results])


ARENA_W = 106416


def build_fused():
    nc = bass.Bass("TRN2", target_bir_lowering=False)
    x = nc.dram_tensor("x", [D, T], F32, kind="ExternalInput").ap()
    xo = nc.dram_tensor("xo", [D, T], F32, kind="ExternalOutput").ap()
    x1 = nc.dram_tensor("x1_d", [D, T], F32, kind="Internal").ap()
    x2 = nc.dram_tensor("x2_d", [D, T], F32, kind="Internal").ap()
    x3 = nc.dram_tensor("x3_d", [D, T], F32, kind="Internal").ap()
    ctx = ExitStack()
    with ctx:
        p = P(nc, ctx)
        p.arena = ctx.enter_context(
            nc.sbuf_tensor("arena", [128, ARENA_W], BF16))
        S1 = ctx.enter_context(nc.psum_tensor("S1", [128, 1024], F32))
        S2 = ctx.enter_context(nc.psum_tensor("S2", [128, 1024], F32))
        S3 = ctx.enter_context(nc.psum_tensor("S3", [128, 1024], F32))
        S4 = ctx.enter_context(nc.psum_tensor("S4", [128, 512], F32))
        S5 = ctx.enter_context(nc.psum_tensor("S5", [128, 512], F32))
        mapA = {"ps1": S1[:, 0:512], "ps2": S1[:, 512:1024],
                "ps3": S2[:, 0:512], "ps4": S2[:, 512:1024],
                "ps5": S3[:, 0:512], "ps6": S3[:, 512:1024],
                "ps7": S4[:], "pstr": S5[:].bitcast(BF16)}
        mapB = {"ps_a": S1[:, 0:512], "ps_b": S1[:, 512:1024],
                "ps_c": S2[:, 0:512], "ps_d": S2[:, 512:1024],
                "ps_n": S4[:], "ps_s2": S3[:]}
        mapC = {"ps_qk0": S4[:], "ps_qk1": S5[:], "ps_s0": S1[:],
                "ps_s1": S2[:], "ps_y": S3[:]}
        mapD = {"ps_sum": S4[0:1, :], "ps_sq": S5[0:1, :],
                "ps_b1": S1[:, 0:512], "ps_b2": S1[:, 512:1024],
                "ps_mm0": S2[:, 0:512], "ps_mm1": S2[:, 512:1024],
                "ps_v0": S3[:, 0:512], "ps_v1": S3[:, 512:1024]}
        blocks = [("A_", build_l0_attn, mapA, x, x1),
                  ("B_", build_l0_ffn, mapB, x1, x2),
                  ("C_", build_l1_attn, mapC, x2, x3),
                  ("D_", build_l1_ffn, mapD, x3, xo)]
        bodies = []
        for pfx, bld, psmap, xi, xout in blocks:
            p.set_block(pfx, psmap)
            proxy = NCProxy(nc, pfx, xi, xout)
            bodies.append(bld(fused={"nc": proxy, "p": p}))
            assert p.arena_off <= 2 * ARENA_W, \
                f"{pfx} arena overflow: {p.arena_off} B"
            if os.environ.get("ARENA_DEBUG"):
                print(f"arena {pfx}: {p.arena_off} / {2 * ARENA_W} B")

        def whole():
            for i, (pfx, bld, psmap, xi, xout) in enumerate(blocks):
                p.set_block(pfx, psmap)
                if i:
                    p.barrier(blocks[i - 1][0])
                bodies[i]()

        emit_all(p, whole)
    return nc


def fused_inmap(x_fm_b, inp):
    m = {"x": np.ascontiguousarray(x_fm_b)}
    for pfx, f in (("A_", l0_attn_inmap), ("B_", l0_ffn_inmap),
                   ("C_", l1_attn_inmap), ("D_", l1_ffn_inmap)):
        for k, v in f(x_fm_b, inp).items():
            if k != "x":
                m[pfx + k] = v
    return m


def kernel(**inp):
    x = np.asarray(inp["L0_attn_x"], dtype=np.float32)
    x_fm = np.ascontiguousarray(np.transpose(x, (0, 2, 1)))
    nc = build_fused()
    in_maps = [fused_inmap(x_fm[b], inp) for b in range(NCORES)]
    res = run_bass_kernel_spmd(nc, in_maps, core_ids=list(range(NCORES)))
    out = np.stack([r["xo"] for r in res.results])
    return np.ascontiguousarray(np.transpose(out, (0, 2, 1)))


def kernel_unfused(**inp):
    x = np.asarray(inp["L0_attn_x"], dtype=np.float32)
    x_fm = np.ascontiguousarray(np.transpose(x, (0, 2, 1)))
    x_fm = run_l0_attn(x_fm, inp)
    x_fm = run_l0_ffn(x_fm, inp)
    x_fm = run_l1_attn(x_fm, inp)
    x_fm = run_l1_ffn(x_fm, inp)
    return np.ascontiguousarray(np.transpose(x_fm, (0, 2, 1)))



def build_l1_attn(fused=None):
    nc = fused["nc"] if fused else bass.Bass("TRN2", target_bir_lowering=False)
    x = nc.dram_tensor("x", [D, T], F32, kind="ExternalInput").ap()
    wq = nc.dram_tensor("wq_t", [64, 64], F32, kind="ExternalInput").ap()
    wk = nc.dram_tensor("wk_t", [64, 64], F32, kind="ExternalInput").ap()
    wv = nc.dram_tensor("wv_t", [64, 64], F32, kind="ExternalInput").ap()
    wu = nc.dram_tensor("wu_t", [D, D], F32, kind="ExternalInput").ap()
    bu = nc.dram_tensor("bu", [128, 8], F32, kind="ExternalInput").ap()
    masks = nc.dram_tensor("masks", [128, 1920], F32,
                           kind="ExternalInput").ap()
    sel = nc.dram_tensor("sel", [64, 31], F32, kind="ExternalInput").ap()
    iden = nc.dram_tensor("iden", [16, 16], F32, kind="ExternalInput").ap()
    xo = nc.dram_tensor("xo", [D, T], F32, kind="ExternalOutput").ap()
    yt_d = nc.dram_tensor("yt_scr", [D, T], BF16, kind="Internal").ap()

    NT = 8
    W = 512
    SCALE = float(D) ** -0.25

    p = fused["p"] if fused else P(nc, ExitStack())
    wq_s = p.sbuf("wq_s", [128, 64], BF16)
    wk_s = p.sbuf("wk_s", [128, 64], BF16)
    wv_s = p.sbuf("wv_s", [128, 64], BF16)
    wut = [p.sbuf(f"wut_{kc}", [128, D], BF16) for kc in range(8)]
    bu_s = p.sbuf("bu_s", [128, 8], F32)
    maskm = p.sbuf("maskm", [128, 1920], BF16)
    mask_s = [maskm[:, 896 - 128 * m:1920 - 128 * m] for m in range(8)]
    sel_s = p.sbuf("sel_s", [64, 31], BF16)
    iden_s = p.sbuf("iden_s", [16, 16], F32)
    ones_col = p.sbuf("ones_col", [128, 1], F32)
    ones_row = p.sbuf("ones_row", [1, 128], F32)
    eps_col = p.sbuf("eps_col", [128, 1], F32)

    xln16 = [p.sbuf(f"xln16_{c}", [128, T], BF16) for c in range(8)]
    qT = p.sbuf("qT", [65, T], BF16)
    kT = p.sbuf("kT", [65, T], BF16)
    vh = p.sbuf("vh", [128, 32 * 65], BF16)
    ptile = [p.sbuf(f"pt_{b}", [128, 1024], BF16) for b in range(2)]
    sq64 = [p.sbuf(f"sq64_{b}", [64, W], BF16) for b in range(2)]
    nq = p.sbuf("nq", [16, T], BF16)
    nk = p.sbuf("nk", [16, T], BF16)
    nkm = p.sbuf("nkm", [16, 1], F32)
    nkr = p.sbuf("nkr", [1, 16], F32)
    kmax1 = p.sbuf("kmax1", [1, 1], F32)
    kmaxb = p.sbuf("kmaxb", [16, 1], F32)
    ball = p.sbuf("ball", [16, T], BF16)
    ytile = p.sbuf("ytile", [64, 1024], BF16)
    rl = p.sbuf("rl", [1, 1024], F32)
    rb64 = p.sbuf("rb64", [64, 1024], F32)

    xch = [[p.sbuf(f"x_{c}_{b}", [128, W], F32) for b in range(2)]
           for c in range(8)]
    sq = [p.sbuf(f"sq_{b}", [128, W], F32) for b in range(2)]
    rb = p.sbuf("rb", [128, W], F32)
    mrb = p.sbuf("mrb", [128, W], F32)
    m_r = p.sbuf("m_r", [1, W], F32)
    q_r = p.sbuf("q_r", [1, W], F32)
    m2_r = p.sbuf("m2_r", [1, W], F32)
    var_r = p.sbuf("var_r", [1, W], F32)
    lnv_r = p.sbuf("lnv_r", [1, W], F32)
    rs_r = p.sbuf("rs_r", [1, W], F32)
    mr_r = p.sbuf("mr_r", [1, W], F32)
    yld = [[xln16[c][:, W * b:W * (b + 1)] for b in range(2)]
           for c in range(8)]
    vout = [rb, mrb]

    ps_qk = [p.psum(f"ps_qk{i}", [128, W]) for i in range(2)]
    ps_s = [p.psum(f"ps_s{i}", [128, 1024]) for i in range(2)]
    ps_y = p.psum("ps_y", [128, 1024])
    ps_sum = ps_s[0]
    ps_sq = ps_s[1]
    ps_b1 = ps_y
    ps_b2 = ps_y

    for s in ("ldc", "ldw", "lda", "ldb", "ldya", "ldyb", "pe", "dv",
              "ac", "st0", "st1", "sty", "qrow"):
        p.sem(s)

    def body():
        for kc in range(8):
            p.dma("gpsimd", wut[kc][:], wu[128 * kc:128 * (kc + 1), :],
                  inc="ldw")
        for wdst, wsrc in ((wq_s, wq), (wk_s, wk), (wv_s, wv)):
            p.dma("gpsimd", wdst[0:64, :], wsrc, inc="ldw")
            p.dma("gpsimd", wdst[64:128, :], wsrc, inc="ldw")
        p.dma("gpsimd", sel_s[:], sel, inc="ldw")
        p.dma("gpsimd", maskm[:], masks, inc="ldw")
        p.dma("sync", bu_s[:], bu, inc="ldc")
        p.dma("sync", iden_s[:], iden, inc="ldc")
        p.op("vector", "memset", ones_col[:], 1.0)
        p.op("vector", "memset", ones_row[:], 1.0)
        p.op("vector", "memset", eps_col[:], 1e-5)
        p.op("vector", "memset", vh[:], 1.0)
        p.op("vector", "memset", kT[64:65, :], 1.0)
        p.op("vector", "memset", nq[:], 0.0)
        p.op("vector", "memset", nk[:], 0.0, inc="dv")
        memset_done = p.val("dv")

        rec = {"rows_read_psum": 0, "copies_done": 0}

        for i in range(NT):
            buf = i % 2
            t0 = i * W
            ldsem = "lda" if buf == 0 else "ldb"
            if i >= 2:
                p.wait("sync", "dv", rec[f"xln_done_{i-2}"])
            for c in range(8):
                r0 = 128 * c
                p.dma("sync", xch[c][buf][:], x[r0:r0 + 128, t0:t0 + W],
                      inc=ldsem)
            ld_tile = p.val(ldsem)
            p.wait("scalar", ldsem, ld_tile)
            p.wait("scalar", "dv", rec.get(f"xln_done_{i-1}", 0))
            p.wait("tensor", "dv", max(rec["rows_read_psum"],
                                       memset_done))
            p.wait("tensor", ldsem, ld_tile)
            pe_after_chunk = []
            for c in range(8):
                if c >= 2:
                    p.wait("scalar", "pe", pe_after_chunk[c - 2])
                p.op("scalar", "activation", sq[c % 2][:], xch[c][buf][:],
                     AF.Square, bias=0.0, scale=1.0, inc="ac")
                p.wait("tensor", "ac")
                p.op("tensor", "matmul", ps_sum[0:1, 0:W], ones_col[:],
                     xch[c][buf][:], start=(c == 0), stop=(c == 7))
                p.op("tensor", "matmul", ps_sq[0:1, 0:W], ones_col[:],
                     sq[c % 2][:],
                     start=(c == 0), stop=(c == 7), inc="pe")
                pe_after_chunk.append(p.val("pe"))
            p.wait("vector", "pe")
            p.op("vector", "tensor_scalar", m_r[:], ps_sum[0:1, 0:W],
                 1.0 / D, 0.0, ALU.mult, ALU.add)
            p.op("vector", "tensor_scalar", q_r[:], ps_sq[0:1, 0:W],
                 1.0 / D, 0.0, ALU.mult, ALU.add, inc="dv")
            rec["rows_read_psum"] = p.val("dv")
            p.op("vector", "tensor_tensor", m2_r[:], m_r[:], m_r[:],
                 ALU.mult)
            p.op("vector", "tensor_tensor", var_r[:], q_r[:], m2_r[:],
                 ALU.subtract, inc="dv")
            p.wait("scalar", "dv")
            p.op("scalar", "activation", lnv_r[:], var_r[:], AF.Ln,
                 bias=eps_col[0:1, 0:1], scale=1.0)
            p.op("scalar", "activation", rs_r[:], lnv_r[:], AF.Exp,
                 bias=0.0, scale=-0.5, inc="ac")
            p.wait("vector", "ac")
            p.op("vector", "tensor_tensor", mr_r[:], m_r[:], rs_r[:],
                 ALU.mult, inc="dv")
            p.wait("tensor", "dv")
            p.wait("tensor", "ac", rec["copies_done"])
            p.op("tensor", "matmul", ps_b1[0:128, 0:W], ones_row[:],
                 rs_r[:], start=True, stop=True)
            p.op("tensor", "matmul", ps_b2[0:128, W:2 * W], ones_row[:],
                 mr_r[:], start=True, stop=True, inc="pe")
            p.wait("scalar", "pe")
            p.op("scalar", "copy", rb[:], ps_b1[0:128, 0:W], inc="ac")
            p.op("scalar", "copy", mrb[:], ps_b2[0:128, W:2 * W],
                 inc="ac")
            rec["copies_done"] = p.val("ac")
            p.wait("vector", "ac")
            for c in range(8):
                p.op("vector", "tensor_tensor", sq[c % 2][:],
                     xch[c][buf][:], rb[:], ALU.mult)
                p.op("vector", "tensor_tensor",
                     xln16[c][:, t0:t0 + W], sq[c % 2][:], mrb[:],
                     ALU.subtract, inc="dv" if c == 7 else None)
            rec[f"xln_done_{i}"] = p.val("dv")
        xln_all = p.val("dv")

        p.wait("tensor", "ldw", p.val("ldw"))
        p.wait("tensor", "ldc", 32)
        p.wait("tensor", "dv", xln_all)
        for j in range(NT):
            t0 = j * W
            p.wait("tensor", "dv", rec.get("norm_read", 0))
            for h in range(16):
                ch, half = h // 2, (h % 2) * 64
                for qk, (wmat, ps_norm) in enumerate(
                        ((wq_s, ps_b1), (wk_s, ps_b2))):
                    bank = (2 * h + qk) % 2
                    p.wait("tensor", "ac", p.val("ac"))
                    p.op("tensor", "matmul", ps_qk[bank][0:64, :],
                         wmat[half:half + 64, :],
                         xln16[ch][half:half + 64, t0:t0 + W],
                         start=True, stop=True, inc="pe")
                    p.wait("scalar", "pe")
                    p.op("scalar", "activation", sq64[bank][:],
                         ps_qk[bank][0:64, :], AF.Square, bias=0.0,
                         scale=SCALE, inc="ac")
                    p.wait("tensor", "ac")
                    nsl = (slice(0, 16), slice(0, W)) if qk == 0 else \
                        (slice(0, 16), slice(W, 2 * W))
                    p.op("tensor", "matmul", ps_norm[nsl[0], nsl[1]],
                         sel_s[:, 15 - h:31 - h], sq64[bank][:],
                         start=(h == 0), stop=(h == 15),
                         inc="pe" if h == 15 else None)
            p.wait("vector", "pe")
            p.op("vector", "tensor_copy", nq[:, t0:t0 + W],
                 ps_b1[0:16, 0:W])
            p.op("vector", "tensor_copy", nk[:, t0:t0 + W],
                 ps_b2[0:16, W:2 * W], inc="dv")
            rec["norm_read"] = p.val("dv")
        p.op("vector", "tensor_reduce", nkm[:], nk[:], AX.X, ALU.max,
             inc="dv")
        p.wait("tensor", "dv")
        p.wait("tensor", "ac", p.val("ac"))
        p.op("tensor", "matmul", ps_qk[0][0:1, 0:16], nkm[:], iden_s[:],
             start=True, stop=True, inc="pe")
        p.wait("vector", "pe")
        p.op("vector", "tensor_copy", nkr[:], ps_qk[0][0:1, 0:16])
        p.op("vector", "tensor_reduce", kmax1[:], nkr[:], AX.X, ALU.max,
             inc="dv")
        p.wait("tensor", "dv")
        p.op("tensor", "matmul", ps_qk[1][0:16, 0:1], ones_row[0:1, 0:16],
             kmax1[:], start=True, stop=True, inc="pe")
        p.wait("vector", "pe")
        p.op("vector", "tensor_copy", kmaxb[:], ps_qk[1][0:16, 0:1])
        p.op("vector", "tensor_scalar", nq[:], nq[:], kmaxb[:, 0:1], 0.0,
             ALU.mult, ALU.add, inc="dv")
        p.wait("scalar", "dv")
        p.op("scalar", "activation", nk[:], nq[:], AF.Sqrt, bias=0.0,
             scale=1.0, inc="ac")
        p.wait("vector", "ac")
        p.op("vector", "tensor_scalar", ball[:], nk[:], -1.0, 0.0,
             ALU.mult, ALU.add, inc="dv")
        ball_done = p.val("dv")

        rec["pv_done"] = 0
        rec["exp_done"] = [0, 0]
        rec["mask_done"] = [0, 0]
        rec["pv_par"] = [0, 0]
        rec["y_evac"] = 0
        rec["pj_ac0"] = rec["pj_ac1"] = p.val("ac")
        rec["pj_dv0"] = rec["pj_dv1"] = p.val("dv")
        for h in range(16):
            ch, half = h // 2, (h % 2) * 64
            p.wait("sync", "dv", ball_done)
            p.wait("sync", "pe", rec["pv_done"])
            p.dma("sync", qT[64:65, :], ball[h:h + 1, :], inc="qrow")
            p.wait("scalar", "pe", rec["pv_done"])
            p.wait("vector", "pe", rec["pv_done"])
            nxt = 0
            for j in range(8):
                t0 = j * W
                for (wmat, dstT) in ((wq_s, qT), (wk_s, kT)):
                    bk = nxt
                    nxt ^= 1
                    p.wait("tensor", "ac", rec.get(f"pj_ac{bk}", 0))
                    p.wait("tensor", "dv", rec.get(f"pj_dv{bk}", 0))
                    p.op("tensor", "matmul", ps_qk[bk][0:64, :],
                         wmat[half:half + 64, :],
                         xln16[ch][half:half + 64, t0:t0 + W],
                         start=True, stop=True, inc="pe")
                    p.wait("scalar", "pe")
                    p.op("scalar", "activation", dstT[0:64, t0:t0 + W],
                         ps_qk[bk][0:64, :], AF.Copy, bias=0.0,
                         scale=SCALE, inc="ac")
                    rec[f"pj_ac{bk}"] = p.val("ac")
                for s4 in range(4):
                    st = t0 + 128 * s4
                    sidx = j * 4 + s4
                    bk = nxt
                    nxt ^= 1
                    p.wait("tensor", "ac", rec.get(f"pj_ac{bk}", 0))
                    p.wait("tensor", "dv", rec.get(f"pj_dv{bk}", 0))
                    p.op("tensor", "matmul", ps_qk[bk][0:128, 0:64],
                         xln16[ch][half:half + 64, st:st + 128],
                         wv_s[half:half + 64, :], start=True, stop=True,
                         inc="pe")
                    p.wait("vector", "pe")
                    p.op("vector", "tensor_copy",
                         vh[:, 65 * sidx:65 * sidx + 64],
                         ps_qk[bk][0:128, 0:64], inc="dv")
                    rec[f"pj_dv{bk}"] = p.val("dv")
            qkv_done = p.val("dv")
            ac_qk = p.val("ac")
            p.wait("tensor", "qrow")
            p.wait("tensor", "dv", qkv_done)
            p.wait("tensor", "ac", ac_qk)

            for j4 in range(4):
                q0 = 1024 * j4
                nkc = 8 * j4 + 8
                exps_at = {}

                def emit_pv(kcp):
                    parp = kcp % 2
                    p.wait("tensor", "ac", exps_at[kcp])
                    if kcp == 0:
                        p.wait("tensor", "dv", rec["y_evac"])
                    for hf in range(2):
                        p.op("tensor", "matmul",
                             ps_y[0:65, 512 * hf:512 * (hf + 1)],
                             vh[:, 65 * kcp:65 * (kcp + 1)],
                             ptile[parp][:, 512 * hf:512 * (hf + 1)],
                             start=(kcp == 0), stop=(kcp == nkc - 1),
                             inc="pe" if hf == 1 else None)
                    rec["pv_par"][parp] = p.val("pe")

                for kc in range(nkc):
                    par = kc % 2
                    k0 = 128 * kc
                    p.wait("tensor", "dv", rec["mask_done"][par])
                    p.wait("tensor", "ac", rec["exp_done"][par])
                    for hf in range(2):
                        p.op("tensor", "matmul",
                             ps_s[par][:, 512 * hf:512 * (hf + 1)],
                             kT[:, k0:k0 + 128],
                             qT[:, q0 + 512 * hf:q0 + 512 * (hf + 1)],
                             start=True, stop=True,
                             inc="pe" if hf == 1 else None)
                    s_done = p.val("pe")
                    m = kc - 8 * j4
                    if 0 <= m < 8:
                        p.wait("vector", "pe", s_done)
                        p.op("vector", "tensor_tensor", ps_s[par][:],
                             ps_s[par][:], mask_s[m], ALU.add,
                             inc="dv")
                        rec["mask_done"][par] = p.val("dv")
                        p.wait("scalar", "dv", rec["mask_done"][par])
                    else:
                        p.wait("scalar", "pe", s_done)
                    p.wait("scalar", "pe", rec["pv_par"][par])
                    p.op("scalar", "activation", ptile[par][:],
                         ps_s[par][:], AF.Exp, bias=0.0, scale=1.0,
                         inc="ac")
                    rec["exp_done"][par] = p.val("ac")
                    exps_at[kc] = p.val("ac")
                    if kc >= 1:
                        emit_pv(kc - 1)
                emit_pv(nkc - 1)
                rec["pv_done"] = p.val("pe")
                rec["vh_rd"] = rec["pv_done"]
                p.wait("vector", "pe", rec["pv_done"])
                p.op("vector", "reciprocal", rl[0:1, 0:512],
                     ps_y[64:65, 0:512])
                p.op("vector", "reciprocal", rl[0:1, 512:1024],
                     ps_y[64:65, 512:1024], inc="dv")
                p.wait("tensor", "dv")
                p.wait("tensor", "ac", p.val("ac"))
                for hf in range(2):
                    p.op("tensor", "matmul", ps_qk[hf][0:64, :],
                         ones_row[0:1, 0:64],
                         rl[0:1, 512 * hf:512 * (hf + 1)],
                         start=True, stop=True, inc="pe")
                p.wait("scalar", "pe")
                p.op("scalar", "copy", rb64[:, 0:512], ps_qk[0][0:64, :],
                     inc="ac")
                p.op("scalar", "copy", rb64[:, 512:1024],
                     ps_qk[1][0:64, :], inc="ac")
                p.wait("vector", "ac")
                p.op("vector", "tensor_tensor", ytile[:],
                     ps_y[0:64, :], rb64[:], ALU.mult, inc="dv")
                rec["y_evac"] = p.val("dv")
                rec["pj_ac0"] = rec["pj_ac1"] = p.val("ac")
                p.wait("sync", "dv", rec["y_evac"])
                p.dma("sync", yt_d[64 * h:64 * h + 64, q0:q0 + 1024],
                      ytile[:], inc="sty")
                p.wait("vector", "sty")

        rec2 = {}
        for i in range(NT):
            buf = i % 2
            t0 = i * W
            ldsem = "lda" if buf == 0 else "ldb"
            ldy = "ldya" if buf == 0 else "ldyb"
            if i == 0:
                p.wait("sync", "sty", p.val("sty"))
            if i >= 2:
                p.wait("sync", "st0", rec2[f"st0_{i-2}"])
                p.wait("sync", "st1", rec2[f"st1_{i-2}"])
            for c in range(8):
                r0 = 128 * c
                p.dma("sync", xch[c][buf][:], x[r0:r0 + 128, t0:t0 + W],
                      inc=ldsem)
                p.dma("sync", yld[c][buf],
                      yt_d[r0:r0 + 128, t0:t0 + W], inc=ldy)
            ld_tile = p.val(ldsem)
            ldy_tile = p.val(ldy)
            p.wait("tensor", ldsem, ld_tile)
            p.wait("tensor", ldy, ldy_tile)
            if i == 0:
                p.wait("vector", "ldc", 32)
            for oc in range(8):
                bank = oc % 2
                p.wait("tensor", "dv", rec2.get(f"fin_{bank}", 0))
                for kc in range(8):
                    p.op("tensor", "matmul", ps_qk[bank][:],
                         wut[kc][:, 128 * oc:128 * (oc + 1)],
                         yld[kc][buf], start=(kc == 0), stop=(kc == 7),
                         inc="pe" if kc == 7 else None)
                p.wait("vector", "pe")
                if f"store_{bank}" in rec2:
                    p.wait("vector", f"st{bank}", rec2[f"store_{bank}"])
                p.op("vector", "scalar_tensor_tensor", vout[bank][:],
                     ps_qk[bank][:], bu_s[:, oc:oc + 1], xch[oc][buf][:],
                     ALU.add, ALU.add, inc="dv")
                rec2[f"fin_{bank}"] = p.val("dv")
                p.wait("sync", "dv")
                p.dma("sync", xo[128 * oc:128 * (oc + 1), t0:t0 + W],
                      vout[bank][:], inc=f"st{bank}")
                rec2[f"store_{bank}"] = p.val(f"st{bank}")
            rec2[f"st0_{i}"] = p.val("st0")
            rec2[f"st1_{i}"] = p.val("st1")
        p.wait("gpsimd", "st0", p.val("st0"))
        p.wait("gpsimd", "st1", p.val("st1"))

    if fused:
        return body
    emit_all(p, body)
    return nc


def l1_attn_consts():
    r = np.arange(128)[:, None]
    u = np.arange(1920)[None, :]
    masks = np.where(u < r + 896, -1e38, 0.0).astype(np.float32)
    sel = np.zeros((64, 31), dtype=np.float32)
    sel[:, 15] = 1.0
    iden = np.eye(16, dtype=np.float32)
    return masks, sel, iden


def l1_attn_inmap(x_fm_b, inp):
    masks, sel, iden = l1_attn_consts()
    return {
        "x": np.ascontiguousarray(x_fm_b),
        "wq_t": np.ascontiguousarray(np.asarray(inp["L1_attn_Wq"]).T),
        "wk_t": np.ascontiguousarray(np.asarray(inp["L1_attn_Wk"]).T),
        "wv_t": np.ascontiguousarray(np.asarray(inp["L1_attn_Wv"]).T),
        "wu_t": np.ascontiguousarray(np.asarray(inp["L1_attn_Wu"]).T),
        "bu": np.ascontiguousarray(
            np.asarray(inp["L1_attn_bu"]).reshape(8, 128).T),
        "masks": masks, "sel": sel, "iden": iden,
    }


def run_l1_attn(x_fm, inp):
    nc = build_l1_attn()
    in_maps = [l1_attn_inmap(x_fm[b], inp) for b in range(NCORES)]
    res = run_bass_kernel_spmd(nc, in_maps, core_ids=list(range(NCORES)))
    return np.stack([r["xo"] for r in res.results])



def build_l0_ffn(debug=False, dbg_lay=0, fused=None):
    nc = fused["nc"] if fused else bass.Bass("TRN2", target_bir_lowering=False)
    x = nc.dram_tensor("x", [D, T], F32, kind="ExternalInput").ap()
    wqkv = nc.dram_tensor("wqkv", [2, 8, 128, 1536], F32,
                          kind="ExternalInput").ap()
    wo = nc.dram_tensor("wo", [2, 4, 128, 1024], F32,
                        kind="ExternalInput").ap()
    w1r = nc.dram_tensor("w1r", [2, 21, 128, 2048], F32,
                         kind="ExternalInput").ap()
    w1t = nc.dram_tensor("w1t", [2, 128, 672], F32, kind="ExternalInput").ap()
    w2r = nc.dram_tensor("w2r", [2, 21, 128, 1024], F32,
                         kind="ExternalInput").ap()
    w2t = nc.dram_tensor("w2t", [2, 42, 1024], F32, kind="ExternalInput").ap()
    b1r = nc.dram_tensor("b1r", [2, 128, 44], F32, kind="ExternalInput").ap()
    b2c = nc.dram_tensor("b2c", [2, 128, 8], F32, kind="ExternalInput").ap()
    qsc = nc.dram_tensor("qsc", [2, 128, 1], F32, kind="ExternalInput").ap()
    ksc = nc.dram_tensor("ksc", [2, 128, 1], F32, kind="ExternalInput").ap()
    lgc = nc.dram_tensor("lgc", [2, 128, 8], F32, kind="ExternalInput").ap()
    lbc = nc.dram_tensor("lbc", [2, 128, 8], F32, kind="ExternalInput").ap()
    c1q = nc.dram_tensor("c1q", [128, T], F32, kind="ExternalInput").ap()
    c2q = nc.dram_tensor("c2q", [128, T], F32, kind="ExternalInput").ap()
    c1k = nc.dram_tensor("c1k", [128, T], F32, kind="ExternalInput").ap()
    c2k = nc.dram_tensor("c2k", [128, T], F32, kind="ExternalInput").ap()
    prot = nc.dram_tensor("prot", [128, 128], F32, kind="ExternalInput").ap()
    masklo = nc.dram_tensor("masklo", [128, 128], F32,
                            kind="ExternalInput").ap()
    maskhi = nc.dram_tensor("maskhi", [128, 128], F32,
                            kind="ExternalInput").ap()
    idenf = nc.dram_tensor("idenf", [128, 128], F32,
                           kind="ExternalInput").ap()
    nsel2 = nc.dram_tensor("nsel2", [128, 2], F32, kind="ExternalInput").ap()
    bsel2 = nc.dram_tensor("bsel2", [2, 128], F32, kind="ExternalInput").ap()
    xo = nc.dram_tensor("xo", [D, T], F32, kind="ExternalOutput").ap()
    xb_d = nc.dram_tensor("xb_scr", [D, T], F32,
                          kind="ExternalOutput" if debug else "Internal").ap()
    if debug:
        dbg_xbl = nc.dram_tensor("dbg_xbl", [D, W0 := 512], F32,
                                 kind="ExternalOutput").ap()
        dbg_qrot = nc.dram_tensor("dbg_qrot", [512, 512], F32,
                                  kind="ExternalOutput").ap()
        dbg_krot = nc.dram_tensor("dbg_krot", [512, 512], F32,
                                  kind="ExternalOutput").ap()
        dbg_o = nc.dram_tensor("dbg_o", [512, 512], F32,
                               kind="ExternalOutput").ap()
        dbg_xmid = nc.dram_tensor("dbg_xmid", [D, 512], F32,
                                  kind="ExternalOutput").ap()

    NT = 8
    W = 512

    p = fused["p"] if fused else P(nc, ExitStack())
    wqkv_s = [p.sbuf(f"wqkv_{kc}", [128, 1536], BF16) for kc in range(8)]
    wo_s = [p.sbuf(f"wo_{kc}", [128, 1024], BF16) for kc in range(4)]
    w1sl = [p.sbuf(f"w1sl_{b}", [128, 1024], BF16) for b in range(4)]
    w2sl = [p.sbuf(f"w2sl_{b}", [128, 512], BF16) for b in range(4)]
    lsems = ("ldwfa", "ldwfb", "ldwfc", "ldwfd")
    b1_s = p.sbuf("b1_s", [128, 44], F32)
    b2_s = p.sbuf("b2_s", [128, 8], F32)
    qs_s = p.sbuf("qs_s", [128, 1], F32)
    ks_s = p.sbuf("ks_s", [128, 1], F32)
    lg_s = p.sbuf("lg_s", [128, 8], F32)
    lb_s = p.sbuf("lb_s", [128, 8], F32)
    prot_s = p.sbuf("prot_s", [128, 128], F32)
    mlo_s = p.sbuf("mlo_s", [128, 128], BF16)
    mhi_s = p.sbuf("mhi_s", [128, 128], BF16)
    idf_s = p.sbuf("idf_s", [128, 128], BF16)
    nsel_s = p.sbuf("nsel_s", [128, 2], F32)
    bsel_s = p.sbuf("bsel_s", [2, 128], F32)
    ct = {nm: p.sbuf(f"ct_{nm}", [128, W], F32)
          for nm in ("c1q", "c2q", "c1k", "c2k")}
    ones_col = p.sbuf("ones_col", [128, 1], F32)
    ones_row = p.sbuf("ones_row", [1, 128], F32)
    eps_col = p.sbuf("eps_col", [128, 1], F32)

    xch = [[p.sbuf(f"x_{c}_{b}", [128, W + 1], F32) for b in range(2)]
           for c in range(8)]
    xbl = [p.sbuf(f"xbl_{c}", [128, W + 1], F32) for c in range(8)]
    xh16 = [p.sbuf(f"xh16_{c}", [128, W + 1], BF16) for c in range(8)]
    xmid = [p.sbuf(f"xmid_{c}", [128, W + 1], F32) for c in range(8)]
    hn16 = [xh16[c][:, 0:W] for c in range(8)]
    sq = [p.sbuf(f"sq_{b}", [128, W], F32) for b in range(2)]
    rb = p.sbuf("rb", [128, W], F32)
    mrb = p.sbuf("mrb", [128, W], F32)
    m_r = p.sbuf("m_r", [1, W], F32)
    q_r = p.sbuf("q_r", [1, W], F32)
    m2_r = p.sbuf("m2_r", [1, W], F32)
    var_r = p.sbuf("var_r", [1, W], F32)
    lnv_r = p.sbuf("lnv_r", [1, W], F32)
    rs_r = p.sbuf("rs_r", [1, W], F32)
    mr_r = p.sbuf("mr_r", [1, W], F32)
    qf = p.sbuf("qf", [128, W], F32)
    rn2 = p.sbuf("rn2", [2, W], F32)
    lnn = p.sbuf("lnn", [2, W], F32)
    qhat = p.sbuf("qhat", [128, W], F32)
    glu = p.sbuf("glu", [128, W], F32)
    t1f = glu
    t2f = None
    qrot = [p.sbuf(f"qrot_{c}", [128, W], BF16) for c in range(4)]
    krot = [[p.sbuf(f"krot_{c}_{b}", [128, W], BF16) for b in range(2)]
            for c in range(4)]
    vtok = [[p.sbuf(f"vtok_{s}_{b}", [128, W], BF16) for b in range(2)]
            for s in range(4)]
    ptile = p.sbuf("ptile", [128, 640], BF16)
    mcol = p.sbuf("mcol", [128, 1], F32)
    nmcol = p.sbuf("nmcol", [128, 1], F32)
    lcol = p.sbuf("lcol", [128, 1], F32)
    rlc = p.sbuf("rlc", [128, 1], F32)
    diag = p.sbuf("diag", [128, 128], BF16)
    ptr = ptile
    o16 = qrot
    gat = [p.sbuf(f"gat_{pc}", [128, W], BF16) for pc in range(21)]
    gat_t = p.sbuf("gat_t", [42, W], BF16)
    vout = [p.sbuf(f"vout_{b}", [128, W], F32) for b in range(2)]

    ps_a = p.psum("ps_a", [128, W])
    ps_b = p.psum("ps_b", [128, W])
    ps_c = p.psum("ps_c", [128, W])
    ps_d = p.psum("ps_d", [128, W])
    ps_n = p.psum("ps_n", [128, W])
    ps_s2 = p.psum("ps_s2", [128, 1024])
    ps_tr = ps_s2

    for s in ("ldc", "ldw", "ldwv", "lda", "ldb", "ldwfa", "ldwfb",
              "ldwfc", "ldwfd", "pe", "dv", "ac", "st0", "st1", "sta0",
              "sta1", "xcp", "dbg", "dbgs"):
        p.sem(s)

    def body():
        p.op("vector", "memset", ones_col[:], 1.0)
        p.op("vector", "memset", ones_row[:], 1.0)
        p.op("vector", "memset", eps_col[:], 1e-5, inc="dv")
        memset_done = p.val("dv")
        for dst, srcd in ((prot_s, prot), (idf_s, idenf),
                          (mlo_s, masklo), (mhi_s, maskhi),
                          (nsel_s, nsel2), (bsel_s, bsel2)):
            p.dma("gpsimd", dst[:], srcd, inc="ldc")
        ldc_all = p.val("ldc")

        rec = {"rows_read_psum": 0, "copies_done": 0, "lay_pe": 0,
               "xbl_done": 0}

        for lay in range(2):
            if lay == 1:
                p.wait("gpsimd", "pe", rec["lay_pe"])
                p.wait("sync", "pe", rec["lay_pe"])
                for c in range(8):
                    p.dma("sync", xo[128 * c:128 * c + 128, :],
                          x[128 * c:128 * c + 128, :], inc="xcp")
            for kc in range(8):
                p.dma("gpsimd", wqkv_s[kc][:], wqkv[lay, kc], inc="ldw")
            for kc in range(4):
                p.dma("gpsimd", wo_s[kc][:], wo[lay, kc], inc="ldw")
            for dst, srcd in ((qs_s, qsc[lay]), (ks_s, ksc[lay]),
                              (lg_s, lgc[lay]), (lb_s, lbc[lay]),
                              (b1_s, b1r[lay]), (b2_s, b2c[lay])):
                p.dma("sync", dst[:], srcd, inc="ldwv")
            ldw_lay = p.val("ldw")
            ldwv_lay = p.val("ldwv")

            p.wait("vector", "pe", rec["lay_pe"])
            for c in range(4):
                p.op("vector", "memset", krot[c][1][:], 0.0)
                p.op("vector", "memset", vtok[c][1][:], 0.0)
            for c in range(8):
                p.op("vector", "memset", xh16[c][:, 0:1], 0.0)
                p.op("vector", "memset", xbl[c][:, 0:1], 0.0)
                p.op("vector", "memset", xmid[c][:, 0:1], 0.0,
                     inc="dv" if c == 7 else None)
            lay_memset = p.val("dv")

            for i in range(NT):
                buf = i % 2
                pbuf = 1 - buf
                t0 = i * W
                ldsem = "lda" if buf == 0 else "ldb"
                src_d = x if lay == 0 else xb_d
                sem0 = "st0" if lay == 0 else "sta0"
                sem1 = "st1" if lay == 0 else "sta1"
                if i >= 2:
                    p.wait("sync", sem0, rec[f"{sem0}_{i-2}"])
                    p.wait("sync", sem1, rec[f"{sem1}_{i-2}"])
                elif lay == 1 and i < 2:
                    p.wait("sync", "st0", rec["lay0_st0"])
                    p.wait("sync", "st1", rec["lay0_st1"])
                for c in range(8):
                    r0 = 128 * c
                    if i == 0:
                        p.dma("sync", xch[c][buf][:, 1:W + 1],
                              src_d[r0:r0 + 128, 0:W], inc=ldsem)
                    else:
                        p.dma("sync", xch[c][buf][:, 0:W + 1],
                              src_d[r0:r0 + 128, t0 - 1:t0 + W],
                              inc=ldsem)
                if i >= 1:
                    p.wait("sync", "dv", rec[f"qkrot_{i-1}"])
                for nm, csrc in (("c1q", c1q), ("c2q", c2q),
                                 ("c1k", c1k), ("c2k", c2k)):
                    p.dma("sync", ct[nm][:], csrc[:, t0:t0 + W],
                          inc=ldsem)
                ld_tile = p.val(ldsem)

                if lay == 0:
                    p.wait("scalar", ldsem, ld_tile)
                    p.wait("scalar", "dv", rec["xbl_done"])
                    p.wait("tensor", "dv",
                           max(rec["rows_read_psum"], memset_done,
                               lay_memset))
                    p.wait("tensor", ldsem, ld_tile)
                    pe_after = []
                    for c in range(8):
                        if c >= 2:
                            p.wait("scalar", "pe", pe_after[c - 2])
                        p.op("scalar", "activation", sq[c % 2][:],
                             xch[c][buf][:, 1:W + 1], AF.Square,
                             bias=0.0, scale=1.0, inc="ac")
                        p.wait("tensor", "ac")
                        p.op("tensor", "matmul", ps_a[0:1, :],
                             ones_col[:], xch[c][buf][:, 1:W + 1],
                             start=(c == 0), stop=(c == 7))
                        p.op("tensor", "matmul", ps_b[0:1, :],
                             ones_col[:], sq[c % 2][:],
                             start=(c == 0), stop=(c == 7), inc="pe")
                        pe_after.append(p.val("pe"))
                    p.wait("vector", "pe")
                    p.op("vector", "tensor_scalar", m_r[:], ps_a[0:1, :],
                         1.0 / D, 0.0, ALU.mult, ALU.add)
                    p.op("vector", "tensor_scalar", q_r[:], ps_b[0:1, :],
                         1.0 / D, 0.0, ALU.mult, ALU.add, inc="dv")
                    rec["rows_read_psum"] = p.val("dv")
                    p.op("vector", "tensor_tensor", m2_r[:], m_r[:],
                         m_r[:], ALU.mult)
                    p.op("vector", "tensor_tensor", var_r[:], q_r[:],
                         m2_r[:], ALU.subtract, inc="dv")
                    p.wait("scalar", "dv")
                    p.op("scalar", "activation", lnv_r[:], var_r[:],
                         AF.Ln, bias=eps_col[0:1, 0:1], scale=1.0)
                    p.op("scalar", "activation", rs_r[:], lnv_r[:],
                         AF.Exp, bias=0.0, scale=-0.5, inc="ac")
                    p.wait("vector", "ac")
                    p.op("vector", "tensor_tensor", mr_r[:], m_r[:],
                         rs_r[:], ALU.mult, inc="dv")
                    p.wait("tensor", "dv")
                    p.wait("tensor", "ac", rec["copies_done"])
                    p.op("tensor", "matmul", ps_c[:], ones_row[:],
                         rs_r[:], start=True, stop=True)
                    p.op("tensor", "matmul", ps_d[:], ones_row[:],
                         mr_r[:], start=True, stop=True, inc="pe")
                    p.wait("scalar", "pe")
                    p.op("scalar", "copy", rb[:], ps_c[:], inc="ac")
                    p.op("scalar", "copy", mrb[:], ps_d[:], inc="ac")
                    rec["copies_done"] = p.val("ac")
                    p.wait("vector", "ac")
                    if debug and i == 1 and lay == dbg_lay:
                        p.wait("vector", "dbgs", 256)
                    for c in range(8):
                        if i > 0:
                            p.op("vector", "tensor_copy",
                                 xbl[c][:, 0:1], xbl[c][:, W:W + 1])
                            p.op("vector", "tensor_copy",
                                 xh16[c][:, 0:1], xbl[c][:, 0:1])
                        p.op("vector", "tensor_tensor",
                             sq[c % 2][:], xch[c][buf][:, 1:W + 1],
                             rb[:], ALU.mult)
                        p.op("vector", "tensor_tensor",
                             xbl[c][:, 1:W + 1], sq[c % 2][:], mrb[:],
                             ALU.subtract)
                        p.op("vector", "tensor_copy",
                             xh16[c][:, 1:W + 1], xbl[c][:, 1:W + 1],
                             inc="dv" if c == 7 else None)
                    rec["xbl_done"] = p.val("dv")
                else:
                    p.wait("vector", ldsem, ld_tile)
                    p.wait("vector", "dv", rec["xbl_done"])
                    p.wait("vector", "pe", rec.get("qkv_pe", 0))
                    if debug and i == 1 and lay == dbg_lay:
                        p.wait("vector", "dbgs", 256)
                    for c in range(8):
                        if i > 0:
                            p.op("vector", "tensor_copy",
                                 xbl[c][:, 0:1], xbl[c][:, W:W + 1])
                            p.op("vector", "tensor_copy",
                                 xh16[c][:, 0:1], xbl[c][:, 0:1])
                        p.op("vector", "tensor_copy",
                             xbl[c][:, 1:W + 1],
                             xch[c][buf][:, 1:W + 1])
                        p.op("vector", "tensor_copy",
                             xh16[c][:, 1:W + 1],
                             xch[c][buf][:, 1:W + 1],
                             inc="dv" if c == 7 else None)
                    rec["xbl_done"] = p.val("dv")
                xbl_done = rec["xbl_done"]

                def hsl(c, wdt=W):
                    return xh16[c][:, 1:wdt + 1] if c < 4 else \
                        xh16[c][:, 0:wdt]

                if debug and lay == dbg_lay and i == 1:
                    p.wait("vector", "dbg", 192)
                p.wait("tensor", "dv", xbl_done)
                if i == 0:
                    p.wait("tensor", "ldw", ldw_lay)
                    p.wait("tensor", "ldwv", ldwv_lay)
                    p.wait("tensor", "ldc", ldc_all)
                    p.wait("vector", "ldwv", ldwv_lay)
                    p.wait("scalar", "ldwv", ldwv_lay)
                for of in range(8):
                    isq = of < 4
                    cc = of % 4
                    bank = ps_a if of % 2 == 0 else ps_b
                    p.wait("tensor", "ac", rec.get(f"qk_ev{of % 2}", 0))
                    p.wait("tensor", "dv", rec.get("rot_dn_last", 0))
                    for kc in range(8):
                        p.op("tensor", "matmul", bank[:],
                             wqkv_s[kc][:, 128 * of:128 * of + 128],
                             hsl(kc), start=(kc == 0), stop=(kc == 7),
                             inc="pe" if kc == 7 else None)
                    p.wait("scalar", "pe")
                    p.op("scalar", "copy", qf[:], bank[:], inc="ac")
                    p.wait("scalar", "ac")
                    p.op("scalar", "activation", sq[0][:], qf[:],
                         AF.Square, bias=0.0, scale=1.0, inc="ac")
                    rec[f"qk_ev{of % 2}"] = p.val("ac")
                    p.wait("tensor", "ac", rec[f"qk_ev{of % 2}"])
                    p.op("tensor", "matmul", ps_n[0:2, :], nsel_s[:],
                         sq[0][:], start=True, stop=True, inc="pe")
                    p.wait("vector", "pe")
                    p.op("vector", "tensor_scalar", rn2[:], ps_n[0:2, :],
                         1.0, 1e-24, ALU.mult, ALU.max, inc="dv")
                    p.wait("scalar", "dv")
                    p.op("scalar", "activation", lnn[:], rn2[:], AF.Ln,
                         bias=0.0, scale=1.0)
                    p.op("scalar", "activation", rn2[:], lnn[:], AF.Exp,
                         bias=0.0, scale=-0.5, inc="ac")
                    p.wait("tensor", "ac")
                    p.op("tensor", "matmul", ps_n[:], bsel_s[:], rn2[:],
                         start=True, stop=True, inc="pe")
                    p.wait("vector", "pe")
                    p.op("vector", "tensor_tensor", qhat[:], qf[:],
                         ps_n[:], ALU.mult)
                    p.op("vector", "tensor_scalar", qhat[:], qhat[:],
                         qs_s[:, 0:1] if isq else ks_s[:, 0:1], 0.0,
                         ALU.mult, ALU.add, inc="dv")
                    qhat_dn = p.val("dv")
                    p.wait("tensor", "dv", qhat_dn)
                    p.op("tensor", "matmul", ps_n[:], prot_s[:],
                         qhat[:], start=True, stop=True, inc="pe")
                    c1 = ct["c1q" if isq else "c1k"]
                    c2 = ct["c2q" if isq else "c2k"]
                    p.wait("vector", "pe")
                    p.op("vector", "tensor_tensor", t1f[:], qhat[:],
                         c1[:], ALU.mult)
                    p.op("vector", "tensor_tensor", sq[1][:], ps_n[:],
                         c2[:], ALU.mult)
                    dst = qrot[cc] if isq else krot[cc][buf]
                    p.op("vector", "tensor_tensor", dst[:], t1f[:],
                         sq[1][:], ALU.add, inc="dv")
                    rec[f"rot_dn{of % 2}"] = p.val("dv")
                    rec["rot_dn_last"] = p.val("dv")
                qk_done = p.val("dv")
                rec[f"qkrot_{i}"] = qk_done
                if debug and lay == dbg_lay and i == 0:
                    p.wait("sync", "dv", qk_done)
                    p.wait("gpsimd", "dv", qk_done)
                    for c in range(8):
                        p.dma("sync", dbg_xbl[128 * c:128 * c + 128, :],
                              xbl[c][:, 1:W + 1], inc="dbgs")
                    for c in range(4):
                        p.dma("gpsimd", dbg_qrot[128 * c:128 * c + 128, :],
                              qrot[c][:], inc="dbg")
                        p.dma("gpsimd", dbg_krot[128 * c:128 * c + 128, :],
                              krot[c][buf][:], inc="dbg")
                    rec["dbg_q"] = p.val("dbg")

                for s4 in range(4):
                    bank = ps_a if s4 % 2 == 0 else ps_b
                    p.wait("tensor", "dv",
                           rec.get(f"v_ev{s4 % 2}", 0))
                    for kc in range(8):
                        st = (1 if kc < 4 else 0) + 128 * s4
                        p.op("tensor", "matmul", bank[:],
                             xh16[kc][:, st:st + 128],
                             wqkv_s[kc][:, 1024:1536],
                             start=(kc == 0), stop=(kc == 7),
                             inc="pe" if kc == 7 else None)
                    p.wait("vector", "pe")
                    p.op("vector", "tensor_copy", vtok[s4][buf][:],
                         bank[:], inc="dv")
                    rec[f"v_ev{s4 % 2}"] = p.val("dv")
                v_done = p.val("dv")

                if debug and lay == dbg_lay and i == 0:
                    p.wait("vector", "dbg", rec["dbg_q"])
                p.wait("tensor", "dv", v_done)
                for h in range(8):
                    hc, hb = h // 2, 64 * (h % 2)
                    for s4 in range(4):
                        wprev = W - 128 * s4
                        p.wait("tensor", "ac",
                               max(rec.get("exp_dn", 0),
                                   rec.get("ptr_ev", 0)))
                        p.wait("tensor", "dv", rec.get("msk_dn", 0))
                        qsl = qrot[hc][hb:hb + 64,
                                       128 * s4:128 * s4 + 128]
                        if i > 0:
                            p.op("tensor", "matmul",
                                 ps_s2[:, 0:wprev], qsl,
                                 krot[hc][pbuf][hb:hb + 64,
                                                128 * s4:W],
                                 start=True, stop=True)
                        spans = ([(wprev, W), (W, 640)]
                                 if wprev < W else [(W, 640)])
                        for (lo, hi) in spans:
                            p.op("tensor", "matmul",
                                 ps_s2[:, lo:hi], qsl,
                                 krot[hc][buf][hb:hb + 64,
                                               lo - wprev:hi - wprev],
                                 start=True, stop=True,
                                 inc="pe" if hi == 640 else None)
                        sdone = p.val("pe")
                        p.wait("vector", "pe", sdone)
                        if i == 0:
                            p.op("vector", "memset",
                                 ps_s2[:, 0:wprev], -1e30)
                        else:
                            p.op("vector", "tensor_tensor",
                                 ps_s2[:, 0:128], ps_s2[:, 0:128],
                                 mlo_s[:], ALU.add)
                        p.op("vector", "tensor_tensor",
                             ps_s2[:, 512:640], ps_s2[:, 512:640],
                             mhi_s[:], ALU.add, inc="dv")
                        rec["msk_dn"] = p.val("dv")
                        p.op("vector", "tensor_reduce", mcol[:],
                             ps_s2[:, 0:640], AX.X, ALU.max)
                        p.op("vector", "tensor_scalar", nmcol[:],
                             mcol[:], -1.0, 0.0, ALU.mult, ALU.add,
                             inc="dv")
                        p.wait("scalar", "dv")
                        p.op("scalar", "activation", ptile[:],
                             ps_s2[:, 0:640], AF.Exp, bias=nmcol[:],
                             scale=1.0, accum_out=lcol[:], inc="ac")
                        rec["exp_dn"] = p.val("ac")
                        p.wait("vector", "ac", rec["exp_dn"])
                        p.op("vector", "reciprocal", rlc[:], lcol[:])
                        p.op("vector", "tensor_scalar", diag[:],
                             idf_s[:], rlc[:, 0:1], 0.0, ALU.mult,
                             ALU.add, inc="dv")
                        diag_dn = p.val("dv")
                        p.wait("tensor", "dv", diag_dn)
                        for uc in range(5):
                            p.op("tensor", "matmul",
                                 ps_tr[:, 128 * uc:128 * uc + 128],
                                 ptile[:, 128 * uc:128 * uc + 128],
                                 diag[:], start=True, stop=True,
                                 inc="pe" if uc == 4 else None)
                        p.wait("scalar", "pe")
                        p.op("scalar", "copy", ptr[:],
                             ps_tr[:, 0:640], inc="ac")
                        rec["ptr_ev"] = p.val("ac")
                        p.wait("tensor", "ac", rec["ptr_ev"])
                        p.wait("tensor", "dv", rec.get("y_ev", 0))
                        for uc in range(5):
                            ku = 128 * s4 + 128 * uc
                            vpar = pbuf if ku < W else buf
                            vsub = (ku % W) // 128
                            p.op("tensor", "matmul", ps_d[0:64, 0:128],
                                 vtok[vsub][vpar][:, 64 * h:64 * h + 64],
                                 ptr[:, 128 * uc:128 * uc + 128],
                                 start=(uc == 0), stop=(uc == 4),
                                 inc="pe" if uc == 4 else None)
                        p.wait("vector", "pe")
                        p.op("vector", "tensor_copy",
                             o16[hc][hb:hb + 64,
                                     128 * s4:128 * s4 + 128],
                             ps_d[0:64, 0:128], inc="dv")
                        rec["y_ev"] = p.val("dv")
                attn_done = p.val("dv")
                if debug and lay == dbg_lay and i == 0:
                    p.wait("gpsimd", "dv", attn_done)
                    for c in range(4):
                        p.dma("gpsimd", dbg_o[128 * c:128 * c + 128, :],
                              o16[c][:], inc="dbg")

                p.wait("tensor", "dv", attn_done)
                if debug and i == 1 and lay == dbg_lay:
                    p.wait("vector", "dbgs", p.val("dbgs"))
                for oc in range(8):
                    bank = ps_a if oc % 2 == 0 else ps_b
                    p.wait("tensor", "dv", rec.get(f"xm_dn{oc % 2}", 0))
                    for kc in range(4):
                        p.op("tensor", "matmul", bank[:],
                             wo_s[kc][:, 128 * oc:128 * oc + 128],
                             o16[kc][:], start=(kc == 0), stop=(kc == 3),
                             inc="pe" if kc == 3 else None)
                    p.wait("vector", "pe")
                    if oc < 8:
                        pass
                    p.op("vector", "tensor_copy", xmid[oc][:, 0:1],
                         xmid[oc][:, W:W + 1]) if i > 0 else None
                    p.op("vector", "tensor_tensor",
                         xmid[oc][:, 1:W + 1], bank[:],
                         xbl[oc][:, 1:W + 1], ALU.add,
                         inc="dv")
                    rec[f"xm_dn{oc % 2}"] = p.val("dv")
                xmid_done = p.val("dv")
                if debug and lay == dbg_lay and i == 0:
                    p.wait("sync", "dv", xmid_done)
                    for c in range(8):
                        p.dma("sync", dbg_xmid[128 * c:128 * c + 128, :],
                              xmid[c][:, 1:W + 1], inc="dbgs")

                def h2sl(c, wdt=W):
                    return xmid[c][:, 1:wdt + 1] if c < 4 else \
                        xmid[c][:, 0:wdt]
                p.wait("scalar", "dv", xmid_done)
                p.wait("tensor", "dv", xmid_done)
                pe_after = []
                for c in range(8):
                    if c >= 2:
                        p.wait("scalar", "pe", pe_after[c - 2])
                    p.op("scalar", "activation", sq[c % 2][:], h2sl(c),
                         AF.Square, bias=0.0, scale=1.0, inc="ac")
                    p.wait("tensor", "ac")
                    p.op("tensor", "matmul", ps_a[0:1, :], ones_col[:],
                         h2sl(c), start=(c == 0), stop=(c == 7))
                    p.op("tensor", "matmul", ps_b[0:1, :], ones_col[:],
                         sq[c % 2][:], start=(c == 0), stop=(c == 7),
                         inc="pe")
                    pe_after.append(p.val("pe"))
                p.wait("vector", "pe")
                p.op("vector", "tensor_scalar", m_r[:], ps_a[0:1, :],
                     1.0 / D, 0.0, ALU.mult, ALU.add)
                p.op("vector", "tensor_scalar", q_r[:], ps_b[0:1, :],
                     1.0 / D, 0.0, ALU.mult, ALU.add, inc="dv")
                p.op("vector", "tensor_tensor", m2_r[:], m_r[:], m_r[:],
                     ALU.mult)
                p.op("vector", "tensor_tensor", var_r[:], q_r[:],
                     m2_r[:], ALU.subtract, inc="dv")
                p.wait("scalar", "dv")
                p.op("scalar", "activation", lnv_r[:], var_r[:], AF.Ln,
                     bias=eps_col[0:1, 0:1], scale=1.0)
                p.op("scalar", "activation", rs_r[:], lnv_r[:], AF.Exp,
                     bias=0.0, scale=-0.5, inc="ac")
                p.wait("vector", "ac")
                p.op("vector", "tensor_tensor", mr_r[:], m_r[:], rs_r[:],
                     ALU.mult, inc="dv")
                p.wait("tensor", "dv")
                p.wait("tensor", "ac", rec["copies_done"])
                p.op("tensor", "matmul", ps_c[:], ones_row[:], rs_r[:],
                     start=True, stop=True)
                p.op("tensor", "matmul", ps_d[:], ones_row[:], mr_r[:],
                     start=True, stop=True, inc="pe")
                p.wait("scalar", "pe")
                p.op("scalar", "copy", rb[:], ps_c[:], inc="ac")
                p.op("scalar", "copy", mrb[:], ps_d[:], inc="ac")
                rec["copies_done"] = p.val("ac")
                p.wait("vector", "ac")
                for c in range(8):
                    p.op("vector", "tensor_tensor", sq[c % 2][:],
                         h2sl(c), rb[:], ALU.mult)
                    p.op("vector", "tensor_tensor", sq[c % 2][:],
                         sq[c % 2][:], mrb[:], ALU.subtract)
                    p.op("vector", "tensor_scalar", hn16[c],
                         sq[c % 2][:], lg_s[:, c:c + 1],
                         lb_s[:, c:c + 1], ALU.mult, ALU.add,
                         inc="dv" if c == 7 else None)
                hn_done = p.val("dv")

                p.wait("tensor", "dv", hn_done)
                units = [(pc, ph) for pc in range(21) for ph in range(2)]
                units.append((21, 0))
                for ui, (pc, ph) in enumerate(units):
                    wb = ui % 4
                    lsem = lsems[wb]
                    p.wait("gpsimd", "pe", rec.get(f"w1_used_{wb}", 0))
                    p.wait("gpsimd", lsem, p.val(lsem))
                    if pc < 21:
                        p.dma("gpsimd", w1sl[wb][:],
                              w1r[lay, pc][:, 1024 * ph:1024 * ph + 1024],
                              inc=lsem)
                    else:
                        p.dma("gpsimd", w1sl[wb][:, 0:672],
                              w1t[lay], inc=lsem)
                    ldwf_now = p.val(lsem)
                    par = pc % 2
                    bank_a, bank_g = (ps_a, ps_b) if par == 0 else \
                        (ps_c, ps_d)
                    if ph == 0:
                        p.wait("tensor", "dv", rec.get(f"gate_dn{par}", 0))
                        p.wait("tensor", "ac", rec.get(f"glu_dn{par}", 0))
                    p.wait("tensor", lsem, ldwf_now)
                    if pc < 21:
                        for kk2 in range(4):
                            kc = 4 * ph + kk2
                            p.op("tensor", "matmul", bank_a[:],
                                 w1sl[wb][:, 256 * kk2:256 * kk2 + 128],
                                 hn16[kc], start=(kc == 0),
                                 stop=(kc == 7))
                            p.op("tensor", "matmul", bank_g[:],
                                 w1sl[wb][:, 256 * kk2 + 128:256 * kk2 + 256],
                                 hn16[kc], start=(kc == 0),
                                 stop=(kc == 7),
                                 inc="pe" if kk2 == 3 else None)
                        rec[f"w1_used_{wb}"] = p.val("pe")
                        if ph == 0:
                            continue
                        asl, gsl = bank_a[:], bank_g[:]
                        gdst = gat[pc][:]
                        bia = b1_s[:, 2 * pc:2 * pc + 1]
                        big = b1_s[:, 2 * pc + 1:2 * pc + 2]
                    else:
                        for kc in range(8):
                            p.op("tensor", "matmul", bank_a[0:42, :],
                                 w1sl[wb][:, 84 * kc:84 * kc + 42],
                                 hn16[kc], start=(kc == 0),
                                 stop=(kc == 7))
                            p.op("tensor", "matmul", bank_g[0:42, :],
                                 w1sl[wb][:, 84 * kc + 42:84 * kc + 84],
                                 hn16[kc], start=(kc == 0),
                                 stop=(kc == 7),
                                 inc="pe" if kc == 7 else None)
                        rec[f"w1_used_{wb}"] = p.val("pe")
                        asl, gsl = bank_a[0:42, :], bank_g[0:42, :]
                        gdst = gat_t[:]
                        bia = b1_s[0:42, 42:43]
                        big = b1_s[0:42, 43:44]
                    p.wait("scalar", "pe")
                    p.op("scalar", "activation", gdst, gsl, AF.Gelu,
                         bias=big, scale=1.0, inc="ac")
                    rec[f"glu_dn{par}"] = p.val("ac")
                    p.wait("vector", "ac", rec[f"glu_dn{par}"])
                    p.op("vector", "scalar_tensor_tensor", gdst, asl,
                         bia, gdst, ALU.add, ALU.mult, inc="dv")
                    rec[f"gate_dn{par}"] = p.val("dv")
                gate_all = p.val("dv")

                p.wait("tensor", "dv", gate_all)
                for half in range(2):
                    for pc in range(22):
                        wb = pc % 4
                        lsem = lsems[wb]
                        p.wait("gpsimd", "pe", rec.get(
                            f"w2_used_{wb}", 0))
                        p.wait("gpsimd", lsem, p.val(lsem))
                        if pc < 21:
                            p.dma("gpsimd", w2sl[wb][:],
                                  w2r[lay, pc][:, 512 * half:
                                               512 * half + 512],
                                  inc=lsem)
                            wsl = w2sl[wb]
                            gsl2 = gat[pc][:]
                        else:
                            p.dma("gpsimd", w2sl[wb][0:42, :],
                                  w2t[lay][:, 512 * half:512 * half + 512],
                                  inc=lsem)
                            wsl = w2sl[wb]
                            gsl2 = gat_t[:]
                        ldwf_now = p.val(lsem)
                        p.wait("tensor", lsem, ldwf_now)
                        for q4 in range(4):
                            bankq = [ps_a, ps_b, ps_c, ps_d][q4]
                            if pc == 0:
                                p.wait("tensor", "dv",
                                       rec.get(f"ffn_out{q4}", 0))
                            if pc < 21:
                                p.op("tensor", "matmul", bankq[:],
                                     wsl[:, 128 * q4:128 * q4 + 128],
                                     gsl2, start=(pc == 0), stop=False,
                                     inc="pe" if q4 == 3 else None)
                            else:
                                p.op("tensor", "matmul", bankq[:],
                                     wsl[0:42, 128 * q4:128 * q4 + 128],
                                     gsl2, start=False, stop=True,
                                     inc="pe" if q4 == 3 else None)
                        rec[f"w2_used_{wb}"] = p.val("pe")
                    w2_pe = p.val("pe")
                    p.wait("vector", "pe", w2_pe)
                    if lay == 1 and i == 0:
                        pass
                    for q4 in range(4):
                        oc = 4 * half + q4
                        bankq = [ps_a, ps_b, ps_c, ps_d][q4]
                        sb = oc % 2
                        stsem = sem0 if sb == 0 else sem1
                        if f"store_{stsem}" in rec:
                            p.wait("vector", stsem,
                                   rec[f"store_{stsem}"])
                        p.op("vector", "scalar_tensor_tensor",
                             vout[sb][:], bankq[:],
                             b2_s[:, oc:oc + 1], xmid[oc][:, 1:W + 1],
                             ALU.add, ALU.add, inc="dv")
                        rec[f"ffn_out{q4}"] = p.val("dv")
                        if lay == 0:
                            p.wait("sync", "dv")
                            p.dma("sync",
                                  xb_d[128 * oc:128 * oc + 128,
                                       t0:t0 + W],
                                  vout[sb][:], inc=stsem)
                        else:
                            p.wait("gpsimd", "dv")
                            p.wait("gpsimd", "xcp", 128)
                            p.dma("gpsimd",
                                  xo[128 * oc:128 * oc + 128, t0:t0 + W],
                                  vout[sb][:], accum_op=ALU.add,
                                  inc=stsem)
                        rec[f"store_{stsem}"] = p.val(stsem)
                rec[f"{sem0}_{i}"] = p.val(sem0)
                rec[f"{sem1}_{i}"] = p.val(sem1)
                rec["qkv_pe"] = p.val("pe")
            rec["lay_pe"] = p.val("pe")
            rec["lay0_st0"] = p.val("st0")
            rec["lay0_st1"] = p.val("st1")
            for k in ("st0_0", "st1_0"):
                pass
        p.wait("gpsimd", "st0", p.val("st0"))
        p.wait("gpsimd", "st1", p.val("st1"))
        p.wait("gpsimd", "sta0", p.val("sta0"))
        p.wait("gpsimd", "sta1", p.val("sta1"))
        if debug:
            p.wait("gpsimd", "dbg", p.val("dbg"))
            p.wait("gpsimd", "dbgs", p.val("dbgs"))

    if fused:
        return body
    emit_all(p, body)
    return nc


def l0_ffn_consts():
    half = np.arange(0, 64, 2, dtype=np.float64)
    inv_freq = 1.0 / 10000.0 ** (half / 64.0)
    tpos = np.arange(T, dtype=np.float64)
    freqs = tpos[:, None] * inv_freq[None, :]
    freqs = np.concatenate([freqs, freqs], axis=-1)
    base = (half + 0.4 * 64) / (1.4 * 64)
    power = (tpos - T // 2) / 256.0
    scale = base[None, :] ** power[:, None]
    scale = np.concatenate([scale, scale], axis=-1)
    cos, sin = np.cos(freqs), np.sin(freqs)

    def dup(a):
        a = np.ascontiguousarray(a.T.astype(np.float32))
        return np.concatenate([a, a], axis=0)
    c1q = dup(cos * scale * 8.0)
    c2q = dup(sin * scale * 8.0)
    c1k = dup(cos / scale)
    c2k = dup(sin / scale)
    pr = np.zeros((128, 128), dtype=np.float32)
    for blk in range(2):
        b0 = 64 * blk
        for c in range(32):
            pr[b0 + c + 32, b0 + c] = -1.0
            pr[b0 + c, b0 + c + 32] = 1.0
    r = np.arange(128)[:, None]
    u = np.arange(128)[None, :]
    masklo = np.where(u < r, -1e38, 0.0).astype(np.float32)
    maskhi = np.where(u > r, -1e38, 0.0).astype(np.float32)
    idenf = np.eye(128, dtype=np.float32)
    nsel2 = np.zeros((128, 2), dtype=np.float32)
    nsel2[0:64, 0] = 1.0
    nsel2[64:128, 1] = 1.0
    bsel2 = np.zeros((2, 128), dtype=np.float32)
    bsel2[0, 0:64] = 1.0
    bsel2[1, 64:128] = 1.0
    return dict(c1q=c1q, c2q=c2q, c1k=c1k, c2k=c2k, prot=pr,
                masklo=masklo, maskhi=maskhi, idenf=idenf,
                nsel2=nsel2, bsel2=bsel2)


def l0_ffn_inmap(x_fm_b, inp):
    HID = 2730
    wqkv = np.asarray(inp["L0_ffn_wqkv"], np.float32)
    wo_ = np.asarray(inp["L0_ffn_wo"], np.float32)
    w1 = np.asarray(inp["L0_ffn_w1"], np.float32)
    b1 = np.asarray(inp["L0_ffn_b1"], np.float32)
    w2 = np.asarray(inp["L0_ffn_w2"], np.float32)
    b2 = np.asarray(inp["L0_ffn_b2"], np.float32)
    qs = np.asarray(inp["L0_ffn_q_scale"], np.float32)
    ks = np.asarray(inp["L0_ffn_k_scale"], np.float32)
    lg = np.asarray(inp["L0_ffn_ln_g"], np.float32)
    lb = np.asarray(inp["L0_ffn_ln_b"], np.float32)
    w1r = np.zeros((2, 21, 128, 2048), np.float32)
    w1t = np.zeros((2, 128, 672), np.float32)
    b1r = np.zeros((2, 128, 44), np.float32)
    for lay in range(2):
        for pc in range(21):
            a = w1[lay][:, 128 * pc:128 * pc + 128]
            g = w1[lay][:, HID + 128 * pc:HID + 128 * pc + 128]
            sl = np.concatenate([a, g], axis=1)
            w1r[lay, pc] = sl.reshape(8, 128, 256).transpose(
                1, 0, 2).reshape(128, 2048)
            b1r[lay, 0:128, 2 * pc] = b1[lay][128 * pc:128 * pc + 128]
            b1r[lay, 0:128, 2 * pc + 1] = b1[lay][HID + 128 * pc:
                                                  HID + 128 * pc + 128]
        at = w1[lay][:, 2688:2730]
        gt = w1[lay][:, HID + 2688:HID + 2730]
        sl = np.concatenate([at, gt], axis=1)
        w1t[lay] = sl.reshape(8, 128, 84).transpose(1, 0, 2).reshape(128, 672)
        b1r[lay, 0:42, 42] = b1[lay][2688:2730]
        b1r[lay, 0:42, 43] = b1[lay][HID + 2688:HID + 2730]
    w2rr = np.zeros((2, 21, 128, 1024), np.float32)
    w2tt = np.zeros((2, 42, 1024), np.float32)
    for lay in range(2):
        for pc in range(21):
            w2rr[lay, pc] = w2[lay][128 * pc:128 * pc + 128, :]
        w2tt[lay] = w2[lay][2688:2730, :]
    m = dict(
        x=np.ascontiguousarray(x_fm_b),
        wqkv=np.ascontiguousarray(wqkv.reshape(2, 8, 128, 1536)),
        wo=np.ascontiguousarray(wo_.reshape(2, 4, 128, 1024)),
        w1r=w1r, w1t=w1t, w2r=w2rr, w2t=w2tt, b1r=b1r,
        b2c=np.ascontiguousarray(
            b2.reshape(2, 8, 128).transpose(0, 2, 1)),
        qsc=np.ascontiguousarray(
            np.tile(qs, (1, 2)).reshape(2, 128, 1)),
        ksc=np.ascontiguousarray(
            np.tile(ks, (1, 2)).reshape(2, 128, 1)),
        lgc=np.ascontiguousarray(lg.reshape(2, 8, 128).transpose(0, 2, 1)),
        lbc=np.ascontiguousarray(lb.reshape(2, 8, 128).transpose(0, 2, 1)),
    )
    m.update(l0_ffn_consts())
    return m


def run_l0_ffn(x_fm, inp):
    nc = build_l0_ffn()
    in_maps = [l0_ffn_inmap(x_fm[b], inp) for b in range(NCORES)]
    res = run_bass_kernel_spmd(nc, in_maps, core_ids=list(range(NCORES)))
    return np.stack([r["xo"] for r in res.results])



def build_l0_attn(debug=False, fused=None):
    nc = fused["nc"] if fused else bass.Bass("TRN2", target_bir_lowering=False)
    x = nc.dram_tensor("x", [D, T], F32, kind="ExternalInput").ap()
    wq = nc.dram_tensor("wq", [8, 128, D], F32, kind="ExternalInput").ap()
    wk = nc.dram_tensor("wk", [8, 128, D], F32, kind="ExternalInput").ap()
    wv = nc.dram_tensor("wv", [8, 128, D], F32, kind="ExternalInput").ap()
    wo = nc.dram_tensor("wo", [8, 128, D], F32, kind="ExternalInput").ap()
    wf1 = nc.dram_tensor("wf1", [8, 128, 4096], F32,
                         kind="ExternalInput").ap()
    wf2 = nc.dram_tensor("wf2", [8, 128, 4096], F32,
                         kind="ExternalInput").ap()
    thw1 = nc.dram_tensor("thw1", [8, 128, 48], F32,
                          kind="ExternalInput").ap()
    thw2 = nc.dram_tensor("thw2", [48, 512], F32, kind="ExternalInput").ap()
    bqr = nc.dram_tensor("bqr", [1, D], F32, kind="ExternalInput").ap()
    bkr = nc.dram_tensor("bkr", [1, D], F32, kind="ExternalInput").ap()
    bvr = nc.dram_tensor("bvr", [1, D], F32, kind="ExternalInput").ap()
    bor = nc.dram_tensor("bor", [128, 8], F32, kind="ExternalInput").ap()
    bf1r = nc.dram_tensor("bf1r", [128, 32], F32, kind="ExternalInput").ap()
    bf2r = nc.dram_tensor("bf2r", [128, 8], F32, kind="ExternalInput").ap()
    n1gr = nc.dram_tensor("n1gr", [128, 8], F32, kind="ExternalInput").ap()
    n1br = nc.dram_tensor("n1br", [128, 8], F32, kind="ExternalInput").ap()
    n2gr = nc.dram_tensor("n2gr", [128, 8], F32, kind="ExternalInput").ap()
    n2br = nc.dram_tensor("n2br", [128, 8], F32, kind="ExternalInput").ap()
    thbr = nc.dram_tensor("thbr", [1, 512], F32, kind="ExternalInput").ap()
    lbr = nc.dram_tensor("lbr", [1, 512], F32, kind="ExternalInput").ap()
    etar = nc.dram_tensor("etar", [1, 512], F32, kind="ExternalInput").ap()
    ltp = nc.dram_tensor("ltp", [128, 128], F32, kind="ExternalInput").ap()
    ltn = nc.dram_tensor("ltn", [128, 128], F32, kind="ExternalInput").ap()
    idenf = nc.dram_tensor("idenf", [128, 128], F32,
                           kind="ExternalInput").ap()
    gmask = nc.dram_tensor("gmask", [128, 128], F32,
                           kind="ExternalInput").ap()
    xo = nc.dram_tensor("xo", [D, T], F32, kind="ExternalOutput").ap()
    kk = "ExternalOutput" if debug else "Internal"
    if debug:
        dbg_th = nc.dram_tensor("dbg_th", [128, 512], F32,
                                kind="ExternalOutput").ap()
        dbg_lm = nc.dram_tensor("dbg_lm", [128, 512], F32,
                                kind="ExternalOutput").ap()
        dbg_phi = nc.dram_tensor("dbg_phi", [128, 512], F32,
                                 kind="ExternalOutput").ap()
        dbg_rt = nc.dram_tensor("dbg_rt", [128, 512], F32,
                                kind="ExternalOutput").ap()
        dbg_qre = nc.dram_tensor("dbg_qre", [128, 512], F32,
                                 kind="ExternalOutput").ap()
        dbg_kre = nc.dram_tensor("dbg_kre", [128, 512], F32,
                                 kind="ExternalOutput").ap()
        dbg_ure = nc.dram_tensor("dbg_ure", [128, 512], F32,
                                 kind="ExternalOutput").ap()
        dbg_uim = nc.dram_tensor("dbg_uim", [128, 512], F32,
                                 kind="ExternalOutput").ap()
        dbg_qtk = nc.dram_tensor("dbg_qtk", [128, 1024], BF16,
                                 kind="ExternalOutput").ap()
    xln_d = nc.dram_tensor("xln_scr", [D, T], F32, kind=kk).ap()
    q_d = nc.dram_tensor("q_scr", [T, D], BF16, kind=kk).ap()
    k_d = nc.dram_tensor("k_scr", [T, D], BF16, kind=kk).ap()
    v_d = nc.dram_tensor("v_scr", [T, D], BF16, kind=kk).ap()
    at_d = nc.dram_tensor("at_scr", [D, T], BF16, kind=kk).ap()

    NT = 8
    W = 512
    NCH = 32
    PI = float(np.pi)
    MAGIC = 12582912.0

    p = fused["p"] if fused else P(nc, ExitStack())
    wpool = p.sbuf("wpool", [128, 3 * 8 * 1024], BF16)
    wq_s = [wpool[:, 1024 * kc:1024 * (kc + 1)] for kc in range(8)]
    wk_s = [wpool[:, 8192 + 1024 * kc:8192 + 1024 * (kc + 1)]
            for kc in range(8)]
    wv_s = [wpool[:, 16384 + 1024 * kc:16384 + 1024 * (kc + 1)]
            for kc in range(8)]
    wf1sl = [wpool[:, 2048 * b:2048 * (b + 1)] for b in range(4)]
    wf2sl = [wpool[:, 8192 + 2048 * b:8192 + 2048 * (b + 1)]
             for b in range(4)]
    lsems4 = ("ldwfa", "ldwfb", "ldwfc", "ldwfd")
    h1pool = wpool[:, 16384:24576]
    bpool = p.sbuf("bpool", [128, 8192], BF16)
    bqb = bpool[:, 5120:6144]
    bkb = bpool[:, 6144:7168]
    bvb = bpool[:, 7168:8192]
    qkvt = bpool[:, 0:1024]
    h1t = [h1pool[:, 512 * i:512 * (i + 1)] for i in range(16)] + \
          [bpool[:, 512 * i:512 * (i + 1)] for i in range(16)]
    thw1_s = p.sbuf("thw1_s", [128, 8 * 48], F32)
    thw2_s = p.sbuf("thw2_s", [48, 512], F32)
    bo_s = p.sbuf("bo_s", [128, 8], F32)
    bf1_s = p.sbuf("bf1_s", [128, 32], F32)
    bf2_s = p.sbuf("bf2_s", [128, 8], F32)
    n1g_s = p.sbuf("n1g_s", [128, 8], F32)
    n1b_s = p.sbuf("n1b_s", [128, 8], F32)
    n2g_s = p.sbuf("n2g_s", [128, 8], F32)
    n2b_s = p.sbuf("n2b_s", [128, 8], F32)
    ltp_s = p.sbuf("ltp_s", [128, 128], F32)
    ltn_s = p.sbuf("ltn_s", [128, 128], F32)
    idf_s = p.sbuf("idf_s", [128, 128], F32)
    idb_s = p.sbuf("idb_s", [128, 128], BF16)
    gmask_s = p.sbuf("gmask_s", [128, 128], BF16)
    _bqkv_note = "bqb/bkb/bvb alias bpool tail; thbb etc alias xlnf"

    brow = p.sbuf("brow", [1, 1024], F32)
    ones_col = p.sbuf("ones_col", [128, 1], F32)
    ones_row = p.sbuf("ones_row", [1, 128], F32)
    eps5_col = p.sbuf("eps5_col", [128, 1], F32)
    eps6_col = p.sbuf("eps6_col", [128, 1], F32)
    pih_col = p.sbuf("pih_col", [128, 1], F32)

    xch = [[p.sbuf(f"x_{c}", [128, W], F32)] * 2 for c in range(8)]
    xlnf = [p.sbuf(f"xlnf_{c}", [128, W], F32) for c in range(8)]
    thbb = xlnf[0][:]
    lbb = xlnf[1][:]
    etab = xlnf[2][:]
    xnf = xlnf
    xn16 = [p.sbuf(f"xn16_{c}", [128, W], BF16) for c in range(8)]
    sq = [p.sbuf(f"sq_{b}", [128, W], F32) for b in range(2)]
    rb = p.sbuf("rb", [128, W], F32)
    mrb = p.sbuf("mrb", [128, W], F32)
    m_r = p.sbuf("m_r", [1, W], F32)
    q_r = p.sbuf("q_r", [1, W], F32)
    var_r = p.sbuf("var_r", [1, W], F32)
    rs_r = p.sbuf("rs_r", [1, W], F32)
    lnv_r = q_r
    mr_r = m_r
    midf = p.sbuf("midf", [48, T], F32)

    qtk = [p.sbuf(f"qtk_{b}", [128, 1024], BF16) for b in range(2)]
    ktk = [p.sbuf(f"ktk_{b}", [128, 1024], BF16) for b in range(2)]
    vtk = [p.sbuf(f"vtk_{b}", [128, 1024], BF16) for b in range(2)]
    _wo_list = None
    kphi = p.sbuf("kphi", [128, 1024], F32)
    wo_s = [qtk[0][:], qtk[1][:], ktk[0][:], ktk[1][:], vtk[0][:],
            vtk[1][:], kphi[:].bitcast(BF16)[:, 0:1024],
            kphi[:].bitcast(BF16)[:, 1024:2048]]
    tht = p.sbuf("tht", [128, 512], F32)
    lmt = p.sbuf("lmt", [128, 512], F32)
    phit = p.sbuf("phit", [128, 512], F32)
    rtt = p.sbuf("rtt", [128, 512], F32)
    uret = p.sbuf("uret", [128, 512], F32)
    uimt = p.sbuf("uimt", [128, 512], F32)
    scr1 = p.sbuf("scr1", [128, 512], F32)
    scr2 = p.sbuf("scr2", [128, 512], F32)
    qrot_re = bpool[:, 0:512]
    qrot_im = bpool[:, 512:1024]
    krot_re = bpool[:, 1024:1536]
    krot_imn = bpool[:, 1536:2048]
    efat = p.sbuf("efat", [128, 512], F32)
    rendb = p.sbuf("rendb", [128, 512], F32)
    rendr = p.sbuf("rendr", [1, 512], F32)
    eendt = p.sbuf("eendt", [128, 512], F32)
    kre_t = bpool[:, 2048:3072]
    rf = [p.sbuf(f"rf_{g}", [128, 128], F32) for g in range(4)]
    phif = [p.sbuf(f"phif_{g}", [128, 128], F32) for g in range(4)]
    efaf = [p.sbuf(f"efaf_{g}", [128, 128], F32) for g in range(4)]
    qr_f = [p.sbuf(f"qrf_{g}", [128, 128], BF16) for g in range(4)]
    qi_f = [p.sbuf(f"qif_{g}", [128, 128], BF16) for g in range(4)]
    kr_f = [p.sbuf(f"krf_{g}", [128, 128], BF16) for g in range(4)]
    ki_f = [p.sbuf(f"kif_{g}", [128, 128], BF16) for g in range(4)]
    pq_re = [p.sbuf(f"pqre_{g}", [128, 128], BF16) for g in range(4)]
    pq_im = [p.sbuf(f"pqim_{g}", [128, 128], BF16) for g in range(4)]
    qrall_re = bpool[:, 3072:4096]
    qrall_im = bpool[:, 4096:5120]
    krall_re = [p.sbuf(f"kar_{g}", [128, 128], BF16) for g in range(4)]
    krall_imn = [p.sbuf(f"kai_{g}", [128, 128], BF16) for g in range(4)]
    earg = kphi
    eargk = p.sbuf("eargk", [128, 128], F32)
    dend = [p.sbuf(f"dend_{g}", [128, 1], F32) for g in range(4)]
    uer_c = [p.sbuf(f"uerc_{g}", [128, 1], F32) for g in range(4)]
    uei_c = [p.sbuf(f"ueic_{g}", [128, 1], F32) for g in range(4)]
    col_s1 = p.sbuf("col_s1", [128, 1], F32)
    col_s2 = p.sbuf("col_s2", [128, 1], F32)
    cre = [p.sbuf(f"cre_{g}", [128, 64], BF16) for g in range(4)]
    cimn = [p.sbuf(f"cimn_{g}", [128, 64], BF16) for g in range(4)]
    tr1 = p.sbuf("tr1", [128, 64], F32)
    tr2 = p.sbuf("tr2", [128, 64], F32)
    vp = [bpool[:, 5120 + 256 * i:5120 + 256 * (i + 1)]
          for i in range(8)]
    gts = [bpool[:, 7168 + 128 * i:7168 + 128 * (i + 1)]
           for i in range(8)]
    of16 = [p.sbuf(f"of16_{c}", [128, 128], BF16) for c in range(8)]

    at16 = xn16
    hn16 = xn16
    vout = [tht, lmt]

    ps1 = p.psum("ps1", [128, W])
    ps2 = p.psum("ps2", [128, W])
    ps3 = p.psum("ps3", [128, W])
    ps4 = p.psum("ps4", [128, W])
    ps5 = p.psum("ps5", [128, W])
    ps6 = p.psum("ps6", [128, W])
    ps7 = p.psum("ps7", [128, W])
    pstr = p.psum("pstr", [128, 1024], BF16)

    for s in ("ldc", "ldw", "ldwv", "lda", "ldb", "lqa", "lqb", "ldwfa",
              "ldwfb", "ldwfc", "ldwfd", "pe", "dv", "ac", "stq", "stx",
              "stb", "sta0", "sta1", "xcp", "vpd", "dbgs"):
        p.sem(s)

    def body():
        for kc in range(8):
            p.dma("gpsimd", wq_s[kc], wq[kc], inc="ldw")
            p.dma("gpsimd", wk_s[kc], wk[kc], inc="ldw")
            p.dma("gpsimd", wv_s[kc], wv[kc], inc="ldw")
            p.dma("gpsimd", thw1_s[:, 48 * kc:48 * kc + 48], thw1[kc],
                  inc="ldw")
        p.dma("gpsimd", idb_s[:], idenf, inc="ldw")
        p.dma("gpsimd", gmask_s[:], gmask, inc="ldw")
        ldw_all = p.val("ldw")
        for dst, srcd in ((thw2_s, thw2), (bo_s, bor), (bf1_s, bf1r),
                          (bf2_s, bf2r), (n1g_s, n1gr), (n1b_s, n1br),
                          (n2g_s, n2gr), (n2b_s, n2br), (ltp_s, ltp),
                          (ltn_s, ltn), (idf_s, idenf)):
            p.dma("sync", dst[:], srcd, inc="ldwv")
        ldwv_all = p.val("ldwv")
        p.op("vector", "memset", ones_col[:], 1.0)
        p.op("vector", "memset", ones_row[:], 1.0)
        p.op("vector", "memset", eps5_col[:], 1e-5)
        p.op("vector", "memset", eps6_col[:], 1e-6)
        p.op("vector", "memset", pih_col[:], PI / 2.0)
        for g in range(4):
            p.op("vector", "memset", cre[g][:], 0.0)
            p.op("vector", "memset", cimn[g][:], 0.0)
        p.op("vector", "memset", ps1[:], 0.0)
        p.op("vector", "memset", ps2[:], 0.0, inc="dv")
        memset_done = p.val("dv")

        def bias_bcast(items):
            for name, srcd, dstt, width in items:
                p.wait("sync", "pe", p.val("pe"))
                p.dma("sync", brow[0:1, 0:width], srcd, inc="ldc")
                ldc_now = p.val("ldc")
                p.wait("tensor", "ldc", ldc_now)
                p.wait("tensor", "dv", memset_done)
                for seg in range(width // 512):
                    p.wait("tensor", "ac", p.val("ac"))
                    p.op("tensor", "matmul", ps1[:], ones_row[:],
                         brow[0:1, 512 * seg:512 * seg + 512],
                         start=True, stop=True, inc="pe")
                    p.wait("scalar", "pe")
                    p.op("scalar", "copy",
                         dstt[:, 512 * seg:512 * seg + 512], ps1[:],
                         inc="ac")
        bias_bcast((("bq", bqr, bqb, 1024), ("bk", bkr, bkb, 1024),
                    ("bv", bvr, bvb, 1024)))
        bias_done = p.val("ac")

        rec = {"rows_read_psum": 0, "copies_done": 0}

        def ln_rows(eps_ap):
            p.wait("vector", "pe")
            p.op("vector", "tensor_scalar", m_r[:], ps1[0:1, :],
                 1.0 / D, 0.0, ALU.mult, ALU.add)
            p.op("vector", "tensor_scalar", q_r[:], ps2[0:1, :],
                 1.0 / D, 0.0, ALU.mult, ALU.add, inc="dv")
            rec["rows_read_psum"] = p.val("dv")
            p.op("vector", "tensor_tensor", var_r[:], m_r[:], m_r[:],
                 ALU.mult)
            p.op("vector", "tensor_tensor", var_r[:], q_r[:], var_r[:],
                 ALU.subtract, inc="dv")
            p.wait("scalar", "dv")
            p.op("scalar", "activation", lnv_r[:], var_r[:], AF.Ln,
                 bias=eps_ap, scale=1.0)
            p.op("scalar", "activation", rs_r[:], lnv_r[:], AF.Exp,
                 bias=0.0, scale=-0.5, inc="ac")
            p.wait("vector", "ac")
            p.op("vector", "tensor_tensor", mr_r[:], m_r[:], rs_r[:],
                 ALU.mult, inc="dv")
            p.wait("tensor", "dv")
            p.wait("tensor", "ac", rec["copies_done"])
            p.op("tensor", "matmul", ps3[:], ones_row[:], rs_r[:],
                 start=True, stop=True)
            p.op("tensor", "matmul", ps4[:], ones_row[:], mr_r[:],
                 start=True, stop=True, inc="pe")
            p.wait("scalar", "pe")
            p.op("scalar", "copy", rb[:], ps3[:], inc="ac")
            p.op("scalar", "copy", mrb[:], ps4[:], inc="ac")
            rec["copies_done"] = p.val("ac")
            p.wait("vector", "ac")

        def ln_stats(srcs):
            pe_after = []
            for c in range(8):
                if c >= 2:
                    p.wait("scalar", "pe", pe_after[c - 2])
                p.op("scalar", "activation", sq[c % 2][:], srcs(c),
                     AF.Square, bias=0.0, scale=1.0, inc="ac")
                p.wait("tensor", "ac")
                p.op("tensor", "matmul", ps1[0:1, :], ones_col[:],
                     srcs(c), start=(c == 0), stop=(c == 7))
                p.op("tensor", "matmul", ps2[0:1, :], ones_col[:],
                     sq[c % 2][:], start=(c == 0), stop=(c == 7),
                     inc="pe")
                pe_after.append(p.val("pe"))

        for i in range(NT):
            buf = i % 2
            t0 = i * W
            ldsem = "lda" if buf == 0 else "ldb"
            if i >= 1:
                p.wait("sync", "dv", rec[f"xln_dn_{i-1}"])
            for c in range(8):
                p.dma("sync", xch[c][buf][:],
                      x[128 * c:128 * c + 128, t0:t0 + W], inc=ldsem)
            ld_tile = p.val(ldsem)
            p.wait("scalar", ldsem, ld_tile)
            p.wait("scalar", "dv", rec.get("apply_dn", 0))
            p.wait("tensor", "dv", max(rec["rows_read_psum"],
                                       memset_done))
            p.wait("tensor", ldsem, ld_tile)
            p.wait("tensor", "ac", bias_done)
            if i == 0:
                p.wait("vector", "ldwv", ldwv_all)
            ln_stats(lambda c: xch[c][buf][:])
            ln_rows(eps5_col[0:1, 0:1])
            for c in range(8):
                p.op("vector", "tensor_tensor", sq[c % 2][:],
                     xch[c][buf][:], rb[:], ALU.mult)
                p.op("vector", "tensor_tensor", xlnf[c][:], sq[c % 2][:],
                     mrb[:], ALU.subtract, inc="dv" if c == 7 else None)
            xln_dn = p.val("dv")
            rec[f"xln_dn_{i}"] = xln_dn
            p.wait("sync", "dv", xln_dn)
            for c in range(8):
                p.dma("sync", xln_d[128 * c:128 * c + 128, t0:t0 + W],
                      xlnf[c][:], inc="stx")
            p.wait("scalar", "dv", xln_dn)
            p.wait("tensor", "dv", xln_dn)
            ln_stats(lambda c: xlnf[c][:])
            ln_rows(eps6_col[0:1, 0:1])
            p.wait("vector", "stx", p.val("stx"))
            for c in range(8):
                p.op("vector", "tensor_tensor", sq[c % 2][:], xlnf[c][:],
                     rb[:], ALU.mult)
                p.op("vector", "tensor_tensor", xnf[c][:], sq[c % 2][:],
                     mrb[:], ALU.subtract)
                p.op("vector", "tensor_scalar", xnf[c][:], xnf[c][:],
                     n1g_s[:, c:c + 1], n1b_s[:, c:c + 1], ALU.mult,
                     ALU.add)
                p.op("vector", "tensor_copy", xn16[c][:], xnf[c][:],
                     inc="dv" if c == 7 else None)
            rec["apply_dn"] = p.val("dv")
            xn_dn = p.val("dv")

            p.wait("tensor", "dv", xn_dn)
            if i == 0:
                p.wait("tensor", "ldw", ldw_all)
            for (wmat, bb, dst_d) in ((wq_s, bqb, q_d), (wk_s, bkb, k_d),
                                      (wv_s, bvb, v_d)):
                for ts4 in range(4):
                    tb = t0 + 128 * ts4
                    for half in range(2):
                        bank = ps5 if half == 0 else ps6
                        p.wait("tensor", "dv",
                               rec.get(f"qkv_ev{half}", 0))
                        for kc in range(8):
                            p.op("tensor", "matmul", bank[:],
                                 xn16[kc][:, 128 * ts4:128 * ts4 + 128],
                                 wmat[kc][:, 512 * half:512 * half + 512],
                                 start=(kc == 0), stop=(kc == 7),
                                 inc="pe" if kc == 7 else None)
                        p.wait("vector", "pe")
                        if half == 0:
                            p.wait("vector", "stq",
                                   rec.get("qkvt_st", 0))
                        p.op("vector", "tensor_tensor",
                             qkvt[:, 512 * half:512 * half + 512],
                             bank[:], bb[:, 512 * half:512 * half + 512],
                             ALU.add, inc="dv")
                        rec[f"qkv_ev{half}"] = p.val("dv")
                    p.wait("sync", "dv")
                    p.dma("sync", dst_d[tb:tb + 128, :], qkvt[:],
                          inc="stq")
                    rec["qkvt_st"] = p.val("stq")
            p.wait("tensor", "dv", p.val("dv"))
            p.wait("tensor", "ac", rec.get("mid_ev", 0))
            for kc in range(8):
                p.op("tensor", "matmul", ps7[0:48, :],
                     thw1_s[:, 48 * kc:48 * kc + 48], xnf[kc][:],
                     start=(kc == 0), stop=(kc == 7),
                     inc="pe" if kc == 7 else None)
            p.wait("scalar", "pe")
            p.op("scalar", "activation", midf[:, t0:t0 + W],
                 ps7[0:48, :], AF.Tanh, bias=0.0, scale=1.0, inc="ac")
            rec["mid_ev"] = p.val("ac")
        phaseA_stq = p.val("stq")
        phaseA_stx = p.val("stx")

        p.wait("sync", "stq", phaseA_stq)
        p.wait("sync", "stx", phaseA_stx)
        p.wait("gpsimd", "stq", phaseA_stq)
        p.wait("sync", "dv", p.val("dv"))
        p.wait("sync", "ac", p.val("ac"))
        bias_bcast((("thb", thbr, thbb, 512), ("lb", lbr, lbb, 512),
                    ("eta", etar, etab, 512)))
        thb_done = p.val("ac")
        for n in range(NCH):
            buf = n % 2
            t0 = n * 128
            lq = "lqa" if buf == 0 else "lqb"
            if n >= 2:
                p.wait("sync", "pe", rec[f"chunk_pe_{n-2}"])
            p.dma("sync", qtk[buf][:], q_d[t0:t0 + 128, :], inc=lq)
            p.dma("sync", ktk[buf][:], k_d[t0:t0 + 128, :], inc=lq)
            p.dma("sync", vtk[buf][:], v_d[t0:t0 + 128, :], inc=lq)
            lq_now = p.val(lq)

            p.wait("tensor", "dv", rec.get("lam_read", 0))
            p.wait("tensor", "ac", max(thb_done, rec.get("mid_ev", 0)))
            p.op("tensor", "matmul", ps7[:], midf[:, t0:t0 + 128],
                 thw2_s[:], start=True, stop=True, inc="pe")
            p.wait("vector", "pe")
            p.op("vector", "tensor_tensor", tht[:], ps7[:], thbb[:],
                 ALU.add)
            p.op("vector", "tensor_scalar", tht[:], tht[:], PI / 2.0,
                 -PI / 2.0, ALU.min, ALU.max, inc="dv")
            th_dn = p.val("dv")
            p.wait("scalar", "dv", th_dn)
            p.op("scalar", "activation", scr1[:], tht[:], AF.Square,
                 bias=0.0, scale=1.0, inc="ac")
            p.wait("vector", "ac")
            p.op("vector", "tensor_tensor", lmt[:], etab[:], scr1[:],
                 ALU.mult)
            p.op("vector", "tensor_tensor", lmt[:], lmt[:], lbb[:],
                 ALU.add, inc="dv")
            lm_dn = p.val("dv")
            if debug and n == 0:
                p.wait("sync", "dv", lm_dn)
                p.dma("sync", dbg_th, tht[:], inc="stq")
                p.dma("sync", dbg_lm, lmt[:], inc="stq")
                p.wait("tensor", "stq", p.val("stq"))
            p.wait("tensor", "dv", lm_dn)
            if n == 0:
                p.wait("tensor", "ldwv", ldwv_all)
            p.wait("tensor", "ac", rec.get("cums_ev", 0))
            p.op("tensor", "matmul", ps5[:], ltp_s[:], tht[:],
                 start=True, stop=True)
            p.op("tensor", "matmul", ps6[:], ltn_s[:], lmt[:],
                 start=True, stop=True, inc="pe")
            p.wait("scalar", "pe")
            p.op("scalar", "copy", phit[:], ps5[:], inc="ac")
            p.op("scalar", "copy", rtt[:], ps6[:], inc="ac")
            rec["cums_ev"] = p.val("ac")
            cums_dn = p.val("ac")
            p.wait("vector", "ac", cums_dn)
            p.op("vector", "tensor_scalar", scr1[:], phit[:],
                 1.0 / (2 * PI), MAGIC, ALU.mult, ALU.add)
            p.op("vector", "tensor_scalar", scr1[:], scr1[:], 1.0,
                 -MAGIC, ALU.mult, ALU.add)
            p.op("vector", "scalar_tensor_tensor", scr1[:], scr1[:],
                 -2 * PI, phit[:], ALU.mult, ALU.add, inc="dv")
            p.wait("scalar", "dv")
            p.op("scalar", "activation", uimt[:], scr1[:], AF.Sin,
                 bias=0.0, scale=1.0, inc="ac")
            p.op("vector", "tensor_scalar", scr2[:], phit[:],
                 1.0 / (2 * PI), 0.25, ALU.mult, ALU.add)
            p.op("vector", "tensor_scalar", scr2[:], scr2[:], 1.0,
                 MAGIC, ALU.mult, ALU.add)
            p.op("vector", "tensor_scalar", scr2[:], scr2[:], 1.0,
                 -MAGIC, ALU.mult, ALU.add)
            p.op("vector", "tensor_scalar", scr2[:], scr2[:], 1.0,
                 -0.25, ALU.mult, ALU.add)
            p.op("vector", "scalar_tensor_tensor", scr2[:], scr2[:],
                 -2 * PI, phit[:], ALU.mult, ALU.add, inc="dv")
            p.wait("scalar", "dv")
            p.op("scalar", "activation", uret[:], scr2[:], AF.Sin,
                 bias=0.0, scale=1.0, inc="ac")
            trig_dn = p.val("ac")

            p.wait("vector", lq, lq_now)
            p.wait("vector", "ac", trig_dn)
            p.wait("vector", "pe", rec.get("rotread", 0))
            p.op("vector", "tensor_scalar", kphi[:], ktk[buf][:], 0.0,
                 0.0, ALU.min, ALU.add, inc="dv")
            p.wait("scalar", "dv")
            p.op("scalar", "activation", kphi[:], kphi[:], AF.Exp,
                 bias=0.0, scale=1.0, inc="ac")
            p.wait("vector", "ac")
            p.op("vector", "scalar_tensor_tensor", kphi[:], ktk[buf][:],
                 0.0, kphi[:], ALU.max, ALU.add)
            qv = qtk[buf].rearrange("p (h c) -> p h c", c=64)
            kv = kphi.rearrange("p (h c) -> p h c", c=64)
            urv = uret.rearrange("p (h j) -> p h j", j=32)
            uiv = uimt.rearrange("p (h j) -> p h j", j=32)
            s1v = scr1.rearrange("p (h j) -> p h j", j=32)
            s2v = scr2.rearrange("p (h j) -> p h j", j=32)
            for (srcv, dre, dimm, neg) in (
                    (qv, qrot_re, qrot_im, False),
                    (kv, krot_re, krot_imn, True)):
                p.op("vector", "tensor_tensor", s1v, srcv[:, :, 0:32],
                     urv, ALU.mult)
                p.op("vector", "tensor_tensor", s2v, srcv[:, :, 32:64],
                     uiv, ALU.mult)
                p.op("vector", "tensor_tensor", dre[:], scr1[:],
                     scr2[:], ALU.add)
                p.op("vector", "tensor_tensor", s1v, srcv[:, :, 0:32],
                     uiv, ALU.mult)
                p.op("vector", "tensor_tensor", s2v, srcv[:, :, 32:64],
                     urv, ALU.mult)
                p.op("vector", "tensor_tensor", dimm[:], scr1[:],
                     scr2[:], ALU.subtract,
                     inc="dv" if neg else None)
            rot_dn = p.val("dv")
            if debug and n == 0:
                p.wait("sync", "dv", rot_dn)
                p.wait("gpsimd", "dv", rot_dn)
                p.dma("sync", dbg_phi, phit[:], inc="stq")
                p.dma("sync", dbg_rt, rtt[:], inc="stq")
                p.dma("gpsimd", dbg_qre, qrot_re, inc="dbgs")
                p.dma("gpsimd", dbg_kre, krot_re, inc="dbgs")
                p.dma("gpsimd", dbg_ure, uret[:], inc="dbgs")
                p.dma("gpsimd", dbg_uim, uimt[:], inc="dbgs")
                p.dma("gpsimd", dbg_qtk, qtk[buf][:], inc="dbgs")
                p.wait("tensor", "stq", p.val("stq"))
                p.wait("vector", "dbgs", p.val("dbgs"))
                p.wait("tensor", "dbgs", p.val("dbgs"))
                p.wait("gpsimd", "dbgs", p.val("dbgs"))

            p.wait("tensor", "dv", rot_dn)
            for g in range(4):
                p.wait("tensor", "ac", rec.get("tp_ev", 0))
                p.op("tensor", "matmul", ps7[:, 0:128],
                     rtt[:, 128 * g:128 * g + 128], idf_s[:],
                     start=True, stop=True)
                p.op("tensor", "matmul", ps7[:, 128:256],
                     phit[:, 128 * g:128 * g + 128], idf_s[:],
                     start=True, stop=True, inc="pe")
                for si, (srct, dstf) in enumerate(
                        ((qrot_re, qr_f[g]), (qrot_im, qi_f[g]),
                         (krot_re, kr_f[g]), (krot_imn, ki_f[g]))):
                    p.op("tensor", "transpose",
                         pstr[:, 128 * si:128 * si + 128],
                         srct[:, 128 * g:128 * g + 128], idb_s[:],
                         inc="pe" if si == 3 else None)
                p.wait("scalar", "pe")
                p.op("scalar", "copy", rf[g][:], ps7[:, 0:128])
                p.op("scalar", "copy", phif[g][:], ps7[:, 128:256])
                for si, dstf in enumerate((qr_f[g], qi_f[g], kr_f[g],
                                           ki_f[g])):
                    p.op("scalar", "copy", dstf[:],
                         pstr[:, 128 * si:128 * si + 128],
                         inc="ac" if si == 3 else None)
                rec["tp_ev"] = p.val("ac")
            tp_dn = p.val("ac")

            p.wait("tensor", "dv", p.val("dv"))
            p.wait("tensor", "ac", max(tp_dn, rec.get("eend_ev", 0)))
            for g in range(4):
                p.op("tensor", "matmul", ps7[0:1, 128 * g:128 * g + 128],
                     rf[g][:, 127:128], idf_s[:], start=True, stop=True,
                     inc="pe" if g == 3 else None)
            p.wait("scalar", "pe")
            p.op("scalar", "copy", rendr[:], ps7[0:1, 0:512], inc="ac")
            p.wait("tensor", "ac")
            p.op("tensor", "matmul", ps7[:], ones_row[:], rendr[:],
                 start=True, stop=True, inc="pe")
            p.wait("scalar", "pe")
            p.op("scalar", "copy", rendb[:], ps7[:], inc="ac")
            rec["eend_ev"] = p.val("ac")
            p.wait("vector", "ac", rec["eend_ev"])
            p.op("vector", "tensor_tensor", scr2[:], rendb[:], rtt[:],
                 ALU.subtract, inc="dv")
            rec["lam_read"] = p.val("dv")
            p.wait("scalar", "dv")
            p.op("scalar", "activation", eendt[:], scr2[:], AF.Exp,
                 bias=0.0, scale=1.0, inc="ac")
            p.wait("vector", "ac")
            kev = kre_t.rearrange("p (h c) -> p h c", c=64)
            p.op("vector", "tensor_tensor", kev[:, :, 0:32],
                 krot_re.rearrange("p (h j) -> p h j", j=32),
                 eendt[:].rearrange("p (h j) -> p h j", j=32), ALU.mult)
            p.op("vector", "tensor_tensor", kev[:, :, 32:64],
                 krot_imn.rearrange("p (h j) -> p h j", j=32),
                 eendt[:].rearrange("p (h j) -> p h j", j=32), ALU.mult,
                 inc="dv")
            kre_dn = p.val("dv")

            p.wait("gpsimd", "pe", rec.get("y_pe", 0))
            for pp in range(8):
                vsrc = v_d[t0 + 16 * pp:t0 + 16 * pp + 16, :].rearrange(
                    "r (hi g vc) -> g r hi vc", hi=4, g=4, vc=64)
                for g2 in range(4):
                    p.dma("gpsimd",
                          vp[pp][32 * g2:32 * g2 + 16, :].rearrange(
                              "r (hi vc) -> r hi vc", vc=64),
                          vsrc[g2], inc="vpd")
            vp_dn = p.val("vpd")
            p.wait("tensor", "vpd", vp_dn)

            gb = [ps1, ps2]
            for g in range(4):
                p.wait("scalar", "ac", tp_dn)
                p.wait("scalar", "dv", rec.get("pq_dn", 0))
                p.op("scalar", "activation", efaf[g][:], rf[g][:],
                     AF.Exp, bias=0.0, scale=1.0, inc="ac")
                p.wait("vector", "ac")
                p.wait("vector", "pe", rec.get("y_pe", 0))
                p.op("vector", "tensor_tensor", pq_re[g][:], qr_f[g][:],
                     efaf[g][:], ALU.mult)
                p.op("vector", "tensor_tensor", pq_im[g][:], qi_f[g][:],
                     efaf[g][:], ALU.mult, inc="dv")
                rec["pq_dn"] = p.val("dv")
                p.op("vector", "tensor_tensor",
                     eargk[:].rearrange("p (b d) -> p b d", d=16),
                     _anchor16(rf[g]),
                     rf[g][:].rearrange("p (b d) -> p b d", d=16),
                     ALU.subtract, inc="dv")
                p.wait("scalar", "dv")
                p.op("scalar", "activation", eargk[:], eargk[:], AF.Exp,
                     bias=0.0, scale=1.0, inc="ac")
                p.wait("vector", "ac")
                p.op("vector", "tensor_tensor", krall_re[g][:],
                     kr_f[g][:], eargk[:], ALU.mult)
                p.op("vector", "tensor_tensor", krall_imn[g][:],
                     ki_f[g][:], eargk[:], ALU.mult)
                p.op("vector", "tensor_tensor",
                     earg[:].rearrange("p (b t) -> p b t", t=128),
                     _tile8(rf[g]), _anchor8(rf[g]), ALU.subtract)
                p.op("vector", "tensor_scalar", earg[:], earg[:],
                     80.0, 0.0, ALU.min, ALU.add, inc="dv")
                p.wait("scalar", "dv")
                p.op("scalar", "activation", earg[:], earg[:], AF.Exp,
                     bias=0.0, scale=1.0, inc="ac")
                p.wait("vector", "ac")
                p.op("vector", "tensor_tensor",
                     qrall_re.rearrange("p (b t) -> p b t", t=128),
                     earg[:].rearrange("p (b t) -> p b t", t=128),
                     _tile8(qr_f[g]), ALU.mult)
                p.op("vector", "tensor_tensor",
                     qrall_im.rearrange("p (b t) -> p b t", t=128),
                     earg[:].rearrange("p (b t) -> p b t", t=128),
                     _tile8(qi_f[g]), ALU.mult)
                p.op("vector", "tensor_copy", col_s1[:],
                     phif[g][:, 127:128])
                p.op("vector", "tensor_scalar", col_s1[:], col_s1[:],
                     1.0 / (2 * PI), MAGIC, ALU.mult, ALU.add)
                p.op("vector", "tensor_scalar", col_s1[:], col_s1[:],
                     1.0, -MAGIC, ALU.mult, ALU.add)
                p.op("vector", "scalar_tensor_tensor", col_s1[:],
                     col_s1[:], -2 * PI, phif[g][:, 127:128], ALU.mult,
                     ALU.add)
                p.op("vector", "tensor_copy", col_s2[:],
                     phif[g][:, 127:128])
                p.op("vector", "tensor_scalar", col_s2[:], col_s2[:],
                     1.0 / (2 * PI), 0.25, ALU.mult, ALU.add)
                p.op("vector", "tensor_scalar", col_s2[:], col_s2[:],
                     1.0, MAGIC, ALU.mult, ALU.add)
                p.op("vector", "tensor_scalar", col_s2[:], col_s2[:],
                     1.0, -MAGIC, ALU.mult, ALU.add)
                p.op("vector", "tensor_scalar", col_s2[:], col_s2[:],
                     1.0, -0.25, ALU.mult, ALU.add)
                p.op("vector", "scalar_tensor_tensor", col_s2[:],
                     col_s2[:], -2 * PI, phif[g][:, 127:128], ALU.mult,
                     ALU.add, inc="dv")
                fb_dn = p.val("dv")
                rec["fb_dn"] = fb_dn
                p.wait("scalar", "dv", fb_dn)
                p.op("scalar", "activation", uei_c[g][:], col_s1[:],
                     AF.Sin, bias=0.0, scale=1.0)
                p.op("scalar", "activation", uer_c[g][:], col_s2[:],
                     AF.Sin, bias=0.0, scale=1.0)
                p.op("scalar", "activation", dend[g][:],
                     rf[g][:, 127:128], AF.Exp, bias=0.0, scale=1.0,
                     inc="ac")
                cols_dn = p.val("ac")

                p.wait("tensor", "dv", fb_dn)
                p.wait("tensor", "ac", cols_dn)
                for pp in range(8):
                    par = pp % 2
                    nwid = 128 - 16 * pp
                    p.wait("tensor", "dv", rec.get(f"g_ev{par}", 0))
                    for hh in range(4):
                        base = 32 * hh
                        tp = (96, 96) if hh == 3 else None
                        p.op("tensor", "matmul",
                             gb[par][base:base + 8, 0:8],
                             krall_re[g][base:base + 32,
                                         16 * pp:16 * pp + 8],
                             qrall_re[base:base + 32,
                                      128 * pp + 16 * pp:
                                      128 * pp + 16 * pp + 8],
                             start=True, stop=False,
                             tile_position=tp)
                        p.op("tensor", "matmul",
                             gb[par][base:base + 8, 0:8],
                             krall_imn[g][base:base + 32,
                                          16 * pp:16 * pp + 8],
                             qrall_im[base:base + 32,
                                      128 * pp + 16 * pp:
                                      128 * pp + 16 * pp + 8],
                             start=False, stop=True,
                             tile_position=tp)
                        p.op("tensor", "matmul",
                             gb[par][base:base + 16, 8:nwid],
                             krall_re[g][base:base + 32,
                                         16 * pp:16 * pp + 16],
                             qrall_re[base:base + 32,
                                      128 * pp + 16 * pp + 8:
                                      128 * pp + 128],
                             start=True, stop=False,
                             tile_position=tp)
                        p.op("tensor", "matmul",
                             gb[par][base:base + 16, 8:nwid],
                             krall_imn[g][base:base + 32,
                                          16 * pp:16 * pp + 16],
                             qrall_im[base:base + 32,
                                      128 * pp + 16 * pp + 8:
                                      128 * pp + 128],
                             start=False, stop=True,
                             tile_position=tp,
                             inc="pe" if hh == 3 else None)
                    p.wait("vector", "pe")
                    p.op("vector", "tensor_tensor", gts[pp][:, 0:nwid],
                         gb[par][:, 0:nwid], gmask_s[:, 0:nwid],
                         ALU.mult, inc="dv")
                    rec[f"g_ev{par}"] = p.val("dv")
                g_dn = p.val("dv")

                p.wait("tensor", "dv", g_dn)
                for hh in range(4):
                    h = 4 * g + hh
                    base = 32 * hh
                    tp = (96, 0) if hh == 3 else None
                    p.wait("tensor", "ac", rec.get("y_ev", 0))
                    p.op("tensor", "matmul", ps3[0:64, 0:128],
                         cre[g][base:base + 32, :],
                         pq_re[g][base:base + 32, :],
                         start=True, stop=False, tile_position=tp)
                    p.op("tensor", "matmul", ps3[0:64, 0:128],
                         cimn[g][base:base + 32, :],
                         pq_im[g][base:base + 32, :],
                         start=False, stop=False, tile_position=tp)
                    for pp in range(8):
                        nwid = 128 - 16 * pp
                        p.op("tensor", "matmul",
                             ps3[0:64, 16 * pp:128],
                             vp[pp][base:base + 16,
                                    64 * (h // 4):64 * (h // 4) + 64],
                             gts[pp][base:base + 16, 0:nwid],
                             start=False, stop=(pp == 7),
                             tile_position=tp,
                             inc="pe" if pp == 7 else None)
                    p.wait("scalar", "pe")
                    p.wait("scalar", "stb", rec.get("stb_prev", 0))
                    p.op("scalar", "copy",
                         of16[h // 2][64 * (h % 2):64 * (h % 2) + 64, :],
                         ps3[0:64, 0:128], inc="ac")
                    rec["y_ev"] = p.val("ac")
                rec["y_pe"] = p.val("pe")

                p.wait("tensor", "dv", max(kre_dn,
                                           rec.get(f"u_read_{g}", 0)))
                for hh in range(4):
                    h = 4 * g + hh
                    tp = (0, 96) if hh == 3 else None
                    p.op("tensor", "matmul",
                         ps4[32 * hh:32 * hh + 32, 0:64],
                         kre_t[:, 64 * h:64 * h + 32],
                         vtk[buf][:, 64 * h:64 * h + 64],
                         start=True, stop=True, tile_position=tp)
                    p.op("tensor", "matmul",
                         ps4[32 * hh:32 * hh + 32, 64:128],
                         kre_t[:, 64 * h + 32:64 * h + 64],
                         vtk[buf][:, 64 * h:64 * h + 64],
                         start=True, stop=True, tile_position=tp,
                         inc="pe" if hh == 3 else None)
                p.wait("vector", "pe")
                p.op("vector", "scalar_tensor_tensor", tr1[:],
                     cre[g][:], dend[g][:, 0:1], ps4[:, 0:64],
                     ALU.mult, ALU.add)
                p.op("vector", "scalar_tensor_tensor", tr2[:],
                     cimn[g][:], dend[g][:, 0:1], ps4[:, 64:128],
                     ALU.mult, ALU.add, inc="dv")
                p.op("vector", "tensor_scalar", cre[g][:], tr1[:],
                     uer_c[g][:, 0:1], 0.0, ALU.mult, ALU.add)
                p.op("vector", "scalar_tensor_tensor", cre[g][:],
                     tr2[:], uei_c[g][:, 0:1], cre[g][:], ALU.mult,
                     ALU.add)
                p.op("vector", "tensor_scalar", cimn[g][:], tr2[:],
                     uer_c[g][:, 0:1], 0.0, ALU.mult, ALU.add)
                p.op("vector", "tensor_scalar", tr1[:], tr1[:],
                     uei_c[g][:, 0:1], 0.0, ALU.mult, ALU.add)
                p.op("vector", "tensor_tensor", cimn[g][:], cimn[g][:],
                     tr1[:], ALU.subtract, inc="dv")
                rec[f"u_read_{g}"] = p.val("dv")
            rec["rotread"] = p.val("pe")

            y_all = p.val("ac")
            p.wait("sync", "ac", y_all)
            for c in range(8):
                p.dma("sync", at_d[128 * c:128 * c + 128, t0:t0 + 128],
                      of16[c][:], inc="stb")
            rec["stb_prev"] = p.val("stb")
            rec[f"chunk_pe_{n}"] = p.val("pe")
        phaseB_stq = p.val("stb")

        p.wait("sync", "stb", phaseB_stq)
        p.wait("gpsimd", "pe", p.val("pe"))
        p.wait("gpsimd", "dv", p.val("dv"))
        for kc in range(8):
            p.dma("gpsimd", wo_s[kc], wo[kc], inc="ldw")
        ldw_c = p.val("ldw")
        for c in range(8):
            p.dma("sync", xo[128 * c:128 * c + 128, :],
                  x[128 * c:128 * c + 128, :], inc="xcp")
        for i in range(NT):
            buf = i % 2
            t0 = i * W
            ldsem = "lda" if buf == 0 else "ldb"
            if i >= 1:
                p.wait("sync", "dv", rec["c_tile_dn"])
            if i >= 2:
                p.wait("sync", "sta0", rec[f"sta0_{i-2}"])
                p.wait("sync", "sta1", rec[f"sta1_{i-2}"])
            for c in range(8):
                p.dma("sync", xlnf[c][:],
                      xln_d[128 * c:128 * c + 128, t0:t0 + W], inc=ldsem)
                p.dma("sync", at16[c][:],
                      at_d[128 * c:128 * c + 128, t0:t0 + W], inc=ldsem)
            ld_tile = p.val(ldsem)
            p.wait("tensor", ldsem, ld_tile)
            if i == 0:
                p.wait("tensor", "ldw", ldw_c)
            p.wait("tensor", "dv", rec.get("x1_dn", 0))
            for oc in range(8):
                bank = ps5 if oc % 2 == 0 else ps6
                p.wait("tensor", "dv", rec.get(f"wo_ev{oc % 2}", 0))
                for kc in range(8):
                    p.op("tensor", "matmul", bank[:],
                         wo_s[kc][:, 128 * oc:128 * oc + 128],
                         at16[kc][:], start=(kc == 0), stop=(kc == 7),
                         inc="pe" if kc == 7 else None)
                p.wait("vector", "pe")
                p.op("vector", "scalar_tensor_tensor", xlnf[oc][:],
                     bank[:], bo_s[:, oc:oc + 1], xlnf[oc][:],
                     ALU.add, ALU.add, inc="dv")
                rec[f"wo_ev{oc % 2}"] = p.val("dv")
            x1_dn = p.val("dv")
            rec["x1_dn"] = x1_dn
            p.wait("scalar", "dv", x1_dn)
            p.wait("tensor", "dv", x1_dn)
            ln_stats(lambda c: xlnf[c][:])
            ln_rows(eps6_col[0:1, 0:1])
            for c in range(8):
                p.op("vector", "tensor_tensor", sq[c % 2][:], xlnf[c][:],
                     rb[:], ALU.mult)
                p.op("vector", "tensor_tensor", sq[c % 2][:],
                     sq[c % 2][:], mrb[:], ALU.subtract)
                p.op("vector", "tensor_scalar", hn16[c][:], sq[c % 2][:],
                     n2g_s[:, c:c + 1], n2b_s[:, c:c + 1], ALU.mult,
                     ALU.add, inc="dv" if c == 7 else None)
            hn_dn = p.val("dv")
            p.wait("tensor", "dv", hn_dn)
            units = [(pcu, phu) for pcu in range(8) for phu in range(2)]
            for ui, (pc, ph) in enumerate(units):
                wb = ui % 4
                lsem = lsems4[wb]
                p.wait("gpsimd", "pe", rec.get(f"wf1_used_{wb}", 0))
                p.wait("gpsimd", lsem, p.val(lsem))
                p.dma("gpsimd", wf1sl[wb],
                      wf1[pc][:, 2048 * ph:2048 * ph + 2048], inc=lsem)
                ldwf_now = p.val(lsem)
                p.wait("tensor", lsem, ldwf_now)
                for sub in range(4):
                    bank = [ps1, ps2, ps3, ps4][sub]
                    if ph == 0:
                        p.wait("tensor", "ac", rec.get(f"h1_ev{sub}", 0))
                    for kk2 in range(4):
                        kc = 4 * ph + kk2
                        p.op("tensor", "matmul", bank[:],
                             wf1sl[wb][:, 512 * kk2 + 128 * sub:
                                       512 * kk2 + 128 * sub + 128],
                             hn16[kc][:], start=(kc == 0), stop=(kc == 7),
                             inc="pe" if kk2 == 3 else None)
                    rec[f"wf1_used_{wb}"] = p.val("pe")
                    if ph == 1:
                        p.wait("scalar", "pe")
                        p.op("scalar", "activation", h1t[4 * pc + sub],
                             bank[:], AF.Gelu_apprx_tanh,
                             bias=bf1_s[:, 4 * pc + sub:4 * pc + sub + 1],
                             scale=1.0, inc="ac")
                        rec[f"h1_ev{sub}"] = p.val("ac")
            h1_dn = p.val("ac")
            p.wait("tensor", "ac", h1_dn)
            for half in range(2):
                for pc2 in range(8):
                    wb = pc2 % 4
                    lsem = lsems4[wb]
                    p.wait("gpsimd", "pe", rec.get(f"wf2_used_{wb}", 0))
                    p.wait("gpsimd", lsem, p.val(lsem))
                    p.dma("gpsimd",
                          wf2sl[wb].rearrange("p (kk c) -> p kk c", c=512),
                          wf2[pc2].rearrange("p (kk c) -> p kk c", c=1024)
                          [:, :, 512 * half:512 * half + 512], inc=lsem)
                    ldwf_now = p.val(lsem)
                    p.wait("tensor", lsem, ldwf_now)
                    for kk in range(4):
                        for q4 in range(4):
                            bankq = [ps1, ps2, ps3, ps4][q4]
                            if pc2 == 0 and kk == 0:
                                p.wait("tensor", "dv",
                                       rec.get(f"ffn_out{q4}", 0))
                            p.op("tensor", "matmul", bankq[:],
                                 wf2sl[wb][:, 512 * kk + 128 * q4:
                                           512 * kk + 128 * q4 + 128],
                                 h1t[4 * pc2 + kk],
                                 start=(pc2 == 0 and kk == 0),
                                 stop=(pc2 == 7 and kk == 3),
                                 inc="pe" if q4 == 3 else None)
                    rec[f"wf2_used_{wb}"] = p.val("pe")
                w2_pe = p.val("pe")
                p.wait("vector", "pe", w2_pe)
                if i == 0 and half == 0:
                    p.wait("gpsimd", "xcp", 128)
                for q4 in range(4):
                    oc = 4 * half + q4
                    bankq = [ps1, ps2, ps3, ps4][q4]
                    sb = oc % 2
                    stsem = "sta0" if sb == 0 else "sta1"
                    if f"store_{stsem}" in rec:
                        p.wait("vector", stsem, rec[f"store_{stsem}"])
                    p.op("vector", "scalar_tensor_tensor", vout[sb][:],
                         bankq[:], bf2_s[:, oc:oc + 1], xlnf[oc][:],
                         ALU.add, ALU.add, inc="dv")
                    rec[f"ffn_out{q4}"] = p.val("dv")
                    p.wait("gpsimd", "dv")
                    p.dma("gpsimd",
                          xo[128 * oc:128 * oc + 128, t0:t0 + W],
                          vout[sb][:], accum_op=ALU.add, inc=stsem)
                    rec[f"store_{stsem}"] = p.val(stsem)
            rec[f"sta0_{i}"] = p.val("sta0")
            rec[f"sta1_{i}"] = p.val("sta1")
            rec["c_tile_dn"] = p.val("dv")
        p.wait("gpsimd", "sta0", p.val("sta0"))
        p.wait("gpsimd", "sta1", p.val("sta1"))

    if fused:
        return body
    emit_all(p, body)
    return nc


def _anchor16(hT):
    return hT[:].rearrange("p (b d) -> p b d", d=16)[:, :, 8:9].broadcast_to(
        [128, 8, 16])


def _anchor8(hT):
    return hT[:].rearrange("p (b d) -> p b d", d=16)[:, :, 8:9].broadcast_to(
        [128, 8, 128])


def _tile8(hT):
    return hT[:].unsqueeze(1).broadcast_to([128, 8, 128])


def l0_attn_consts():
    s = np.arange(128)[:, None]
    t = np.arange(128)[None, :]
    ltp = (s <= t).astype(np.float32)
    ltn = -ltp
    idenf = np.eye(128, dtype=np.float32)
    gmask = np.zeros((128, 128), dtype=np.float32)
    for g in range(4):
        for rr in range(16):
            gmask[32 * g + rr, rr:] = 1.0
    return ltp, ltn, idenf, gmask


def l0_attn_inmap(x_fm_b, inp):
    perm = np.zeros(1024, dtype=np.int64)
    for h in range(16):
        for j in range(32):
            perm[64 * h + j] = 64 * h + 2 * j
            perm[64 * h + 32 + j] = 64 * h + 2 * j + 1
    wq = np.asarray(inp["L0_attn_Wq"], np.float32)[:, perm]
    wk = np.asarray(inp["L0_attn_Wk"], np.float32)[:, perm]
    wv = np.asarray(inp["L0_attn_Wv"], np.float32)
    wo_ = np.asarray(inp["L0_attn_Wo"], np.float32)
    wf1_ = np.asarray(inp["L0_attn_Wf1"], np.float32)
    wf2_ = np.asarray(inp["L0_attn_Wf2"], np.float32)
    thw1_ = np.asarray(inp["L0_attn_theta_w1"], np.float32)
    thw2_ = np.asarray(inp["L0_attn_theta_w2"], np.float32)
    wf1r = np.zeros((8, 128, 4096), np.float32)
    wf2r = np.zeros((8, 128, 4096), np.float32)
    for pc in range(8):
        sl = wf1_[:, 512 * pc:512 * pc + 512]
        wf1r[pc] = sl.reshape(8, 128, 512).transpose(1, 0, 2).reshape(
            128, 4096)
        s2 = wf2_[512 * pc:512 * pc + 512, :]
        wf2r[pc] = s2.reshape(4, 128, 1024).transpose(1, 0, 2).reshape(
            128, 4096)
    ltp, ltn, idenf, gmask = l0_attn_consts()

    def col8(v):
        return np.ascontiguousarray(np.asarray(v, np.float32).reshape(
            8, 128).T)
    return dict(
        x=np.ascontiguousarray(x_fm_b),
        wq=np.ascontiguousarray(wq.reshape(8, 128, 1024)),
        wk=np.ascontiguousarray(wk.reshape(8, 128, 1024)),
        wv=np.ascontiguousarray(wv.reshape(8, 128, 1024)),
        wo=np.ascontiguousarray(wo_.reshape(8, 128, 1024)),
        wf1=wf1r, wf2=wf2r,
        thw1=np.ascontiguousarray(thw1_.reshape(8, 128, 48)),
        thw2=np.ascontiguousarray(thw2_),
        bqr=np.ascontiguousarray(
            np.asarray(inp["L0_attn_bq"], np.float32)[perm][None, :]),
        bkr=np.ascontiguousarray(
            np.asarray(inp["L0_attn_bk"], np.float32)[perm][None, :]),
        bvr=np.ascontiguousarray(
            np.asarray(inp["L0_attn_bv"], np.float32)[None, :]),
        bor=col8(inp["L0_attn_bo"]),
        bf1r=np.ascontiguousarray(
            np.asarray(inp["L0_attn_bf1"], np.float32).reshape(32, 128).T),
        bf2r=col8(inp["L0_attn_bf2"]),
        n1gr=col8(inp["L0_attn_n1g"]), n1br=col8(inp["L0_attn_n1b"]),
        n2gr=col8(inp["L0_attn_n2g"]), n2br=col8(inp["L0_attn_n2b"]),
        thbr=np.ascontiguousarray(
            np.asarray(inp["L0_attn_theta_base"], np.float32).reshape(1, 512)),
        lbr=np.ascontiguousarray(
            np.asarray(inp["L0_attn_lambda_base"],
                       np.float32).reshape(1, 512)),
        etar=np.ascontiguousarray(
            np.asarray(inp["L0_attn_eta"], np.float32).reshape(1, 512)),
        ltp=ltp, ltn=ltn, idenf=idenf, gmask=gmask,
    )


def run_l0_attn(x_fm, inp):
    nc = build_l0_attn()
    in_maps = [l0_attn_inmap(x_fm[b], inp) for b in range(NCORES)]
    res = run_bass_kernel_spmd(nc, in_maps, core_ids=list(range(NCORES)))
    return np.stack([r["xo"] for r in res.results])


# version 05
# speedup vs baseline: 1.3493x; 1.0196x over previous
import numpy as np
import os
import ml_dtypes
import concourse.bass as bass
import concourse.mybir as mybir
from concourse.bass_utils import run_bass_kernel_spmd
from contextlib import ExitStack

F32 = mybir.dt.float32
BF16 = mybir.dt.bfloat16
AF = mybir.ActivationFunctionType
ALU = mybir.AluOpType
AX = mybir.AxisListType

T = 4096
D = 1024
NCORES = 8

ENGINES = ("sync", "tensor", "vector", "scalar", "gpsimd")


class Sem:
    __slots__ = ("h", "count")

    def __init__(self, h):
        self.h = h
        self.count = 0


class P:

    def __init__(self, nc, ctx):
        self.nc = nc
        self.ctx = ctx
        self.sems = {}
        self.cur = None
        self.sbufs = {}
        self.psums = {}
        self.block_tag = ""
        self.sem_pfx = ""
        self.arena = None
        self.arena_off = 0
        self.psum_map = None
        self._hwm = {}

    def set_block(self, pfx, psmap=None):
        self._hwm[self.block_tag] = max(
            self._hwm.get(self.block_tag, 0), self.arena_off)
        self.block_tag = pfx
        self.sem_pfx = pfx
        self.psum_map = psmap
        self.arena_off = self._hwm.get(pfx, 0)

    def barrier(self, prev_pfx):
        for name, s in self.sems.items():
            if name.startswith(prev_pfx) and s.count > 0:
                if self.cur is not None:
                    getattr(self.nc, self.cur).wait_ge(s.h, s.count)

    def sem(self, name):
        name = self.sem_pfx + name
        if name not in self.sems:
            self.sems[name] = Sem(self.ctx.enter_context(self.nc.semaphore(name)))
        return self.sems[name]

    def sbuf(self, name, shape, dtype):
        key = (self.block_tag, name)
        if key not in self.sbufs:
            if self.arena is None:
                self.sbufs[key] = self.ctx.enter_context(
                    self.nc.sbuf_tensor(name, list(shape), dtype))
            else:
                pp, cc = shape
                nbytes = cc * (4 if dtype == F32 else 2)
                off = self.arena_off
                self.arena_off = off + ((nbytes + 31) // 32) * 32
                sl = self.arena[0:pp, off // 2: off // 2 + ((nbytes + 1) // 2)]
                self.sbufs[key] = sl if dtype == BF16 else sl.bitcast(dtype)
        return self.sbufs[key]

    def psum(self, name, shape, dtype=F32):
        key = (self.block_tag, name)
        if key not in self.psums:
            if self.psum_map is None:
                self.psums[key] = self.ctx.enter_context(
                    self.nc.psum_tensor(name, list(shape), dtype))
            else:
                self.psums[key] = self.psum_map[name]
        return self.psums[key]

    def begin(self, eng):
        self.cur = eng
        for s in self.sems.values():
            s.count = 0

    def wait(self, eng, sem, val=None):
        s = self.sems[self.sem_pfx + sem]
        v = s.count if val is None else val
        if self.cur == eng and v > 0:
            getattr(self.nc, eng).wait_ge(s.h, v)
        return v

    def op(self, eng, method, *args, inc=None, **kw):
        r = None
        if self.cur == eng:
            e = getattr(self.nc, eng)
            if eng in ("vector", "scalar", "gpsimd") and not \
                    method.startswith("dma"):
                e.drain()
            r = getattr(e, method)(*args, **kw)
        if inc is not None:
            s = self.sems[self.sem_pfx + inc]
            step = 16 if method.startswith("dma") else 1
            s.count += step
            if r is not None:
                r.then_inc(s.h, step)
        return r

    def dma(self, eng, out, in_, inc=None, **kw):
        return self.op(eng, "dma_start", inc=inc, out=out, in_=in_, **kw)

    def val(self, sem):
        return self.sems[self.sem_pfx + sem].count


class NCProxy:

    def __init__(self, nc, pfx, x_ap, xo_ap):
        self._nc = nc
        self._pfx = pfx
        self._x = x_ap
        self._xo = xo_ap

    def dram_tensor(self, name, shape, dtype, kind="Internal"):
        if name == "x" and kind == "ExternalInput":
            return _FakeT(self._x)
        if name == "xo" and kind == "ExternalOutput":
            return _FakeT(self._xo)
        k = "ExternalInput" if kind == "ExternalInput" else "Internal"
        return self._nc.dram_tensor(self._pfx + name, shape, dtype, kind=k)

    def __getattr__(self, a):
        return getattr(self._nc, a)


class _FakeT:
    def __init__(self, ap):
        self._ap = ap

    def ap(self):
        return self._ap


def emit_all(p, body):
    with p.nc.Block() as block:
        for eng in ENGINES:
            def make(e):
                def f(_):
                    p.begin(e)
                    body()
                return f
            getattr(block, eng)(make(eng))



def build_l1_ffn(fused=None):
    nc = fused["nc"] if fused else bass.Bass("TRN2", target_bir_lowering=False)
    x = nc.dram_tensor("x", [D, T], F32, kind="ExternalInput").ap()
    wr = nc.dram_tensor("wr_t", [D, D], F32, kind="ExternalInput").ap()
    wk = nc.dram_tensor("wk_t", [D, D], F32, kind="ExternalInput").ap()
    wv = nc.dram_tensor("wv_t", [D, D], F32, kind="ExternalInput").ap()
    mu_r = nc.dram_tensor("mu_r", [128, 8], F32, kind="ExternalInput").ap()
    mu_k = nc.dram_tensor("mu_k", [128, 8], F32, kind="ExternalInput").ap()
    xo = nc.dram_tensor("xo", [D, T], F32, kind="ExternalOutput").ap()

    NT = 8
    W = 512

    p = fused["p"] if fused else P(nc, ExitStack())
    wtiles = {}
    for nm in ("wr", "wk", "wv"):
        wtiles[nm] = [p.sbuf(f"{nm}_{kc}", [128, D], BF16) for kc in range(8)]
    mur = p.sbuf("mur", [128, 8], F32)
    muk = p.sbuf("muk", [128, 8], F32)
    omur = p.sbuf("omur", [128, 8], F32)
    omuk = p.sbuf("omuk", [128, 8], F32)
    ones_col = p.sbuf("ones_col", [128, 1], F32)
    ones_row = p.sbuf("ones_row", [1, 128], F32)
    eps_col = p.sbuf("eps_col", [128, 1], F32)

    xch = [[p.sbuf(f"x_{c}_{b}", [128, W + 1], F32) for b in range(2)]
           for c in range(8)]
    xln = [[p.sbuf(f"xln_{c}_{b}", [128, W + 1], F32) for b in range(2)]
           for c in range(8)]
    sq = [p.sbuf(f"sq_{b}", [128, W], F32) for b in range(2)]
    rb = p.sbuf("rb", [128, W], F32)
    mrb = p.sbuf("mrb", [128, W], F32)
    tsc = p.sbuf("tsc", [128, W], F32)
    xr16 = [p.sbuf(f"xr16_{c}", [128, W], BF16) for c in range(8)]
    xk16 = [p.sbuf(f"xk16_{c}", [128, W], BF16) for c in range(8)]
    rsig = [p.sbuf(f"rsig_{c}", [128, W], F32) for c in range(8)]
    k16a = [p.sbuf(f"k16a_{c}", [128, W], BF16) for c in range(8)]
    kr = [p.sbuf(f"kr_{b}", [128, W], F32) for b in range(2)]
    vout = [p.sbuf(f"vout_{b}", [128, W], F32) for b in range(2)]
    m_r = p.sbuf("m_r", [1, W], F32)
    q_r = p.sbuf("q_r", [1, W], F32)
    m2_r = p.sbuf("m2_r", [1, W], F32)
    var_r = p.sbuf("var_r", [1, W], F32)
    lnv_r = p.sbuf("lnv_r", [1, W], F32)
    rs_r = p.sbuf("rs_r", [1, W], F32)
    mr_r = p.sbuf("mr_r", [1, W], F32)

    ps_sum = p.psum("ps_sum", [1, W])
    ps_sq = p.psum("ps_sq", [1, W])
    ps_b1 = p.psum("ps_b1", [128, W])
    ps_b2 = p.psum("ps_b2", [128, W])
    ps_mm = [p.psum(f"ps_mm{i}", [128, W]) for i in range(2)]
    ps_v = [p.psum(f"ps_v{i}", [128, W]) for i in range(2)]

    for s in ("ldc", "ldw", "lda", "ldb", "pe", "dv", "ac", "st0", "st1"):
        p.sem(s)

    def body():
        for nm, src in (("wr", wr), ("wk", wk), ("wv", wv)):
            for kc in range(8):
                p.dma("gpsimd", wtiles[nm][kc][:],
                      src[128 * kc:128 * (kc + 1), :], inc="ldw")
        p.dma("sync", mur[:], mu_r, inc="ldc")
        p.dma("sync", muk[:], mu_k, inc="ldc")
        p.op("vector", "memset", ones_col[:], 1.0)
        p.op("vector", "memset", ones_row[:], 1.0)
        p.op("vector", "memset", eps_col[:], 1e-5, inc="dv")
        p.wait("scalar", "dv")
        p.wait("vector", "ldc", 32)
        p.op("vector", "tensor_scalar", omur[:], mur[:], -1.0, 1.0,
             ALU.mult, ALU.add)
        p.op("vector", "tensor_scalar", omuk[:], muk[:], -1.0, 1.0,
             ALU.mult, ALU.add)
        for c in range(8):
            p.op("vector", "memset", xln[c][0][:, 0:1], 0.0, inc="dv")

        rec = {"rows_read_psum": 0, "copies_done": 0,
               "sig_evac": [0, 0], "v_evac": [0, 0]}

        for i in range(NT):
            buf = i % 2
            t0 = i * W
            ldsem = "lda" if buf == 0 else "ldb"
            if i >= 2:
                p.wait("sync", "st0", rec[f"st_tile0_{i-2}"])
                p.wait("sync", "st1", rec[f"st_tile1_{i-2}"])
            for c in range(8):
                r0 = 128 * c
                if i == 0:
                    p.dma("sync", xch[c][buf][:, 1:W + 1],
                          x[r0:r0 + 128, 0:W], inc=ldsem)
                else:
                    p.dma("sync", xch[c][buf][:, 0:W + 1],
                          x[r0:r0 + 128, t0 - 1:t0 + W], inc=ldsem)
            ld_tile = p.val(ldsem)

            p.wait("scalar", ldsem, ld_tile)
            p.wait("tensor", "dv", rec["rows_read_psum"])
            pe_after_chunk = []
            for c in range(8):
                if c >= 2:
                    p.wait("scalar", "pe", pe_after_chunk[c - 2])
                p.op("scalar", "activation", sq[c % 2][:],
                     xch[c][buf][:, 1:W + 1], AF.Square, bias=0.0,
                     scale=1.0, inc="ac")
                p.wait("tensor", "ac")
                p.op("tensor", "matmul", ps_sum[:], ones_col[:],
                     xch[c][buf][:, 1:W + 1], start=(c == 0), stop=(c == 7))
                p.op("tensor", "matmul", ps_sq[:], ones_col[:], sq[c % 2][:],
                     start=(c == 0), stop=(c == 7), inc="pe")
                pe_after_chunk.append(p.val("pe"))

            p.wait("vector", "pe")
            p.op("vector", "tensor_scalar", m_r[:], ps_sum[:], 1.0 / D, 0.0,
                 ALU.mult, ALU.add)
            p.op("vector", "tensor_scalar", q_r[:], ps_sq[:], 1.0 / D, 0.0,
                 ALU.mult, ALU.add, inc="dv")
            rec["rows_read_psum"] = p.val("dv")
            p.op("vector", "tensor_tensor", m2_r[:], m_r[:], m_r[:], ALU.mult)
            p.op("vector", "tensor_tensor", var_r[:], q_r[:], m2_r[:],
                 ALU.subtract, inc="dv")
            p.wait("scalar", "dv")
            p.op("scalar", "activation", lnv_r[:], var_r[:], AF.Ln,
                 bias=eps_col[0:1, 0:1], scale=1.0)
            p.op("scalar", "activation", rs_r[:], lnv_r[:], AF.Exp,
                 bias=0.0, scale=-0.5, inc="ac")
            p.wait("vector", "ac")
            p.op("vector", "tensor_tensor", mr_r[:], m_r[:], rs_r[:],
                 ALU.mult, inc="dv")

            p.wait("tensor", "dv")
            p.wait("tensor", "ac", rec["copies_done"])
            p.op("tensor", "matmul", ps_b1[:], ones_row[:], rs_r[:],
                 start=True, stop=True)
            p.op("tensor", "matmul", ps_b2[:], ones_row[:], mr_r[:],
                 start=True, stop=True, inc="pe")
            p.wait("scalar", "pe")
            p.op("scalar", "copy", rb[:], ps_b1[:], inc="ac")
            p.op("scalar", "copy", mrb[:], ps_b2[:], inc="ac")
            rec["copies_done"] = p.val("ac")

            p.wait("vector", "ac")
            for c in range(8):
                if i > 0:
                    p.op("vector", "tensor_copy", xln[c][buf][:, 0:1],
                         xln[c][1 - buf][:, W:W + 1])
                p.op("vector", "tensor_tensor", xln[c][buf][:, 1:W + 1],
                     xch[c][buf][:, 1:W + 1], rb[:], ALU.mult)
                p.op("vector", "tensor_tensor", xln[c][buf][:, 1:W + 1],
                     xln[c][buf][:, 1:W + 1], mrb[:], ALU.subtract)
            for c in range(8):
                p.op("vector", "tensor_scalar", tsc[:],
                     xln[c][buf][:, 1:W + 1], mur[:, c:c + 1], 0.0,
                     ALU.mult, ALU.add)
                p.op("vector", "scalar_tensor_tensor", xr16[c][:],
                     xln[c][buf][:, 0:W], omur[:, c:c + 1], tsc[:],
                     ALU.mult, ALU.add)
                p.op("vector", "tensor_scalar", tsc[:],
                     xln[c][buf][:, 1:W + 1], muk[:, c:c + 1], 0.0,
                     ALU.mult, ALU.add)
                p.op("vector", "scalar_tensor_tensor", xk16[c][:],
                     xln[c][buf][:, 0:W], omuk[:, c:c + 1], tsc[:],
                     ALU.mult, ALU.add, inc="dv" if c == 7 else None)
            mixes_done = p.val("dv")

            p.wait("tensor", "dv", mixes_done)
            if i == 0:
                p.wait("tensor", "ldw", 16 * 24)
            for oc in range(8):
                bank = oc % 2
                p.wait("tensor", "ac", rec["sig_evac"][bank])
                for kc in range(8):
                    p.op("tensor", "matmul", ps_mm[bank][:],
                         wtiles["wr"][kc][:, 128 * oc:128 * (oc + 1)],
                         xr16[kc][:], start=(kc == 0), stop=(kc == 7),
                         inc="pe" if kc == 7 else None)
                p.wait("scalar", "pe")
                p.op("scalar", "activation", rsig[oc][:], ps_mm[bank][:],
                     AF.Sigmoid, bias=0.0, scale=1.0, inc="ac")
                rec["sig_evac"][bank] = p.val("ac")
            for oc in range(8):
                bank = oc % 2
                p.wait("tensor", "dv", rec["v_evac"][bank])
                for kc in range(8):
                    p.op("tensor", "matmul", ps_v[bank][:],
                         wtiles["wk"][kc][:, 128 * oc:128 * (oc + 1)],
                         xk16[kc][:], start=(kc == 0), stop=(kc == 7),
                         inc="pe" if kc == 7 else None)
                p.wait("vector", "pe")
                if oc >= 2:
                    p.wait("vector", "ac", rec[f"ksq_{bank}"])
                p.op("vector", "tensor_scalar", kr[bank][:], ps_v[bank][:],
                     0.0, 0.0, ALU.max, ALU.add, inc="dv")
                rec["v_evac"][bank] = p.val("dv")
                p.wait("scalar", "dv")
                p.op("scalar", "activation", k16a[oc][:], kr[bank][:],
                     AF.Square, bias=0.0, scale=1.0, inc="ac")
                rec[f"ksq_{bank}"] = p.val("ac")
            ksq_all = p.val("ac")
            for oc in range(8):
                bank = oc % 2
                p.wait("tensor", "ac", ksq_all)
                p.wait("tensor", "dv", rec.get(f"fin_{bank}", 0))
                for kc in range(8):
                    p.op("tensor", "matmul", ps_mm[bank][:],
                         wtiles["wv"][kc][:, 128 * oc:128 * (oc + 1)],
                         k16a[kc][:], start=(kc == 0), stop=(kc == 7),
                         inc="pe" if kc == 7 else None)
                p.wait("vector", "pe")
                if f"store_{bank}" in rec:
                    p.wait("vector", f"st{bank}", rec[f"store_{bank}"])
                p.op("vector", "tensor_tensor", vout[bank][:],
                     ps_mm[bank][:], rsig[oc][:], ALU.mult)
                p.op("vector", "tensor_tensor", vout[bank][:], vout[bank][:],
                     xch[oc][buf][:, 1:W + 1], ALU.add, inc="dv")
                rec[f"fin_{bank}"] = p.val("dv")
                p.wait("sync", "dv")
                p.dma("sync", xo[128 * oc:128 * (oc + 1), t0:t0 + W],
                      vout[bank][:], inc=f"st{bank}")
                rec[f"store_{bank}"] = p.val(f"st{bank}")
            rec[f"st_tile0_{i}"] = p.val("st0")
            rec[f"st_tile1_{i}"] = p.val("st1")
        p.wait("gpsimd", "st0", p.val("st0"))
        p.wait("gpsimd", "st1", p.val("st1"))

    if fused:
        return body
    emit_all(p, body)
    return nc



def l1_ffn_inmap(x_fm_b, inp):
    return {
        "x": np.ascontiguousarray(x_fm_b),
        "wr_t": np.ascontiguousarray(np.asarray(inp["L1_ffn_W_r"]).T),
        "wk_t": np.ascontiguousarray(np.asarray(inp["L1_ffn_W_k"]).T),
        "wv_t": np.ascontiguousarray(np.asarray(inp["L1_ffn_W_v"]).T),
        "mu_r": np.ascontiguousarray(
            np.asarray(inp["L1_ffn_mu_r"]).reshape(8, 128).T),
        "mu_k": np.ascontiguousarray(
            np.asarray(inp["L1_ffn_mu_k"]).reshape(8, 128).T),
    }


def run_l1_ffn(x_fm, inp):
    nc = build_l1_ffn()
    in_maps = [l1_ffn_inmap(x_fm[b], inp) for b in range(NCORES)]
    res = run_bass_kernel_spmd(nc, in_maps, core_ids=list(range(NCORES)))
    return np.stack([r["xo"] for r in res.results])


ARENA_W = 106416


def build_fused():
    nc = bass.Bass("TRN2", target_bir_lowering=False)
    x = nc.dram_tensor("x", [D, T], F32, kind="ExternalInput").ap()
    xo = nc.dram_tensor("xo", [D, T], F32, kind="ExternalOutput").ap()
    x1 = nc.dram_tensor("x1_d", [D, T], F32, kind="Internal").ap()
    x2 = nc.dram_tensor("x2_d", [D, T], F32, kind="Internal").ap()
    x3 = nc.dram_tensor("x3_d", [D, T], F32, kind="Internal").ap()
    ctx = ExitStack()
    with ctx:
        p = P(nc, ctx)
        p.arena = ctx.enter_context(
            nc.sbuf_tensor("arena", [128, ARENA_W], BF16))
        S1 = ctx.enter_context(nc.psum_tensor("S1", [128, 1024], F32))
        S2 = ctx.enter_context(nc.psum_tensor("S2", [128, 1024], F32))
        S3 = ctx.enter_context(nc.psum_tensor("S3", [128, 1024], F32))
        S4 = ctx.enter_context(nc.psum_tensor("S4", [128, 512], F32))
        S5 = ctx.enter_context(nc.psum_tensor("S5", [128, 512], F32))
        mapA = {"ps1": S1[:, 0:512], "ps2": S1[:, 512:1024],
                "ps3": S2[:, 0:512], "ps4": S2[:, 512:1024],
                "ps5": S3[:, 0:512], "ps6": S3[:, 512:1024],
                "ps7": S4[:], "pstr": S5[:].bitcast(BF16)}
        mapB = {"ps_a": S1[:, 0:512], "ps_b": S1[:, 512:1024],
                "ps_c": S2[:, 0:512], "ps_d": S2[:, 512:1024],
                "ps_n": S4[:], "ps_s2": S3[:], "ps_e": S5[:]}
        mapC = {"ps_qk0": S4[:], "ps_qk1": S5[:], "ps_s0": S1[:],
                "ps_s1": S2[:], "ps_y": S3[:]}
        mapD = {"ps_sum": S4[0:1, :], "ps_sq": S5[0:1, :],
                "ps_b1": S1[:, 0:512], "ps_b2": S1[:, 512:1024],
                "ps_mm0": S2[:, 0:512], "ps_mm1": S2[:, 512:1024],
                "ps_v0": S3[:, 0:512], "ps_v1": S3[:, 512:1024]}
        blocks = [("A_", build_l0_attn, mapA, x, x1),
                  ("B_", build_l0_ffn, mapB, x1, x2),
                  ("C_", build_l1_attn, mapC, x2, x3),
                  ("D_", build_l1_ffn, mapD, x3, xo)]
        bodies = []
        for pfx, bld, psmap, xi, xout in blocks:
            p.set_block(pfx, psmap)
            proxy = NCProxy(nc, pfx, xi, xout)
            bodies.append(bld(fused={"nc": proxy, "p": p}))
            assert p.arena_off <= 2 * ARENA_W, \
                f"{pfx} arena overflow: {p.arena_off} B"
            if os.environ.get("ARENA_DEBUG"):
                print(f"arena {pfx}: {p.arena_off} / {2 * ARENA_W} B")

        def whole():
            for i, (pfx, bld, psmap, xi, xout) in enumerate(blocks):
                p.set_block(pfx, psmap)
                if i:
                    p.barrier(blocks[i - 1][0])
                bodies[i]()

        emit_all(p, whole)
    return nc


def fused_inmap(x_fm_b, inp):
    m = {"x": np.ascontiguousarray(x_fm_b)}
    for pfx, f in (("A_", l0_attn_inmap), ("B_", l0_ffn_inmap),
                   ("C_", l1_attn_inmap), ("D_", l1_ffn_inmap)):
        for k, v in f(x_fm_b, inp).items():
            if k != "x":
                m[pfx + k] = v
    return m


def kernel(**inp):
    x = np.asarray(inp["L0_attn_x"], dtype=np.float32)
    x_fm = np.ascontiguousarray(np.transpose(x, (0, 2, 1)))
    nc = build_fused()
    in_maps = [fused_inmap(x_fm[b], inp) for b in range(NCORES)]
    res = run_bass_kernel_spmd(nc, in_maps, core_ids=list(range(NCORES)))
    out = np.stack([r["xo"] for r in res.results])
    return np.ascontiguousarray(np.transpose(out, (0, 2, 1)))


def kernel_unfused(**inp):
    x = np.asarray(inp["L0_attn_x"], dtype=np.float32)
    x_fm = np.ascontiguousarray(np.transpose(x, (0, 2, 1)))
    x_fm = run_l0_attn(x_fm, inp)
    x_fm = run_l0_ffn(x_fm, inp)
    x_fm = run_l1_attn(x_fm, inp)
    x_fm = run_l1_ffn(x_fm, inp)
    return np.ascontiguousarray(np.transpose(x_fm, (0, 2, 1)))



def build_l1_attn(fused=None):
    nc = fused["nc"] if fused else bass.Bass("TRN2", target_bir_lowering=False)
    x = nc.dram_tensor("x", [D, T], F32, kind="ExternalInput").ap()
    wq = nc.dram_tensor("wq_t", [64, 64], F32, kind="ExternalInput").ap()
    wk = nc.dram_tensor("wk_t", [64, 64], F32, kind="ExternalInput").ap()
    wv = nc.dram_tensor("wv_t", [64, 64], F32, kind="ExternalInput").ap()
    wu = nc.dram_tensor("wu_t", [D, D], F32, kind="ExternalInput").ap()
    bu = nc.dram_tensor("bu", [128, 8], F32, kind="ExternalInput").ap()
    masks = nc.dram_tensor("masks", [128, 1920], F32,
                           kind="ExternalInput").ap()
    sel = nc.dram_tensor("sel", [64, 31], F32, kind="ExternalInput").ap()
    iden = nc.dram_tensor("iden", [16, 16], F32, kind="ExternalInput").ap()
    xo = nc.dram_tensor("xo", [D, T], F32, kind="ExternalOutput").ap()
    yt_d = nc.dram_tensor("yt_scr", [D, T], BF16, kind="Internal").ap()

    NT = 8
    W = 512
    SCALE = float(D) ** -0.25

    p = fused["p"] if fused else P(nc, ExitStack())
    wq_s = p.sbuf("wq_s", [128, 64], BF16)
    wk_s = p.sbuf("wk_s", [128, 64], BF16)
    wv_s = p.sbuf("wv_s", [128, 64], BF16)
    wut = [p.sbuf(f"wut_{kc}", [128, D], BF16) for kc in range(8)]
    bu_s = p.sbuf("bu_s", [128, 8], F32)
    maskm = p.sbuf("maskm", [128, 1920], BF16)
    mask_s = [maskm[:, 896 - 128 * m:1920 - 128 * m] for m in range(8)]
    sel_s = p.sbuf("sel_s", [64, 31], BF16)
    iden_s = p.sbuf("iden_s", [16, 16], F32)
    ones_col = p.sbuf("ones_col", [128, 1], F32)
    ones_row = p.sbuf("ones_row", [1, 128], F32)
    eps_col = p.sbuf("eps_col", [128, 1], F32)

    xln16 = [p.sbuf(f"xln16_{c}", [128, T], BF16) for c in range(8)]
    qT = p.sbuf("qT", [65, T], BF16)
    kT = p.sbuf("kT", [65, T], BF16)
    vh = p.sbuf("vh", [128, 32 * 65], BF16)
    ptile = [p.sbuf(f"pt_{b}", [128, 1024], BF16) for b in range(2)]
    sq64 = [p.sbuf(f"sq64_{b}", [64, W], BF16) for b in range(2)]
    nq = p.sbuf("nq", [16, T], BF16)
    nk = p.sbuf("nk", [16, T], BF16)
    nkm = p.sbuf("nkm", [16, 1], F32)
    nkr = p.sbuf("nkr", [1, 16], F32)
    kmax1 = p.sbuf("kmax1", [1, 1], F32)
    kmaxb = p.sbuf("kmaxb", [16, 1], F32)
    ball = p.sbuf("ball", [16, T], BF16)
    ytile = p.sbuf("ytile", [64, 1024], BF16)
    rl = p.sbuf("rl", [1, 1024], F32)
    rb64 = p.sbuf("rb64", [64, 1024], F32)

    xch = [[p.sbuf(f"x_{c}_{b}", [128, W], F32) for b in range(2)]
           for c in range(8)]
    sq = [p.sbuf(f"sq_{b}", [128, W], F32) for b in range(2)]
    rb = p.sbuf("rb", [128, W], F32)
    mrb = p.sbuf("mrb", [128, W], F32)
    m_r = p.sbuf("m_r", [1, W], F32)
    q_r = p.sbuf("q_r", [1, W], F32)
    m2_r = p.sbuf("m2_r", [1, W], F32)
    var_r = p.sbuf("var_r", [1, W], F32)
    lnv_r = p.sbuf("lnv_r", [1, W], F32)
    rs_r = p.sbuf("rs_r", [1, W], F32)
    mr_r = p.sbuf("mr_r", [1, W], F32)
    yld = [[xln16[c][:, W * b:W * (b + 1)] for b in range(2)]
           for c in range(8)]
    vout = [rb, mrb]

    ps_qk = [p.psum(f"ps_qk{i}", [128, W]) for i in range(2)]
    ps_s = [p.psum(f"ps_s{i}", [128, 1024]) for i in range(2)]
    ps_y = p.psum("ps_y", [128, 1024])
    ps_sum = ps_s[0]
    ps_sq = ps_s[1]
    ps_b1 = ps_y
    ps_b2 = ps_y

    for s in ("ldc", "ldw", "lda", "ldb", "ldya", "ldyb", "pe", "dv",
              "ac", "st0", "st1", "sty", "qrow"):
        p.sem(s)

    def body():
        for kc in range(8):
            p.dma("gpsimd", wut[kc][:], wu[128 * kc:128 * (kc + 1), :],
                  inc="ldw")
        for wdst, wsrc in ((wq_s, wq), (wk_s, wk), (wv_s, wv)):
            p.dma("gpsimd", wdst[0:64, :], wsrc, inc="ldw")
            p.dma("gpsimd", wdst[64:128, :], wsrc, inc="ldw")
        p.dma("gpsimd", sel_s[:], sel, inc="ldw")
        p.dma("gpsimd", maskm[:], masks, inc="ldw")
        p.dma("sync", bu_s[:], bu, inc="ldc")
        p.dma("sync", iden_s[:], iden, inc="ldc")
        p.op("vector", "memset", ones_col[:], 1.0)
        p.op("vector", "memset", ones_row[:], 1.0)
        p.op("vector", "memset", eps_col[:], 1e-5)
        p.op("vector", "memset", vh[:], 1.0)
        p.op("vector", "memset", kT[64:65, :], 1.0)
        p.op("vector", "memset", nq[:], 0.0)
        p.op("vector", "memset", nk[:], 0.0, inc="dv")
        memset_done = p.val("dv")

        rec = {"rows_read_psum": 0, "copies_done": 0}

        for i in range(NT):
            buf = i % 2
            t0 = i * W
            ldsem = "lda" if buf == 0 else "ldb"
            if i >= 2:
                p.wait("sync", "dv", rec[f"xln_done_{i-2}"])
            for c in range(8):
                r0 = 128 * c
                p.dma("sync", xch[c][buf][:], x[r0:r0 + 128, t0:t0 + W],
                      inc=ldsem)
            ld_tile = p.val(ldsem)
            p.wait("scalar", ldsem, ld_tile)
            p.wait("scalar", "dv", rec.get(f"xln_done_{i-1}", 0))
            p.wait("tensor", "dv", max(rec["rows_read_psum"],
                                       memset_done))
            p.wait("tensor", ldsem, ld_tile)
            pe_after_chunk = []
            for c in range(8):
                if c >= 2:
                    p.wait("scalar", "pe", pe_after_chunk[c - 2])
                p.op("scalar", "activation", sq[c % 2][:], xch[c][buf][:],
                     AF.Square, bias=0.0, scale=1.0, inc="ac")
                p.wait("tensor", "ac")
                p.op("tensor", "matmul", ps_sum[0:1, 0:W], ones_col[:],
                     xch[c][buf][:], start=(c == 0), stop=(c == 7))
                p.op("tensor", "matmul", ps_sq[0:1, 0:W], ones_col[:],
                     sq[c % 2][:],
                     start=(c == 0), stop=(c == 7), inc="pe")
                pe_after_chunk.append(p.val("pe"))
            p.wait("vector", "pe")
            p.op("vector", "tensor_scalar", m_r[:], ps_sum[0:1, 0:W],
                 1.0 / D, 0.0, ALU.mult, ALU.add)
            p.op("vector", "tensor_scalar", q_r[:], ps_sq[0:1, 0:W],
                 1.0 / D, 0.0, ALU.mult, ALU.add, inc="dv")
            rec["rows_read_psum"] = p.val("dv")
            p.op("vector", "tensor_tensor", m2_r[:], m_r[:], m_r[:],
                 ALU.mult)
            p.op("vector", "tensor_tensor", var_r[:], q_r[:], m2_r[:],
                 ALU.subtract, inc="dv")
            p.wait("scalar", "dv")
            p.op("scalar", "activation", lnv_r[:], var_r[:], AF.Ln,
                 bias=eps_col[0:1, 0:1], scale=1.0)
            p.op("scalar", "activation", rs_r[:], lnv_r[:], AF.Exp,
                 bias=0.0, scale=-0.5, inc="ac")
            p.wait("vector", "ac")
            p.op("vector", "tensor_tensor", mr_r[:], m_r[:], rs_r[:],
                 ALU.mult, inc="dv")
            p.wait("tensor", "dv")
            p.wait("tensor", "ac", rec["copies_done"])
            p.op("tensor", "matmul", ps_b1[0:128, 0:W], ones_row[:],
                 rs_r[:], start=True, stop=True)
            p.op("tensor", "matmul", ps_b2[0:128, W:2 * W], ones_row[:],
                 mr_r[:], start=True, stop=True, inc="pe")
            p.wait("scalar", "pe")
            p.op("scalar", "copy", rb[:], ps_b1[0:128, 0:W], inc="ac")
            p.op("scalar", "copy", mrb[:], ps_b2[0:128, W:2 * W],
                 inc="ac")
            rec["copies_done"] = p.val("ac")
            p.wait("vector", "ac")
            for c in range(8):
                p.op("vector", "tensor_tensor", sq[c % 2][:],
                     xch[c][buf][:], rb[:], ALU.mult)
                p.op("vector", "tensor_tensor",
                     xln16[c][:, t0:t0 + W], sq[c % 2][:], mrb[:],
                     ALU.subtract, inc="dv" if c == 7 else None)
            rec[f"xln_done_{i}"] = p.val("dv")
        xln_all = p.val("dv")

        p.wait("tensor", "ldw", p.val("ldw"))
        p.wait("tensor", "ldc", 32)
        p.wait("tensor", "dv", xln_all)
        for j in range(NT):
            t0 = j * W
            p.wait("tensor", "dv", rec.get("norm_read", 0))
            for h in range(16):
                ch, half = h // 2, (h % 2) * 64
                for qk, (wmat, ps_norm) in enumerate(
                        ((wq_s, ps_b1), (wk_s, ps_b2))):
                    bank = (2 * h + qk) % 2
                    p.wait("tensor", "ac", p.val("ac"))
                    p.op("tensor", "matmul", ps_qk[bank][0:64, :],
                         wmat[half:half + 64, :],
                         xln16[ch][half:half + 64, t0:t0 + W],
                         start=True, stop=True, inc="pe")
                    p.wait("scalar", "pe")
                    p.op("scalar", "activation", sq64[bank][:],
                         ps_qk[bank][0:64, :], AF.Square, bias=0.0,
                         scale=SCALE, inc="ac")
                    p.wait("tensor", "ac")
                    nsl = (slice(0, 16), slice(0, W)) if qk == 0 else \
                        (slice(0, 16), slice(W, 2 * W))
                    p.op("tensor", "matmul", ps_norm[nsl[0], nsl[1]],
                         sel_s[:, 15 - h:31 - h], sq64[bank][:],
                         start=(h == 0), stop=(h == 15),
                         inc="pe" if h == 15 else None)
            p.wait("vector", "pe")
            p.op("vector", "tensor_copy", nq[:, t0:t0 + W],
                 ps_b1[0:16, 0:W])
            p.op("vector", "tensor_copy", nk[:, t0:t0 + W],
                 ps_b2[0:16, W:2 * W], inc="dv")
            rec["norm_read"] = p.val("dv")
        p.op("vector", "tensor_reduce", nkm[:], nk[:], AX.X, ALU.max,
             inc="dv")
        p.wait("tensor", "dv")
        p.wait("tensor", "ac", p.val("ac"))
        p.op("tensor", "matmul", ps_qk[0][0:1, 0:16], nkm[:], iden_s[:],
             start=True, stop=True, inc="pe")
        p.wait("vector", "pe")
        p.op("vector", "tensor_copy", nkr[:], ps_qk[0][0:1, 0:16])
        p.op("vector", "tensor_reduce", kmax1[:], nkr[:], AX.X, ALU.max,
             inc="dv")
        p.wait("tensor", "dv")
        p.op("tensor", "matmul", ps_qk[1][0:16, 0:1], ones_row[0:1, 0:16],
             kmax1[:], start=True, stop=True, inc="pe")
        p.wait("vector", "pe")
        p.op("vector", "tensor_copy", kmaxb[:], ps_qk[1][0:16, 0:1])
        p.op("vector", "tensor_scalar", nq[:], nq[:], kmaxb[:, 0:1], 0.0,
             ALU.mult, ALU.add, inc="dv")
        p.wait("scalar", "dv")
        p.op("scalar", "activation", nk[:], nq[:], AF.Sqrt, bias=0.0,
             scale=1.0, inc="ac")
        p.wait("vector", "ac")
        p.op("vector", "tensor_scalar", ball[:], nk[:], -1.0, 0.0,
             ALU.mult, ALU.add, inc="dv")
        ball_done = p.val("dv")

        rec["pv_done"] = 0
        rec["exp_done"] = [0, 0]
        rec["mask_done"] = [0, 0]
        rec["pv_par"] = [0, 0]
        rec["y_evac"] = 0
        rec["pj_ac0"] = rec["pj_ac1"] = p.val("ac")
        rec["pj_dv0"] = rec["pj_dv1"] = p.val("dv")
        for h in range(16):
            ch, half = h // 2, (h % 2) * 64
            p.wait("sync", "dv", ball_done)
            p.wait("sync", "pe", rec["pv_done"])
            p.dma("sync", qT[64:65, :], ball[h:h + 1, :], inc="qrow")
            p.wait("scalar", "pe", rec["pv_done"])
            p.wait("vector", "pe", rec["pv_done"])
            nxt = 0
            for j in range(8):
                t0 = j * W
                for (wmat, dstT) in ((wq_s, qT), (wk_s, kT)):
                    bk = nxt
                    nxt ^= 1
                    p.wait("tensor", "ac", rec.get(f"pj_ac{bk}", 0))
                    p.wait("tensor", "dv", rec.get(f"pj_dv{bk}", 0))
                    p.op("tensor", "matmul", ps_qk[bk][0:64, :],
                         wmat[half:half + 64, :],
                         xln16[ch][half:half + 64, t0:t0 + W],
                         start=True, stop=True, inc="pe")
                    p.wait("scalar", "pe")
                    p.op("scalar", "activation", dstT[0:64, t0:t0 + W],
                         ps_qk[bk][0:64, :], AF.Copy, bias=0.0,
                         scale=SCALE, inc="ac")
                    rec[f"pj_ac{bk}"] = p.val("ac")
                for s4 in range(4):
                    st = t0 + 128 * s4
                    sidx = j * 4 + s4
                    bk = nxt
                    nxt ^= 1
                    p.wait("tensor", "ac", rec.get(f"pj_ac{bk}", 0))
                    p.wait("tensor", "dv", rec.get(f"pj_dv{bk}", 0))
                    p.op("tensor", "matmul", ps_qk[bk][0:128, 0:64],
                         xln16[ch][half:half + 64, st:st + 128],
                         wv_s[half:half + 64, :], start=True, stop=True,
                         inc="pe")
                    p.wait("vector", "pe")
                    p.op("vector", "tensor_copy",
                         vh[:, 65 * sidx:65 * sidx + 64],
                         ps_qk[bk][0:128, 0:64], inc="dv")
                    rec[f"pj_dv{bk}"] = p.val("dv")
            qkv_done = p.val("dv")
            ac_qk = p.val("ac")
            p.wait("tensor", "qrow")
            p.wait("tensor", "dv", qkv_done)
            p.wait("tensor", "ac", ac_qk)

            for j4 in range(4):
                q0 = 1024 * j4
                nkc = 8 * j4 + 8
                exps_at = {}

                def emit_pv(kcp):
                    parp = kcp % 2
                    p.wait("tensor", "ac", exps_at[kcp])
                    if kcp == 0:
                        p.wait("tensor", "dv", rec["y_evac"])
                    for hf in range(2):
                        p.op("tensor", "matmul",
                             ps_y[0:65, 512 * hf:512 * (hf + 1)],
                             vh[:, 65 * kcp:65 * (kcp + 1)],
                             ptile[parp][:, 512 * hf:512 * (hf + 1)],
                             start=(kcp == 0), stop=(kcp == nkc - 1),
                             inc="pe" if hf == 1 else None)
                    rec["pv_par"][parp] = p.val("pe")

                for kc in range(nkc):
                    par = kc % 2
                    k0 = 128 * kc
                    p.wait("tensor", "dv", rec["mask_done"][par])
                    p.wait("tensor", "ac", rec["exp_done"][par])
                    for hf in range(2):
                        p.op("tensor", "matmul",
                             ps_s[par][:, 512 * hf:512 * (hf + 1)],
                             kT[:, k0:k0 + 128],
                             qT[:, q0 + 512 * hf:q0 + 512 * (hf + 1)],
                             start=True, stop=True,
                             inc="pe" if hf == 1 else None)
                    s_done = p.val("pe")
                    m = kc - 8 * j4
                    if 0 <= m < 8:
                        p.wait("vector", "pe", s_done)
                        p.op("vector", "tensor_tensor", ps_s[par][:],
                             ps_s[par][:], mask_s[m], ALU.add,
                             inc="dv")
                        rec["mask_done"][par] = p.val("dv")
                        p.wait("scalar", "dv", rec["mask_done"][par])
                    else:
                        p.wait("scalar", "pe", s_done)
                    p.wait("scalar", "pe", rec["pv_par"][par])
                    p.op("scalar", "activation", ptile[par][:],
                         ps_s[par][:], AF.Exp, bias=0.0, scale=1.0,
                         inc="ac")
                    rec["exp_done"][par] = p.val("ac")
                    exps_at[kc] = p.val("ac")
                    if kc >= 1:
                        emit_pv(kc - 1)
                emit_pv(nkc - 1)
                rec["pv_done"] = p.val("pe")
                rec["vh_rd"] = rec["pv_done"]
                p.wait("vector", "pe", rec["pv_done"])
                p.op("vector", "reciprocal", rl[0:1, 0:512],
                     ps_y[64:65, 0:512])
                p.op("vector", "reciprocal", rl[0:1, 512:1024],
                     ps_y[64:65, 512:1024], inc="dv")
                p.wait("tensor", "dv")
                p.wait("tensor", "ac", p.val("ac"))
                for hf in range(2):
                    p.op("tensor", "matmul", ps_qk[hf][0:64, :],
                         ones_row[0:1, 0:64],
                         rl[0:1, 512 * hf:512 * (hf + 1)],
                         start=True, stop=True, inc="pe")
                p.wait("scalar", "pe")
                p.op("scalar", "copy", rb64[:, 0:512], ps_qk[0][0:64, :],
                     inc="ac")
                p.op("scalar", "copy", rb64[:, 512:1024],
                     ps_qk[1][0:64, :], inc="ac")
                p.wait("vector", "ac")
                p.op("vector", "tensor_tensor", ytile[:],
                     ps_y[0:64, :], rb64[:], ALU.mult, inc="dv")
                rec["y_evac"] = p.val("dv")
                rec["pj_ac0"] = rec["pj_ac1"] = p.val("ac")
                p.wait("sync", "dv", rec["y_evac"])
                p.dma("sync", yt_d[64 * h:64 * h + 64, q0:q0 + 1024],
                      ytile[:], inc="sty")
                p.wait("vector", "sty")

        rec2 = {}
        for i in range(NT):
            buf = i % 2
            t0 = i * W
            ldsem = "lda" if buf == 0 else "ldb"
            ldy = "ldya" if buf == 0 else "ldyb"
            if i == 0:
                p.wait("sync", "sty", p.val("sty"))
            if i >= 2:
                p.wait("sync", "st0", rec2[f"st0_{i-2}"])
                p.wait("sync", "st1", rec2[f"st1_{i-2}"])
            for c in range(8):
                r0 = 128 * c
                p.dma("sync", xch[c][buf][:], x[r0:r0 + 128, t0:t0 + W],
                      inc=ldsem)
                p.dma("sync", yld[c][buf],
                      yt_d[r0:r0 + 128, t0:t0 + W], inc=ldy)
            ld_tile = p.val(ldsem)
            ldy_tile = p.val(ldy)
            p.wait("tensor", ldsem, ld_tile)
            p.wait("tensor", ldy, ldy_tile)
            if i == 0:
                p.wait("vector", "ldc", 32)
            for oc in range(8):
                bank = oc % 2
                p.wait("tensor", "dv", rec2.get(f"fin_{bank}", 0))
                for kc in range(8):
                    p.op("tensor", "matmul", ps_qk[bank][:],
                         wut[kc][:, 128 * oc:128 * (oc + 1)],
                         yld[kc][buf], start=(kc == 0), stop=(kc == 7),
                         inc="pe" if kc == 7 else None)
                p.wait("vector", "pe")
                if f"store_{bank}" in rec2:
                    p.wait("vector", f"st{bank}", rec2[f"store_{bank}"])
                p.op("vector", "scalar_tensor_tensor", vout[bank][:],
                     ps_qk[bank][:], bu_s[:, oc:oc + 1], xch[oc][buf][:],
                     ALU.add, ALU.add, inc="dv")
                rec2[f"fin_{bank}"] = p.val("dv")
                p.wait("sync", "dv")
                p.dma("sync", xo[128 * oc:128 * (oc + 1), t0:t0 + W],
                      vout[bank][:], inc=f"st{bank}")
                rec2[f"store_{bank}"] = p.val(f"st{bank}")
            rec2[f"st0_{i}"] = p.val("st0")
            rec2[f"st1_{i}"] = p.val("st1")
        p.wait("gpsimd", "st0", p.val("st0"))
        p.wait("gpsimd", "st1", p.val("st1"))

    if fused:
        return body
    emit_all(p, body)
    return nc


def l1_attn_consts():
    r = np.arange(128)[:, None]
    u = np.arange(1920)[None, :]
    masks = np.where(u < r + 896, -1e38, 0.0).astype(np.float32)
    sel = np.zeros((64, 31), dtype=np.float32)
    sel[:, 15] = 1.0
    iden = np.eye(16, dtype=np.float32)
    return masks, sel, iden


def l1_attn_inmap(x_fm_b, inp):
    masks, sel, iden = l1_attn_consts()
    return {
        "x": np.ascontiguousarray(x_fm_b),
        "wq_t": np.ascontiguousarray(np.asarray(inp["L1_attn_Wq"]).T),
        "wk_t": np.ascontiguousarray(np.asarray(inp["L1_attn_Wk"]).T),
        "wv_t": np.ascontiguousarray(np.asarray(inp["L1_attn_Wv"]).T),
        "wu_t": np.ascontiguousarray(np.asarray(inp["L1_attn_Wu"]).T),
        "bu": np.ascontiguousarray(
            np.asarray(inp["L1_attn_bu"]).reshape(8, 128).T),
        "masks": masks, "sel": sel, "iden": iden,
    }


def run_l1_attn(x_fm, inp):
    nc = build_l1_attn()
    in_maps = [l1_attn_inmap(x_fm[b], inp) for b in range(NCORES)]
    res = run_bass_kernel_spmd(nc, in_maps, core_ids=list(range(NCORES)))
    return np.stack([r["xo"] for r in res.results])



def build_l0_ffn(debug=False, dbg_lay=0, fused=None):
    nc = fused["nc"] if fused else bass.Bass("TRN2", target_bir_lowering=False)
    x = nc.dram_tensor("x", [D, T], F32, kind="ExternalInput").ap()
    wqkv = nc.dram_tensor("wqkv", [2, 8, 128, 1536], F32,
                          kind="ExternalInput").ap()
    wo = nc.dram_tensor("wo", [2, 4, 128, 1024], F32,
                        kind="ExternalInput").ap()
    w1r = nc.dram_tensor("w1r", [2, 21, 128, 2048], F32,
                         kind="ExternalInput").ap()
    w1t = nc.dram_tensor("w1t", [2, 128, 672], F32, kind="ExternalInput").ap()
    w2r = nc.dram_tensor("w2r", [2, 21, 128, 1024], F32,
                         kind="ExternalInput").ap()
    w2t = nc.dram_tensor("w2t", [2, 42, 1024], F32, kind="ExternalInput").ap()
    b1r = nc.dram_tensor("b1r", [2, 128, 44], F32, kind="ExternalInput").ap()
    b2c = nc.dram_tensor("b2c", [2, 128, 8], F32, kind="ExternalInput").ap()
    qsc = nc.dram_tensor("qsc", [2, 128, 1], F32, kind="ExternalInput").ap()
    ksc = nc.dram_tensor("ksc", [2, 128, 1], F32, kind="ExternalInput").ap()
    lgc = nc.dram_tensor("lgc", [2, 128, 8], F32, kind="ExternalInput").ap()
    lbc = nc.dram_tensor("lbc", [2, 128, 8], F32, kind="ExternalInput").ap()
    c1q = nc.dram_tensor("c1q", [128, T], F32, kind="ExternalInput").ap()
    c2q = nc.dram_tensor("c2q", [128, T], F32, kind="ExternalInput").ap()
    c1k = nc.dram_tensor("c1k", [128, T], F32, kind="ExternalInput").ap()
    c2k = nc.dram_tensor("c2k", [128, T], F32, kind="ExternalInput").ap()
    prot = nc.dram_tensor("prot", [128, 128], F32, kind="ExternalInput").ap()
    masklo = nc.dram_tensor("masklo", [128, 128], F32,
                            kind="ExternalInput").ap()
    maskhi = nc.dram_tensor("maskhi", [128, 128], F32,
                            kind="ExternalInput").ap()
    idenf = nc.dram_tensor("idenf", [128, 128], F32,
                           kind="ExternalInput").ap()
    nsel2 = nc.dram_tensor("nsel2", [128, 2], F32, kind="ExternalInput").ap()
    bsel2 = nc.dram_tensor("bsel2", [2, 128], F32, kind="ExternalInput").ap()
    xo = nc.dram_tensor("xo", [D, T], F32, kind="ExternalOutput").ap()
    xb_d = nc.dram_tensor("xb_scr", [D, T], F32,
                          kind="ExternalOutput" if debug else "Internal").ap()
    if debug:
        dbg_xbl = nc.dram_tensor("dbg_xbl", [D, W0 := 512], F32,
                                 kind="ExternalOutput").ap()
        dbg_qrot = nc.dram_tensor("dbg_qrot", [512, 512], F32,
                                  kind="ExternalOutput").ap()
        dbg_krot = nc.dram_tensor("dbg_krot", [512, 512], F32,
                                  kind="ExternalOutput").ap()
        dbg_o = nc.dram_tensor("dbg_o", [512, 512], F32,
                               kind="ExternalOutput").ap()
        dbg_xmid = nc.dram_tensor("dbg_xmid", [D, 512], F32,
                                  kind="ExternalOutput").ap()

    NT = 8
    W = 512

    p = fused["p"] if fused else P(nc, ExitStack())
    wqkv_s = [p.sbuf(f"wqkv_{kc}", [128, 1536], BF16) for kc in range(8)]
    wo_s = [p.sbuf(f"wo_{kc}", [128, 1024], BF16) for kc in range(4)]
    w1sl = [p.sbuf(f"w1sl_{b}", [128, 1024], BF16) for b in range(4)]
    w2sl = [p.sbuf(f"w2sl_{b}", [128, 512], BF16) for b in range(4)]
    lsems = ("ldwfa", "ldwfb", "ldwfc", "ldwfd")
    b1_s = p.sbuf("b1_s", [128, 44], F32)
    b2_s = p.sbuf("b2_s", [128, 8], F32)
    qs_s = p.sbuf("qs_s", [128, 1], F32)
    ks_s = p.sbuf("ks_s", [128, 1], F32)
    lg_s = p.sbuf("lg_s", [128, 8], F32)
    lb_s = p.sbuf("lb_s", [128, 8], F32)
    prot_s = p.sbuf("prot_s", [128, 128], F32)
    mlo_s = p.sbuf("mlo_s", [128, 128], BF16)
    mhi_s = p.sbuf("mhi_s", [128, 128], BF16)
    idf_s = p.sbuf("idf_s", [128, 128], BF16)
    nsel_s = p.sbuf("nsel_s", [128, 2], F32)
    bsel_s = p.sbuf("bsel_s", [2, 128], F32)
    ct = {nm: p.sbuf(f"ct_{nm}", [128, W], F32)
          for nm in ("c1q", "c2q", "c1k", "c2k")}
    ones_col = p.sbuf("ones_col", [128, 1], F32)
    ones_row = p.sbuf("ones_row", [1, 128], F32)
    eps_col = p.sbuf("eps_col", [128, 1], F32)

    xch = [[p.sbuf(f"x_{c}_{b}", [128, W + 1], F32) for b in range(2)]
           for c in range(8)]
    xbl = [p.sbuf(f"xbl_{c}", [128, W + 1], F32) for c in range(8)]
    xh16 = [p.sbuf(f"xh16_{c}", [128, W + 1], BF16) for c in range(8)]
    xmid = [p.sbuf(f"xmid_{c}", [128, W + 1], F32) for c in range(8)]
    hn16 = [xh16[c][:, 0:W] for c in range(8)]
    sq = [p.sbuf(f"sq_{b}", [128, W], F32) for b in range(2)]
    rb = p.sbuf("rb", [128, W], F32)
    mrb = p.sbuf("mrb", [128, W], F32)
    m_r = p.sbuf("m_r", [1, W], F32)
    q_r = p.sbuf("q_r", [1, W], F32)
    m2_r = p.sbuf("m2_r", [1, W], F32)
    var_r = p.sbuf("var_r", [1, W], F32)
    lnv_r = p.sbuf("lnv_r", [1, W], F32)
    rs_r = p.sbuf("rs_r", [1, W], F32)
    mr_r = p.sbuf("mr_r", [1, W], F32)
    qf = p.sbuf("qf", [128, W], F32)
    rn2 = p.sbuf("rn2", [2, W], F32)
    lnn = p.sbuf("lnn", [2, W], F32)
    qhat = p.sbuf("qhat", [128, W], F32)
    glu = p.sbuf("glu", [128, W], F32)
    t1f = glu
    t2f = None
    qrot = [p.sbuf(f"qrot_{c}", [128, W], BF16) for c in range(4)]
    krot = [[p.sbuf(f"krot_{c}_{b}", [128, W], BF16) for b in range(2)]
            for c in range(4)]
    vtok = [[p.sbuf(f"vtok_{s}_{b}", [128, W], BF16) for b in range(2)]
            for s in range(4)]
    ptile = p.sbuf("ptile", [128, 640], BF16)
    mcol = p.sbuf("mcol", [128, 1], F32)
    nmcol = p.sbuf("nmcol", [128, 1], F32)
    lcol = p.sbuf("lcol", [128, 1], F32)
    rlc = p.sbuf("rlc", [128, 1], F32)
    diag = p.sbuf("diag", [128, 128], BF16)
    diag_b = p.sbuf("diag_b", [128, 128], BF16)
    diag2 = [diag, diag_b]
    ptile2 = [ptile, glu[:].bitcast(BF16)[:, 0:640]]
    o16 = qrot
    gat = [p.sbuf(f"gat_{pc}", [128, W], BF16) for pc in range(21)]
    gat_t = p.sbuf("gat_t", [42, W], BF16)
    vout = [p.sbuf(f"vout_{b}", [128, W], F32) for b in range(2)]

    ps_a = p.psum("ps_a", [128, W])
    ps_b = p.psum("ps_b", [128, W])
    ps_c = p.psum("ps_c", [128, W])
    ps_d = p.psum("ps_d", [128, W])
    ps_n = p.psum("ps_n", [128, W])
    ps_s2 = p.psum("ps_s2", [128, 1024])
    ps_e = p.psum("ps_e", [128, W])

    for s in ("ldc", "ldw", "ldwv", "lda", "ldb", "ldwfa", "ldwfb",
              "ldwfc", "ldwfd", "pe", "dv", "ac", "st0", "st1", "sta0",
              "sta1", "xcp", "dbg", "dbgs"):
        p.sem(s)

    def body():
        p.op("vector", "memset", ones_col[:], 1.0)
        p.op("vector", "memset", ones_row[:], 1.0)
        p.op("vector", "memset", eps_col[:], 1e-5, inc="dv")
        memset_done = p.val("dv")
        for dst, srcd in ((prot_s, prot), (idf_s, idenf),
                          (mlo_s, masklo), (mhi_s, maskhi),
                          (nsel_s, nsel2), (bsel_s, bsel2)):
            p.dma("gpsimd", dst[:], srcd, inc="ldc")
        ldc_all = p.val("ldc")

        rec = {"rows_read_psum": 0, "copies_done": 0, "lay_pe": 0,
               "xbl_done": 0}

        for lay in range(2):
            if lay == 1:
                p.wait("gpsimd", "pe", rec["lay_pe"])
                p.wait("sync", "pe", rec["lay_pe"])
                for c in range(8):
                    p.dma("sync", xo[128 * c:128 * c + 128, :],
                          x[128 * c:128 * c + 128, :], inc="xcp")
            for kc in range(8):
                p.dma("gpsimd", wqkv_s[kc][:], wqkv[lay, kc], inc="ldw")
            for kc in range(4):
                p.dma("gpsimd", wo_s[kc][:], wo[lay, kc], inc="ldw")
            for dst, srcd in ((qs_s, qsc[lay]), (ks_s, ksc[lay]),
                              (lg_s, lgc[lay]), (lb_s, lbc[lay]),
                              (b1_s, b1r[lay]), (b2_s, b2c[lay])):
                p.dma("sync", dst[:], srcd, inc="ldwv")
            ldw_lay = p.val("ldw")
            ldwv_lay = p.val("ldwv")

            p.wait("vector", "pe", rec["lay_pe"])
            for c in range(4):
                p.op("vector", "memset", krot[c][1][:], 0.0)
                p.op("vector", "memset", vtok[c][1][:], 0.0)
            for c in range(8):
                p.op("vector", "memset", xh16[c][:, 0:1], 0.0)
                p.op("vector", "memset", xbl[c][:, 0:1], 0.0)
                p.op("vector", "memset", xmid[c][:, 0:1], 0.0,
                     inc="dv" if c == 7 else None)
            lay_memset = p.val("dv")

            for i in range(NT):
                buf = i % 2
                pbuf = 1 - buf
                t0 = i * W
                ldsem = "lda" if buf == 0 else "ldb"
                src_d = x if lay == 0 else xb_d
                sem0 = "st0" if lay == 0 else "sta0"
                sem1 = "st1" if lay == 0 else "sta1"
                if i >= 2:
                    p.wait("sync", sem0, rec[f"{sem0}_{i-2}"])
                    p.wait("sync", sem1, rec[f"{sem1}_{i-2}"])
                elif lay == 1 and i < 2:
                    p.wait("sync", "st0", rec["lay0_st0"])
                    p.wait("sync", "st1", rec["lay0_st1"])
                for c in range(8):
                    r0 = 128 * c
                    if i == 0:
                        p.dma("sync", xch[c][buf][:, 1:W + 1],
                              src_d[r0:r0 + 128, 0:W], inc=ldsem)
                    else:
                        p.dma("sync", xch[c][buf][:, 0:W + 1],
                              src_d[r0:r0 + 128, t0 - 1:t0 + W],
                              inc=ldsem)
                if i >= 1:
                    p.wait("sync", "dv", rec[f"qkrot_{i-1}"])
                for nm, csrc in (("c1q", c1q), ("c2q", c2q),
                                 ("c1k", c1k), ("c2k", c2k)):
                    p.dma("sync", ct[nm][:], csrc[:, t0:t0 + W],
                          inc=ldsem)
                ld_tile = p.val(ldsem)

                if lay == 0:
                    p.wait("scalar", ldsem, ld_tile)
                    p.wait("scalar", "dv", rec["xbl_done"])
                    p.wait("tensor", "dv",
                           max(rec["rows_read_psum"], memset_done,
                               lay_memset))
                    p.wait("tensor", ldsem, ld_tile)
                    pe_after = []
                    for c in range(8):
                        if c >= 2:
                            p.wait("scalar", "pe", pe_after[c - 2])
                        p.op("scalar", "activation", sq[c % 2][:],
                             xch[c][buf][:, 1:W + 1], AF.Square,
                             bias=0.0, scale=1.0, inc="ac")
                        p.wait("tensor", "ac")
                        p.op("tensor", "matmul", ps_a[0:1, :],
                             ones_col[:], xch[c][buf][:, 1:W + 1],
                             start=(c == 0), stop=(c == 7))
                        p.op("tensor", "matmul", ps_b[0:1, :],
                             ones_col[:], sq[c % 2][:],
                             start=(c == 0), stop=(c == 7), inc="pe")
                        pe_after.append(p.val("pe"))
                    p.wait("vector", "pe")
                    p.op("vector", "tensor_scalar", m_r[:], ps_a[0:1, :],
                         1.0 / D, 0.0, ALU.mult, ALU.add)
                    p.op("vector", "tensor_scalar", q_r[:], ps_b[0:1, :],
                         1.0 / D, 0.0, ALU.mult, ALU.add, inc="dv")
                    rec["rows_read_psum"] = p.val("dv")
                    p.op("vector", "tensor_tensor", m2_r[:], m_r[:],
                         m_r[:], ALU.mult)
                    p.op("vector", "tensor_tensor", var_r[:], q_r[:],
                         m2_r[:], ALU.subtract, inc="dv")
                    p.wait("scalar", "dv")
                    p.op("scalar", "activation", lnv_r[:], var_r[:],
                         AF.Ln, bias=eps_col[0:1, 0:1], scale=1.0)
                    p.op("scalar", "activation", rs_r[:], lnv_r[:],
                         AF.Exp, bias=0.0, scale=-0.5, inc="ac")
                    p.wait("vector", "ac")
                    p.op("vector", "tensor_tensor", mr_r[:], m_r[:],
                         rs_r[:], ALU.mult, inc="dv")
                    p.wait("tensor", "dv")
                    p.wait("tensor", "ac", rec["copies_done"])
                    p.op("tensor", "matmul", ps_c[:], ones_row[:],
                         rs_r[:], start=True, stop=True)
                    p.op("tensor", "matmul", ps_d[:], ones_row[:],
                         mr_r[:], start=True, stop=True, inc="pe")
                    p.wait("scalar", "pe")
                    p.op("scalar", "copy", rb[:], ps_c[:], inc="ac")
                    p.op("scalar", "copy", mrb[:], ps_d[:], inc="ac")
                    rec["copies_done"] = p.val("ac")
                    p.wait("vector", "ac")
                    if debug and i == 1 and lay == dbg_lay:
                        p.wait("vector", "dbgs", 256)
                    for c in range(8):
                        if i > 0:
                            p.op("vector", "tensor_copy",
                                 xbl[c][:, 0:1], xbl[c][:, W:W + 1])
                            p.op("vector", "tensor_copy",
                                 xh16[c][:, 0:1], xbl[c][:, 0:1])
                        p.op("vector", "tensor_tensor",
                             sq[c % 2][:], xch[c][buf][:, 1:W + 1],
                             rb[:], ALU.mult)
                        p.op("vector", "tensor_tensor",
                             xbl[c][:, 1:W + 1], sq[c % 2][:], mrb[:],
                             ALU.subtract)
                        p.op("vector", "tensor_copy",
                             xh16[c][:, 1:W + 1], xbl[c][:, 1:W + 1],
                             inc="dv" if c == 7 else None)
                    rec["xbl_done"] = p.val("dv")
                else:
                    p.wait("vector", ldsem, ld_tile)
                    p.wait("vector", "dv", rec["xbl_done"])
                    p.wait("vector", "pe", rec.get("qkv_pe", 0))
                    if debug and i == 1 and lay == dbg_lay:
                        p.wait("vector", "dbgs", 256)
                    for c in range(8):
                        if i > 0:
                            p.op("vector", "tensor_copy",
                                 xbl[c][:, 0:1], xbl[c][:, W:W + 1])
                            p.op("vector", "tensor_copy",
                                 xh16[c][:, 0:1], xbl[c][:, 0:1])
                        p.op("vector", "tensor_copy",
                             xbl[c][:, 1:W + 1],
                             xch[c][buf][:, 1:W + 1])
                        p.op("vector", "tensor_copy",
                             xh16[c][:, 1:W + 1],
                             xch[c][buf][:, 1:W + 1],
                             inc="dv" if c == 7 else None)
                    rec["xbl_done"] = p.val("dv")
                xbl_done = rec["xbl_done"]

                def hsl(c, wdt=W):
                    return xh16[c][:, 1:wdt + 1] if c < 4 else \
                        xh16[c][:, 0:wdt]

                if debug and lay == dbg_lay and i == 1:
                    p.wait("vector", "dbg", 192)
                p.wait("tensor", "dv", xbl_done)
                if i == 0:
                    p.wait("tensor", "ldw", ldw_lay)
                    p.wait("tensor", "ldwv", ldwv_lay)
                    p.wait("tensor", "ldc", ldc_all)
                    p.wait("vector", "ldwv", ldwv_lay)
                    p.wait("scalar", "ldwv", ldwv_lay)
                for of in range(8):
                    isq = of < 4
                    cc = of % 4
                    bank = ps_a if of % 2 == 0 else ps_b
                    p.wait("tensor", "ac", rec.get(f"qk_ev{of % 2}", 0))
                    p.wait("tensor", "dv", rec.get("rot_dn_last", 0))
                    for kc in range(8):
                        p.op("tensor", "matmul", bank[:],
                             wqkv_s[kc][:, 128 * of:128 * of + 128],
                             hsl(kc), start=(kc == 0), stop=(kc == 7),
                             inc="pe" if kc == 7 else None)
                    p.wait("scalar", "pe")
                    p.op("scalar", "copy", qf[:], bank[:], inc="ac")
                    p.wait("scalar", "ac")
                    p.op("scalar", "activation", sq[0][:], qf[:],
                         AF.Square, bias=0.0, scale=1.0, inc="ac")
                    rec[f"qk_ev{of % 2}"] = p.val("ac")
                    p.wait("tensor", "ac", rec[f"qk_ev{of % 2}"])
                    p.op("tensor", "matmul", ps_n[0:2, :], nsel_s[:],
                         sq[0][:], start=True, stop=True, inc="pe")
                    p.wait("vector", "pe")
                    p.op("vector", "tensor_scalar", rn2[:], ps_n[0:2, :],
                         1.0, 1e-24, ALU.mult, ALU.max, inc="dv")
                    p.wait("scalar", "dv")
                    p.op("scalar", "activation", lnn[:], rn2[:], AF.Ln,
                         bias=0.0, scale=1.0)
                    p.op("scalar", "activation", rn2[:], lnn[:], AF.Exp,
                         bias=0.0, scale=-0.5, inc="ac")
                    p.wait("tensor", "ac")
                    p.op("tensor", "matmul", ps_n[:], bsel_s[:], rn2[:],
                         start=True, stop=True, inc="pe")
                    p.wait("vector", "pe")
                    p.op("vector", "tensor_tensor", qhat[:], qf[:],
                         ps_n[:], ALU.mult)
                    p.op("vector", "tensor_scalar", qhat[:], qhat[:],
                         qs_s[:, 0:1] if isq else ks_s[:, 0:1], 0.0,
                         ALU.mult, ALU.add, inc="dv")
                    qhat_dn = p.val("dv")
                    p.wait("tensor", "dv", qhat_dn)
                    p.op("tensor", "matmul", ps_n[:], prot_s[:],
                         qhat[:], start=True, stop=True, inc="pe")
                    c1 = ct["c1q" if isq else "c1k"]
                    c2 = ct["c2q" if isq else "c2k"]
                    p.wait("vector", "pe")
                    p.op("vector", "tensor_tensor", t1f[:], qhat[:],
                         c1[:], ALU.mult)
                    p.op("vector", "tensor_tensor", sq[1][:], ps_n[:],
                         c2[:], ALU.mult)
                    dst = qrot[cc] if isq else krot[cc][buf]
                    p.op("vector", "tensor_tensor", dst[:], t1f[:],
                         sq[1][:], ALU.add, inc="dv")
                    rec[f"rot_dn{of % 2}"] = p.val("dv")
                    rec["rot_dn_last"] = p.val("dv")
                qk_done = p.val("dv")
                rec[f"qkrot_{i}"] = qk_done
                if debug and lay == dbg_lay and i == 0:
                    p.wait("sync", "dv", qk_done)
                    p.wait("gpsimd", "dv", qk_done)
                    for c in range(8):
                        p.dma("sync", dbg_xbl[128 * c:128 * c + 128, :],
                              xbl[c][:, 1:W + 1], inc="dbgs")
                    for c in range(4):
                        p.dma("gpsimd", dbg_qrot[128 * c:128 * c + 128, :],
                              qrot[c][:], inc="dbg")
                        p.dma("gpsimd", dbg_krot[128 * c:128 * c + 128, :],
                              krot[c][buf][:], inc="dbg")
                    rec["dbg_q"] = p.val("dbg")

                for s4 in range(4):
                    bank = ps_a if s4 % 2 == 0 else ps_b
                    p.wait("tensor", "dv",
                           rec.get(f"v_ev{s4 % 2}", 0))
                    for kc in range(8):
                        st = (1 if kc < 4 else 0) + 128 * s4
                        p.op("tensor", "matmul", bank[:],
                             xh16[kc][:, st:st + 128],
                             wqkv_s[kc][:, 1024:1536],
                             start=(kc == 0), stop=(kc == 7),
                             inc="pe" if kc == 7 else None)
                    p.wait("vector", "pe")
                    p.op("vector", "tensor_copy", vtok[s4][buf][:],
                         bank[:], inc="dv")
                    rec[f"v_ev{s4 % 2}"] = p.val("dv")
                v_done = p.val("dv")

                if debug and lay == dbg_lay and i == 0:
                    p.wait("vector", "dbg", rec["dbg_q"])
                p.wait("tensor", "dv", v_done)
                diag_at = {}

                def emit_fold_pv(cp):
                    par = cp % 2
                    hp, s4p = cp // 4, cp % 4
                    hcp, hbp = hp // 2, 64 * (hp % 2)
                    p.wait("tensor", "dv", diag_at[cp])
                    p.wait("tensor", "ac", rec.get("ptr_ev", 0))
                    for uc in range(5):
                        dst = (ps_n[:, 128 * uc:128 * uc + 128]
                               if uc < 4 else ps_e[:, 0:128])
                        p.op("tensor", "matmul", dst,
                             ptile2[par][:, 128 * uc:128 * uc + 128],
                             diag2[par][:], start=True, stop=True,
                             inc="pe" if uc == 4 else None)
                    p.wait("scalar", "pe")
                    p.op("scalar", "copy", ptile2[par][:, 0:512],
                         ps_n[:], inc="ac")
                    p.op("scalar", "copy", ptile2[par][:, 512:640],
                         ps_e[:, 0:128], inc="ac")
                    rec["ptr_ev"] = p.val("ac")
                    p.wait("tensor", "ac", rec["ptr_ev"])
                    p.wait("tensor", "dv", rec.get("y_ev", 0))
                    for uc in range(5):
                        ku = 128 * s4p + 128 * uc
                        vpar = pbuf if ku < W else buf
                        vsub = (ku % W) // 128
                        p.op("tensor", "matmul", ps_d[0:64, 0:128],
                             vtok[vsub][vpar][:, 64 * hp:64 * hp + 64],
                             ptile2[par][:, 128 * uc:128 * uc + 128],
                             start=(uc == 0), stop=(uc == 4),
                             inc="pe" if uc == 4 else None)
                    rec[f"ptwar{par}"] = p.val("pe")
                    p.wait("vector", "pe")
                    p.op("vector", "tensor_copy",
                         o16[hcp][hbp:hbp + 64,
                                  128 * s4p:128 * s4p + 128],
                         ps_d[0:64, 0:128], inc="dv")
                    rec["y_ev"] = p.val("dv")

                for c in range(32):
                    h, s4 = c // 4, c % 4
                    hc, hb = h // 2, 64 * (h % 2)
                    par = c % 2
                    wprev = W - 128 * s4
                    p.wait("tensor", "ac", rec.get("exp_dn", 0))
                    p.wait("tensor", "dv", rec.get("msk_dn", 0))
                    qsl = qrot[hc][hb:hb + 64,
                                   128 * s4:128 * s4 + 128]
                    if i > 0:
                        p.op("tensor", "matmul",
                             ps_s2[:, 0:wprev], qsl,
                             krot[hc][pbuf][hb:hb + 64,
                                            128 * s4:W],
                             start=True, stop=True)
                    spans = ([(wprev, W), (W, 640)]
                             if wprev < W else [(W, 640)])
                    for (lo, hi) in spans:
                        p.op("tensor", "matmul",
                             ps_s2[:, lo:hi], qsl,
                             krot[hc][buf][hb:hb + 64,
                                           lo - wprev:hi - wprev],
                             start=True, stop=True,
                             inc="pe" if hi == 640 else None)
                    sdone = p.val("pe")
                    p.wait("vector", "pe", sdone)
                    if i == 0:
                        p.op("vector", "memset",
                             ps_s2[:, 0:wprev], -1e30)
                    else:
                        p.op("vector", "tensor_tensor",
                             ps_s2[:, 0:128], ps_s2[:, 0:128],
                             mlo_s[:], ALU.add)
                    p.op("vector", "tensor_tensor",
                         ps_s2[:, 512:640], ps_s2[:, 512:640],
                         mhi_s[:], ALU.add)
                    p.op("vector", "tensor_reduce", mcol[:],
                         ps_s2[:, 0:640], AX.X, ALU.max)
                    p.op("vector", "tensor_scalar", nmcol[:],
                         mcol[:], -1.0, 0.0, ALU.mult, ALU.add,
                         inc="dv")
                    rec["msk_dn"] = p.val("dv")
                    p.wait("scalar", "dv", rec["msk_dn"])
                    p.wait("scalar", "pe", rec.get(f"ptwar{par}", 0))
                    p.op("scalar", "activation", ptile2[par][:],
                         ps_s2[:, 0:640], AF.Exp, bias=nmcol[:],
                         scale=1.0, accum_out=lcol[:], inc="ac")
                    rec["exp_dn"] = p.val("ac")
                    p.wait("vector", "ac", rec["exp_dn"])
                    p.op("vector", "reciprocal", rlc[:], lcol[:])
                    p.op("vector", "tensor_scalar", diag2[par][:],
                         idf_s[:], rlc[:, 0:1], 0.0, ALU.mult,
                         ALU.add, inc="dv")
                    diag_at[c] = p.val("dv")
                    if c >= 1:
                        emit_fold_pv(c - 1)
                emit_fold_pv(31)
                attn_done = p.val("dv")
                if debug and lay == dbg_lay and i == 0:
                    p.wait("gpsimd", "dv", attn_done)
                    for c in range(4):
                        p.dma("gpsimd", dbg_o[128 * c:128 * c + 128, :],
                              o16[c][:], inc="dbg")

                p.wait("tensor", "dv", attn_done)
                if debug and i == 1 and lay == dbg_lay:
                    p.wait("vector", "dbgs", p.val("dbgs"))
                for oc in range(8):
                    bank = ps_a if oc % 2 == 0 else ps_b
                    p.wait("tensor", "dv", rec.get(f"xm_dn{oc % 2}", 0))
                    for kc in range(4):
                        p.op("tensor", "matmul", bank[:],
                             wo_s[kc][:, 128 * oc:128 * oc + 128],
                             o16[kc][:], start=(kc == 0), stop=(kc == 3),
                             inc="pe" if kc == 3 else None)
                    p.wait("vector", "pe")
                    if oc < 8:
                        pass
                    p.op("vector", "tensor_copy", xmid[oc][:, 0:1],
                         xmid[oc][:, W:W + 1]) if i > 0 else None
                    p.op("vector", "tensor_tensor",
                         xmid[oc][:, 1:W + 1], bank[:],
                         xbl[oc][:, 1:W + 1], ALU.add,
                         inc="dv")
                    rec[f"xm_dn{oc % 2}"] = p.val("dv")
                xmid_done = p.val("dv")
                if debug and lay == dbg_lay and i == 0:
                    p.wait("sync", "dv", xmid_done)
                    for c in range(8):
                        p.dma("sync", dbg_xmid[128 * c:128 * c + 128, :],
                              xmid[c][:, 1:W + 1], inc="dbgs")

                def h2sl(c, wdt=W):
                    return xmid[c][:, 1:wdt + 1] if c < 4 else \
                        xmid[c][:, 0:wdt]
                p.wait("scalar", "dv", xmid_done)
                p.wait("tensor", "dv", xmid_done)
                pe_after = []
                for c in range(8):
                    if c >= 2:
                        p.wait("scalar", "pe", pe_after[c - 2])
                    p.op("scalar", "activation", sq[c % 2][:], h2sl(c),
                         AF.Square, bias=0.0, scale=1.0, inc="ac")
                    p.wait("tensor", "ac")
                    p.op("tensor", "matmul", ps_a[0:1, :], ones_col[:],
                         h2sl(c), start=(c == 0), stop=(c == 7))
                    p.op("tensor", "matmul", ps_b[0:1, :], ones_col[:],
                         sq[c % 2][:], start=(c == 0), stop=(c == 7),
                         inc="pe")
                    pe_after.append(p.val("pe"))
                p.wait("vector", "pe")
                p.op("vector", "tensor_scalar", m_r[:], ps_a[0:1, :],
                     1.0 / D, 0.0, ALU.mult, ALU.add)
                p.op("vector", "tensor_scalar", q_r[:], ps_b[0:1, :],
                     1.0 / D, 0.0, ALU.mult, ALU.add, inc="dv")
                p.op("vector", "tensor_tensor", m2_r[:], m_r[:], m_r[:],
                     ALU.mult)
                p.op("vector", "tensor_tensor", var_r[:], q_r[:],
                     m2_r[:], ALU.subtract, inc="dv")
                p.wait("scalar", "dv")
                p.op("scalar", "activation", lnv_r[:], var_r[:], AF.Ln,
                     bias=eps_col[0:1, 0:1], scale=1.0)
                p.op("scalar", "activation", rs_r[:], lnv_r[:], AF.Exp,
                     bias=0.0, scale=-0.5, inc="ac")
                p.wait("vector", "ac")
                p.op("vector", "tensor_tensor", mr_r[:], m_r[:], rs_r[:],
                     ALU.mult, inc="dv")
                p.wait("tensor", "dv")
                p.wait("tensor", "ac", rec["copies_done"])
                p.op("tensor", "matmul", ps_c[:], ones_row[:], rs_r[:],
                     start=True, stop=True)
                p.op("tensor", "matmul", ps_d[:], ones_row[:], mr_r[:],
                     start=True, stop=True, inc="pe")
                p.wait("scalar", "pe")
                p.op("scalar", "copy", rb[:], ps_c[:], inc="ac")
                p.op("scalar", "copy", mrb[:], ps_d[:], inc="ac")
                rec["copies_done"] = p.val("ac")
                p.wait("vector", "ac")
                for c in range(8):
                    p.op("vector", "tensor_tensor", sq[c % 2][:],
                         h2sl(c), rb[:], ALU.mult)
                    p.op("vector", "tensor_tensor", sq[c % 2][:],
                         sq[c % 2][:], mrb[:], ALU.subtract)
                    p.op("vector", "tensor_scalar", hn16[c],
                         sq[c % 2][:], lg_s[:, c:c + 1],
                         lb_s[:, c:c + 1], ALU.mult, ALU.add,
                         inc="dv" if c == 7 else None)
                hn_done = p.val("dv")

                p.wait("tensor", "dv", hn_done)
                units = [(pc, ph) for pc in range(21) for ph in range(2)]
                units.append((21, 0))
                for ui, (pc, ph) in enumerate(units):
                    wb = ui % 4
                    lsem = lsems[wb]
                    p.wait("gpsimd", "pe", rec.get(f"w1_used_{wb}", 0))
                    p.wait("gpsimd", lsem, p.val(lsem))
                    if pc < 21:
                        p.dma("gpsimd", w1sl[wb][:],
                              w1r[lay, pc][:, 1024 * ph:1024 * ph + 1024],
                              inc=lsem)
                    else:
                        p.dma("gpsimd", w1sl[wb][:, 0:672],
                              w1t[lay], inc=lsem)
                    ldwf_now = p.val(lsem)
                    par = pc % 2
                    bank_a, bank_g = (ps_a, ps_b) if par == 0 else \
                        (ps_c, ps_d)
                    if ph == 0:
                        p.wait("tensor", "dv", rec.get(f"gate_dn{par}", 0))
                        p.wait("tensor", "ac", rec.get(f"glu_dn{par}", 0))
                    p.wait("tensor", lsem, ldwf_now)
                    if pc < 21:
                        for kk2 in range(4):
                            kc = 4 * ph + kk2
                            p.op("tensor", "matmul", bank_a[:],
                                 w1sl[wb][:, 256 * kk2:256 * kk2 + 128],
                                 hn16[kc], start=(kc == 0),
                                 stop=(kc == 7))
                            p.op("tensor", "matmul", bank_g[:],
                                 w1sl[wb][:, 256 * kk2 + 128:256 * kk2 + 256],
                                 hn16[kc], start=(kc == 0),
                                 stop=(kc == 7),
                                 inc="pe" if kk2 == 3 else None)
                        rec[f"w1_used_{wb}"] = p.val("pe")
                        if ph == 0:
                            continue
                        asl, gsl = bank_a[:], bank_g[:]
                        gdst = gat[pc][:]
                        bia = b1_s[:, 2 * pc:2 * pc + 1]
                        big = b1_s[:, 2 * pc + 1:2 * pc + 2]
                    else:
                        for kc in range(8):
                            p.op("tensor", "matmul", bank_a[0:42, :],
                                 w1sl[wb][:, 84 * kc:84 * kc + 42],
                                 hn16[kc], start=(kc == 0),
                                 stop=(kc == 7))
                            p.op("tensor", "matmul", bank_g[0:42, :],
                                 w1sl[wb][:, 84 * kc + 42:84 * kc + 84],
                                 hn16[kc], start=(kc == 0),
                                 stop=(kc == 7),
                                 inc="pe" if kc == 7 else None)
                        rec[f"w1_used_{wb}"] = p.val("pe")
                        asl, gsl = bank_a[0:42, :], bank_g[0:42, :]
                        gdst = gat_t[:]
                        bia = b1_s[0:42, 42:43]
                        big = b1_s[0:42, 43:44]
                    p.wait("scalar", "pe")
                    p.op("scalar", "activation", gdst, gsl, AF.Gelu,
                         bias=big, scale=1.0, inc="ac")
                    rec[f"glu_dn{par}"] = p.val("ac")
                    p.wait("vector", "ac", rec[f"glu_dn{par}"])
                    p.op("vector", "scalar_tensor_tensor", gdst, asl,
                         bia, gdst, ALU.add, ALU.mult, inc="dv")
                    rec[f"gate_dn{par}"] = p.val("dv")
                gate_all = p.val("dv")

                p.wait("tensor", "dv", gate_all)
                for half in range(2):
                    for pc in range(22):
                        wb = pc % 4
                        lsem = lsems[wb]
                        p.wait("gpsimd", "pe", rec.get(
                            f"w2_used_{wb}", 0))
                        p.wait("gpsimd", lsem, p.val(lsem))
                        if pc < 21:
                            p.dma("gpsimd", w2sl[wb][:],
                                  w2r[lay, pc][:, 512 * half:
                                               512 * half + 512],
                                  inc=lsem)
                            wsl = w2sl[wb]
                            gsl2 = gat[pc][:]
                        else:
                            p.dma("gpsimd", w2sl[wb][0:42, :],
                                  w2t[lay][:, 512 * half:512 * half + 512],
                                  inc=lsem)
                            wsl = w2sl[wb]
                            gsl2 = gat_t[:]
                        ldwf_now = p.val(lsem)
                        p.wait("tensor", lsem, ldwf_now)
                        for q4 in range(4):
                            bankq = [ps_a, ps_b, ps_c, ps_d][q4]
                            if pc == 0:
                                p.wait("tensor", "dv",
                                       rec.get(f"ffn_out{q4}", 0))
                            if pc < 21:
                                p.op("tensor", "matmul", bankq[:],
                                     wsl[:, 128 * q4:128 * q4 + 128],
                                     gsl2, start=(pc == 0), stop=False,
                                     inc="pe" if q4 == 3 else None)
                            else:
                                p.op("tensor", "matmul", bankq[:],
                                     wsl[0:42, 128 * q4:128 * q4 + 128],
                                     gsl2, start=False, stop=True,
                                     inc="pe" if q4 == 3 else None)
                        rec[f"w2_used_{wb}"] = p.val("pe")
                    w2_pe = p.val("pe")
                    p.wait("vector", "pe", w2_pe)
                    if lay == 1 and i == 0:
                        pass
                    for q4 in range(4):
                        oc = 4 * half + q4
                        bankq = [ps_a, ps_b, ps_c, ps_d][q4]
                        sb = oc % 2
                        stsem = sem0 if sb == 0 else sem1
                        if f"store_{stsem}" in rec:
                            p.wait("vector", stsem,
                                   rec[f"store_{stsem}"])
                        p.op("vector", "scalar_tensor_tensor",
                             vout[sb][:], bankq[:],
                             b2_s[:, oc:oc + 1], xmid[oc][:, 1:W + 1],
                             ALU.add, ALU.add, inc="dv")
                        rec[f"ffn_out{q4}"] = p.val("dv")
                        if lay == 0:
                            p.wait("sync", "dv")
                            p.dma("sync",
                                  xb_d[128 * oc:128 * oc + 128,
                                       t0:t0 + W],
                                  vout[sb][:], inc=stsem)
                        else:
                            p.wait("gpsimd", "dv")
                            p.wait("gpsimd", "xcp", 128)
                            p.dma("gpsimd",
                                  xo[128 * oc:128 * oc + 128, t0:t0 + W],
                                  vout[sb][:], accum_op=ALU.add,
                                  inc=stsem)
                        rec[f"store_{stsem}"] = p.val(stsem)
                rec[f"{sem0}_{i}"] = p.val(sem0)
                rec[f"{sem1}_{i}"] = p.val(sem1)
                rec["qkv_pe"] = p.val("pe")
            rec["lay_pe"] = p.val("pe")
            rec["lay0_st0"] = p.val("st0")
            rec["lay0_st1"] = p.val("st1")
            for k in ("st0_0", "st1_0"):
                pass
        p.wait("gpsimd", "st0", p.val("st0"))
        p.wait("gpsimd", "st1", p.val("st1"))
        p.wait("gpsimd", "sta0", p.val("sta0"))
        p.wait("gpsimd", "sta1", p.val("sta1"))
        if debug:
            p.wait("gpsimd", "dbg", p.val("dbg"))
            p.wait("gpsimd", "dbgs", p.val("dbgs"))

    if fused:
        return body
    emit_all(p, body)
    return nc


def l0_ffn_consts():
    half = np.arange(0, 64, 2, dtype=np.float64)
    inv_freq = 1.0 / 10000.0 ** (half / 64.0)
    tpos = np.arange(T, dtype=np.float64)
    freqs = tpos[:, None] * inv_freq[None, :]
    freqs = np.concatenate([freqs, freqs], axis=-1)
    base = (half + 0.4 * 64) / (1.4 * 64)
    power = (tpos - T // 2) / 256.0
    scale = base[None, :] ** power[:, None]
    scale = np.concatenate([scale, scale], axis=-1)
    cos, sin = np.cos(freqs), np.sin(freqs)

    def dup(a):
        a = np.ascontiguousarray(a.T.astype(np.float32))
        return np.concatenate([a, a], axis=0)
    c1q = dup(cos * scale * 8.0)
    c2q = dup(sin * scale * 8.0)
    c1k = dup(cos / scale)
    c2k = dup(sin / scale)
    pr = np.zeros((128, 128), dtype=np.float32)
    for blk in range(2):
        b0 = 64 * blk
        for c in range(32):
            pr[b0 + c + 32, b0 + c] = -1.0
            pr[b0 + c, b0 + c + 32] = 1.0
    r = np.arange(128)[:, None]
    u = np.arange(128)[None, :]
    masklo = np.where(u < r, -1e38, 0.0).astype(np.float32)
    maskhi = np.where(u > r, -1e38, 0.0).astype(np.float32)
    idenf = np.eye(128, dtype=np.float32)
    nsel2 = np.zeros((128, 2), dtype=np.float32)
    nsel2[0:64, 0] = 1.0
    nsel2[64:128, 1] = 1.0
    bsel2 = np.zeros((2, 128), dtype=np.float32)
    bsel2[0, 0:64] = 1.0
    bsel2[1, 64:128] = 1.0
    return dict(c1q=c1q, c2q=c2q, c1k=c1k, c2k=c2k, prot=pr,
                masklo=masklo, maskhi=maskhi, idenf=idenf,
                nsel2=nsel2, bsel2=bsel2)


def l0_ffn_inmap(x_fm_b, inp):
    HID = 2730
    wqkv = np.asarray(inp["L0_ffn_wqkv"], np.float32)
    wo_ = np.asarray(inp["L0_ffn_wo"], np.float32)
    w1 = np.asarray(inp["L0_ffn_w1"], np.float32)
    b1 = np.asarray(inp["L0_ffn_b1"], np.float32)
    w2 = np.asarray(inp["L0_ffn_w2"], np.float32)
    b2 = np.asarray(inp["L0_ffn_b2"], np.float32)
    qs = np.asarray(inp["L0_ffn_q_scale"], np.float32)
    ks = np.asarray(inp["L0_ffn_k_scale"], np.float32)
    lg = np.asarray(inp["L0_ffn_ln_g"], np.float32)
    lb = np.asarray(inp["L0_ffn_ln_b"], np.float32)
    w1r = np.zeros((2, 21, 128, 2048), np.float32)
    w1t = np.zeros((2, 128, 672), np.float32)
    b1r = np.zeros((2, 128, 44), np.float32)
    for lay in range(2):
        for pc in range(21):
            a = w1[lay][:, 128 * pc:128 * pc + 128]
            g = w1[lay][:, HID + 128 * pc:HID + 128 * pc + 128]
            sl = np.concatenate([a, g], axis=1)
            w1r[lay, pc] = sl.reshape(8, 128, 256).transpose(
                1, 0, 2).reshape(128, 2048)
            b1r[lay, 0:128, 2 * pc] = b1[lay][128 * pc:128 * pc + 128]
            b1r[lay, 0:128, 2 * pc + 1] = b1[lay][HID + 128 * pc:
                                                  HID + 128 * pc + 128]
        at = w1[lay][:, 2688:2730]
        gt = w1[lay][:, HID + 2688:HID + 2730]
        sl = np.concatenate([at, gt], axis=1)
        w1t[lay] = sl.reshape(8, 128, 84).transpose(1, 0, 2).reshape(128, 672)
        b1r[lay, 0:42, 42] = b1[lay][2688:2730]
        b1r[lay, 0:42, 43] = b1[lay][HID + 2688:HID + 2730]
    w2rr = np.zeros((2, 21, 128, 1024), np.float32)
    w2tt = np.zeros((2, 42, 1024), np.float32)
    for lay in range(2):
        for pc in range(21):
            w2rr[lay, pc] = w2[lay][128 * pc:128 * pc + 128, :]
        w2tt[lay] = w2[lay][2688:2730, :]
    m = dict(
        x=np.ascontiguousarray(x_fm_b),
        wqkv=np.ascontiguousarray(wqkv.reshape(2, 8, 128, 1536)),
        wo=np.ascontiguousarray(wo_.reshape(2, 4, 128, 1024)),
        w1r=w1r, w1t=w1t, w2r=w2rr, w2t=w2tt, b1r=b1r,
        b2c=np.ascontiguousarray(
            b2.reshape(2, 8, 128).transpose(0, 2, 1)),
        qsc=np.ascontiguousarray(
            np.tile(qs, (1, 2)).reshape(2, 128, 1)),
        ksc=np.ascontiguousarray(
            np.tile(ks, (1, 2)).reshape(2, 128, 1)),
        lgc=np.ascontiguousarray(lg.reshape(2, 8, 128).transpose(0, 2, 1)),
        lbc=np.ascontiguousarray(lb.reshape(2, 8, 128).transpose(0, 2, 1)),
    )
    m.update(l0_ffn_consts())
    return m


def run_l0_ffn(x_fm, inp):
    nc = build_l0_ffn()
    in_maps = [l0_ffn_inmap(x_fm[b], inp) for b in range(NCORES)]
    res = run_bass_kernel_spmd(nc, in_maps, core_ids=list(range(NCORES)))
    return np.stack([r["xo"] for r in res.results])



def build_l0_attn(debug=False, fused=None):
    nc = fused["nc"] if fused else bass.Bass("TRN2", target_bir_lowering=False)
    x = nc.dram_tensor("x", [D, T], F32, kind="ExternalInput").ap()
    wq = nc.dram_tensor("wq", [8, 128, D], F32, kind="ExternalInput").ap()
    wk = nc.dram_tensor("wk", [8, 128, D], F32, kind="ExternalInput").ap()
    wv = nc.dram_tensor("wv", [8, 128, D], F32, kind="ExternalInput").ap()
    wo = nc.dram_tensor("wo", [8, 128, D], F32, kind="ExternalInput").ap()
    wf1 = nc.dram_tensor("wf1", [8, 128, 4096], F32,
                         kind="ExternalInput").ap()
    wf2 = nc.dram_tensor("wf2", [8, 128, 4096], F32,
                         kind="ExternalInput").ap()
    thw1 = nc.dram_tensor("thw1", [8, 128, 48], F32,
                          kind="ExternalInput").ap()
    thw2 = nc.dram_tensor("thw2", [48, 512], F32, kind="ExternalInput").ap()
    bqr = nc.dram_tensor("bqr", [1, D], F32, kind="ExternalInput").ap()
    bkr = nc.dram_tensor("bkr", [1, D], F32, kind="ExternalInput").ap()
    bvr = nc.dram_tensor("bvr", [1, D], F32, kind="ExternalInput").ap()
    bor = nc.dram_tensor("bor", [128, 8], F32, kind="ExternalInput").ap()
    bf1r = nc.dram_tensor("bf1r", [128, 32], F32, kind="ExternalInput").ap()
    bf2r = nc.dram_tensor("bf2r", [128, 8], F32, kind="ExternalInput").ap()
    n1gr = nc.dram_tensor("n1gr", [128, 8], F32, kind="ExternalInput").ap()
    n1br = nc.dram_tensor("n1br", [128, 8], F32, kind="ExternalInput").ap()
    n2gr = nc.dram_tensor("n2gr", [128, 8], F32, kind="ExternalInput").ap()
    n2br = nc.dram_tensor("n2br", [128, 8], F32, kind="ExternalInput").ap()
    thbr = nc.dram_tensor("thbr", [1, 512], F32, kind="ExternalInput").ap()
    lbr = nc.dram_tensor("lbr", [1, 512], F32, kind="ExternalInput").ap()
    etar = nc.dram_tensor("etar", [1, 512], F32, kind="ExternalInput").ap()
    ltp = nc.dram_tensor("ltp", [128, 128], F32, kind="ExternalInput").ap()
    ltn = nc.dram_tensor("ltn", [128, 128], F32, kind="ExternalInput").ap()
    idenf = nc.dram_tensor("idenf", [128, 128], F32,
                           kind="ExternalInput").ap()
    gmask = nc.dram_tensor("gmask", [128, 128], F32,
                           kind="ExternalInput").ap()
    xo = nc.dram_tensor("xo", [D, T], F32, kind="ExternalOutput").ap()
    kk = "ExternalOutput" if debug else "Internal"
    if debug:
        dbg_th = nc.dram_tensor("dbg_th", [128, 512], F32,
                                kind="ExternalOutput").ap()
        dbg_lm = nc.dram_tensor("dbg_lm", [128, 512], F32,
                                kind="ExternalOutput").ap()
        dbg_phi = nc.dram_tensor("dbg_phi", [128, 512], F32,
                                 kind="ExternalOutput").ap()
        dbg_rt = nc.dram_tensor("dbg_rt", [128, 512], F32,
                                kind="ExternalOutput").ap()
        dbg_qre = nc.dram_tensor("dbg_qre", [128, 512], F32,
                                 kind="ExternalOutput").ap()
        dbg_kre = nc.dram_tensor("dbg_kre", [128, 512], F32,
                                 kind="ExternalOutput").ap()
        dbg_ure = nc.dram_tensor("dbg_ure", [128, 512], F32,
                                 kind="ExternalOutput").ap()
        dbg_uim = nc.dram_tensor("dbg_uim", [128, 512], F32,
                                 kind="ExternalOutput").ap()
        dbg_qtk = nc.dram_tensor("dbg_qtk", [128, 1024], BF16,
                                 kind="ExternalOutput").ap()
    xln_d = nc.dram_tensor("xln_scr", [D, T], F32, kind=kk).ap()
    q_d = nc.dram_tensor("q_scr", [T, D], BF16, kind=kk).ap()
    k_d = nc.dram_tensor("k_scr", [T, D], BF16, kind=kk).ap()
    v_d = nc.dram_tensor("v_scr", [T, D], BF16, kind=kk).ap()
    at_d = nc.dram_tensor("at_scr", [D, T], BF16, kind=kk).ap()

    NT = 8
    W = 512
    NCH = 32
    PI = float(np.pi)
    MAGIC = 12582912.0

    p = fused["p"] if fused else P(nc, ExitStack())
    wpool = p.sbuf("wpool", [128, 3 * 8 * 1024], BF16)
    wq_s = [wpool[:, 1024 * kc:1024 * (kc + 1)] for kc in range(8)]
    wk_s = [wpool[:, 8192 + 1024 * kc:8192 + 1024 * (kc + 1)]
            for kc in range(8)]
    wv_s = [wpool[:, 16384 + 1024 * kc:16384 + 1024 * (kc + 1)]
            for kc in range(8)]
    wf1sl = [wpool[:, 2048 * b:2048 * (b + 1)] for b in range(4)]
    wf2sl = [wpool[:, 8192 + 2048 * b:8192 + 2048 * (b + 1)]
             for b in range(4)]
    lsems4 = ("ldwfa", "ldwfb", "ldwfc", "ldwfd")
    h1pool = wpool[:, 16384:24576]
    bpool = p.sbuf("bpool", [128, 8192], BF16)
    bqb = bpool[:, 5120:6144]
    bkb = bpool[:, 6144:7168]
    bvb = bpool[:, 7168:8192]
    qkvt = bpool[:, 0:1024]
    h1t = [h1pool[:, 512 * i:512 * (i + 1)] for i in range(16)] + \
          [bpool[:, 512 * i:512 * (i + 1)] for i in range(16)]
    thw1_s = p.sbuf("thw1_s", [128, 8 * 48], F32)
    thw2_s = p.sbuf("thw2_s", [48, 512], F32)
    bo_s = p.sbuf("bo_s", [128, 8], F32)
    bf1_s = p.sbuf("bf1_s", [128, 32], F32)
    bf2_s = p.sbuf("bf2_s", [128, 8], F32)
    n1g_s = p.sbuf("n1g_s", [128, 8], F32)
    n1b_s = p.sbuf("n1b_s", [128, 8], F32)
    n2g_s = p.sbuf("n2g_s", [128, 8], F32)
    n2b_s = p.sbuf("n2b_s", [128, 8], F32)
    ltp_s = p.sbuf("ltp_s", [128, 128], F32)
    ltn_s = p.sbuf("ltn_s", [128, 128], F32)
    idf_s = p.sbuf("idf_s", [128, 128], F32)
    idb_s = p.sbuf("idb_s", [128, 128], BF16)
    gmask_s = p.sbuf("gmask_s", [128, 128], BF16)
    _bqkv_note = "bqb/bkb/bvb alias bpool tail; thbb etc alias xlnf"

    brow = p.sbuf("brow", [1, 1024], F32)
    ones_col = p.sbuf("ones_col", [128, 1], F32)
    ones_row = p.sbuf("ones_row", [1, 128], F32)
    eps5_col = p.sbuf("eps5_col", [128, 1], F32)
    eps6_col = p.sbuf("eps6_col", [128, 1], F32)
    pih_col = p.sbuf("pih_col", [128, 1], F32)

    xch = [[p.sbuf(f"x_{c}", [128, W], F32)] * 2 for c in range(8)]
    xlnf = [p.sbuf(f"xlnf_{c}", [128, W], F32) for c in range(8)]
    thbb = xlnf[0][:]
    lbb = xlnf[1][:]
    etab = xlnf[2][:]
    xnf = xlnf
    xn16 = [p.sbuf(f"xn16_{c}", [128, W], BF16) for c in range(8)]
    sq = [p.sbuf(f"sq_{b}", [128, W], F32) for b in range(2)]
    rb = p.sbuf("rb", [128, W], F32)
    mrb = p.sbuf("mrb", [128, W], F32)
    m_r = p.sbuf("m_r", [1, W], F32)
    q_r = p.sbuf("q_r", [1, W], F32)
    var_r = p.sbuf("var_r", [1, W], F32)
    rs_r = p.sbuf("rs_r", [1, W], F32)
    lnv_r = q_r
    mr_r = m_r
    midf = p.sbuf("midf", [48, T], F32)

    qtk = [p.sbuf(f"qtk_{b}", [128, 1024], BF16) for b in range(2)]
    ktk = [p.sbuf(f"ktk_{b}", [128, 1024], BF16) for b in range(2)]
    vtk = [p.sbuf(f"vtk_{b}", [128, 1024], BF16) for b in range(2)]
    _wo_list = None
    kphi = p.sbuf("kphi", [128, 1024], F32)
    wo_s = [qtk[0][:], qtk[1][:], ktk[0][:], ktk[1][:], vtk[0][:],
            vtk[1][:], kphi[:].bitcast(BF16)[:, 0:1024],
            kphi[:].bitcast(BF16)[:, 1024:2048]]
    tht = p.sbuf("tht", [128, 512], F32)
    lmt = p.sbuf("lmt", [128, 512], F32)
    phit = p.sbuf("phit", [128, 512], F32)
    rtt = p.sbuf("rtt", [128, 512], F32)
    uret = p.sbuf("uret", [128, 512], F32)
    uimt = p.sbuf("uimt", [128, 512], F32)
    scr1 = p.sbuf("scr1", [128, 512], F32)
    scr2 = p.sbuf("scr2", [128, 512], F32)
    qrot_re = bpool[:, 0:512]
    qrot_im = bpool[:, 512:1024]
    krot_re = bpool[:, 1024:1536]
    krot_imn = bpool[:, 1536:2048]
    efat = p.sbuf("efat", [128, 512], F32)
    rendb = p.sbuf("rendb", [128, 512], F32)
    rendr = p.sbuf("rendr", [1, 512], F32)
    eendt = p.sbuf("eendt", [128, 512], F32)
    kre_t = bpool[:, 2048:3072]
    rf = [p.sbuf(f"rf_{g}", [128, 128], F32) for g in range(4)]
    phif = [p.sbuf(f"phif_{g}", [128, 128], F32) for g in range(4)]
    efaf = [p.sbuf(f"efaf_{g}", [128, 128], F32) for g in range(4)]
    qr_f = [p.sbuf(f"qrf_{g}", [128, 128], BF16) for g in range(4)]
    qi_f = [p.sbuf(f"qif_{g}", [128, 128], BF16) for g in range(4)]
    kr_f = [p.sbuf(f"krf_{g}", [128, 128], BF16) for g in range(4)]
    ki_f = [p.sbuf(f"kif_{g}", [128, 128], BF16) for g in range(4)]
    pq_re = [p.sbuf(f"pqre_{g}", [128, 128], BF16) for g in range(4)]
    pq_im = [p.sbuf(f"pqim_{g}", [128, 128], BF16) for g in range(4)]
    qrall_re = bpool[:, 3072:4096]
    qrall_im = bpool[:, 4096:5120]
    krall_re = [p.sbuf(f"kar_{g}", [128, 128], BF16) for g in range(4)]
    krall_imn = [p.sbuf(f"kai_{g}", [128, 128], BF16) for g in range(4)]
    earg = kphi
    eargk = p.sbuf("eargk", [128, 128], F32)
    dend = [p.sbuf(f"dend_{g}", [128, 1], F32) for g in range(4)]
    uer_c = [p.sbuf(f"uerc_{g}", [128, 1], F32) for g in range(4)]
    uei_c = [p.sbuf(f"ueic_{g}", [128, 1], F32) for g in range(4)]
    col_s1 = p.sbuf("col_s1", [128, 1], F32)
    col_s2 = p.sbuf("col_s2", [128, 1], F32)
    cre = [p.sbuf(f"cre_{g}", [128, 64], BF16) for g in range(4)]
    cimn = [p.sbuf(f"cimn_{g}", [128, 64], BF16) for g in range(4)]
    tr1 = p.sbuf("tr1", [128, 64], F32)
    tr2 = p.sbuf("tr2", [128, 64], F32)
    vp = [bpool[:, 5120 + 256 * i:5120 + 256 * (i + 1)]
          for i in range(8)]
    gts = [bpool[:, 7168 + 128 * i:7168 + 128 * (i + 1)]
           for i in range(8)]
    of16 = [p.sbuf(f"of16_{c}", [128, 128], BF16) for c in range(8)]

    at16 = xn16
    hn16 = xn16
    vout = [tht, lmt]

    ps1 = p.psum("ps1", [128, W])
    ps2 = p.psum("ps2", [128, W])
    ps3 = p.psum("ps3", [128, W])
    ps4 = p.psum("ps4", [128, W])
    ps5 = p.psum("ps5", [128, W])
    ps6 = p.psum("ps6", [128, W])
    ps7 = p.psum("ps7", [128, W])
    pstr = p.psum("pstr", [128, 1024], BF16)

    for s in ("ldc", "ldw", "ldwv", "lda", "ldb", "lqa", "lqb", "ldwfa",
              "ldwfb", "ldwfc", "ldwfd", "pe", "dv", "ac", "stq", "stx",
              "stb", "sta0", "sta1", "xcp", "vpd", "dbgs"):
        p.sem(s)

    def body():
        for kc in range(8):
            p.dma("gpsimd", wq_s[kc], wq[kc], inc="ldw")
            p.dma("gpsimd", wk_s[kc], wk[kc], inc="ldw")
            p.dma("gpsimd", wv_s[kc], wv[kc], inc="ldw")
            p.dma("gpsimd", thw1_s[:, 48 * kc:48 * kc + 48], thw1[kc],
                  inc="ldw")
        p.dma("gpsimd", idb_s[:], idenf, inc="ldw")
        p.dma("gpsimd", gmask_s[:], gmask, inc="ldw")
        ldw_all = p.val("ldw")
        for dst, srcd in ((thw2_s, thw2), (bo_s, bor), (bf1_s, bf1r),
                          (bf2_s, bf2r), (n1g_s, n1gr), (n1b_s, n1br),
                          (n2g_s, n2gr), (n2b_s, n2br), (ltp_s, ltp),
                          (ltn_s, ltn), (idf_s, idenf)):
            p.dma("sync", dst[:], srcd, inc="ldwv")
        ldwv_all = p.val("ldwv")
        p.op("vector", "memset", ones_col[:], 1.0)
        p.op("vector", "memset", ones_row[:], 1.0)
        p.op("vector", "memset", eps5_col[:], 1e-5)
        p.op("vector", "memset", eps6_col[:], 1e-6)
        p.op("vector", "memset", pih_col[:], PI / 2.0)
        for g in range(4):
            p.op("vector", "memset", cre[g][:], 0.0)
            p.op("vector", "memset", cimn[g][:], 0.0)
        p.op("vector", "memset", ps1[:], 0.0)
        p.op("vector", "memset", ps2[:], 0.0, inc="dv")
        memset_done = p.val("dv")

        def bias_bcast(items):
            for name, srcd, dstt, width in items:
                p.wait("sync", "pe", p.val("pe"))
                p.dma("sync", brow[0:1, 0:width], srcd, inc="ldc")
                ldc_now = p.val("ldc")
                p.wait("tensor", "ldc", ldc_now)
                p.wait("tensor", "dv", memset_done)
                for seg in range(width // 512):
                    p.wait("tensor", "ac", p.val("ac"))
                    p.op("tensor", "matmul", ps1[:], ones_row[:],
                         brow[0:1, 512 * seg:512 * seg + 512],
                         start=True, stop=True, inc="pe")
                    p.wait("scalar", "pe")
                    p.op("scalar", "copy",
                         dstt[:, 512 * seg:512 * seg + 512], ps1[:],
                         inc="ac")
        bias_bcast((("bq", bqr, bqb, 1024), ("bk", bkr, bkb, 1024),
                    ("bv", bvr, bvb, 1024)))
        bias_done = p.val("ac")

        rec = {"rows_read_psum": 0, "copies_done": 0}

        def ln_rows(eps_ap):
            p.wait("vector", "pe")
            p.op("vector", "tensor_scalar", m_r[:], ps1[0:1, :],
                 1.0 / D, 0.0, ALU.mult, ALU.add)
            p.op("vector", "tensor_scalar", q_r[:], ps2[0:1, :],
                 1.0 / D, 0.0, ALU.mult, ALU.add, inc="dv")
            rec["rows_read_psum"] = p.val("dv")
            p.op("vector", "tensor_tensor", var_r[:], m_r[:], m_r[:],
                 ALU.mult)
            p.op("vector", "tensor_tensor", var_r[:], q_r[:], var_r[:],
                 ALU.subtract, inc="dv")
            p.wait("scalar", "dv")
            p.op("scalar", "activation", lnv_r[:], var_r[:], AF.Ln,
                 bias=eps_ap, scale=1.0)
            p.op("scalar", "activation", rs_r[:], lnv_r[:], AF.Exp,
                 bias=0.0, scale=-0.5, inc="ac")
            p.wait("vector", "ac")
            p.op("vector", "tensor_tensor", mr_r[:], m_r[:], rs_r[:],
                 ALU.mult, inc="dv")
            p.wait("tensor", "dv")
            p.wait("tensor", "ac", rec["copies_done"])
            p.op("tensor", "matmul", ps3[:], ones_row[:], rs_r[:],
                 start=True, stop=True)
            p.op("tensor", "matmul", ps4[:], ones_row[:], mr_r[:],
                 start=True, stop=True, inc="pe")
            p.wait("scalar", "pe")
            p.op("scalar", "copy", rb[:], ps3[:], inc="ac")
            p.op("scalar", "copy", mrb[:], ps4[:], inc="ac")
            rec["copies_done"] = p.val("ac")
            p.wait("vector", "ac")

        def ln_stats(srcs):
            pe_after = []
            for c in range(8):
                if c >= 2:
                    p.wait("scalar", "pe", pe_after[c - 2])
                p.op("scalar", "activation", sq[c % 2][:], srcs(c),
                     AF.Square, bias=0.0, scale=1.0, inc="ac")
                p.wait("tensor", "ac")
                p.op("tensor", "matmul", ps1[0:1, :], ones_col[:],
                     srcs(c), start=(c == 0), stop=(c == 7))
                p.op("tensor", "matmul", ps2[0:1, :], ones_col[:],
                     sq[c % 2][:], start=(c == 0), stop=(c == 7),
                     inc="pe")
                pe_after.append(p.val("pe"))

        for i in range(NT):
            buf = i % 2
            t0 = i * W
            ldsem = "lda" if buf == 0 else "ldb"
            if i >= 1:
                p.wait("sync", "dv", rec[f"xln_dn_{i-1}"])
            for c in range(8):
                p.dma("sync", xch[c][buf][:],
                      x[128 * c:128 * c + 128, t0:t0 + W], inc=ldsem)
            ld_tile = p.val(ldsem)
            p.wait("scalar", ldsem, ld_tile)
            p.wait("scalar", "dv", rec.get("apply_dn", 0))
            p.wait("tensor", "dv", max(rec["rows_read_psum"],
                                       memset_done))
            p.wait("tensor", ldsem, ld_tile)
            p.wait("tensor", "ac", bias_done)
            if i == 0:
                p.wait("vector", "ldwv", ldwv_all)
            ln_stats(lambda c: xch[c][buf][:])
            ln_rows(eps5_col[0:1, 0:1])
            for c in range(8):
                p.op("vector", "tensor_tensor", sq[c % 2][:],
                     xch[c][buf][:], rb[:], ALU.mult)
                p.op("vector", "tensor_tensor", xlnf[c][:], sq[c % 2][:],
                     mrb[:], ALU.subtract, inc="dv" if c == 7 else None)
            xln_dn = p.val("dv")
            rec[f"xln_dn_{i}"] = xln_dn
            p.wait("sync", "dv", xln_dn)
            for c in range(8):
                p.dma("sync", xln_d[128 * c:128 * c + 128, t0:t0 + W],
                      xlnf[c][:], inc="stx")
            p.wait("scalar", "dv", xln_dn)
            p.wait("tensor", "dv", xln_dn)
            ln_stats(lambda c: xlnf[c][:])
            ln_rows(eps6_col[0:1, 0:1])
            p.wait("vector", "stx", p.val("stx"))
            for c in range(8):
                p.op("vector", "tensor_tensor", sq[c % 2][:], xlnf[c][:],
                     rb[:], ALU.mult)
                p.op("vector", "tensor_tensor", xnf[c][:], sq[c % 2][:],
                     mrb[:], ALU.subtract)
                p.op("vector", "tensor_scalar", xnf[c][:], xnf[c][:],
                     n1g_s[:, c:c + 1], n1b_s[:, c:c + 1], ALU.mult,
                     ALU.add)
                p.op("vector", "tensor_copy", xn16[c][:], xnf[c][:],
                     inc="dv" if c == 7 else None)
            rec["apply_dn"] = p.val("dv")
            xn_dn = p.val("dv")

            p.wait("tensor", "dv", xn_dn)
            if i == 0:
                p.wait("tensor", "ldw", ldw_all)
            for (wmat, bb, dst_d) in ((wq_s, bqb, q_d), (wk_s, bkb, k_d),
                                      (wv_s, bvb, v_d)):
                for ts4 in range(4):
                    tb = t0 + 128 * ts4
                    for half in range(2):
                        bank = ps5 if half == 0 else ps6
                        p.wait("tensor", "dv",
                               rec.get(f"qkv_ev{half}", 0))
                        for kc in range(8):
                            p.op("tensor", "matmul", bank[:],
                                 xn16[kc][:, 128 * ts4:128 * ts4 + 128],
                                 wmat[kc][:, 512 * half:512 * half + 512],
                                 start=(kc == 0), stop=(kc == 7),
                                 inc="pe" if kc == 7 else None)
                        p.wait("vector", "pe")
                        if half == 0:
                            p.wait("vector", "stq",
                                   rec.get("qkvt_st", 0))
                        p.op("vector", "tensor_tensor",
                             qkvt[:, 512 * half:512 * half + 512],
                             bank[:], bb[:, 512 * half:512 * half + 512],
                             ALU.add, inc="dv")
                        rec[f"qkv_ev{half}"] = p.val("dv")
                    p.wait("sync", "dv")
                    p.dma("sync", dst_d[tb:tb + 128, :], qkvt[:],
                          inc="stq")
                    rec["qkvt_st"] = p.val("stq")
            p.wait("tensor", "dv", p.val("dv"))
            p.wait("tensor", "ac", rec.get("mid_ev", 0))
            for kc in range(8):
                p.op("tensor", "matmul", ps7[0:48, :],
                     thw1_s[:, 48 * kc:48 * kc + 48], xnf[kc][:],
                     start=(kc == 0), stop=(kc == 7),
                     inc="pe" if kc == 7 else None)
            p.wait("scalar", "pe")
            p.op("scalar", "activation", midf[:, t0:t0 + W],
                 ps7[0:48, :], AF.Tanh, bias=0.0, scale=1.0, inc="ac")
            rec["mid_ev"] = p.val("ac")
        phaseA_stq = p.val("stq")
        phaseA_stx = p.val("stx")

        p.wait("sync", "stq", phaseA_stq)
        p.wait("sync", "stx", phaseA_stx)
        p.wait("gpsimd", "stq", phaseA_stq)
        p.wait("sync", "dv", p.val("dv"))
        p.wait("sync", "ac", p.val("ac"))
        bias_bcast((("thb", thbr, thbb, 512), ("lb", lbr, lbb, 512),
                    ("eta", etar, etab, 512)))
        thb_done = p.val("ac")
        for n in range(NCH):
            buf = n % 2
            t0 = n * 128
            lq = "lqa" if buf == 0 else "lqb"
            if n >= 2:
                p.wait("sync", "pe", rec[f"chunk_pe_{n-2}"])
            p.dma("sync", qtk[buf][:], q_d[t0:t0 + 128, :], inc=lq)
            p.dma("sync", ktk[buf][:], k_d[t0:t0 + 128, :], inc=lq)
            p.dma("sync", vtk[buf][:], v_d[t0:t0 + 128, :], inc=lq)
            lq_now = p.val(lq)

            p.wait("tensor", "dv", rec.get("lam_read", 0))
            p.wait("tensor", "ac", max(thb_done, rec.get("mid_ev", 0)))
            p.op("tensor", "matmul", ps7[:], midf[:, t0:t0 + 128],
                 thw2_s[:], start=True, stop=True, inc="pe")
            p.wait("vector", "pe")
            p.op("vector", "tensor_tensor", tht[:], ps7[:], thbb[:],
                 ALU.add)
            p.op("vector", "tensor_scalar", tht[:], tht[:], PI / 2.0,
                 -PI / 2.0, ALU.min, ALU.max, inc="dv")
            th_dn = p.val("dv")
            p.wait("scalar", "dv", th_dn)
            p.op("scalar", "activation", scr1[:], tht[:], AF.Square,
                 bias=0.0, scale=1.0, inc="ac")
            p.wait("vector", "ac")
            p.op("vector", "tensor_tensor", lmt[:], etab[:], scr1[:],
                 ALU.mult)
            p.op("vector", "tensor_tensor", lmt[:], lmt[:], lbb[:],
                 ALU.add, inc="dv")
            lm_dn = p.val("dv")
            if debug and n == 0:
                p.wait("sync", "dv", lm_dn)
                p.dma("sync", dbg_th, tht[:], inc="stq")
                p.dma("sync", dbg_lm, lmt[:], inc="stq")
                p.wait("tensor", "stq", p.val("stq"))
            p.wait("tensor", "dv", lm_dn)
            if n == 0:
                p.wait("tensor", "ldwv", ldwv_all)
            p.wait("tensor", "ac", rec.get("cums_ev", 0))
            p.op("tensor", "matmul", ps5[:], ltp_s[:], tht[:],
                 start=True, stop=True)
            p.op("tensor", "matmul", ps6[:], ltn_s[:], lmt[:],
                 start=True, stop=True, inc="pe")
            p.wait("scalar", "pe")
            p.op("scalar", "copy", phit[:], ps5[:], inc="ac")
            p.op("scalar", "copy", rtt[:], ps6[:], inc="ac")
            rec["cums_ev"] = p.val("ac")
            cums_dn = p.val("ac")
            p.wait("vector", "ac", cums_dn)
            p.op("vector", "tensor_scalar", scr1[:], phit[:],
                 1.0 / (2 * PI), MAGIC, ALU.mult, ALU.add)
            p.op("vector", "tensor_scalar", scr1[:], scr1[:], 1.0,
                 -MAGIC, ALU.mult, ALU.add)
            p.op("vector", "scalar_tensor_tensor", scr1[:], scr1[:],
                 -2 * PI, phit[:], ALU.mult, ALU.add, inc="dv")
            p.wait("scalar", "dv")
            p.op("scalar", "activation", uimt[:], scr1[:], AF.Sin,
                 bias=0.0, scale=1.0, inc="ac")
            p.op("vector", "tensor_scalar", scr2[:], phit[:],
                 1.0 / (2 * PI), 0.25, ALU.mult, ALU.add)
            p.op("vector", "tensor_scalar", scr2[:], scr2[:], 1.0,
                 MAGIC, ALU.mult, ALU.add)
            p.op("vector", "tensor_scalar", scr2[:], scr2[:], 1.0,
                 -MAGIC, ALU.mult, ALU.add)
            p.op("vector", "tensor_scalar", scr2[:], scr2[:], 1.0,
                 -0.25, ALU.mult, ALU.add)
            p.op("vector", "scalar_tensor_tensor", scr2[:], scr2[:],
                 -2 * PI, phit[:], ALU.mult, ALU.add, inc="dv")
            p.wait("scalar", "dv")
            p.op("scalar", "activation", uret[:], scr2[:], AF.Sin,
                 bias=0.0, scale=1.0, inc="ac")
            trig_dn = p.val("ac")

            p.wait("vector", lq, lq_now)
            p.wait("vector", "ac", trig_dn)
            p.wait("vector", "pe", rec.get("rotread", 0))
            p.op("vector", "tensor_scalar", kphi[:], ktk[buf][:], 0.0,
                 0.0, ALU.min, ALU.add, inc="dv")
            p.wait("scalar", "dv")
            p.op("scalar", "activation", kphi[:], kphi[:], AF.Exp,
                 bias=0.0, scale=1.0, inc="ac")
            p.wait("vector", "ac")
            p.op("vector", "scalar_tensor_tensor", kphi[:], ktk[buf][:],
                 0.0, kphi[:], ALU.max, ALU.add)
            qv = qtk[buf].rearrange("p (h c) -> p h c", c=64)
            kv = kphi.rearrange("p (h c) -> p h c", c=64)
            urv = uret.rearrange("p (h j) -> p h j", j=32)
            uiv = uimt.rearrange("p (h j) -> p h j", j=32)
            s1v = scr1.rearrange("p (h j) -> p h j", j=32)
            s2v = scr2.rearrange("p (h j) -> p h j", j=32)
            for (srcv, dre, dimm, neg) in (
                    (qv, qrot_re, qrot_im, False),
                    (kv, krot_re, krot_imn, True)):
                p.op("vector", "tensor_tensor", s1v, srcv[:, :, 0:32],
                     urv, ALU.mult)
                p.op("vector", "tensor_tensor", s2v, srcv[:, :, 32:64],
                     uiv, ALU.mult)
                p.op("vector", "tensor_tensor", dre[:], scr1[:],
                     scr2[:], ALU.add)
                p.op("vector", "tensor_tensor", s1v, srcv[:, :, 0:32],
                     uiv, ALU.mult)
                p.op("vector", "tensor_tensor", s2v, srcv[:, :, 32:64],
                     urv, ALU.mult)
                p.op("vector", "tensor_tensor", dimm[:], scr1[:],
                     scr2[:], ALU.subtract,
                     inc="dv" if neg else None)
            rot_dn = p.val("dv")
            if debug and n == 0:
                p.wait("sync", "dv", rot_dn)
                p.wait("gpsimd", "dv", rot_dn)
                p.dma("sync", dbg_phi, phit[:], inc="stq")
                p.dma("sync", dbg_rt, rtt[:], inc="stq")
                p.dma("gpsimd", dbg_qre, qrot_re, inc="dbgs")
                p.dma("gpsimd", dbg_kre, krot_re, inc="dbgs")
                p.dma("gpsimd", dbg_ure, uret[:], inc="dbgs")
                p.dma("gpsimd", dbg_uim, uimt[:], inc="dbgs")
                p.dma("gpsimd", dbg_qtk, qtk[buf][:], inc="dbgs")
                p.wait("tensor", "stq", p.val("stq"))
                p.wait("vector", "dbgs", p.val("dbgs"))
                p.wait("tensor", "dbgs", p.val("dbgs"))
                p.wait("gpsimd", "dbgs", p.val("dbgs"))

            p.wait("tensor", "dv", rot_dn)
            for g in range(4):
                p.wait("tensor", "ac", rec.get("tp_ev", 0))
                p.op("tensor", "matmul", ps7[:, 0:128],
                     rtt[:, 128 * g:128 * g + 128], idf_s[:],
                     start=True, stop=True)
                p.op("tensor", "matmul", ps7[:, 128:256],
                     phit[:, 128 * g:128 * g + 128], idf_s[:],
                     start=True, stop=True, inc="pe")
                for si, (srct, dstf) in enumerate(
                        ((qrot_re, qr_f[g]), (qrot_im, qi_f[g]),
                         (krot_re, kr_f[g]), (krot_imn, ki_f[g]))):
                    p.op("tensor", "transpose",
                         pstr[:, 128 * si:128 * si + 128],
                         srct[:, 128 * g:128 * g + 128], idb_s[:],
                         inc="pe" if si == 3 else None)
                p.wait("scalar", "pe")
                p.op("scalar", "copy", rf[g][:], ps7[:, 0:128])
                p.op("scalar", "copy", phif[g][:], ps7[:, 128:256])
                for si, dstf in enumerate((qr_f[g], qi_f[g], kr_f[g],
                                           ki_f[g])):
                    p.op("scalar", "copy", dstf[:],
                         pstr[:, 128 * si:128 * si + 128],
                         inc="ac" if si == 3 else None)
                rec["tp_ev"] = p.val("ac")
            tp_dn = p.val("ac")

            p.wait("tensor", "dv", p.val("dv"))
            p.wait("tensor", "ac", max(tp_dn, rec.get("eend_ev", 0)))
            for g in range(4):
                p.op("tensor", "matmul", ps7[0:1, 128 * g:128 * g + 128],
                     rf[g][:, 127:128], idf_s[:], start=True, stop=True,
                     inc="pe" if g == 3 else None)
            p.wait("scalar", "pe")
            p.op("scalar", "copy", rendr[:], ps7[0:1, 0:512], inc="ac")
            p.wait("tensor", "ac")
            p.op("tensor", "matmul", ps7[:], ones_row[:], rendr[:],
                 start=True, stop=True, inc="pe")
            p.wait("scalar", "pe")
            p.op("scalar", "copy", rendb[:], ps7[:], inc="ac")
            rec["eend_ev"] = p.val("ac")
            p.wait("vector", "ac", rec["eend_ev"])
            p.op("vector", "tensor_tensor", scr2[:], rendb[:], rtt[:],
                 ALU.subtract, inc="dv")
            rec["lam_read"] = p.val("dv")
            p.wait("scalar", "dv")
            p.op("scalar", "activation", eendt[:], scr2[:], AF.Exp,
                 bias=0.0, scale=1.0, inc="ac")
            p.wait("vector", "ac")
            kev = kre_t.rearrange("p (h c) -> p h c", c=64)
            p.op("vector", "tensor_tensor", kev[:, :, 0:32],
                 krot_re.rearrange("p (h j) -> p h j", j=32),
                 eendt[:].rearrange("p (h j) -> p h j", j=32), ALU.mult)
            p.op("vector", "tensor_tensor", kev[:, :, 32:64],
                 krot_imn.rearrange("p (h j) -> p h j", j=32),
                 eendt[:].rearrange("p (h j) -> p h j", j=32), ALU.mult,
                 inc="dv")
            kre_dn = p.val("dv")

            p.wait("gpsimd", "pe", rec.get("y_pe", 0))
            for pp in range(8):
                vsrc = v_d[t0 + 16 * pp:t0 + 16 * pp + 16, :].rearrange(
                    "r (hi g vc) -> g r hi vc", hi=4, g=4, vc=64)
                for g2 in range(4):
                    p.dma("gpsimd",
                          vp[pp][32 * g2:32 * g2 + 16, :].rearrange(
                              "r (hi vc) -> r hi vc", vc=64),
                          vsrc[g2], inc="vpd")
            vp_dn = p.val("vpd")
            p.wait("tensor", "vpd", vp_dn)

            gb = [ps1, ps2]
            for g in range(4):
                p.wait("scalar", "ac", tp_dn)
                p.wait("scalar", "dv", rec.get("pq_dn", 0))
                p.op("scalar", "activation", efaf[g][:], rf[g][:],
                     AF.Exp, bias=0.0, scale=1.0, inc="ac")
                p.wait("vector", "ac")
                p.wait("vector", "pe", rec.get("y_pe", 0))
                p.op("vector", "tensor_tensor", pq_re[g][:], qr_f[g][:],
                     efaf[g][:], ALU.mult)
                p.op("vector", "tensor_tensor", pq_im[g][:], qi_f[g][:],
                     efaf[g][:], ALU.mult, inc="dv")
                rec["pq_dn"] = p.val("dv")
                p.op("vector", "tensor_tensor",
                     eargk[:].rearrange("p (b d) -> p b d", d=16),
                     _anchor16(rf[g]),
                     rf[g][:].rearrange("p (b d) -> p b d", d=16),
                     ALU.subtract, inc="dv")
                p.wait("scalar", "dv")
                p.op("scalar", "activation", eargk[:], eargk[:], AF.Exp,
                     bias=0.0, scale=1.0, inc="ac")
                p.wait("vector", "ac")
                p.op("vector", "tensor_tensor", krall_re[g][:],
                     kr_f[g][:], eargk[:], ALU.mult)
                p.op("vector", "tensor_tensor", krall_imn[g][:],
                     ki_f[g][:], eargk[:], ALU.mult)
                p.op("vector", "tensor_tensor",
                     earg[:].rearrange("p (b t) -> p b t", t=128),
                     _tile8(rf[g]), _anchor8(rf[g]), ALU.subtract)
                p.op("vector", "tensor_scalar", earg[:], earg[:],
                     80.0, 0.0, ALU.min, ALU.add, inc="dv")
                p.wait("scalar", "dv")
                p.op("scalar", "activation", earg[:], earg[:], AF.Exp,
                     bias=0.0, scale=1.0, inc="ac")
                p.wait("vector", "ac")
                p.op("vector", "tensor_tensor",
                     qrall_re.rearrange("p (b t) -> p b t", t=128),
                     earg[:].rearrange("p (b t) -> p b t", t=128),
                     _tile8(qr_f[g]), ALU.mult)
                p.op("vector", "tensor_tensor",
                     qrall_im.rearrange("p (b t) -> p b t", t=128),
                     earg[:].rearrange("p (b t) -> p b t", t=128),
                     _tile8(qi_f[g]), ALU.mult)
                p.op("vector", "tensor_copy", col_s1[:],
                     phif[g][:, 127:128])
                p.op("vector", "tensor_scalar", col_s1[:], col_s1[:],
                     1.0 / (2 * PI), MAGIC, ALU.mult, ALU.add)
                p.op("vector", "tensor_scalar", col_s1[:], col_s1[:],
                     1.0, -MAGIC, ALU.mult, ALU.add)
                p.op("vector", "scalar_tensor_tensor", col_s1[:],
                     col_s1[:], -2 * PI, phif[g][:, 127:128], ALU.mult,
                     ALU.add)
                p.op("vector", "tensor_copy", col_s2[:],
                     phif[g][:, 127:128])
                p.op("vector", "tensor_scalar", col_s2[:], col_s2[:],
                     1.0 / (2 * PI), 0.25, ALU.mult, ALU.add)
                p.op("vector", "tensor_scalar", col_s2[:], col_s2[:],
                     1.0, MAGIC, ALU.mult, ALU.add)
                p.op("vector", "tensor_scalar", col_s2[:], col_s2[:],
                     1.0, -MAGIC, ALU.mult, ALU.add)
                p.op("vector", "tensor_scalar", col_s2[:], col_s2[:],
                     1.0, -0.25, ALU.mult, ALU.add)
                p.op("vector", "scalar_tensor_tensor", col_s2[:],
                     col_s2[:], -2 * PI, phif[g][:, 127:128], ALU.mult,
                     ALU.add, inc="dv")
                fb_dn = p.val("dv")
                rec["fb_dn"] = fb_dn
                p.wait("scalar", "dv", fb_dn)
                p.op("scalar", "activation", uei_c[g][:], col_s1[:],
                     AF.Sin, bias=0.0, scale=1.0)
                p.op("scalar", "activation", uer_c[g][:], col_s2[:],
                     AF.Sin, bias=0.0, scale=1.0)
                p.op("scalar", "activation", dend[g][:],
                     rf[g][:, 127:128], AF.Exp, bias=0.0, scale=1.0,
                     inc="ac")
                cols_dn = p.val("ac")

                p.wait("tensor", "dv", fb_dn)
                p.wait("tensor", "ac", cols_dn)
                for pp in range(8):
                    par = pp % 2
                    nwid = 128 - 16 * pp
                    p.wait("tensor", "dv", rec.get(f"g_ev{par}", 0))
                    for hh in range(4):
                        base = 32 * hh
                        tp = (96, 96) if hh == 3 else None
                        p.op("tensor", "matmul",
                             gb[par][base:base + 8, 0:8],
                             krall_re[g][base:base + 32,
                                         16 * pp:16 * pp + 8],
                             qrall_re[base:base + 32,
                                      128 * pp + 16 * pp:
                                      128 * pp + 16 * pp + 8],
                             start=True, stop=False,
                             tile_position=tp)
                        p.op("tensor", "matmul",
                             gb[par][base:base + 8, 0:8],
                             krall_imn[g][base:base + 32,
                                          16 * pp:16 * pp + 8],
                             qrall_im[base:base + 32,
                                      128 * pp + 16 * pp:
                                      128 * pp + 16 * pp + 8],
                             start=False, stop=True,
                             tile_position=tp)
                        p.op("tensor", "matmul",
                             gb[par][base:base + 16, 8:nwid],
                             krall_re[g][base:base + 32,
                                         16 * pp:16 * pp + 16],
                             qrall_re[base:base + 32,
                                      128 * pp + 16 * pp + 8:
                                      128 * pp + 128],
                             start=True, stop=False,
                             tile_position=tp)
                        p.op("tensor", "matmul",
                             gb[par][base:base + 16, 8:nwid],
                             krall_imn[g][base:base + 32,
                                          16 * pp:16 * pp + 16],
                             qrall_im[base:base + 32,
                                      128 * pp + 16 * pp + 8:
                                      128 * pp + 128],
                             start=False, stop=True,
                             tile_position=tp,
                             inc="pe" if hh == 3 else None)
                    p.wait("vector", "pe")
                    p.op("vector", "tensor_tensor", gts[pp][:, 0:nwid],
                         gb[par][:, 0:nwid], gmask_s[:, 0:nwid],
                         ALU.mult, inc="dv")
                    rec[f"g_ev{par}"] = p.val("dv")
                g_dn = p.val("dv")

                p.wait("tensor", "dv", g_dn)
                for hh in range(4):
                    h = 4 * g + hh
                    base = 32 * hh
                    tp = (96, 0) if hh == 3 else None
                    p.wait("tensor", "ac", rec.get("y_ev", 0))
                    p.op("tensor", "matmul", ps3[0:64, 0:128],
                         cre[g][base:base + 32, :],
                         pq_re[g][base:base + 32, :],
                         start=True, stop=False, tile_position=tp)
                    p.op("tensor", "matmul", ps3[0:64, 0:128],
                         cimn[g][base:base + 32, :],
                         pq_im[g][base:base + 32, :],
                         start=False, stop=False, tile_position=tp)
                    for pp in range(8):
                        nwid = 128 - 16 * pp
                        p.op("tensor", "matmul",
                             ps3[0:64, 16 * pp:128],
                             vp[pp][base:base + 16,
                                    64 * (h // 4):64 * (h // 4) + 64],
                             gts[pp][base:base + 16, 0:nwid],
                             start=False, stop=(pp == 7),
                             tile_position=tp,
                             inc="pe" if pp == 7 else None)
                    p.wait("scalar", "pe")
                    p.wait("scalar", "stb", rec.get("stb_prev", 0))
                    p.op("scalar", "copy",
                         of16[h // 2][64 * (h % 2):64 * (h % 2) + 64, :],
                         ps3[0:64, 0:128], inc="ac")
                    rec["y_ev"] = p.val("ac")
                rec["y_pe"] = p.val("pe")

                p.wait("tensor", "dv", max(kre_dn,
                                           rec.get(f"u_read_{g}", 0)))
                for hh in range(4):
                    h = 4 * g + hh
                    tp = (0, 96) if hh == 3 else None
                    p.op("tensor", "matmul",
                         ps4[32 * hh:32 * hh + 32, 0:64],
                         kre_t[:, 64 * h:64 * h + 32],
                         vtk[buf][:, 64 * h:64 * h + 64],
                         start=True, stop=True, tile_position=tp)
                    p.op("tensor", "matmul",
                         ps4[32 * hh:32 * hh + 32, 64:128],
                         kre_t[:, 64 * h + 32:64 * h + 64],
                         vtk[buf][:, 64 * h:64 * h + 64],
                         start=True, stop=True, tile_position=tp,
                         inc="pe" if hh == 3 else None)
                p.wait("vector", "pe")
                p.op("vector", "scalar_tensor_tensor", tr1[:],
                     cre[g][:], dend[g][:, 0:1], ps4[:, 0:64],
                     ALU.mult, ALU.add)
                p.op("vector", "scalar_tensor_tensor", tr2[:],
                     cimn[g][:], dend[g][:, 0:1], ps4[:, 64:128],
                     ALU.mult, ALU.add, inc="dv")
                p.op("vector", "tensor_scalar", cre[g][:], tr1[:],
                     uer_c[g][:, 0:1], 0.0, ALU.mult, ALU.add)
                p.op("vector", "scalar_tensor_tensor", cre[g][:],
                     tr2[:], uei_c[g][:, 0:1], cre[g][:], ALU.mult,
                     ALU.add)
                p.op("vector", "tensor_scalar", cimn[g][:], tr2[:],
                     uer_c[g][:, 0:1], 0.0, ALU.mult, ALU.add)
                p.op("vector", "tensor_scalar", tr1[:], tr1[:],
                     uei_c[g][:, 0:1], 0.0, ALU.mult, ALU.add)
                p.op("vector", "tensor_tensor", cimn[g][:], cimn[g][:],
                     tr1[:], ALU.subtract, inc="dv")
                rec[f"u_read_{g}"] = p.val("dv")
            rec["rotread"] = p.val("pe")

            y_all = p.val("ac")
            p.wait("sync", "ac", y_all)
            for c in range(8):
                p.dma("sync", at_d[128 * c:128 * c + 128, t0:t0 + 128],
                      of16[c][:], inc="stb")
            rec["stb_prev"] = p.val("stb")
            rec[f"chunk_pe_{n}"] = p.val("pe")
        phaseB_stq = p.val("stb")

        p.wait("sync", "stb", phaseB_stq)
        p.wait("gpsimd", "pe", p.val("pe"))
        p.wait("gpsimd", "dv", p.val("dv"))
        for kc in range(8):
            p.dma("gpsimd", wo_s[kc], wo[kc], inc="ldw")
        ldw_c = p.val("ldw")
        for c in range(8):
            p.dma("sync", xo[128 * c:128 * c + 128, :],
                  x[128 * c:128 * c + 128, :], inc="xcp")
        for i in range(NT):
            buf = i % 2
            t0 = i * W
            ldsem = "lda" if buf == 0 else "ldb"
            if i >= 1:
                p.wait("sync", "dv", rec["c_tile_dn"])
            if i >= 2:
                p.wait("sync", "sta0", rec[f"sta0_{i-2}"])
                p.wait("sync", "sta1", rec[f"sta1_{i-2}"])
            for c in range(8):
                p.dma("sync", xlnf[c][:],
                      xln_d[128 * c:128 * c + 128, t0:t0 + W], inc=ldsem)
                p.dma("sync", at16[c][:],
                      at_d[128 * c:128 * c + 128, t0:t0 + W], inc=ldsem)
            ld_tile = p.val(ldsem)
            p.wait("tensor", ldsem, ld_tile)
            if i == 0:
                p.wait("tensor", "ldw", ldw_c)
            p.wait("tensor", "dv", rec.get("x1_dn", 0))
            for oc in range(8):
                bank = ps5 if oc % 2 == 0 else ps6
                p.wait("tensor", "dv", rec.get(f"wo_ev{oc % 2}", 0))
                for kc in range(8):
                    p.op("tensor", "matmul", bank[:],
                         wo_s[kc][:, 128 * oc:128 * oc + 128],
                         at16[kc][:], start=(kc == 0), stop=(kc == 7),
                         inc="pe" if kc == 7 else None)
                p.wait("vector", "pe")
                p.op("vector", "scalar_tensor_tensor", xlnf[oc][:],
                     bank[:], bo_s[:, oc:oc + 1], xlnf[oc][:],
                     ALU.add, ALU.add, inc="dv")
                rec[f"wo_ev{oc % 2}"] = p.val("dv")
            x1_dn = p.val("dv")
            rec["x1_dn"] = x1_dn
            p.wait("scalar", "dv", x1_dn)
            p.wait("tensor", "dv", x1_dn)
            ln_stats(lambda c: xlnf[c][:])
            ln_rows(eps6_col[0:1, 0:1])
            for c in range(8):
                p.op("vector", "tensor_tensor", sq[c % 2][:], xlnf[c][:],
                     rb[:], ALU.mult)
                p.op("vector", "tensor_tensor", sq[c % 2][:],
                     sq[c % 2][:], mrb[:], ALU.subtract)
                p.op("vector", "tensor_scalar", hn16[c][:], sq[c % 2][:],
                     n2g_s[:, c:c + 1], n2b_s[:, c:c + 1], ALU.mult,
                     ALU.add, inc="dv" if c == 7 else None)
            hn_dn = p.val("dv")
            p.wait("tensor", "dv", hn_dn)
            units = [(pcu, phu) for pcu in range(8) for phu in range(2)]
            for ui, (pc, ph) in enumerate(units):
                wb = ui % 4
                lsem = lsems4[wb]
                p.wait("gpsimd", "pe", rec.get(f"wf1_used_{wb}", 0))
                p.wait("gpsimd", lsem, p.val(lsem))
                p.dma("gpsimd", wf1sl[wb],
                      wf1[pc][:, 2048 * ph:2048 * ph + 2048], inc=lsem)
                ldwf_now = p.val(lsem)
                p.wait("tensor", lsem, ldwf_now)
                for sub in range(4):
                    bank = [ps1, ps2, ps3, ps4][sub]
                    if ph == 0:
                        p.wait("tensor", "ac", rec.get(f"h1_ev{sub}", 0))
                    for kk2 in range(4):
                        kc = 4 * ph + kk2
                        p.op("tensor", "matmul", bank[:],
                             wf1sl[wb][:, 512 * kk2 + 128 * sub:
                                       512 * kk2 + 128 * sub + 128],
                             hn16[kc][:], start=(kc == 0), stop=(kc == 7),
                             inc="pe" if kk2 == 3 else None)
                    rec[f"wf1_used_{wb}"] = p.val("pe")
                    if ph == 1:
                        p.wait("scalar", "pe")
                        p.op("scalar", "activation", h1t[4 * pc + sub],
                             bank[:], AF.Gelu_apprx_tanh,
                             bias=bf1_s[:, 4 * pc + sub:4 * pc + sub + 1],
                             scale=1.0, inc="ac")
                        rec[f"h1_ev{sub}"] = p.val("ac")
            h1_dn = p.val("ac")
            p.wait("tensor", "ac", h1_dn)
            for half in range(2):
                for pc2 in range(8):
                    wb = pc2 % 4
                    lsem = lsems4[wb]
                    p.wait("gpsimd", "pe", rec.get(f"wf2_used_{wb}", 0))
                    p.wait("gpsimd", lsem, p.val(lsem))
                    p.dma("gpsimd",
                          wf2sl[wb].rearrange("p (kk c) -> p kk c", c=512),
                          wf2[pc2].rearrange("p (kk c) -> p kk c", c=1024)
                          [:, :, 512 * half:512 * half + 512], inc=lsem)
                    ldwf_now = p.val(lsem)
                    p.wait("tensor", lsem, ldwf_now)
                    for kk in range(4):
                        for q4 in range(4):
                            bankq = [ps1, ps2, ps3, ps4][q4]
                            if pc2 == 0 and kk == 0:
                                p.wait("tensor", "dv",
                                       rec.get(f"ffn_out{q4}", 0))
                            p.op("tensor", "matmul", bankq[:],
                                 wf2sl[wb][:, 512 * kk + 128 * q4:
                                           512 * kk + 128 * q4 + 128],
                                 h1t[4 * pc2 + kk],
                                 start=(pc2 == 0 and kk == 0),
                                 stop=(pc2 == 7 and kk == 3),
                                 inc="pe" if q4 == 3 else None)
                    rec[f"wf2_used_{wb}"] = p.val("pe")
                w2_pe = p.val("pe")
                p.wait("vector", "pe", w2_pe)
                if i == 0 and half == 0:
                    p.wait("gpsimd", "xcp", 128)
                for q4 in range(4):
                    oc = 4 * half + q4
                    bankq = [ps1, ps2, ps3, ps4][q4]
                    sb = oc % 2
                    stsem = "sta0" if sb == 0 else "sta1"
                    if f"store_{stsem}" in rec:
                        p.wait("vector", stsem, rec[f"store_{stsem}"])
                    p.op("vector", "scalar_tensor_tensor", vout[sb][:],
                         bankq[:], bf2_s[:, oc:oc + 1], xlnf[oc][:],
                         ALU.add, ALU.add, inc="dv")
                    rec[f"ffn_out{q4}"] = p.val("dv")
                    p.wait("gpsimd", "dv")
                    p.dma("gpsimd",
                          xo[128 * oc:128 * oc + 128, t0:t0 + W],
                          vout[sb][:], accum_op=ALU.add, inc=stsem)
                    rec[f"store_{stsem}"] = p.val(stsem)
            rec[f"sta0_{i}"] = p.val("sta0")
            rec[f"sta1_{i}"] = p.val("sta1")
            rec["c_tile_dn"] = p.val("dv")
        p.wait("gpsimd", "sta0", p.val("sta0"))
        p.wait("gpsimd", "sta1", p.val("sta1"))

    if fused:
        return body
    emit_all(p, body)
    return nc


def _anchor16(hT):
    return hT[:].rearrange("p (b d) -> p b d", d=16)[:, :, 8:9].broadcast_to(
        [128, 8, 16])


def _anchor8(hT):
    return hT[:].rearrange("p (b d) -> p b d", d=16)[:, :, 8:9].broadcast_to(
        [128, 8, 128])


def _tile8(hT):
    return hT[:].unsqueeze(1).broadcast_to([128, 8, 128])


def l0_attn_consts():
    s = np.arange(128)[:, None]
    t = np.arange(128)[None, :]
    ltp = (s <= t).astype(np.float32)
    ltn = -ltp
    idenf = np.eye(128, dtype=np.float32)
    gmask = np.zeros((128, 128), dtype=np.float32)
    for g in range(4):
        for rr in range(16):
            gmask[32 * g + rr, rr:] = 1.0
    return ltp, ltn, idenf, gmask


def l0_attn_inmap(x_fm_b, inp):
    perm = np.zeros(1024, dtype=np.int64)
    for h in range(16):
        for j in range(32):
            perm[64 * h + j] = 64 * h + 2 * j
            perm[64 * h + 32 + j] = 64 * h + 2 * j + 1
    wq = np.asarray(inp["L0_attn_Wq"], np.float32)[:, perm]
    wk = np.asarray(inp["L0_attn_Wk"], np.float32)[:, perm]
    wv = np.asarray(inp["L0_attn_Wv"], np.float32)
    wo_ = np.asarray(inp["L0_attn_Wo"], np.float32)
    wf1_ = np.asarray(inp["L0_attn_Wf1"], np.float32)
    wf2_ = np.asarray(inp["L0_attn_Wf2"], np.float32)
    thw1_ = np.asarray(inp["L0_attn_theta_w1"], np.float32)
    thw2_ = np.asarray(inp["L0_attn_theta_w2"], np.float32)
    wf1r = np.zeros((8, 128, 4096), np.float32)
    wf2r = np.zeros((8, 128, 4096), np.float32)
    for pc in range(8):
        sl = wf1_[:, 512 * pc:512 * pc + 512]
        wf1r[pc] = sl.reshape(8, 128, 512).transpose(1, 0, 2).reshape(
            128, 4096)
        s2 = wf2_[512 * pc:512 * pc + 512, :]
        wf2r[pc] = s2.reshape(4, 128, 1024).transpose(1, 0, 2).reshape(
            128, 4096)
    ltp, ltn, idenf, gmask = l0_attn_consts()

    def col8(v):
        return np.ascontiguousarray(np.asarray(v, np.float32).reshape(
            8, 128).T)
    return dict(
        x=np.ascontiguousarray(x_fm_b),
        wq=np.ascontiguousarray(wq.reshape(8, 128, 1024)),
        wk=np.ascontiguousarray(wk.reshape(8, 128, 1024)),
        wv=np.ascontiguousarray(wv.reshape(8, 128, 1024)),
        wo=np.ascontiguousarray(wo_.reshape(8, 128, 1024)),
        wf1=wf1r, wf2=wf2r,
        thw1=np.ascontiguousarray(thw1_.reshape(8, 128, 48)),
        thw2=np.ascontiguousarray(thw2_),
        bqr=np.ascontiguousarray(
            np.asarray(inp["L0_attn_bq"], np.float32)[perm][None, :]),
        bkr=np.ascontiguousarray(
            np.asarray(inp["L0_attn_bk"], np.float32)[perm][None, :]),
        bvr=np.ascontiguousarray(
            np.asarray(inp["L0_attn_bv"], np.float32)[None, :]),
        bor=col8(inp["L0_attn_bo"]),
        bf1r=np.ascontiguousarray(
            np.asarray(inp["L0_attn_bf1"], np.float32).reshape(32, 128).T),
        bf2r=col8(inp["L0_attn_bf2"]),
        n1gr=col8(inp["L0_attn_n1g"]), n1br=col8(inp["L0_attn_n1b"]),
        n2gr=col8(inp["L0_attn_n2g"]), n2br=col8(inp["L0_attn_n2b"]),
        thbr=np.ascontiguousarray(
            np.asarray(inp["L0_attn_theta_base"], np.float32).reshape(1, 512)),
        lbr=np.ascontiguousarray(
            np.asarray(inp["L0_attn_lambda_base"],
                       np.float32).reshape(1, 512)),
        etar=np.ascontiguousarray(
            np.asarray(inp["L0_attn_eta"], np.float32).reshape(1, 512)),
        ltp=ltp, ltn=ltn, idenf=idenf, gmask=gmask,
    )


def run_l0_attn(x_fm, inp):
    nc = build_l0_attn()
    in_maps = [l0_attn_inmap(x_fm[b], inp) for b in range(NCORES)]
    res = run_bass_kernel_spmd(nc, in_maps, core_ids=list(range(NCORES)))
    return np.stack([r["xo"] for r in res.results])
